# version 02
# baseline (speedup 1.0000x reference)
import math
import numpy as np
import ml_dtypes
import concourse.bass as bass
import concourse.mybir as mybir
from concourse.bass_utils import run_bass_kernel_spmd

F32 = mybir.dt.float32
BF16 = mybir.dt.bfloat16
AF = mybir.ActivationFunctionType
ALU = mybir.AluOpType
AX = mybir.AxisListType
BFDT = ml_dtypes.bfloat16

NCORE = 8
TOK = 2048
NT = TOK // 128
D = 1024
NF = D // 128
SEQ = 8192
EPS = 1e-5


class Res:
    __slots__ = ("name", "last_w", "readers")
    def __init__(self, name=""):
        self.name = name
        self.last_w = None
        self.readers = []

class Op:
    __slots__ = ("idx", "stream", "timeline", "kind", "fn", "deps", "inc", "val", "tag",
                 "async_")
    def __init__(self, idx, stream, timeline, kind, fn, tag=""):
        self.async_ = False
        self.idx = idx
        self.stream = stream
        self.timeline = timeline
        self.kind = kind
        self.fn = fn
        self.deps = []
        self.inc = False
        self.val = None
        self.tag = tag

COMPUTE_STREAMS = ("pe", "act", "dve", "pool", "sp")

class Em:
    def __init__(self, nc, chans):
        self.nc = nc
        self.ops = []
        self.streams = {s: [] for s in COMPUTE_STREAMS}
        self.chan_stream = dict(chans)
        self.chan_count = {c: 0 for c in chans}
        self.chan_last = {c: None for c in chans}
        self.grp_rr = {}

    def _mkop(self, stream, timeline, kind, fn, reads, writes, deps, tag):
        op = Op(len(self.ops), stream, timeline, kind, fn, tag)
        dd = []
        for r in reads:
            if r.last_w is not None:
                dd.append(r.last_w)
        for w in writes:
            if w.last_w is not None:
                dd.append(w.last_w)
            dd.extend(w.readers)
        dd.extend(deps)
        keep = []
        seen = set()
        for d in dd:
            if d is op:
                continue
            if (d.timeline == op.timeline and d.kind == "c" and op.kind == "c"
                    and d.timeline == "pe"):
                continue
            if (d.kind == "c" and op.kind == "d" and d.stream == op.stream
                    and not d.async_):
                continue
            if id(d) not in seen:
                seen.add(id(d))
                keep.append(d)
        op.deps = keep
        for r in reads:
            r.readers.append(op)
        for w in writes:
            w.last_w = op
            w.readers = []
        self.ops.append(op)
        self.streams[stream].append(op)
        return op

    def c(self, stream, fn, reads=(), writes=(), deps=(), tag="", async_=False):
        op = self._mkop(stream, stream, "c", fn, reads, writes, deps, tag)
        op.async_ = async_
        return op

    def d(self, chan, out_ap, in_ap, reads=(), writes=(), deps=(), tag="", **kw):
        if isinstance(chan, (list, tuple)):
            key = id(chan)
            i = self.grp_rr.get(key, 0)
            self.grp_rr[key] = i + 1
            chan = chan[i % len(chan)]
        stream = self.chan_stream[chan]
        fn = lambda eng, o=out_ap, i=in_ap, kw=kw: eng.dma_start(out=o, in_=i, **kw)
        deps = list(deps)
        if self.chan_last[chan] is not None:
            deps.append(self.chan_last[chan])
        op = self._mkop(stream, chan, "d", fn, reads, writes, deps, tag)
        self.chan_last[chan] = op
        return op

    def barrier(self):
        lasts = {}
        for op in self.ops:
            if op.kind != "w":
                lasts[op.timeline] = op
        ops = list(lasts.values())
        for s in COMPUTE_STREAMS:
            def mk(eng):
                return None
            o = Op(len(self.ops), s, s, "w", mk, "barrier")
            o.deps = [d for d in ops if not (d.timeline == s and d.kind == "c")]
            self.ops.append(o)
            self.streams[s].append(o)

    def finalize_and_emit(self, block, engines):
        for op in self.ops:
            for dpp in op.deps:
                dpp.inc = True
        cnt = {}
        for op in self.ops:
            if op.kind == "d":
                c = cnt.get(op.timeline, 0) + 16
                cnt[op.timeline] = c
                op.val = c
                op.inc = True
            elif op.inc and op.kind != "w":
                c = cnt.get(op.timeline, 0) + 1
                cnt[op.timeline] = c
                op.val = c
        used = sorted(set(op.timeline for op in self.ops if op.inc))
        sems = {t: self.nc.alloc_semaphore(name=f"sem_{t}") for t in used}
        self.sems = sems
        self.final_counts = dict(cnt)

        def run_stream(eng, stream_name):
            waited = {}
            for op in self.streams[stream_name]:
                for dpp in op.deps:
                    sem = sems[dpp.timeline]
                    if waited.get(dpp.timeline, 0) < dpp.val:
                        eng.wait_ge(sem, dpp.val)
                        waited[dpp.timeline] = dpp.val
                if op.kind == "w":
                    continue
                instr = op.fn(eng)
                if op.inc:
                    instr.then_inc(sems[op.timeline], 16 if op.kind == "d" else 1)

        @block.sync
        def _(sync):
            run_stream(sync, "sp")

        @block.tensor
        def _(t):
            run_stream(t, "pe")

        @block.scalar
        def _(s):
            run_stream(s, "act")

        @block.vector
        def _(v):
            run_stream(v, "dve")

        @block.gpsimd
        def _(g):
            run_stream(g, "pool")



class H:
    pass


_ALLOC_N = [0]

def alloc_sb(ctx, nc, name, shape, dtype):
    _ALLOC_N[0] += 1
    t = ctx.enter_context(nc.sbuf_tensor(f"{name}_{_ALLOC_N[0]}", shape, dtype))
    return t, Res(name)


def alloc_ps(ctx, nc, name, shape, dtype):
    _ALLOC_N[0] += 1
    t = ctx.enter_context(nc.psum_tensor(f"{name}_{_ALLOC_N[0]}", shape, dtype))
    return t, Res(name)


def emit_transpose(em, h, src_ap, dst_ap, psum, psum_res, src_res, dst_res, evac="dve"):
    n = src_ap.shape[1]
    p = src_ap.shape[0]
    pt = psum[0:n, 0:p]
    idn = h.ident[0:p, 0:p]
    em.c("pe", lambda t, o=pt, i=src_ap, e=idn: t.transpose(o, i, e),
         reads=[src_res, h.ident_res], writes=[psum_res], tag="tr")
    em.c(evac, lambda v, o=dst_ap, i=pt: v.tensor_copy(o, i),
         reads=[psum_res], writes=[dst_res], tag="tr_ev")


def emit_ln(em, h, src_ap, src_res, outs, scratch, tag="ln"):
    sq, sq_res = scratch["sq"]
    cols, cols_res = scratch["cols"]
    ssum = cols[:, 0:1]; ssq = cols[:, 1:2]; mean = cols[:, 2:3]
    varr = cols[:, 3:4]; rstd = cols[:, 4:5]; negmr = cols[:, 5:6]
    em.c("dve", lambda v: v.reduce_sum(ssum, src_ap, axis=AX.X),
         reads=[src_res], writes=[cols_res], tag=tag + "_sum")
    em.c("act", lambda s: s.activation(sq[:, :], src_ap, AF.Square, accum_out=ssq),
         reads=[src_res], writes=[sq_res, cols_res], tag=tag + "_sq")
    em.c("dve", lambda v: v.tensor_scalar(mean, ssum, 1.0 / 1024.0, None, ALU.mult),
         reads=[cols_res], writes=[cols_res], tag=tag)
    em.c("dve", lambda v: v.tensor_scalar(varr, ssq, 1.0 / 1024.0, EPS, ALU.mult, ALU.add),
         reads=[cols_res], writes=[cols_res], tag=tag)
    tmp = cols[:, 6:7]
    em.c("dve", lambda v: v.tensor_tensor(tmp, mean, mean, ALU.mult),
         reads=[cols_res], writes=[cols_res], tag=tag)
    em.c("dve", lambda v: v.tensor_tensor(varr, varr, tmp, ALU.subtract),
         reads=[cols_res], writes=[cols_res], tag=tag)
    em.c("act", lambda s: s.activation(rstd, varr, AF.Ln),
         reads=[cols_res], writes=[cols_res], tag=tag + "_ln")
    em.c("act", lambda s: s.activation(rstd, rstd, AF.Exp, scale=-0.5),
         reads=[cols_res], writes=[cols_res], tag=tag + "_exp")
    em.c("dve", lambda v: v.scalar_tensor_tensor(negmr, mean, -1.0, rstd, ALU.mult, ALU.mult),
         reads=[cols_res], writes=[cols_res], tag=tag)
    for dst_ap, dst_res in outs:
        em.c("dve", lambda v, o=dst_ap: v.tensor_scalar(o, src_ap, rstd, negmr, ALU.mult, ALU.add),
             reads=[src_res, cols_res], writes=[dst_res], tag=tag + "_app")



def host_consts():
    c = {}
    c["ident"] = np.eye(128, dtype=BFDT)
    c["ones_row"] = np.ones((1, 512), dtype=BFDT)
    c["ones64c"] = np.ones((64, 1), dtype=BFDT)
    c["ones1x64f"] = np.ones((1, 64), dtype=np.float32)
    c["ones1x128f"] = np.ones((1, 128), dtype=np.float32)
    halfA = np.zeros((128, 1), dtype=BFDT); halfA[0:64] = 1
    halfB = np.zeros((128, 1), dtype=BFDT); halfB[64:128] = 1
    c["halfA"] = halfA
    c["halfB"] = halfB
    return c


def declare_consts(nc, ctx, h):
    specs = [
        ("ident", [128, 128], BF16),
        ("ones_row", [1, 512], BF16),
        ("ones64c", [64, 1], BF16),
        ("ones1x64f", [1, 64], F32),
        ("ones1x128f", [1, 128], F32),
        ("halfA", [128, 1], BF16),
        ("halfB", [128, 1], BF16),
    ]
    h.const_dram = {}
    for name, shape, dt in specs:
        dram = nc.dram_tensor("c_" + name, shape, dt, kind="ExternalInput")
        sb, res = alloc_sb(ctx, nc, "s_" + name, shape, dt)
        h.const_dram[name] = dram
        setattr(h, name, sb)
        setattr(h, name + "_res", res)

def load_consts(em, h):
    for name in h.const_dram:
        dram = h.const_dram[name]
        sb = getattr(h, name)
        res = getattr(h, name + "_res")
        em.d("sk", sb[:, :], dram[:, :], writes=[res], tag="const")



A_PHASES = {0, 1, 2, 3, 4}

def emit_blockA(em, h, ctx, nc, outer_ctx):
    import contextlib

    XHAT, XHAT_res = alloc_sb(outer_ctx, nc, "XHAT", [128, NT * 1024], BF16)
    WOUT, WOUT_res = alloc_sb(outer_ctx, nc, "WOUT", [128, 8 * 1024], BF16)
    BOUT, BOUT_res = alloc_sb(outer_ctx, nc, "BOUT", [1, 1024], BF16)
    SQ, SQ_res = alloc_sb(outer_ctx, nc, "SQln", [128, 1024], F32)
    COLS, COLS_res = alloc_sb(outer_ctx, nc, "COLS", [128, 16], F32)
    ln_scratch = {"sq": (SQ, SQ_res), "cols": (COLS, COLS_res)}
    QMX, QMX_res = alloc_sb(outer_ctx, nc, "QMX", [1, 16], F32)
    KMX, KMX_res = alloc_sb(outer_ctx, nc, "KMX", [1, 16], F32)
    NTMP, NTMP_res = alloc_sb(outer_ctx, nc, "NTMP", [1, 4], F32)
    em.c("dve", lambda v: v.memset(QMX[:, :], 0.0), writes=[QMX_res])
    em.c("dve", lambda v: v.memset(KMX[:, :], 0.0), writes=[KMX_res])

    PS0, PS0_res = alloc_ps(outer_ctx, nc, "PS0", [128, 1024], F32)
    PS1, PS1_res = alloc_ps(outer_ctx, nc, "PS1", [128, 1024], F32)
    PS2, PS2_res = alloc_ps(outer_ctx, nc, "PS2", [128, 1024], F32)
    PS3f, PS3f_res = alloc_ps(outer_ctx, nc, "PS3f", [128, 512], F32)
    PSt, PSt_res = alloc_ps(outer_ctx, nc, "PSt", [128, 1024], BF16)

    for f in range(NF):
        em.d("pw", WOUT[:, 1024 * f:1024 * (f + 1)], h.out_wT[128 * f:128 * (f + 1), :],
             writes=[WOUT_res], tag="woutld")
    em.d("pw", BOUT[:, :], h.out_b[:, :], writes=[BOUT_res])

    with contextlib.ExitStack() as ph12:
        XT, XT_res = alloc_sb(ph12, nc, "XT", [128, NF * 2048], BF16)
        WQKV, WQKV_res = alloc_sb(ph12, nc, "WQKV", [128, 8 * 3072], BF16)
        BQKV, BQKV_res = alloc_sb(ph12, nc, "BQKV", [1, 3072], BF16)
        XBH0, XBH0_res = alloc_sb(ph12, nc, "XBH0", [128, 4 * 1024], BF16)
        XBH1, XBH1_res = alloc_sb(ph12, nc, "XBH1", [128, 4 * 1024], BF16)
        XBT0, XBT0_res = alloc_sb(ph12, nc, "XBT0", [128, 8 * 512], BF16)
        XBT1, XBT1_res = alloc_sb(ph12, nc, "XBT1", [128, 8 * 512], BF16)
        KVE, KVE_res = alloc_sb(ph12, nc, "KVE", [128, 4 * 512], BF16)
        SQ2, SQ2_res = alloc_sb(ph12, nc, "SQ2", [128, 512], BF16)
        XBHs = [(XBH0, XBH0_res), (XBH1, XBH1_res)]
        XBTs = [(XBT0, XBT0_res), (XBT1, XBT1_res)]

        for kk in range(NF):
            em.d("pw", WQKV[:, 3072 * kk:3072 * (kk + 1)],
                 h.qkv_wT[128 * kk:128 * (kk + 1), :], writes=[WQKV_res], tag="wqkvld")
        em.d("pw", BQKV[:, :], h.qkv_b[:, :], writes=[BQKV_res])

        for t in range(NT):
            em.d("px", XHAT[:, 1024 * t:1024 * (t + 1)],
                 h.x_own[128 * t:128 * (t + 1), :], writes=[XHAT_res], tag="xload")
        for t in range(NT):
            for f in range(NF):
                src = XHAT[:, 1024 * t + 128 * f:1024 * t + 128 * (f + 1)]
                dst = XT[:, 2048 * f + 128 * t:2048 * f + 128 * (t + 1)]
                emit_transpose(em, h, src, dst, PSt, PSt_res, XHAT_res, XT_res)

        pspool = [(PS0, PS0_res), (PS1, PS1_res), (PS2, PS2_res)]
        pi = 0
        for f in range(NF if 1 in A_PHASES else 0):
            for s in range(4):
                ps, ps_res = pspool[pi % 3]; pi += 1
                pp = ps[:, 0:512]
                for kk in range(NF):
                    lhsT = WQKV[:, 3072 * kk + 128 * f:3072 * kk + 128 * (f + 1)]
                    rhs = XT[:, 2048 * kk + 512 * s:2048 * kk + 512 * (s + 1)]
                    em.c("pe", lambda t, o=pp, l=lhsT, r=rhs, st=(kk == 0): t.matmul(
                        o, l, r, start=st, stop=False),
                        reads=[WQKV_res, XT_res], writes=[ps_res], tag="qproj")
                brow = BQKV[0:1, 128 * f:128 * (f + 1)]
                em.c("pe", lambda t, o=pp, l=brow: t.matmul(
                    o, l, h.ones_row[0:1, :], start=False, stop=True),
                    reads=[BQKV_res, h.ones_row_res], writes=[ps_res], tag="qbias")
                qe = KVE[:, 0:512]
                em.c("act", lambda s_, o=qe, i=pp: s_.copy(o, i),
                     reads=[ps_res], writes=[KVE_res], tag="qevac")
                em.d("sq", h.Q_dram[128 * f:128 * (f + 1), 512 * s:512 * (s + 1)],
                     qe, reads=[KVE_res], writes=[h.Q_dram_res], tag="qstore")
                em.c("act", lambda s_, i=qe: s_.activation(SQ2[:, :], i, AF.Square),
                     reads=[KVE_res], writes=[SQ2_res], tag="qsq")
                for half, hh in ((h.halfA[:, :], 0), (h.halfB[:, :], 1)):
                    head = 2 * f + hh
                    pn = PS3f[0:1, 0:512]
                    em.c("pe", lambda t, l=half: t.matmul(pn, l, SQ2[:, :], start=True, stop=True),
                         reads=[SQ2_res, h.halfA_res, h.halfB_res], writes=[PS3f_res], tag="qn")
                    em.c("dve", lambda v: v.reduce_max(NTMP[0:1, 0:1], pn, axis=AX.X),
                         reads=[PS3f_res], writes=[NTMP_res], tag="qnev")
                    qmc = QMX[0:1, head:head + 1]
                    em.c("dve", lambda v, o=qmc: v.tensor_tensor(o, o, NTMP[0:1, 0:1], ALU.max),
                         reads=[NTMP_res, QMX_res], writes=[QMX_res], tag="qnmax")

        for s16 in range(16 if 2 in A_PHASES else 0):
            par = s16 % 2
            XBH, XBH_res = XBHs[par]
            XBT, XBT_res = XBTs[par]
            for j in range(4):
                tok0 = 512 * s16 + 128 * j
                em.d("px", XBH[:, 1024 * j:1024 * (j + 1)],
                     h.x_batch[tok0:tok0 + 128, :], writes=[XBH_res], tag="xbload")
            for j in range(4):
                for f in range(NF):
                    src = XBH[:, 1024 * j + 128 * f:1024 * j + 128 * (f + 1)]
                    dst = XBT[:, 512 * f + 128 * j:512 * f + 128 * (j + 1)]
                    emit_transpose(em, h, src, dst, PSt, PSt_res, XBH_res, XBT_res)
            for kvi, (dram, dram_res) in ((1, (h.K_dram, h.K_dram_res)),
                                          (2, (h.V_dram, h.V_dram_res))):
                for f in range(NF):
                    ps, ps_res = pspool[pi % 3]; pi += 1
                    pp = ps[:, 0:512]
                    outc = 1024 * kvi + 128 * f
                    for kk in range(NF):
                        lhsT = WQKV[:, 3072 * kk + outc:3072 * kk + outc + 128]
                        rhs = XBT[:, 512 * kk:512 * (kk + 1)]
                        em.c("pe", lambda t, o=pp, l=lhsT, r=rhs, st=(kk == 0): t.matmul(
                            o, l, r, start=st, stop=False),
                            reads=[WQKV_res, XBT_res], writes=[ps_res], tag="kvproj")
                    brow = BQKV[0:1, outc:outc + 128]
                    em.c("pe", lambda t, o=pp, l=brow: t.matmul(
                        o, l, h.ones_row[0:1, :], start=False, stop=True),
                        reads=[BQKV_res, h.ones_row_res], writes=[ps_res], tag="kvbias")
                    kve = KVE[:, 512 * (kvi - 1 + 2 * (f % 2)):512 * (kvi + 2 * (f % 2))]
                    em.c("act", lambda s_, o=kve, i=pp: s_.copy(o, i),
                         reads=[ps_res], writes=[KVE_res], tag="kvevac")
                    em.d("sk", dram[128 * f:128 * (f + 1), 512 * s16:512 * (s16 + 1)],
                         kve, reads=[KVE_res], writes=[dram_res], tag="kvstore")
                    if kvi == 1:
                        em.c("act", lambda s_, i=kve: s_.activation(SQ2[:, :], i, AF.Square),
                             reads=[KVE_res], writes=[SQ2_res], tag="ksq")
                        for half, hh in ((h.halfA[:, :], 0), (h.halfB[:, :], 1)):
                            head = 2 * f + hh
                            pn = PS3f[0:1, 0:512]
                            em.c("pe", lambda t, l=half: t.matmul(pn, l, SQ2[:, :],
                                                                  start=True, stop=True),
                                 reads=[SQ2_res, h.halfA_res, h.halfB_res],
                                 writes=[PS3f_res], tag="kn")
                            em.c("dve", lambda v: v.reduce_max(NTMP[0:1, 1:2], pn, axis=AX.X),
                                 reads=[PS3f_res], writes=[NTMP_res], tag="knev")
                            kmc = KMX[0:1, head:head + 1]
                            em.c("dve", lambda v, o=kmc: v.tensor_tensor(o, o, NTMP[0:1, 1:2], ALU.max),
                                 reads=[NTMP_res, KMX_res], writes=[KMX_res], tag="knmax")

    em.barrier()

    with contextlib.ExitStack() as ph34:
        ATT, ATT_res = alloc_sb(ph34, nc, "ATT", [128, 8 * 2048], BF16)
        KH0, KH0_res = alloc_sb(ph34, nc, "KH0", [64, 8192], BF16)
        KH1, KH1_res = alloc_sb(ph34, nc, "KH1", [64, 8192], BF16)
        VH0, VH0_res = alloc_sb(ph34, nc, "VH0", [64, 8192], BF16)
        VH1, VH1_res = alloc_sb(ph34, nc, "VH1", [64, 8192], BF16)
        QH0, QH0_res = alloc_sb(ph34, nc, "QH0", [64, 2048], BF16)
        QH1, QH1_res = alloc_sb(ph34, nc, "QH1", [64, 2048], BF16)
        VA, VA_res = alloc_sb(ph34, nc, "VA", [128, 64 * 65], BF16)
        PT0, PT0_res = alloc_sb(ph34, nc, "PTb0", [128, 1024], BF16)
        PT1, PT1_res = alloc_sb(ph34, nc, "PTb1", [128, 1024], BF16)
        RB, RB_res = alloc_sb(ph34, nc, "RB", [64, 1024], F32)
        RCP, RCP_res = alloc_sb(ph34, nc, "RCP", [1, 1024], F32)
        BCOL, BCOL_res = alloc_sb(ph34, nc, "BCOL", [128, 16], F32)
        SCL, SCL_res = alloc_sb(ph34, nc, "SCL", [1, 16], F32)
        KHs = [(KH0, KH0_res), (KH1, KH1_res)]
        VHs = [(VH0, VH0_res), (VH1, VH1_res)]
        QHs = [(QH0, QH0_res), (QH1, QH1_res)]
        PTs = [(PT0, PT0_res), (PT1, PT1_res)]

        em.c("pool", lambda g: g.memset(VA[:, :], 1.0), writes=[VA_res], tag="vaones")

        def issue_head_loads(hd):
            KH, KH_res = KHs[hd % 2]
            VH, VH_res = VHs[hd % 2]
            QH, QH_res = QHs[hd % 2]
            em.d("sk", KH[:, :], h.K_dram[64 * hd:64 * (hd + 1), :],
                 reads=[h.K_dram_res], writes=[KH_res], tag="khld")
            em.d("sk", VH[:, :], h.V_dram[64 * hd:64 * (hd + 1), :],
                 reads=[h.V_dram_res], writes=[VH_res], tag="vhld")
            em.d("sq", QH[:, :], h.Q_dram[64 * hd:64 * (hd + 1), :],
                 reads=[h.Q_dram_res], writes=[QH_res], tag="qhld")

        nheads = 16 if 3 in A_PHASES else 0
        if nheads:
            issue_head_loads(0)
            issue_head_loads(1)
        for hd in range(nheads):
            KH, KH_res = KHs[hd % 2]
            VH, VH_res = VHs[hd % 2]
            QH, QH_res = QHs[hd % 2]
            pr = SCL[0:1, 2:3]; mm_ = SCL[0:1, 3:4]
            em.c("dve", lambda v, a=QMX[0:1, hd:hd + 1], b=KMX[0:1, hd:hd + 1]:
                 v.tensor_tensor(pr, a, b, ALU.mult),
                 reads=[QMX_res, KMX_res], writes=[SCL_res])
            em.c("act", lambda s: s.activation(mm_, pr, AF.Ln),
                 reads=[SCL_res], writes=[SCL_res])
            em.c("act", lambda s: s.activation(mm_, mm_, AF.Exp, scale=0.5),
                 reads=[SCL_res], writes=[SCL_res])
            pb = PS3f[0:128, 0:1]
            em.c("pe", lambda t: t.matmul(pb, h.ones1x128f[0:1, :], mm_, start=True, stop=True),
                 reads=[SCL_res, h.ones1x128f_res], writes=[PS3f_res], tag="bcolmm")
            bcol = BCOL[:, hd:hd + 1]
            em.c("act", lambda s, o=bcol: s.activation(o, pb, AF.Copy, scale=-0.125),
                 reads=[PS3f_res], writes=[BCOL_res], tag="bcol")

            for kt in range(64):
                src = VH[:, 128 * kt:128 * (kt + 1)]
                dst = VA[:, 65 * kt:65 * kt + 64]
                emit_transpose(em, h, src, dst, PSt, PSt_res, VH_res, VA_res)

            for qh in range(2):
                PA, PA_res = PS0, PS0_res
                for kt in range(64):
                    PB, PB_res = (PS1, PS1_res) if kt % 2 == 0 else (PS2, PS2_res)
                    PT, PT_res = PTs[kt % 2]
                    lhsT = KH[:, 128 * kt:128 * (kt + 1)]
                    for j in range(2):
                        rhs = QH[:, 1024 * qh + 512 * j:1024 * qh + 512 * (j + 1)]
                        em.c("pe", lambda t, o=PB[:, 512 * j:512 * (j + 1)], l=lhsT, r=rhs:
                             t.matmul(o, l, r, start=True, stop=True),
                             reads=[KH_res, QH_res], writes=[PB_res], tag="scores")
                    em.c("act", lambda s, o=PT[:, :], i=PB[:, :], b=BCOL[:, hd:hd + 1]:
                         s.activation(o, i, AF.Exp, bias=b, scale=0.125),
                         reads=[PB_res, BCOL_res], writes=[PT_res], tag="exp")
                    va = VA[:, 65 * kt:65 * (kt + 1)]
                    for j in range(2):
                        em.c("pe", lambda t, o=PA[0:65, 512 * j:512 * (j + 1)], l=va,
                             r=PT[:, 512 * j:512 * (j + 1)], st=(kt == 0), sp=(kt == 63):
                             t.matmul(o, l, r, start=st, stop=sp),
                             reads=[VA_res, PT_res], writes=[PA_res], tag="pv")
                em.c("dve", lambda v: v.reciprocal(RCP[0:1, :], PA[64:65, :]),
                     reads=[PA_res], writes=[RCP_res], tag="recip")
                for j in range(2):
                    pbc = PS3f[0:64, 0:512]
                    em.c("pe", lambda t, r=RCP[0:1, 512 * j:512 * (j + 1)]:
                         t.matmul(pbc, h.ones1x64f[0:1, :], r, start=True, stop=True),
                         reads=[RCP_res, h.ones1x64f_res], writes=[PS3f_res], tag="rbmm")
                    em.c("dve", lambda v, o=RB[:, 512 * j:512 * (j + 1)]: v.tensor_copy(o, pbc),
                         reads=[PS3f_res], writes=[RB_res], tag="rbev")
                attd = ATT[64 * (hd % 2):64 * (hd % 2) + 64,
                           2048 * (hd // 2) + 1024 * qh:2048 * (hd // 2) + 1024 * (qh + 1)]
                em.c("dve", lambda v, o=attd: v.tensor_tensor(o, PA[0:64, :], RB[:, :], ALU.mult),
                     reads=[PA_res, RB_res], writes=[ATT_res], tag="attnorm")
            if hd + 2 < nheads:
                issue_head_loads(hd + 2)

        T1, T1_res = alloc_sb(ph34, nc, "T1", [128, 1024], F32)
        XO0, XO0_res = alloc_sb(ph34, nc, "XO0", [128, 1024], F32)
        XO1, XO1_res = alloc_sb(ph34, nc, "XO1", [128, 1024], F32)
        XOs = [(XO0, XO0_res), (XO1, XO1_res)]
        for tt in range(NT if 4 in A_PHASES else 0):
            ps, ps_res = (PS1, PS1_res) if tt % 2 == 0 else (PS2, PS2_res)
            for f in range(NF):
                lhsT = ATT[:, 2048 * f + 128 * tt:2048 * f + 128 * (tt + 1)]
                for j in range(2):
                    rhs = WOUT[:, 1024 * f + 512 * j:1024 * f + 512 * (j + 1)]
                    em.c("pe", lambda t, o=ps[:, 512 * j:512 * (j + 1)], l=lhsT, r=rhs,
                         st=(f == 0): t.matmul(o, l, r, start=st, stop=False),
                         reads=[ATT_res, WOUT_res], writes=[ps_res], tag="oproj")
            for j in range(2):
                em.c("pe", lambda t, o=ps[:, 512 * j:512 * (j + 1)],
                     r=BOUT[0:1, 512 * j:512 * (j + 1)]: t.matmul(
                     o, h.ones_row[0:1, 0:128], r, start=False, stop=True),
                     reads=[BOUT_res, h.ones_row_res], writes=[ps_res], tag="obias")
            xh = XHAT[:, 1024 * tt:1024 * (tt + 1)]
            em.c("dve", lambda v, i=ps[:, :], x=xh: v.tensor_tensor(T1[:, :], i, x, ALU.add),
                 reads=[ps_res, XHAT_res], writes=[T1_res], tag="resid")
            XO, XO_res = XOs[tt % 2]
            emit_ln(em, h, T1[:, :], T1_res, [(XO[:, :], XO_res)], ln_scratch, tag="lnA")
            em.d("so", h.xa[128 * tt:128 * (tt + 1), :], XO[:, :],
                 reads=[XO_res], writes=[h.xa_res], tag="xastore")
        if 4 not in A_PHASES:
            for tt in range(NT):
                em.d("px", h.xa[128 * tt:128 * (tt + 1), :],
                     XHAT[:, 1024 * tt:1024 * (tt + 1)],
                     reads=[XHAT_res], writes=[h.xa_res], tag="xadump")


def build_A(core):
    import contextlib
    nc = bass.Bass("TRN2", target_bir_lowering=False)
    h = H()
    h.x_own = nc.dram_tensor("x_own", [TOK, D], F32, kind="ExternalInput")
    h.x_batch = nc.dram_tensor("x_batch", [SEQ, D], F32, kind="ExternalInput")
    h.qkv_wT = nc.dram_tensor("qkv_wT", [D, 3 * D], F32, kind="ExternalInput")
    h.qkv_b = nc.dram_tensor("qkv_b", [1, 3 * D], F32, kind="ExternalInput")
    h.out_wT = nc.dram_tensor("out_wT", [D, D], F32, kind="ExternalInput")
    h.out_b = nc.dram_tensor("out_b", [1, D], F32, kind="ExternalInput")
    h.xa = nc.dram_tensor("xa", [TOK, D], F32, kind="ExternalOutput")
    h.xa_res = Res("xa")
    h.K_dram = nc.dram_tensor("K_dram", [D, SEQ], BF16)
    h.V_dram = nc.dram_tensor("V_dram", [D, SEQ], BF16)
    h.Q_dram = nc.dram_tensor("Q_dram", [D, TOK], BF16)
    h.K_dram_res = Res("K_dram")
    h.V_dram_res = Res("V_dram")
    h.Q_dram_res = Res("Q_dram")

    chans = {"pw": "pool", "px": "pool", "sk": "sp", "sq": "sp", "so": "sp"}
    em = Em(nc, chans)
    with contextlib.ExitStack() as outer:
        declare_consts(nc, outer, h)
        load_consts(em, h)
        emit_blockA(em, h, None, nc, outer)
        em.barrier()
        with nc.Block() as block:
            em.finalize_and_emit(block, None)
    return nc


def prep_A_inputs(inputs, core):
    b = core // 4
    q0 = 2048 * (core % 4)
    x = np.asarray(inputs["L0_attn_x"])
    cc = host_consts()
    m = {
        "x_own": np.ascontiguousarray(x[b, q0:q0 + 2048, :]),
        "x_batch": np.ascontiguousarray(x[b]),
        "qkv_wT": np.ascontiguousarray(np.asarray(inputs["L0_attn_qkv_w"]).T),
        "qkv_b": np.asarray(inputs["L0_attn_qkv_b"]).reshape(1, -1),
        "out_wT": np.ascontiguousarray(np.asarray(inputs["L0_attn_out_w"]).T),
        "out_b": np.asarray(inputs["L0_attn_out_b"]).reshape(1, -1),
    }
    for k, v in cc.items():
        m["c_" + k] = v
    return m



NE = 8
NI = 22
NIS = 11
B_SLAB = 512
NSL = TOK // B_SLAB
NTT = B_SLAB // 128


def emit_blockB(em, h, nc, outer_ctx):
    import contextlib
    XTOK, XTOK_res = alloc_sb(outer_ctx, nc, "XTOK", [128, NT * 1024], BF16)
    XT, XT_res = alloc_sb(outer_ctx, nc, "XT_B", [128, NF * 2048], BF16)
    SQ, SQ_res = alloc_sb(outer_ctx, nc, "SQln", [128, 1024], F32)
    COLS, COLS_res = alloc_sb(outer_ctx, nc, "COLS", [128, 16], F32)
    ln_scratch = {"sq": (SQ, SQ_res), "cols": (COLS, COLS_res)}

    PS0, PS0_res = alloc_ps(outer_ctx, nc, "PS0", [128, 1024], F32)
    PS1, PS1_res = alloc_ps(outer_ctx, nc, "PS1", [128, 1024], F32)
    PS2, PS2_res = alloc_ps(outer_ctx, nc, "PS2", [128, 1024], F32)
    PSt, PSt_res = alloc_ps(outer_ctx, nc, "PSt", [128, 1024], BF16)

    ACC, ACC_res = alloc_sb(outer_ctx, nc, "ACC", [128, NTT * 1024], F32)
    AES, AES_res = alloc_sb(outer_ctx, nc, "AES", [128, NI * B_SLAB], BF16)
    WGB0, WGB0_res = alloc_sb(outer_ctx, nc, "WGB0", [128, 2048], BF16)
    WGB1, WGB1_res = alloc_sb(outer_ctx, nc, "WGB1", [128, 2048], BF16)
    WDE, WDE_res = alloc_sb(outer_ctx, nc, "WDE", [128, NI * 1024], BF16)
    SDB, SDB_res = alloc_sb(outer_ctx, nc, "SDB", [128, NIS * 1024], BF16)
    GWB, GWB_res = alloc_sb(outer_ctx, nc, "GWB", [128, 64], BF16)
    LG, LG_res = alloc_sb(outer_ctx, nc, "LG", [128, 8], F32)
    GCOL, GCOL_res = alloc_sb(outer_ctx, nc, "GCOL", [128, 8], F32)
    CMB, CMB_res = alloc_sb(outer_ctx, nc, "CMB", [128, NT * 8], F32)
    TMP, TMP_res = alloc_sb(outer_ctx, nc, "TMPB", [128, 512], BF16)
    T1, T1_res = alloc_sb(outer_ctx, nc, "T1", [128, 1024], F32)
    SAS, SAS_res = AES, AES_res
    XO0, XO0_res = alloc_sb(outer_ctx, nc, "XO0", [128, 1024], F32)
    XO1, XO1_res = alloc_sb(outer_ctx, nc, "XO1", [128, 1024], F32)
    XOs = [(XO0, XO0_res), (XO1, XO1_res)]
    WGBs = [(WGB0, WGB0_res), (WGB1, WGB1_res)]
    WCH = ["p0", "p1", "p2", "p3"]

    for t in range(NT):
        em.d(["px", "p0"], XTOK[:, 1024 * t:1024 * (t + 1)],
             h.xb_in[128 * t:128 * (t + 1), :], reads=[h.xb_in_res],
             writes=[XTOK_res], tag="xload")
    for t in range(NT):
        for f in range(NF):
            src = XTOK[:, 1024 * t + 128 * f:1024 * t + 128 * (f + 1)]
            dst = XT[:, 2048 * f + 128 * t:2048 * f + 128 * (t + 1)]
            emit_transpose(em, h, src, dst, PSt, PSt_res, XTOK_res, XT_res)
    for kk in range(NF):
        em.d("p1", GWB[:, 8 * kk:8 * (kk + 1)], h.gate_w[128 * kk:128 * (kk + 1), :],
             writes=[GWB_res], tag="gwld")
    for i in range(NIS):
        em.d("p2", SDB[:, 1024 * i:1024 * (i + 1)], h.Sd[128 * i:128 * (i + 1), :],
             writes=[SDB_res])

    for tt in range(NT):
        pg = PS0[:, 0:8]
        for kk in range(NF):
            lhsT = XT[:, 2048 * kk + 128 * tt:2048 * kk + 128 * (tt + 1)]
            gr = GWB[:, 8 * kk:8 * (kk + 1)]
            em.c("pe", lambda t_, l=lhsT, r=gr, st=(kk == 0), sp=(kk == NF - 1):
                 t_.matmul(pg, l, r, start=st, stop=sp),
                 reads=[XT_res, GWB_res], writes=[PS0_res], tag="gate")
        em.c("dve", lambda v: v.tensor_copy(LG[:, :], pg),
             reads=[PS0_res], writes=[LG_res], tag="gateev")
        c0 = COLS[:, 8:9]; c1 = COLS[:, 9:10]; c2 = COLS[:, 10:11]
        c3 = COLS[:, 11:12]; c4 = COLS[:, 12:13]
        em.c("dve", lambda v: v.reduce_max(c0, LG[:, :], axis=AX.X),
             reads=[LG_res], writes=[COLS_res], tag="g_v1")
        em.c("dve", lambda v: v.tensor_scalar(GCOL[:, :], LG[:, :], c0, None, ALU.is_equal),
             reads=[LG_res, COLS_res], writes=[GCOL_res], tag="g_m1")
        em.c("dve", lambda v: v.scalar_tensor_tensor(LG[:, :], GCOL[:, :], -1e30, LG[:, :],
                                                     ALU.mult, ALU.add),
             reads=[GCOL_res, LG_res], writes=[LG_res], tag="g_l2")
        em.c("dve", lambda v: v.reduce_max(c1, LG[:, :], axis=AX.X),
             reads=[LG_res], writes=[COLS_res], tag="g_v2")
        em.c("dve", lambda v: v.tensor_tensor(c2, c1, c0, ALU.subtract),
             reads=[COLS_res], writes=[COLS_res], tag="g_d21")
        em.c("act", lambda s: s.activation(c2, c2, AF.Exp),
             reads=[COLS_res], writes=[COLS_res], tag="g_exp")
        em.c("dve", lambda v: v.tensor_scalar(c2, c2, 1.0, None, ALU.add),
             reads=[COLS_res], writes=[COLS_res], tag="g_1p")
        em.c("dve", lambda v: v.reciprocal(c3, c2),
             reads=[COLS_res], writes=[COLS_res], tag="g_p1")
        em.c("dve", lambda v: v.tensor_scalar(c4, c3, -1.0, 1.0, ALU.mult, ALU.add),
             reads=[COLS_res], writes=[COLS_res], tag="g_p2")
        em.c("dve", lambda v: v.tensor_scalar(LG[:, :], LG[:, :], c1, None, ALU.is_equal),
             reads=[LG_res, COLS_res], writes=[LG_res], tag="g_m2")
        cmb = CMB[:, 8 * tt:8 * (tt + 1)]
        em.c("dve", lambda v, o=cmb: v.tensor_scalar(o, GCOL[:, :], c3, None, ALU.mult),
             reads=[GCOL_res, COLS_res], writes=[CMB_res], tag="g_c1")
        em.c("dve", lambda v, o=cmb: v.scalar_tensor_tensor(o, LG[:, :], c4, o,
                                                            ALU.mult, ALU.add),
             reads=[LG_res, COLS_res, CMB_res], writes=[CMB_res], tag="g_c2")

    first_wde_load = [True]

    def load_wde(e):
        for i in range(NI):
            em.d(WCH[i % 4], WDE[:, 1024 * i:1024 * (i + 1)],
                 h.Wd[e, 128 * i:128 * (i + 1), :], writes=[WDE_res], tag="wdld")

    for s in range(NSL):
        for i in range(NIS):
            WGB, WGB_res_ = WGBs[i % 2]
            em.d(WCH[(2 * i) % 4], WGB[:, 0:1024], h.Sg_pre[i],
                 writes=[WGB_res_], tag="sgld")
            em.d(WCH[(2 * i + 1) % 4], WGB[:, 1024:2048], h.Su_pre[i],
                 writes=[WGB_res_], tag="suld")
            for which, (ps, ps_res) in enumerate(((PS0, PS0_res), (PS1, PS1_res))):
                for kk in range(NF):
                    lhsT = WGB[:, 1024 * which + 128 * kk:1024 * which + 128 * (kk + 1)]
                    rhs = XT[:, 2048 * kk + B_SLAB * s:2048 * kk + B_SLAB * (s + 1)]
                    em.c("pe", lambda t_, o=ps[:, 0:B_SLAB], l=lhsT, r=rhs, st=(kk == 0),
                         sp=(kk == NF - 1): t_.matmul(o, l, r, start=st, stop=sp),
                         reads=[WGB_res_, XT_res], writes=[ps_res], tag="sgu")
            em.c("act", lambda s_, i_=PS0[:, 0:B_SLAB]: s_.activation(
                TMP[:, :], i_, AF.Silu), reads=[PS0_res], writes=[TMP_res], tag="ssilu")
            dst = AES[:, B_SLAB * 0:B_SLAB] if False else SAS[:, B_SLAB * i:B_SLAB * (i + 1)]
            em.c("dve", lambda v, o=dst: v.tensor_tensor(o, TMP[:, :], PS1[:, 0:B_SLAB],
                                                         ALU.mult),
                 reads=[TMP_res, PS1_res], writes=[SAS_res], tag="smul")
        for tt in range(NTT):
            ps, ps_res = (PS2, PS2_res) if tt % 2 == 0 else (PS0, PS0_res)
            for i in range(NIS):
                lhsT = SAS[:, B_SLAB * i + 128 * tt:B_SLAB * i + 128 * (tt + 1)]
                for j in range(2):
                    rhs = SDB[:, 1024 * i + 512 * j:1024 * i + 512 * (j + 1)]
                    em.c("pe", lambda t_, o=ps[:, 512 * j:512 * (j + 1)], l=lhsT, r=rhs,
                         st=(i == 0), sp=(i == NIS - 1): t_.matmul(o, l, r, start=st, stop=sp),
                         reads=[SAS_res, SDB_res], writes=[ps_res], tag="sd")
            acc = ACC[:, 1024 * tt:1024 * (tt + 1)]
            em.c("act", lambda s_, o=acc, i_=ps[:, :]: s_.copy(o, i_),
                 reads=[ps_res], writes=[ACC_res], tag="accinit")

        for e in range(NE):
            if first_wde_load[0]:
                load_wde(e)
                first_wde_load[0] = False
            for i in range(NI):
                WGB, WGB_res_ = WGBs[i % 2]
                em.d(WCH[(2 * i) % 4], WGB[:, 0:1024], h.Wg_pre[e, i],
                     writes=[WGB_res_], tag="wgld")
                em.d(WCH[(2 * i + 1) % 4], WGB[:, 1024:2048], h.Wu_pre[e, i],
                     writes=[WGB_res_], tag="wuld")
                for which, ps, ps_res in ((0, PS0, PS0_res), (1, PS1, PS1_res)):
                    for kk in range(NF):
                        lhsT = WGB[:, 1024 * which + 128 * kk:1024 * which + 128 * (kk + 1)]
                        rhs = XT[:, 2048 * kk + B_SLAB * s:2048 * kk + B_SLAB * (s + 1)]
                        em.c("pe", lambda t_, o=ps[:, 0:B_SLAB], l=lhsT, r=rhs, st=(kk == 0),
                             sp=(kk == NF - 1): t_.matmul(o, l, r, start=st, stop=sp),
                             reads=[WGB_res_, XT_res], writes=[ps_res], tag="gu")
                em.c("act", lambda s_, i_=PS0[:, 0:B_SLAB]: s_.activation(
                    TMP[:, :], i_, AF.Silu), reads=[PS0_res], writes=[TMP_res], tag="esilu")
                dst = AES[:, B_SLAB * i:B_SLAB * (i + 1)]
                em.c("dve", lambda v, o=dst: v.tensor_tensor(o, TMP[:, :], PS1[:, 0:B_SLAB],
                                                             ALU.mult),
                     reads=[TMP_res, PS1_res], writes=[AES_res], tag="emul")
            for tt in range(NTT):
                ps, ps_res = (PS2, PS2_res) if tt % 2 == 0 else (PS1, PS1_res)
                for i in range(NI):
                    lhsT = AES[:, B_SLAB * i + 128 * tt:B_SLAB * i + 128 * (tt + 1)]
                    for j in range(2):
                        rhs = WDE[:, 1024 * i + 512 * j:1024 * i + 512 * (j + 1)]
                        em.c("pe", lambda t_, o=ps[:, 512 * j:512 * (j + 1)], l=lhsT, r=rhs,
                             st=(i == 0), sp=(i == NI - 1): t_.matmul(o, l, r, start=st,
                                                                      stop=sp),
                             reads=[AES_res, WDE_res], writes=[ps_res], tag="d")
                gtt = NTT * s + tt
                ccol = CMB[:, 8 * gtt + e:8 * gtt + e + 1]
                acc = ACC[:, 1024 * tt:1024 * (tt + 1)]
                em.c("dve", lambda v, o=acc, i_=ps[:, :], sc=ccol: v.scalar_tensor_tensor(
                    o, i_, sc, o, ALU.mult, ALU.add),
                    reads=[ps_res, CMB_res, ACC_res], writes=[ACC_res], tag="dmerge")
            if e + 1 < NE:
                load_wde(e + 1)
        first_wde_load[0] = True

        for tt in range(NTT):
            gtt = NTT * s + tt
            xtk = XTOK[:, 1024 * gtt:1024 * (gtt + 1)]
            acc = ACC[:, 1024 * tt:1024 * (tt + 1)]
            em.c("dve", lambda v, x=xtk, a=acc: v.tensor_tensor(T1[:, :], a, x, ALU.add),
                 reads=[ACC_res, XTOK_res], writes=[T1_res], tag="bresid")
            XO, XO_res = XOs[tt % 2]
            emit_ln(em, h, T1[:, :], T1_res, [(XO[:, :], XO_res)], ln_scratch, tag="lnB")
            em.d("so", h.xb[128 * gtt:128 * (gtt + 1), :], XO[:, :],
                 reads=[XO_res], writes=[h.xb_res], tag="xbstore")


def build_B():
    import contextlib
    nc = bass.Bass("TRN2", target_bir_lowering=False)
    h = H()
    h.xb_in = nc.dram_tensor("xb_in", [TOK, D], F32, kind="ExternalInput")
    h.gate_w = nc.dram_tensor("gate_w", [D, NE], F32, kind="ExternalInput")
    h.Wg_pre = nc.dram_tensor("Wg_pre", [NE, NI, 128, 1024], F32, kind="ExternalInput")
    h.Wu_pre = nc.dram_tensor("Wu_pre", [NE, NI, 128, 1024], F32, kind="ExternalInput")
    h.Wd = nc.dram_tensor("Wd", [NE, 2816, 1024], F32, kind="ExternalInput")
    h.Sg_pre = nc.dram_tensor("Sg_pre", [NIS, 128, 1024], F32, kind="ExternalInput")
    h.Su_pre = nc.dram_tensor("Su_pre", [NIS, 128, 1024], F32, kind="ExternalInput")
    h.Sd = nc.dram_tensor("Sd", [1408, 1024], F32, kind="ExternalInput")
    h.xb = nc.dram_tensor("xb", [TOK, D], F32, kind="ExternalOutput")
    h.xb_res = Res("xb")
    h.xb_in_res = Res("xb_in")

    chans = {"p0": "pool", "p1": "pool", "p2": "pool", "p3": "pool", "px": "pool",
             "sk": "sp", "so": "sp"}
    em = Em(nc, chans)
    with contextlib.ExitStack() as outer:
        declare_consts(nc, outer, h)
        load_consts(em, h)
        emit_blockB(em, h, nc, outer)
        em.barrier()
        with nc.Block() as block:
            em.finalize_and_emit(block, None)
    return nc


def prep_B_inputs(inputs, core, xa_full):
    b = core // 4
    q0 = 2048 * (core % 4)
    cc = host_consts()
    Wg = np.asarray(inputs["L0_ffn_Wg"])
    Wu = np.asarray(inputs["L0_ffn_Wu"])
    Sg = np.asarray(inputs["L0_ffn_Sg"])
    Su = np.asarray(inputs["L0_ffn_Su"])
    pre = lambda W: np.ascontiguousarray(
        W.reshape(NE, 8, 128, NI, 128).transpose(0, 3, 2, 1, 4).reshape(NE, NI, 128, 1024))
    pres = lambda W: np.ascontiguousarray(
        W.reshape(8, 128, NIS, 128).transpose(2, 1, 0, 3).reshape(NIS, 128, 1024))
    m = {
        "xb_in": np.ascontiguousarray(xa_full[b, q0:q0 + 2048, :]),
        "gate_w": np.asarray(inputs["L0_ffn_gate_w"]),
        "Wg_pre": pre(Wg),
        "Wu_pre": pre(Wu),
        "Wd": np.asarray(inputs["L0_ffn_Wd"]),
        "Sg_pre": pres(Sg),
        "Su_pre": pres(Su),
        "Sd": np.asarray(inputs["L0_ffn_Sd"]),
    }
    for k, v in cc.items():
        m["c_" + k] = v
    return m



NH = 16
HD = 64
NPRE = 48
C_SLAB = 512
NIC = 22
GAMMAS = [1.0 - 2.0 ** (-5.0 - hh) for hh in range(NH)]


def emit_ln_small(em, h, src_ap, src_res, n, outs, scratch, tag="gn"):
    sq, sq_res = scratch["sq"]
    cols, cols_res = scratch["cols"]
    ssum = cols[:, 0:1]; ssq = cols[:, 1:2]; mean = cols[:, 2:3]
    varr = cols[:, 3:4]; rstd = cols[:, 4:5]; negmr = cols[:, 5:6]; tmp = cols[:, 6:7]
    em.c("dve", lambda v: v.reduce_sum(ssum, src_ap, axis=AX.X),
         reads=[src_res], writes=[cols_res], tag=tag)
    em.c("act", lambda s: s.activation(sq[:, 0:n], src_ap, AF.Square, accum_out=ssq),
         reads=[src_res], writes=[sq_res, cols_res], tag=tag)
    em.c("dve", lambda v: v.tensor_scalar(mean, ssum, 1.0 / n, None, ALU.mult),
         reads=[cols_res], writes=[cols_res], tag=tag)
    em.c("dve", lambda v: v.tensor_scalar(varr, ssq, 1.0 / n, EPS, ALU.mult, ALU.add),
         reads=[cols_res], writes=[cols_res], tag=tag)
    em.c("dve", lambda v: v.tensor_tensor(tmp, mean, mean, ALU.mult),
         reads=[cols_res], writes=[cols_res], tag=tag)
    em.c("dve", lambda v: v.tensor_tensor(varr, varr, tmp, ALU.subtract),
         reads=[cols_res], writes=[cols_res], tag=tag)
    em.c("act", lambda s: s.activation(rstd, varr, AF.Ln),
         reads=[cols_res], writes=[cols_res], tag=tag)
    em.c("act", lambda s: s.activation(rstd, rstd, AF.Exp, scale=-0.5),
         reads=[cols_res], writes=[cols_res], tag=tag)
    em.c("dve", lambda v: v.scalar_tensor_tensor(negmr, mean, -1.0, rstd, ALU.mult, ALU.mult),
         reads=[cols_res], writes=[cols_res], tag=tag)
    for dst_ap, dst_res in outs:
        em.c("dve", lambda v, o=dst_ap: v.tensor_scalar(o, src_ap, rstd, negmr,
                                                        ALU.mult, ALU.add),
             reads=[src_res, cols_res], writes=[dst_res], tag=tag + "_app")


def emit_blockC(em, h, nc, outer_ctx):
    import contextlib
    XTOK, XTOK_res = alloc_sb(outer_ctx, nc, "XTOKC", [128, NT * 1024], BF16)
    XNA, XNA_res = alloc_sb(outer_ctx, nc, "XNA", [128, NT * 1024], BF16)
    SQ, SQ_res = alloc_sb(outer_ctx, nc, "SQln", [128, 1024], F32)
    COLS, COLS_res = alloc_sb(outer_ctx, nc, "COLS", [128, 16], F32)
    ln_scratch = {"sq": (SQ, SQ_res), "cols": (COLS, COLS_res)}
    T1, T1_res = alloc_sb(outer_ctx, nc, "T1", [128, 1024], F32)
    XO0, XO0_res = alloc_sb(outer_ctx, nc, "XO0", [128, 1024], F32)
    XOs = [(XO0, XO0_res), (XO0, XO0_res)]
    BCT, BCT_res = alloc_sb(outer_ctx, nc, "BCT", [128, 4 * 1024], BF16)
    GQ, GQ_res = alloc_sb(outer_ctx, nc, "GQ", [128, NH], F32)
    GKN, GKN_res = alloc_sb(outer_ctx, nc, "GKN", [128, NH], F32)
    GKS, GKS_res = alloc_sb(outer_ctx, nc, "GKS", [128, NH], F32)
    GKSP, GKSP_res = alloc_sb(outer_ctx, nc, "GKSP", [128, NPRE * NH], F32)
    GDEC, GDEC_res = alloc_sb(outer_ctx, nc, "GDEC", [64, NPRE * NH], F32)
    MASKT, MASKT_res = alloc_sb(outer_ctx, nc, "MASKT", [128, 128], BF16)
    SST, SST_res = alloc_sb(outer_ctx, nc, "SST", [64, NH * HD], F32)
    SSTB, SSTB_res = alloc_sb(outer_ctx, nc, "SSTB", [64, NH * HD], BF16)
    BR, BR_res = alloc_sb(outer_ctx, nc, "BR", [1, 5 * 1024], BF16)

    PS0, PS0_res = alloc_ps(outer_ctx, nc, "PS0", [128, 1024], F32)
    PS1, PS1_res = alloc_ps(outer_ctx, nc, "PS1", [128, 1024], F32)
    PSC, PSC_res = alloc_ps(outer_ctx, nc, "PSC", [128, 512], F32)
    PSO, PSO_res = alloc_ps(outer_ctx, nc, "PSO", [128, 512], F32)
    PSS, PSS_res = alloc_ps(outer_ctx, nc, "PSS", [64, 512], F32)
    PSt, PSt_res = alloc_ps(outer_ctx, nc, "PSt", [128, 1024], BF16)

    for t, d, r in [(GQ, h.GQ_d, GQ_res), (GKN, h.GKN_d, GKN_res), (GKS, h.GKS_d, GKS_res),
                    (GKSP, h.GKSP_d, GKSP_res), (GDEC, h.GDEC_d, GDEC_res),
                    (MASKT, h.MASKT_d, MASKT_res)]:
        em.d("sk", t[:, :], d[:, :], writes=[r], tag="gld")
    off = 0
    bias_off = {}
    for nm, dd, ln in [("bq", h.bq, 1024), ("bk", h.bk, 1024), ("bv", h.bv, 1024),
                       ("bg", h.bg, 1024), ("bo", h.bo, 1024)]:
        em.d("p1", BR[0:1, off:off + ln], dd[:, :], writes=[BR_res], tag="bld")
        bias_off[nm] = off
        off += ln
    NRMR, NRMR_res = alloc_sb(outer_ctx, nc, "NRMR", [1, 1024], F32)
    for j, dd in enumerate([h.norm1_w, h.norm1_b, h.gn_w, h.gn_b]):
        em.d("sk", NRMR[0:1, :], dd[:, :], reads=[NRMR_res], writes=[NRMR_res])
        for jj in range(2):
            pb = PS0[:, 0:512]
            em.c("pe", lambda t, r=NRMR[0:1, 512 * jj:512 * (jj + 1)]:
                 t.matmul(pb, h.ones1x128f[0:1, :], r, start=True, stop=True),
                 reads=[NRMR_res, h.ones1x128f_res], writes=[PS0_res], tag="bcmm")
            dst = BCT[:, 1024 * j + 512 * jj:1024 * j + 512 * (jj + 1)]
            em.c("dve", lambda v, o=dst: v.tensor_copy(o, pb),
                 reads=[PS0_res], writes=[BCT_res], tag="bcev")
    W1B = BCT[:, 0:1024]; B1B = BCT[:, 1024:2048]
    GNW = BCT[:, 2048:3072]; GNB = BCT[:, 3072:4096]

    em.c("dve", lambda v: v.memset(SST[:, :], 0.0), writes=[SST_res])
    em.c("dve", lambda v: v.memset(SSTB[:, :], 0.0), writes=[SSTB_res])

    for t in range(NT):
        em.d(["px", "p0"], XTOK[:, 1024 * t:1024 * (t + 1)],
             h.xc_in[128 * t:128 * (t + 1), :], reads=[h.xc_in_res],
             writes=[XTOK_res], tag="xload")

    for tt in range(NT):
        src = XTOK[:, 1024 * tt:1024 * (tt + 1)]
        emit_ln(em, h, src, XTOK_res, [(T1[:, :], T1_res)], ln_scratch, tag="ln1")
        em.c("dve", lambda v: v.tensor_tensor(T1[:, :], T1[:, :], W1B, ALU.mult),
             reads=[T1_res, BCT_res], writes=[T1_res], tag="xnw")
        dst = XNA[:, 1024 * tt:1024 * (tt + 1)]
        em.c("dve", lambda v, o=dst: v.tensor_tensor(o, T1[:, :], B1B, ALU.add),
             reads=[T1_res, BCT_res], writes=[XNA_res], tag="xnb")

    with contextlib.ExitStack() as ph2:
        WQB, WQB_res = alloc_sb(ph2, nc, "WQB", [128, 8 * 1024], BF16)
        WKB, WKB_res = alloc_sb(ph2, nc, "WKB", [128, 8 * 1024], BF16)
        WVB, WVB_res = alloc_sb(ph2, nc, "WVB", [128, 8 * 1024], BF16)
        WGB_, WGB__res = alloc_sb(ph2, nc, "WGB_", [128, 8 * 1024], BF16)
        WOB, WOB_res = alloc_sb(ph2, nc, "WOB", [128, 8 * 1024], BF16)
        XNC, XNC_res = alloc_sb(ph2, nc, "XNC", [128, 8 * 128], BF16)
        QTC, QTC_res = alloc_sb(ph2, nc, "QTC", [128, 1024], BF16)
        KTC, KTC_res = alloc_sb(ph2, nc, "KTC", [128, 1024], BF16)
        VTC, VTC_res = alloc_sb(ph2, nc, "VTC", [128, 1024], BF16)
        GTC, GTC_res = alloc_sb(ph2, nc, "GTC", [128, 1024], BF16)
        QS, QS_res = alloc_sb(ph2, nc, "QS", [128, 64], BF16)
        KS1, KS1_res = alloc_sb(ph2, nc, "KS1", [128, 64], BF16)
        KS2, KS2_res = alloc_sb(ph2, nc, "KS2", [128, 64], BF16)
        QSF, QSF_res = alloc_sb(ph2, nc, "QSF", [64, 128], BF16)
        KSF, KSF_res = alloc_sb(ph2, nc, "KSF", [64, 128], BF16)
        AIN, AIN_res = alloc_sb(ph2, nc, "AIN", [128, 128], BF16)
        TGN, TGN_res = alloc_sb(ph2, nc, "TGN", [128, 64], F32)
        RGC, RGC_res = alloc_sb(ph2, nc, "RGC", [128, 1024], BF16)
        RGT, RGT_res = XNC, XNC_res
        XPH, XPH_res = QTC, QTC_res

        for kk in range(NF):
            em.d("p1", WQB[:, 1024 * kk:1024 * (kk + 1)], h.wq[128 * kk:128 * (kk + 1), :],
                 writes=[WQB_res])
            em.d("p2", WKB[:, 1024 * kk:1024 * (kk + 1)], h.wk[128 * kk:128 * (kk + 1), :],
                 writes=[WKB_res])
            em.d("p3", WVB[:, 1024 * kk:1024 * (kk + 1)], h.wv[128 * kk:128 * (kk + 1), :],
                 writes=[WVB_res])
            em.d("p1", WGB_[:, 1024 * kk:1024 * (kk + 1)], h.wg[128 * kk:128 * (kk + 1), :],
                 writes=[WGB__res])
            em.d("p2", WOB[:, 1024 * kk:1024 * (kk + 1)], h.wo[128 * kk:128 * (kk + 1), :],
                 writes=[WOB_res])

        def proj(dst, dst_res, wsb, wsb_res, bias_name, ps, ps_res, act=None):
            for j in range(2):
                for kk in range(NF):
                    lhsT = XNC[:, 128 * kk:128 * (kk + 1)]
                    rhs = wsb[:, 1024 * kk + 512 * j:1024 * kk + 512 * (j + 1)]
                    em.c("pe", lambda t, o=ps[:, 512 * j:512 * (j + 1)], l=lhsT, r=rhs,
                         st=(kk == 0): t.matmul(o, l, r, start=st, stop=False),
                         reads=[XNC_res, wsb_res], writes=[ps_res], tag="cproj")
                boff = bias_off[bias_name]
                em.c("pe", lambda t, o=ps[:, 512 * j:512 * (j + 1)],
                     r=BR[0:1, boff + 512 * j:boff + 512 * (j + 1)]:
                     t.matmul(o, h.ones_row[0:1, 0:128], r, start=False, stop=True),
                     reads=[BR_res, h.ones_row_res], writes=[ps_res], tag="cbias")
            if act is None:
                em.c("act", lambda s, o=dst, i=ps[:, :]: s.copy(o, i),
                     reads=[ps_res], writes=[dst_res], tag="cevac")
            else:
                em.c("act", lambda s, o=dst, i=ps[:, :], a=act: s.activation(o, i, a),
                     reads=[ps_res], writes=[dst_res], tag="cevac")

        def state_update(hd_, scale_col, scale_res, dec, dec_res):
            em.c("act", lambda s, sc=scale_col: s.activation(
                KS2[:, :], KTC[:, 64 * hd_:64 * (hd_ + 1)], AF.Copy, scale=sc),
                reads=[KTC_res, scale_res], writes=[KS2_res], tag="ks2")
            pss = PSS[0:64, 0:64]
            em.c("pe", lambda t: t.matmul(pss, KS2[:, :], VTC[:, 64 * hd_:64 * (hd_ + 1)],
                                          start=True, stop=True),
                 reads=[KS2_res, VTC_res], writes=[PSS_res], tag="smm")
            scol = SST[:, 64 * hd_:64 * (hd_ + 1)]
            if dec_res is None:
                em.c("dve", lambda v, o=scol, d=dec: v.scalar_tensor_tensor(
                    o, o, d, pss, ALU.mult, ALU.add),
                    reads=[SST_res, PSS_res], writes=[SST_res], tag="sdec")
            else:
                em.c("dve", lambda v, o=scol, d=dec: v.scalar_tensor_tensor(
                    o, o, d, pss, ALU.mult, ALU.add),
                    reads=[SST_res, PSS_res, dec_res], writes=[SST_res], tag="sdec")
            sbcol = SSTB[:, 64 * hd_:64 * (hd_ + 1)]
            em.c("dve", lambda v, o=sbcol, i=scol: v.tensor_copy(o, i),
                 reads=[SST_res], writes=[SSTB_res], tag="sbf")

        for c in range(NPRE):
            em.d(["px", "p0"], XPH[:, :], h.xprefix[128 * c:128 * (c + 1), :],
                 reads=[h.xprefix_res], writes=[XPH_res], tag="xpload")
            emit_ln(em, h, XPH[:, :], XPH_res, [(T1[:, :], T1_res)], ln_scratch, tag="lnp")
            em.c("dve", lambda v: v.tensor_tensor(T1[:, :], T1[:, :], W1B, ALU.mult),
                 reads=[T1_res, BCT_res], writes=[T1_res], tag="xnpw")
            em.c("dve", lambda v: v.tensor_tensor(T1[:, :], T1[:, :], B1B, ALU.add),
                 reads=[T1_res, BCT_res], writes=[T1_res], tag="xnpb")
            em.c("dve", lambda v: v.tensor_copy(GTC[:, :], T1[:, :]),
                 reads=[T1_res], writes=[GTC_res], tag="xnpcast")
            for f in range(NF):
                src = GTC[:, 128 * f:128 * (f + 1)]
                dst = XNC[:, 128 * f:128 * (f + 1)]
                emit_transpose(em, h, src, dst, PSt, PSt_res, GTC_res, XNC_res)
            proj(KTC[:, :], KTC_res, WKB, WKB_res, "bk", PS0, PS0_res)
            proj(VTC[:, :], VTC_res, WVB, WVB_res, "bv", PS1, PS1_res)
            for hd_ in range(NH):
                state_update(hd_, GKSP[:, NH * c + hd_:NH * c + hd_ + 1], GKSP_res,
                             GDEC[:, NH * c + hd_:NH * c + hd_ + 1], GDEC_res)

        for c in range(NT):
            for f in range(NF):
                src = XNA[:, 1024 * c + 128 * f:1024 * c + 128 * (f + 1)]
                dst = XNC[:, 128 * f:128 * (f + 1)]
                emit_transpose(em, h, src, dst, PSt, PSt_res, XNA_res, XNC_res)
            proj(QTC[:, :], QTC_res, WQB, WQB_res, "bq", PS0, PS0_res)
            proj(KTC[:, :], KTC_res, WKB, WKB_res, "bk", PS1, PS1_res)
            proj(VTC[:, :], VTC_res, WVB, WVB_res, "bv", PS0, PS0_res)
            proj(GTC[:, :], GTC_res, WGB_, WGB__res, "bg", PS1, PS1_res, act=AF.Silu)
            if c == 0 and getattr(h, "dbg", None) is not None:
                for nm, t_, r_ in [("q", QTC, QTC_res), ("k", KTC, KTC_res),
                                   ("v", VTC, VTC_res), ("g", GTC, GTC_res)]:
                    em.d("so", h.dbg[nm][:, :], t_[:, :], reads=[r_],
                         writes=[h.dbg_res[nm]], tag="dbg")
            for hd_ in range(NH):
                qslice = QTC[:, 64 * hd_:64 * (hd_ + 1)]
                kslice = KTC[:, 64 * hd_:64 * (hd_ + 1)]
                vslice = VTC[:, 64 * hd_:64 * (hd_ + 1)]
                sbslice = SSTB[:, 64 * hd_:64 * (hd_ + 1)]
                em.c("act", lambda s, sc=GQ[:, hd_:hd_ + 1], i_=qslice: s.activation(
                    QS[:, :], i_, AF.Copy, scale=sc),
                    reads=[QTC_res, GQ_res], writes=[QS_res], tag="qsc")
                em.c("act", lambda s, sc=GKN[:, hd_:hd_ + 1], i_=kslice: s.activation(
                    KS1[:, :], i_, AF.Copy, scale=sc),
                    reads=[KTC_res, GKN_res], writes=[KS1_res], tag="ksc")
                emit_transpose(em, h, QS[:, :], QSF[:, :], PSt, PSt_res, QS_res, QSF_res)
                emit_transpose(em, h, KS1[:, :], KSF[:, :], PSt, PSt_res, KS1_res, KSF_res)
                psc = PSC[0:128, 0:128]
                em.c("pe", lambda t: t.matmul(psc, KSF[:, :], QSF[:, :], start=True,
                                              stop=True),
                     reads=[KSF_res, QSF_res], writes=[PSC_res], tag="sc_mm")
                em.c("dve", lambda v: v.tensor_tensor(AIN[:, :], psc, MASKT[:, :], ALU.mult),
                     reads=[PSC_res, MASKT_res], writes=[AIN_res], tag="mask")
                pso = PSO[0:128, 0:64]
                em.c("pe", lambda t, r=vslice: t.matmul(pso, AIN[:, :], r,
                                                        start=True, stop=False),
                     reads=[AIN_res, VTC_res], writes=[PSO_res], tag="o_intra")
                em.c("pe", lambda t, r=sbslice: t.matmul(pso, QSF[:, :], r,
                                                         start=False, stop=True),
                     reads=[QSF_res, SSTB_res], writes=[PSO_res], tag="o_inter")
                if c == 0 and hd_ == 0 and getattr(h, "dbg", None) is not None:
                    em.d("px", h.dbg["qs"][:, 0:64], QS[:, :], reads=[QS_res],
                         writes=[h.dbg_res["qs"]], tag="dbg")
                    em.d("px", h.dbg["ks"][:, 0:64], KS1[:, :], reads=[KS1_res],
                         writes=[h.dbg_res["ks"]], tag="dbg")
                    em.d("px", h.dbg["ain"][:, 0:128], AIN[:, :], reads=[AIN_res],
                         writes=[h.dbg_res["ain"]], tag="dbg")
                    em.c("dve", lambda v: v.tensor_copy(TGN[:, :], pso),
                         reads=[PSO_res], writes=[TGN_res], tag="dbgret")
                    em.d("so", h.dbg["ret"][:, 0:64], TGN[:, :], reads=[TGN_res],
                         writes=[h.dbg_res["ret"]], tag="dbg")
                state_update(hd_, GKS[:, hd_:hd_ + 1], GKS_res, GAMMAS[hd_] ** 128, None)
                emit_ln_small(em, h, pso, PSO_res, 64, [(TGN[:, :], TGN_res)], ln_scratch)
                em.c("dve", lambda v, w=GNW[:, 64 * hd_:64 * (hd_ + 1)]: v.tensor_tensor(
                    TGN[:, :], TGN[:, :], w, ALU.mult),
                    reads=[TGN_res, BCT_res], writes=[TGN_res], tag="gnw")
                em.c("dve", lambda v, b=GNB[:, 64 * hd_:64 * (hd_ + 1)]: v.tensor_tensor(
                    TGN[:, :], TGN[:, :], b, ALU.add),
                    reads=[TGN_res, BCT_res], writes=[TGN_res], tag="gnb")
                dst = RGC[:, 64 * hd_:64 * (hd_ + 1)]
                em.c("dve", lambda v, o=dst, g=GTC[:, 64 * hd_:64 * (hd_ + 1)]:
                     v.tensor_tensor(o, TGN[:, :], g, ALU.mult),
                     reads=[TGN_res, GTC_res], writes=[RGC_res], tag="gmul")
            if c == 0 and getattr(h, "dbg", None) is not None:
                em.d("so", h.dbg["rg"][:, :], RGC[:, :], reads=[RGC_res],
                     writes=[h.dbg_res["rg"]], tag="dbg")
            for f in range(NF):
                src = RGC[:, 128 * f:128 * (f + 1)]
                dst = RGT[:, 128 * f:128 * (f + 1)]
                emit_transpose(em, h, src, dst, PSt, PSt_res, RGC_res, RGT_res)
            for j in range(2):
                for kk in range(NF):
                    lhsT = RGT[:, 128 * kk:128 * (kk + 1)]
                    rhs = WOB[:, 1024 * kk + 512 * j:1024 * kk + 512 * (j + 1)]
                    em.c("pe", lambda t, o=PS0[:, 512 * j:512 * (j + 1)], l=lhsT, r=rhs,
                         st=(kk == 0): t.matmul(o, l, r, start=st, stop=False),
                         reads=[RGT_res, WOB_res], writes=[PS0_res], tag="oproj")
                boff = bias_off["bo"]
                em.c("pe", lambda t, o=PS0[:, 512 * j:512 * (j + 1)],
                     r=BR[0:1, boff + 512 * j:boff + 512 * (j + 1)]:
                     t.matmul(o, h.ones_row[0:1, 0:128], r, start=False, stop=True),
                     reads=[BR_res, h.ones_row_res], writes=[PS0_res], tag="obias")
            xt = XTOK[:, 1024 * c:1024 * (c + 1)]
            em.c("dve", lambda v, x=xt: v.tensor_tensor(T1[:, :], PS0[:, :], x, ALU.add),
                 reads=[PS0_res, XTOK_res], writes=[T1_res], tag="midres")
            XO, XO_res = XOs[c % 2]
            em.c("dve", lambda v, o=XO[:, :]: v.tensor_copy(o, T1[:, :]),
                 reads=[T1_res], writes=[XO_res], tag="midcp")
            em.d("so", h.xmid_dram[128 * c:128 * (c + 1), :], XO[:, :],
                 reads=[XO_res], writes=[h.xmid_res], tag="midstore")

    em.barrier()

    with contextlib.ExitStack() as ph3:
        XMH, XMH_res = alloc_sb(ph3, nc, "XMH", [128, 1024], F32)
        T2, T2_res = alloc_sb(ph3, nc, "T2", [128, 1024], F32)
        BR3, BR3_res = alloc_sb(ph3, nc, "BR3", [1, 2 * 2816 + 1024], BF16)
        AFS, AFS_res = alloc_sb(ph3, nc, "AFS", [128, NIC * C_SLAB], BF16)
        WIN0, WIN0_res = alloc_sb(ph3, nc, "WIN0", [128, 2048], BF16)
        WIN1, WIN1_res = alloc_sb(ph3, nc, "WIN1", [128, 2048], BF16)
        WOU0, WOU0_res = alloc_sb(ph3, nc, "WOU0", [128, 1024], BF16)
        WOU1, WOU1_res = alloc_sb(ph3, nc, "WOU1", [128, 1024], BF16)
        TMP, TMP_res = alloc_sb(ph3, nc, "TMPC", [128, 512], BF16)
        X2C, X2C_res = alloc_sb(ph3, nc, "X2C", [128, 1024], BF16)
        WINs = [(WIN0, WIN0_res), (WIN1, WIN1_res)]
        WOUs = [(WOU0, WOU0_res), (WOU1, WOU1_res)]
        for nm, dd, ln in [("biu", h.b_in_u, 2816), ("big", h.b_in_g, 2816),
                           ("bout", h.b_out, 1024)]:
            em.d("p1", BR3[0:1, off - 5120:off - 5120 + ln], dd[:, :], writes=[BR3_res])
            bias_off[nm] = off
            off += ln
        BCT3, BCT3_res = alloc_sb(ph3, nc, "BCT3", [128, 2048], BF16)
        for j, dd in enumerate([h.norm2_w, h.norm2_b]):
            em.d("sk", NRMR[0:1, :], dd[:, :], reads=[NRMR_res], writes=[NRMR_res])
            for jj in range(2):
                pb = PS0[:, 0:512]
                em.c("pe", lambda t, r=NRMR[0:1, 512 * jj:512 * (jj + 1)]:
                     t.matmul(pb, h.ones1x128f[0:1, :], r, start=True, stop=True),
                     reads=[NRMR_res, h.ones1x128f_res], writes=[PS0_res], tag="bcmm3")
                dst = BCT3[:, 1024 * j + 512 * jj:1024 * j + 512 * (jj + 1)]
                em.c("dve", lambda v, o=dst: v.tensor_copy(o, pb),
                     reads=[PS0_res], writes=[BCT3_res], tag="bcev3")
        W2B = BCT3[:, 0:1024]
        B2B = BCT3[:, 1024:2048]

        for tt in range(NT):
            em.d(["px", "p0"], XMH[:, :], h.xmid_dram[128 * tt:128 * (tt + 1), :],
                 reads=[h.xmid_res], writes=[XMH_res], tag="xmload")
            emit_ln(em, h, XMH[:, :], XMH_res, [(T1[:, :], T1_res)], ln_scratch, tag="ln2")
            em.c("dve", lambda v: v.tensor_tensor(T1[:, :], T1[:, :], W2B, ALU.mult),
                 reads=[T1_res, BCT3_res], writes=[T1_res], tag="x2w")
            em.c("dve", lambda v: v.tensor_tensor(T2[:, :], T1[:, :], B2B, ALU.add),
                 reads=[T1_res, BCT3_res], writes=[T2_res], tag="x2b")
            em.c("dve", lambda v: v.tensor_copy(X2C[:, :], T2[:, :]),
                 reads=[T2_res], writes=[X2C_res], tag="x2cast")
            for f in range(NF):
                src = X2C[:, 128 * f:128 * (f + 1)]
                dst = XNA[:, 2048 * f + 128 * tt:2048 * f + 128 * (tt + 1)]
                emit_transpose(em, h, src, dst, PSt, PSt_res, X2C_res, XNA_res)
        X2T, X2T_res = XNA, XNA_res

        for s in range(4):
            for i in range(NIC):
                WIN, WIN_res = WINs[i % 2]
                em.d(WCH2[(2 * i) % 4], WIN[:, 0:1024], h.win_u_pre[i],
                     writes=[WIN_res], tag="wiuld")
                em.d(WCH2[(2 * i + 1) % 4], WIN[:, 1024:2048], h.win_g_pre[i],
                     writes=[WIN_res], tag="wigld")
                for which, ps, ps_res, bias_name in ((0, PS0, PS0_res, "biu"),
                                                     (1, PS1, PS1_res, "big")):
                    for kk in range(NF):
                        lhsT = WIN[:, 1024 * which + 128 * kk:1024 * which + 128 * (kk + 1)]
                        rhs = X2T[:, 2048 * kk + C_SLAB * s:2048 * kk + C_SLAB * (s + 1)]
                        em.c("pe", lambda t, o=ps[:, 0:C_SLAB], l=lhsT, r=rhs, st=(kk == 0):
                             t.matmul(o, l, r, start=st, stop=False),
                             reads=[WIN_res, X2T_res], writes=[ps_res], tag="ug")
                    boff = bias_off[bias_name] - 5120 + 128 * i
                    em.c("pe", lambda t, o=ps[:, 0:C_SLAB], l=BR3[0:1, boff:boff + 128]:
                         t.matmul(o, l, h.ones_row[0:1, :], start=False, stop=True),
                         reads=[BR3_res, h.ones_row_res], writes=[ps_res], tag="ugbias")
                em.c("act", lambda s_, i_=PS0[:, 0:C_SLAB]: s_.activation(
                    TMP[:, :], i_, AF.Silu), reads=[PS0_res], writes=[TMP_res], tag="usilu")
                dst = AFS[:, C_SLAB * i:C_SLAB * (i + 1)]
                em.c("dve", lambda v, o=dst: v.tensor_tensor(o, TMP[:, :], PS1[:, 0:C_SLAB],
                                                             ALU.mult),
                     reads=[TMP_res, PS1_res], writes=[AFS_res], tag="umul")
            for half in range(2):
                pss = [(PS0, PS0_res), (PS1, PS1_res)]
                for i in range(NIC):
                    WOU, WOU_res = WOUs[i % 2]
                    em.d(WCH2[i % 4], WOU[:, :], h.w_out_pad[128 * i:128 * (i + 1), :],
                         writes=[WOU_res], tag="would")
                    for tt2 in range(2):
                        tt = 2 * half + tt2
                        ps, ps_res = pss[tt2]
                        lhsT = AFS[:, C_SLAB * i + 128 * tt:C_SLAB * i + 128 * (tt + 1)]
                        for j in range(2):
                            rhs = WOU[:, 512 * j:512 * (j + 1)]
                            em.c("pe", lambda t, o=ps[:, 512 * j:512 * (j + 1)], l=lhsT,
                                 r=rhs, st=(i == 0): t.matmul(o, l, r, start=st, stop=False),
                                 reads=[AFS_res, WOU_res], writes=[ps_res], tag="wout")
                for tt2 in range(2):
                    tt = 2 * half + tt2
                    ps, ps_res = pss[tt2]
                    boff = bias_off["bout"] - 5120
                    for j in range(2):
                        em.c("pe", lambda t, o=ps[:, 512 * j:512 * (j + 1)],
                             r=BR3[0:1, boff + 512 * j:boff + 512 * (j + 1)]:
                             t.matmul(o, h.ones_row[0:1, 0:128], r, start=False, stop=True),
                             reads=[BR3_res, h.ones_row_res], writes=[ps_res], tag="woutb")
                    gtt = 4 * s + tt
                    em.d(["px", "p0"], XMH[:, :], h.xmid_dram[128 * gtt:128 * (gtt + 1), :],
                         reads=[h.xmid_res], writes=[XMH_res], tag="xm2")
                    em.c("dve", lambda v, i_=ps[:, :]: v.tensor_tensor(T1[:, :], i_,
                                                                       XMH[:, :], ALU.add),
                         reads=[ps_res, XMH_res], writes=[T1_res], tag="finres1")
                    xt = XTOK[:, 1024 * gtt:1024 * (gtt + 1)]
                    em.c("dve", lambda v, x=xt: v.tensor_tensor(T1[:, :], T1[:, :], x,
                                                                ALU.add),
                         reads=[T1_res, XTOK_res], writes=[T1_res], tag="finres2")
                    XO, XO_res = XOs[tt2]
                    emit_ln(em, h, T1[:, :], T1_res, [(XO[:, :], XO_res)], ln_scratch,
                            tag="lnC")
                    em.d("so", h.xcout[128 * gtt:128 * (gtt + 1), :], XO[:, :],
                         reads=[XO_res], writes=[h.xcout_res], tag="xcstore")


WCH2 = ["p0", "p1", "p2", "p3"]


def build_C(core, debug=False):
    import contextlib
    nc = bass.Bass("TRN2", target_bir_lowering=False)
    h = H()
    if debug:
        h.dbg = {}
        h.dbg_res = {}
        for nm in ["q", "k", "v", "g", "rg"]:
            h.dbg[nm] = nc.dram_tensor("dbg_" + nm, [128, 1024], BF16, kind="ExternalOutput")
            h.dbg_res[nm] = Res("dbg_" + nm)
        for nm in ["qs", "ks", "ain", "ret"]:
            h.dbg[nm] = nc.dram_tensor("dbg_" + nm, [128, 128], F32, kind="ExternalOutput")
            h.dbg_res[nm] = Res("dbg_" + nm)
    else:
        h.dbg = None
    h.xc_in = nc.dram_tensor("xc_in", [TOK, D], F32, kind="ExternalInput")
    h.xprefix = nc.dram_tensor("xprefix", [NPRE * 128, D], F32, kind="ExternalInput")
    for nm in ["wq", "wk", "wv", "wg", "wo"]:
        setattr(h, nm, nc.dram_tensor(nm, [D, D], F32, kind="ExternalInput"))
    for nm in ["bq", "bk", "bv", "bg", "bo", "b_out"]:
        setattr(h, nm, nc.dram_tensor(nm, [1, D], F32, kind="ExternalInput"))
    for nm in ["norm1_w", "norm1_b", "gn_w", "gn_b", "norm2_w", "norm2_b"]:
        setattr(h, nm, nc.dram_tensor(nm, [1, D], F32, kind="ExternalInput"))
    h.b_in_u = nc.dram_tensor("b_in_u", [1, 2816], F32, kind="ExternalInput")
    h.b_in_g = nc.dram_tensor("b_in_g", [1, 2816], F32, kind="ExternalInput")
    h.win_u_pre = nc.dram_tensor("win_u_pre", [NIC, 128, 1024], F32, kind="ExternalInput")
    h.win_g_pre = nc.dram_tensor("win_g_pre", [NIC, 128, 1024], F32, kind="ExternalInput")
    h.w_out_pad = nc.dram_tensor("w_out_pad", [NIC * 128, 1024], F32, kind="ExternalInput")
    h.GQ_d = nc.dram_tensor("GQ_d", [128, NH], F32, kind="ExternalInput")
    h.GKN_d = nc.dram_tensor("GKN_d", [128, NH], F32, kind="ExternalInput")
    h.GKS_d = nc.dram_tensor("GKS_d", [128, NH], F32, kind="ExternalInput")
    h.GKSP_d = nc.dram_tensor("GKSP_d", [128, NPRE * NH], F32, kind="ExternalInput")
    h.GDEC_d = nc.dram_tensor("GDEC_d", [64, NPRE * NH], F32, kind="ExternalInput")
    h.MASKT_d = nc.dram_tensor("MASKT_d", [128, 128], BF16, kind="ExternalInput")
    h.xcout = nc.dram_tensor("xcout", [TOK, D], F32, kind="ExternalOutput")
    h.xcout_res = Res("xcout")
    h.xc_in_res = Res("xc_in")
    h.xprefix_res = Res("xprefix")
    h.xmid_dram = nc.dram_tensor("xmid_dram", [TOK, D], F32)
    h.xmid_res = Res("xmid")

    chans = {"p0": "pool", "p1": "pool", "p2": "pool", "p3": "pool", "px": "pool",
             "sk": "sp", "so": "sp"}
    em = Em(nc, chans)
    with contextlib.ExitStack() as outer:
        declare_consts(nc, outer, h)
        load_consts(em, h)
        emit_blockC(em, h, nc, outer)
        em.barrier()
        with nc.Block() as block:
            em.finalize_and_emit(block, None)
    return nc


def prep_C_inputs(inputs, core, xb_full):
    b = core // 4
    q0 = 2048 * (core % 4)
    cc = host_consts()
    xpre = np.zeros((NPRE * 128, D), np.float32)
    xpre[:q0] = xb_full[b, :q0]
    nvalid = q0 // 128
    ii = np.arange(128, dtype=np.float64)
    GQ = np.stack([(g ** ii) / 8.0 for g in GAMMAS], 1).astype(np.float32)
    GKN = np.stack([g ** (-ii) for g in GAMMAS], 1).astype(np.float32)
    GKS = np.stack([g ** (128 - ii) for g in GAMMAS], 1).astype(np.float32)
    GKSP = np.zeros((128, NPRE * NH), np.float32)
    GDEC = np.ones((64, NPRE * NH), np.float32)
    for c in range(NPRE):
        for hh in range(NH):
            if c < nvalid:
                GKSP[:, NH * c + hh] = GKS[:, hh]
                GDEC[:, NH * c + hh] = GAMMAS[hh] ** 128
    maskT = np.zeros((128, 128), BFDT)
    for m_ in range(128):
        maskT[m_, m_:] = 1
    w_in = np.asarray(inputs["L1_attn_w_in"])
    b_in = np.asarray(inputs["L1_attn_b_in"])
    w_out = np.asarray(inputs["L1_attn_w_out"])
    wu = np.zeros((D, NIC * 128), np.float32); wu[:, :2730] = w_in[:, :2730]
    wgg = np.zeros((D, NIC * 128), np.float32); wgg[:, :2730] = w_in[:, 2730:]
    biu = np.zeros((1, NIC * 128), np.float32); biu[0, :2730] = b_in[:2730]
    big = np.zeros((1, NIC * 128), np.float32); big[0, :2730] = b_in[2730:]
    wop = np.zeros((NIC * 128, D), np.float32); wop[:2730] = w_out
    pre = lambda W: np.ascontiguousarray(
        W.reshape(8, 128, NIC, 128).transpose(2, 1, 0, 3).reshape(NIC, 128, 1024))
    m = {
        "xc_in": np.ascontiguousarray(xb_full[b, q0:q0 + 2048, :]),
        "xprefix": xpre,
        "wq": np.asarray(inputs["L1_attn_wq"]), "wk": np.asarray(inputs["L1_attn_wk"]),
        "wv": np.asarray(inputs["L1_attn_wv"]), "wg": np.asarray(inputs["L1_attn_wg"]),
        "wo": np.asarray(inputs["L1_attn_wo"]),
        "bq": np.asarray(inputs["L1_attn_bq"]).reshape(1, -1),
        "bk": np.asarray(inputs["L1_attn_bk"]).reshape(1, -1),
        "bv": np.asarray(inputs["L1_attn_bv"]).reshape(1, -1),
        "bg": np.asarray(inputs["L1_attn_bg"]).reshape(1, -1),
        "bo": np.asarray(inputs["L1_attn_bo"]).reshape(1, -1),
        "b_out": np.asarray(inputs["L1_attn_b_out"]).reshape(1, -1),
        "norm1_w": np.asarray(inputs["L1_attn_norm1_w"]).reshape(1, -1),
        "norm1_b": np.asarray(inputs["L1_attn_norm1_b"]).reshape(1, -1),
        "gn_w": np.asarray(inputs["L1_attn_gn_w"]).reshape(1, -1),
        "gn_b": np.asarray(inputs["L1_attn_gn_b"]).reshape(1, -1),
        "norm2_w": np.asarray(inputs["L1_attn_norm2_w"]).reshape(1, -1),
        "norm2_b": np.asarray(inputs["L1_attn_norm2_b"]).reshape(1, -1),
        "b_in_u": biu, "b_in_g": big,
        "win_u_pre": pre(wu), "win_g_pre": pre(wgg),
        "w_out_pad": wop,
        "GQ_d": GQ, "GKN_d": GKN, "GKS_d": GKS, "GKSP_d": GKSP, "GDEC_d": GDEC,
        "MASKT_d": maskT,
    }
    for k, v in cc.items():
        m["c_" + k] = v
    return m



NEXP = 64
PSLOT = 32
NLOC = 8


def emit_blockD(em, h, nc, outer_ctx):
    import contextlib
    XTOK, XTOK_res = alloc_sb(outer_ctx, nc, "XTOKD", [128, NT * 1024], BF16)
    SQ, SQ_res = alloc_sb(outer_ctx, nc, "SQln", [128, 1024], F32)
    COLS, COLS_res = alloc_sb(outer_ctx, nc, "COLS", [128, 16], F32)
    ln_scratch = {"sq": (SQ, SQ_res), "cols": (COLS, COLS_res)}
    T1, T1_res = alloc_sb(outer_ctx, nc, "T1", [128, 1024], F32)
    XO0, XO0_res = alloc_sb(outer_ctx, nc, "XO0", [128, 1024], F32)
    XOs = [(XO0, XO0_res), (XO0, XO0_res)]
    XPE, XPE_res = alloc_sb(outer_ctx, nc, "XPE", [128, 1024], F32)
    XS0, XS0_res = XPE, XPE_res
    XTN, XTN_res = alloc_sb(outer_ctx, nc, "XTN", [128, NF * 256], BF16)
    X2C, X2C_res = alloc_sb(outer_ctx, nc, "X2CD", [128, 1024], BF16)
    ph12 = contextlib.ExitStack()
    XT, XT_res = alloc_sb(ph12, nc, "XT_D", [128, NF * 2048], BF16)
    PHB, PHB_res = alloc_sb(ph12, nc, "PHB", [128, NF * 2048], BF16)
    DHB, DHB_res = alloc_sb(ph12, nc, "DHB", [128, NT * 2048], BF16)
    ESB, ESB_res = alloc_sb(ph12, nc, "ESB", [128, 2048], F32)
    CHT, CHT_res = alloc_sb(ph12, nc, "CHT", [128, 2048], BF16)
    RSP, RSP_res = alloc_sb(ph12, nc, "RSP", [128, 64], F32)
    RSN, RSN_res = alloc_sb(ph12, nc, "RSN", [128, 32], F32)
    PBR, PBR_res = alloc_sb(ph12, nc, "PBR", [1, 2048], BF16)

    PSL, PSL_res = alloc_ps(outer_ctx, nc, "PSL", [128, 2048], F32)
    PS0, PS0_res = alloc_ps(outer_ctx, nc, "PS0", [128, 1024], F32)
    PSt, PSt_res = alloc_ps(outer_ctx, nc, "PSt", [128, 1024], BF16)
    PSx, PSx_res = alloc_ps(outer_ctx, nc, "PSx", [128, 512], F32)

    for t in range(NT):
        em.d(["px", "p0"], XTOK[:, 1024 * t:1024 * (t + 1)],
             h.xd_in[128 * t:128 * (t + 1), :], reads=[h.xd_in_res],
             writes=[XTOK_res], tag="xload")
    for t in range(NT):
        for f in range(NF):
            src = XTOK[:, 1024 * t + 128 * f:1024 * t + 128 * (f + 1)]
            dst = XT[:, 2048 * f + 128 * t:2048 * f + 128 * (t + 1)]
            emit_transpose(em, h, src, dst, PSt, PSt_res, XTOK_res, XT_res)
    for kk in range(NF):
        em.d("p1", PHB[:, 2048 * kk:2048 * (kk + 1)],
             h.phi_wT[128 * kk:128 * (kk + 1), :], writes=[PHB_res], tag="phld")
    em.d("p2", PBR[:, :], h.phi_b[:, :], writes=[PBR_res])

    for tt in range(NT):
        for j in range(4):
            pl = PSL[:, 512 * j:512 * (j + 1)]
            for kk in range(NF):
                lhsT = XT[:, 2048 * kk + 128 * tt:2048 * kk + 128 * (tt + 1)]
                rhs = PHB[:, 2048 * kk + 512 * j:2048 * kk + 512 * (j + 1)]
                em.c("pe", lambda t, o=pl, l=lhsT, r=rhs, st=(kk == 0):
                     t.matmul(o, l, r, start=st, stop=False),
                     reads=[XT_res, PHB_res], writes=[PSL_res], tag="phi")
            em.c("pe", lambda t, o=pl, r=PBR[0:1, 512 * j:512 * (j + 1)]:
                 t.matmul(o, h.ones_row[0:1, 0:128], r, start=False, stop=True),
                 reads=[PBR_res, h.ones_row_res], writes=[PSL_res], tag="phib")
        nmx = COLS[:, 8:9]
        em.c("dve", lambda v: v.reduce_max(nmx, PSL[:, :], axis=AX.X, negate=True),
             reads=[PSL_res], writes=[COLS_res], tag="dmax")
        em.c("act", lambda s: s.activation(ESB[:, :], PSL[:, :], AF.Exp, bias=nmx),
             reads=[PSL_res, COLS_res], writes=[ESB_res], tag="dexp")
        ev_np = ESB[:, :].rearrange("p (n q) -> p n q", q=PSLOT)
        em.c("dve", lambda v: v.reduce_sum(RSP[:, :], ev_np, axis=AX.X),
             reads=[ESB_res], writes=[RSP_res], tag="dsp")
        ev_pn = ESB[:, :].rearrange("p (n q) -> p q n", q=PSLOT)
        em.c("dve", lambda v: v.reduce_sum(RSN[:, :], ev_pn, axis=AX.X),
             reads=[ESB_res], writes=[RSN_res], tag="dsn")
        em.c("dve", lambda v: v.reciprocal(RSP[:, :], RSP[:, :]),
             reads=[RSP_res], writes=[RSP_res], tag="drsp")
        em.c("dve", lambda v: v.reciprocal(RSN[:, :], RSN[:, :]),
             reads=[RSN_res], writes=[RSN_res], tag="drsn")
        for n_ in range(NEXP):
            dst = DHB[:, 2048 * tt + PSLOT * n_:2048 * tt + PSLOT * (n_ + 1)]
            esl = ESB[:, PSLOT * n_:PSLOT * (n_ + 1)]
            em.c("dve", lambda v, o=dst, i_=esl, sc=RSP[:, n_:n_ + 1]:
                 v.tensor_scalar(o, i_, sc, None, ALU.mult),
                 reads=[ESB_res, RSP_res], writes=[DHB_res], tag="dhat")
        for p_ in range(PSLOT):
            dst = CHT[:, :].rearrange("p (n q) -> p q n", q=PSLOT)[:, p_, :]
            esl = ESB[:, :].rearrange("p (n q) -> p q n", q=PSLOT)[:, p_, :]
            em.c("dve", lambda v, o=dst, i_=esl, sc=RSN[:, p_:p_ + 1]:
                 v.tensor_scalar(o, i_, sc, None, ALU.mult),
                 reads=[ESB_res, RSN_res], writes=[CHT_res], tag="chat")
        for j in range(16):
            src = CHT[:, 128 * j:128 * (j + 1)]
            dst = X2C[:, 0:128]
            emit_transpose(em, h, src, dst, PSt, PSt_res, CHT_res, X2C_res)
            em.d("sq", h.cht_dram[128 * j:128 * (j + 1), 128 * tt:128 * (tt + 1)],
                 X2C[:, 0:128], reads=[X2C_res], writes=[h.cht_res], tag="chtst")

    for j in range(16):
        for jj in range(2):
            ps = PS0[:, 512 * jj:512 * (jj + 1)]
            for tt in range(NT):
                lhsT = DHB[:, 2048 * tt + 128 * j:2048 * tt + 128 * (j + 1)]
                rhs = XTOK[:, 1024 * tt + 512 * jj:1024 * tt + 512 * (jj + 1)]
                em.c("pe", lambda t, o=ps, l=lhsT, r=rhs, st=(tt == 0), sp=(tt == NT - 1):
                     t.matmul(o, l, r, start=st, stop=sp),
                     reads=[DHB_res, XTOK_res], writes=[PS0_res], tag="xtp")
        em.c("act", lambda s: s.copy(XPE[:, :], PS0[:, :]),
             reads=[PS0_res], writes=[XPE_res], tag="xtpev")
        em.d("sq", h.rs_in[128 * j:128 * (j + 1), :], XPE[:, :],
             reads=[XPE_res], writes=[h.rs_in_res], tag="xtpst")
    cc1 = em.c("pool", lambda g: g.collective_compute(
        "ReduceScatter", ALU.add, replica_groups=[list(range(8))],
        ins=[h.rs_in[:, :]], outs=[h.rs_out[:, :]]),
        reads=[h.rs_in_res], writes=[h.rs_out_res], tag="rs", async_=True)
    em.barrier()
    ph12.close()
    WEB0, WEB0_res = alloc_sb(outer_ctx, nc, "WEB0", [128, 8 * 1024], BF16)
    WEB1, WEB1_res = alloc_sb(outer_ctx, nc, "WEB1", [128, 8 * 1024], BF16)
    EBR, EBR_res = alloc_sb(outer_ctx, nc, "EBR", [1, 1024], BF16)
    Y32, Y32_res = alloc_sb(outer_ctx, nc, "Y32", [32, 1024], F32)
    YTB, YTB_res = alloc_sb(outer_ctx, nc, "YTB", [128, 16 * 1024], BF16)
    CTB0, CTB0_res = alloc_sb(outer_ctx, nc, "CTB0", [128, 128], BF16)
    CTB1, CTB1_res = alloc_sb(outer_ctx, nc, "CTB1", [128, 128], BF16)
    WEBs = [(WEB0, WEB0_res), (WEB1, WEB1_res)]
    CTBs = [(CTB0, CTB0_res), (CTB1, CTB1_res)]

    for half in range(2):
        em.d(["px", "p0"], XS0[:, :], h.rs_out[128 * half:128 * (half + 1), :],
             reads=[h.rs_out_res], writes=[XS0_res], tag="rsld")
        em.c("dve", lambda v: v.tensor_scalar_max(XS0[:, :], XS0[:, :], -33000.0),
             reads=[XS0_res], writes=[XS0_res], tag="clip1")
        em.c("dve", lambda v: v.tensor_scalar_min(XS0[:, :], XS0[:, :], 65000.0),
             reads=[XS0_res], writes=[XS0_res], tag="clip2")
        emit_ln(em, h, XS0[:, :], XS0_res, [(T1[:, :], T1_res)], ln_scratch, tag="lnD")
        em.c("act", lambda s: s.activation(X2C[:, :], T1[:, :], AF.Tanh),
             reads=[T1_res], writes=[X2C_res], tag="dtanh")
        for f in range(NF):
            src = X2C[:, 128 * f:128 * (f + 1)]
            dst = XTN[:, 256 * f + 128 * half:256 * f + 128 * (half + 1)]
            emit_transpose(em, h, src, dst, PSt, PSt_res, X2C_res, XTN_res)
    for e in range(NLOC):
        WEB, WEB_res = WEBs[e % 2]
        if e < 2:
            for kk in range(NF):
                em.d(["p1", "p2", "p3"][kk % 3], WEBs[e][0][:, 1024 * kk:1024 * (kk + 1)],
                     h.expert_w[e, 128 * kk:128 * (kk + 1), :],
                     writes=[WEBs[e][1]], tag="ewld")
        em.d("p3", EBR[:, :], h.expert_b[e:e + 1, :], reads=[EBR_res], writes=[EBR_res],
             tag="ebld")
        pp = PS0[0:32, :]
        for jj in range(2):
            for kk in range(NF):
                lhsT = XTN[:, 256 * kk + 32 * e:256 * kk + 32 * (e + 1)]
                rhs = WEB[:, 1024 * kk + 512 * jj:1024 * kk + 512 * (jj + 1)]
                em.c("pe", lambda t, o=pp[:, 512 * jj:512 * (jj + 1)], l=lhsT, r=rhs,
                     st=(kk == 0): t.matmul(o, l, r, start=st, stop=False),
                     reads=[XTN_res, WEB_res], writes=[PS0_res], tag="yt")
            em.c("pe", lambda t, o=pp[:, 512 * jj:512 * (jj + 1)],
                 r=EBR[0:1, 512 * jj:512 * (jj + 1)]: t.matmul(
                 o, h.ones_row[0:1, 0:32], r, start=False, stop=True),
                 reads=[EBR_res, h.ones_row_res], writes=[PS0_res], tag="ytb")
        em.c("dve", lambda v: v.tensor_copy(Y32[:, :], pp),
             reads=[PS0_res], writes=[Y32_res], tag="ytev")
        em.d("sq", h.ag_in[32 * e:32 * (e + 1), :], Y32[:, :],
             reads=[Y32_res], writes=[h.ag_in_res], tag="ytst")
        if e + 2 < NLOC:
            WEBn, WEBn_res = WEBs[e % 2]
            for kk in range(NF):
                em.d(["p1", "p2", "p3"][kk % 3], WEBn[:, 1024 * kk:1024 * (kk + 1)],
                     h.expert_w[e + 2, 128 * kk:128 * (kk + 1), :],
                     writes=[WEBn_res], tag="ewld2")
    cc2 = em.c("pool", lambda g: g.collective_compute(
        "AllGather", ALU.bypass, replica_groups=[list(range(8))],
        ins=[h.ag_in[:, :]], outs=[h.ag_out[:, :]]),
        reads=[h.ag_in_res], writes=[h.ag_out_res], tag="ag", async_=True)

    for j in range(16):
        em.d(["px", "p0"], YTB[:, 1024 * j:1024 * (j + 1)],
             h.ag_out[128 * j:128 * (j + 1), :], reads=[h.ag_out_res],
             writes=[YTB_res], tag="ytld")
    for tt in range(NT):
        if tt == 0:
            for j in range(2):
                CTB, CTB_res = CTBs[j % 2]
                em.d("sk", CTB[:, :], h.cht_dram[128 * j:128 * (j + 1), 0:128],
                     reads=[h.cht_res], writes=[CTB_res], tag="ctld")
        ps = PS0
        for j in range(16):
            CTB, CTB_res = CTBs[j % 2]
            for jj in range(2):
                rhs = YTB[:, 1024 * j + 512 * jj:1024 * j + 512 * (jj + 1)]
                em.c("pe", lambda t, o=ps[:, 512 * jj:512 * (jj + 1)], l=CTB[:, :], r=rhs,
                     st=(j == 0), sp=(j == 15), cr=CTB_res: t.matmul(o, l, r, start=st,
                                                                    stop=sp),
                     reads=[CTB_res, YTB_res], writes=[PS0_res], tag="ymm")
            nj, ntt = (j + 2, tt) if j + 2 < 16 else ((j + 2) % 16, tt + 1)
            if ntt < NT:
                CTBn, CTBn_res = CTBs[j % 2]
                em.d("sk", CTBn[:, :],
                     h.cht_dram[128 * nj:128 * (nj + 1), 128 * ntt:128 * (ntt + 1)],
                     reads=[h.cht_res], writes=[CTBn_res], tag="ctld2")
        xt = XTOK[:, 1024 * tt:1024 * (tt + 1)]
        em.c("dve", lambda v, x=xt: v.tensor_tensor(T1[:, :], PS0[:, :], x, ALU.add),
             reads=[PS0_res, XTOK_res], writes=[T1_res], tag="dres")
        XO, XO_res = XOs[tt % 2]
        emit_ln(em, h, T1[:, :], T1_res, [(XO[:, :], XO_res)], ln_scratch, tag="lnF")
        em.d("so", h.xout[128 * tt:128 * (tt + 1), :], XO[:, :],
             reads=[XO_res], writes=[h.xout_res], tag="xostore")


def build_D():
    import contextlib
    nc = bass.Bass("TRN2", target_bir_lowering=False)
    h = H()
    h.xd_in = nc.dram_tensor("xd_in", [TOK, D], F32, kind="ExternalInput")
    h.phi_wT = nc.dram_tensor("phi_wT", [D, 2048], F32, kind="ExternalInput")
    h.phi_b = nc.dram_tensor("phi_b", [1, 2048], F32, kind="ExternalInput")
    h.expert_w = nc.dram_tensor("expert_w", [NLOC, D, D], F32, kind="ExternalInput")
    h.expert_b = nc.dram_tensor("expert_b", [NLOC, D], F32, kind="ExternalInput")
    h.xout = nc.dram_tensor("xout", [TOK, D], F32, kind="ExternalOutput")
    h.xout_res = Res("xout")
    h.xd_in_res = Res("xd_in")
    h.cht_dram = nc.dram_tensor("cht_dram", [2048, 2048], BF16)
    h.cht_res = Res("cht")
    h.rs_in = nc.dram_tensor("rs_in", [2048, 1024], F32)
    h.rs_in_res = Res("rs_in")
    h.rs_out = nc.dram_tensor("rs_out", [256, 1024], F32)
    h.rs_out_res = Res("rs_out")
    h.ag_in = nc.dram_tensor("ag_in", [256, 1024], F32)
    h.ag_in_res = Res("ag_in")
    h.ag_out = nc.dram_tensor("ag_out", [2048, 1024], F32, addr_space="Shared")
    h.ag_out_res = Res("ag_out")

    chans = {"p0": "pool", "p1": "pool", "p2": "pool", "p3": "pool", "px": "pool",
             "sk": "sp", "sq": "sp", "so": "sp"}
    em = Em(nc, chans)
    with contextlib.ExitStack() as outer:
        declare_consts(nc, outer, h)
        load_consts(em, h)
        emit_blockD(em, h, nc, outer)
        em.barrier()
        with nc.Block() as block:
            em.finalize_and_emit(block, None)
    return nc


def prep_D_inputs(inputs, core, xc_full):
    b = core // 4
    q0 = 2048 * (core % 4)
    cc = host_consts()
    m = {
        "xd_in": np.ascontiguousarray(xc_full[b, q0:q0 + 2048, :]),
        "phi_wT": np.ascontiguousarray(np.asarray(inputs["L1_ffn_phi_w"]).T),
        "phi_b": np.asarray(inputs["L1_ffn_phi_b"]).reshape(1, -1),
        "expert_w": np.ascontiguousarray(np.asarray(inputs["L1_ffn_expert_w"])[8 * core:8 * core + 8]),
        "expert_b": np.ascontiguousarray(np.asarray(inputs["L1_ffn_expert_b"])[8 * core:8 * core + 8]),
    }
    for k, v in cc.items():
        m["c_" + k] = v
    return m



def _gather(res, key):
    out = np.zeros((2, 8192, 1024), np.float32)
    for c in range(NCORE):
        b, q0 = c // 4, 2048 * (c % 4)
        out[b, q0:q0 + 2048] = res.results[c][key]
    return out


def kernel(**inputs):
    nc = build_fused()
    res = run_bass_kernel_spmd(nc, [prep_fused_inputs(inputs, c) for c in range(NCORE)],
                               core_ids=list(range(NCORE)))
    return _gather(res, "xout")



def build_fused():
    import contextlib
    nc = bass.Bass("TRN2", target_bir_lowering=False)
    h = H()
    h.x_own = nc.dram_tensor("x_own", [TOK, D], F32, kind="ExternalInput")
    h.x_batch = nc.dram_tensor("x_batch", [SEQ, D], F32, kind="ExternalInput")
    h.qkv_wT = nc.dram_tensor("qkv_wT", [D, 3 * D], F32, kind="ExternalInput")
    h.qkv_b = nc.dram_tensor("qkv_b", [1, 3 * D], F32, kind="ExternalInput")
    h.out_wT = nc.dram_tensor("out_wT", [D, D], F32, kind="ExternalInput")
    h.out_b = nc.dram_tensor("out_b", [1, D], F32, kind="ExternalInput")
    h.gate_w = nc.dram_tensor("gate_w", [D, NE], F32, kind="ExternalInput")
    h.Wg_pre = nc.dram_tensor("Wg_pre", [NE, NI, 128, 1024], F32, kind="ExternalInput")
    h.Wu_pre = nc.dram_tensor("Wu_pre", [NE, NI, 128, 1024], F32, kind="ExternalInput")
    h.Wd = nc.dram_tensor("Wd", [NE, 2816, 1024], F32, kind="ExternalInput")
    h.Sg_pre = nc.dram_tensor("Sg_pre", [NIS, 128, 1024], F32, kind="ExternalInput")
    h.Su_pre = nc.dram_tensor("Su_pre", [NIS, 128, 1024], F32, kind="ExternalInput")
    h.Sd = nc.dram_tensor("Sd", [1408, 1024], F32, kind="ExternalInput")
    for nm in ["wq", "wk", "wv", "wg", "wo"]:
        setattr(h, nm, nc.dram_tensor(nm, [D, D], F32, kind="ExternalInput"))
    for nm in ["bq", "bk", "bv", "bg", "bo", "b_out"]:
        setattr(h, nm, nc.dram_tensor(nm, [1, D], F32, kind="ExternalInput"))
    for nm in ["norm1_w", "norm1_b", "gn_w", "gn_b", "norm2_w", "norm2_b"]:
        setattr(h, nm, nc.dram_tensor(nm, [1, D], F32, kind="ExternalInput"))
    h.b_in_u = nc.dram_tensor("b_in_u", [1, 2816], F32, kind="ExternalInput")
    h.b_in_g = nc.dram_tensor("b_in_g", [1, 2816], F32, kind="ExternalInput")
    h.win_u_pre = nc.dram_tensor("win_u_pre", [NIC, 128, 1024], F32, kind="ExternalInput")
    h.win_g_pre = nc.dram_tensor("win_g_pre", [NIC, 128, 1024], F32, kind="ExternalInput")
    h.w_out_pad = nc.dram_tensor("w_out_pad", [NIC * 128, 1024], F32, kind="ExternalInput")
    h.GQ_d = nc.dram_tensor("GQ_d", [128, NH], F32, kind="ExternalInput")
    h.GKN_d = nc.dram_tensor("GKN_d", [128, NH], F32, kind="ExternalInput")
    h.GKS_d = nc.dram_tensor("GKS_d", [128, NH], F32, kind="ExternalInput")
    h.GKSP_d = nc.dram_tensor("GKSP_d", [128, NPRE * NH], F32, kind="ExternalInput")
    h.GDEC_d = nc.dram_tensor("GDEC_d", [64, NPRE * NH], F32, kind="ExternalInput")
    h.MASKT_d = nc.dram_tensor("MASKT_d", [128, 128], BF16, kind="ExternalInput")
    h.phi_wT = nc.dram_tensor("phi_wT", [D, 2048], F32, kind="ExternalInput")
    h.phi_b = nc.dram_tensor("phi_b", [1, 2048], F32, kind="ExternalInput")
    h.expert_w = nc.dram_tensor("expert_w", [NLOC, D, D], F32, kind="ExternalInput")
    h.expert_b = nc.dram_tensor("expert_b", [NLOC, D], F32, kind="ExternalInput")
    h.xout = nc.dram_tensor("xout", [TOK, D], F32, kind="ExternalOutput")
    h.xout_res = Res("xout")
    h.K_dram = nc.dram_tensor("K_dram", [D, SEQ], BF16); h.K_dram_res = Res("K")
    h.V_dram = nc.dram_tensor("V_dram", [D, SEQ], BF16); h.V_dram_res = Res("V")
    h.Q_dram = nc.dram_tensor("Q_dram", [D, TOK], BF16); h.Q_dram_res = Res("Q")
    h.xa = nc.dram_tensor("xa_i", [TOK, D], F32); h.xa_res = Res("xa")
    h.xb = nc.dram_tensor("xb_i", [TOK, D], F32); h.xb_res = Res("xb")
    h.xcout = nc.dram_tensor("xc_i", [TOK, D], F32); h.xcout_res = Res("xc")
    h.xb_all = nc.dram_tensor("xb_all", [SEQ, D], F32); h.xb_all_res = Res("xb_all")
    h.xmid_dram = nc.dram_tensor("xmid_dram", [TOK, D], F32); h.xmid_res = Res("xmid")
    h.cht_dram = nc.dram_tensor("cht_dram", [2048, 2048], BF16); h.cht_res = Res("cht")
    h.rs_in = nc.dram_tensor("rs_in", [2048, 1024], F32); h.rs_in_res = Res("rs_in")
    h.rs_out = nc.dram_tensor("rs_out", [256, 1024], F32); h.rs_out_res = Res("rs_out")
    h.ag_in = nc.dram_tensor("ag_in", [256, 1024], F32); h.ag_in_res = Res("ag_in")
    h.ag_out = nc.dram_tensor("ag_out", [2048, 1024], F32, addr_space="Shared")
    h.ag_out_res = Res("ag_out")
    h.xb_in = h.xa
    h.xb_in_res = h.xa_res
    h.xc_in = h.xb
    h.xc_in_res = h.xb_res
    h.xprefix = h.xb_all
    h.xprefix_res = h.xb_all_res
    h.xd_in = h.xcout
    h.xd_in_res = h.xcout_res
    h.dbg = None

    chans = {"p0": "pool", "p1": "pool", "p2": "pool", "p3": "pool", "px": "pool",
             "pw": "pool", "sk": "sp", "sq": "sp", "so": "sp"}
    em = Em(nc, chans)
    with contextlib.ExitStack() as outer:
        declare_consts(nc, outer, h)
        load_consts(em, h)
        with contextlib.ExitStack() as sA:
            emit_blockA(em, h, None, nc, sA)
        em.barrier()
        with contextlib.ExitStack() as sB:
            emit_blockB_fused(em, h, nc, sB)
        em.barrier()
        em.c("pool", lambda g: g.collective_compute(
            "AllGather", ALU.bypass, replica_groups=[[0, 1, 2, 3], [4, 5, 6, 7]],
            ins=[h.xb[:, :]], outs=[h.xb_all[:, :]]),
            reads=[h.xb_res], writes=[h.xb_all_res], tag="agxb", async_=True)
        with contextlib.ExitStack() as sC:
            emit_blockC_fused(em, h, nc, sC)
        em.barrier()
        with contextlib.ExitStack() as sD:
            emit_blockD(em, h, nc, sD)
        em.barrier()
        with nc.Block() as block:
            em.finalize_and_emit(block, None)
    return nc


def emit_blockB_fused(em, h, nc, ctx):
    emit_blockB(em, h, nc, ctx)


def emit_blockC_fused(em, h, nc, ctx):
    emit_blockC(em, h, nc, ctx)


def prep_fused_inputs(inputs, core):
    m = {}
    m.update(prep_A_inputs(inputs, core))
    dummy = np.zeros((2, 8192, 1024), np.float32)
    mb = prep_B_inputs(inputs, core, dummy); mb.pop("xb_in")
    m.update(mb)
    mc = prep_C_inputs(inputs, core, dummy); mc.pop("xc_in"); mc.pop("xprefix")
    m.update(mc)
    md = prep_D_inputs(inputs, core, dummy); md.pop("xd_in")
    m.update(md)
    return m


# version 03
# speedup vs baseline: 1.1455x; 1.1455x over previous
import math
import numpy as np
import ml_dtypes
import concourse.bass as bass
import concourse.mybir as mybir
from concourse.bass_utils import run_bass_kernel_spmd

F32 = mybir.dt.float32
BF16 = mybir.dt.bfloat16
AF = mybir.ActivationFunctionType
ALU = mybir.AluOpType
AX = mybir.AxisListType
BFDT = ml_dtypes.bfloat16

NCORE = 8
TOK = 2048
NT = TOK // 128
D = 1024
NF = D // 128
SEQ = 8192
EPS = 1e-5


class Res:
    __slots__ = ("name", "last_w", "readers")
    def __init__(self, name=""):
        self.name = name
        self.last_w = None
        self.readers = []

class Op:
    __slots__ = ("idx", "stream", "timeline", "kind", "fn", "deps", "inc", "val", "tag",
                 "async_")
    def __init__(self, idx, stream, timeline, kind, fn, tag=""):
        self.async_ = False
        self.idx = idx
        self.stream = stream
        self.timeline = timeline
        self.kind = kind
        self.fn = fn
        self.deps = []
        self.inc = False
        self.val = None
        self.tag = tag

COMPUTE_STREAMS = ("pe", "act", "dve", "pool", "sp")

class Em:
    def __init__(self, nc, chans):
        self.nc = nc
        self.ops = []
        self.streams = {s: [] for s in COMPUTE_STREAMS}
        self.chan_stream = dict(chans)
        self.chan_count = {c: 0 for c in chans}
        self.chan_last = {c: None for c in chans}
        self.grp_rr = {}

    def _mkop(self, stream, timeline, kind, fn, reads, writes, deps, tag):
        op = Op(len(self.ops), stream, timeline, kind, fn, tag)
        dd = []
        for r in reads:
            if r.last_w is not None:
                dd.append(r.last_w)
        for w in writes:
            if w.last_w is not None:
                dd.append(w.last_w)
            dd.extend(w.readers)
        dd.extend(deps)
        keep = []
        seen = set()
        for d in dd:
            if d is op:
                continue
            if (d.timeline == op.timeline and d.kind == "c" and op.kind == "c"
                    and d.timeline == "pe"):
                continue
            if (d.kind == "c" and op.kind == "d" and d.stream == op.stream
                    and not d.async_):
                continue
            if id(d) not in seen:
                seen.add(id(d))
                keep.append(d)
        op.deps = keep
        for r in reads:
            r.readers.append(op)
        for w in writes:
            w.last_w = op
            w.readers = []
        self.ops.append(op)
        self.streams[stream].append(op)
        return op

    def c(self, stream, fn, reads=(), writes=(), deps=(), tag="", async_=False):
        op = self._mkop(stream, stream, "c", fn, reads, writes, deps, tag)
        op.async_ = async_
        return op

    def d(self, chan, out_ap, in_ap, reads=(), writes=(), deps=(), tag="", **kw):
        if isinstance(chan, (list, tuple)):
            key = id(chan)
            i = self.grp_rr.get(key, 0)
            self.grp_rr[key] = i + 1
            chan = chan[i % len(chan)]
        stream = self.chan_stream[chan]
        fn = lambda eng, o=out_ap, i=in_ap, kw=kw: eng.dma_start(out=o, in_=i, **kw)
        deps = list(deps)
        if self.chan_last[chan] is not None:
            deps.append(self.chan_last[chan])
        op = self._mkop(stream, chan, "d", fn, reads, writes, deps, tag)
        self.chan_last[chan] = op
        return op

    def barrier(self):
        lasts = {}
        for op in self.ops:
            if op.kind != "w":
                lasts[op.timeline] = op
        ops = list(lasts.values())
        for s in COMPUTE_STREAMS:
            def mk(eng):
                return None
            o = Op(len(self.ops), s, s, "w", mk, "barrier")
            o.deps = [d for d in ops if not (d.timeline == s and d.kind == "c")]
            self.ops.append(o)
            self.streams[s].append(o)

    def finalize_and_emit(self, block, engines):
        for op in self.ops:
            for dpp in op.deps:
                dpp.inc = True
        cnt = {}
        for op in self.ops:
            if op.kind == "d":
                c = cnt.get(op.timeline, 0) + 16
                cnt[op.timeline] = c
                op.val = c
                op.inc = True
            elif op.inc and op.kind != "w":
                c = cnt.get(op.timeline, 0) + 1
                cnt[op.timeline] = c
                op.val = c
        used = sorted(set(op.timeline for op in self.ops if op.inc))
        sems = {t: self.nc.alloc_semaphore(name=f"sem_{t}") for t in used}
        self.sems = sems
        self.final_counts = dict(cnt)

        def run_stream(eng, stream_name):
            waited = {}
            for op in self.streams[stream_name]:
                for dpp in op.deps:
                    sem = sems[dpp.timeline]
                    if waited.get(dpp.timeline, 0) < dpp.val:
                        eng.wait_ge(sem, dpp.val)
                        waited[dpp.timeline] = dpp.val
                if op.kind == "w":
                    continue
                instr = op.fn(eng)
                if op.inc:
                    instr.then_inc(sems[op.timeline], 16 if op.kind == "d" else 1)

        @block.sync
        def _(sync):
            run_stream(sync, "sp")

        @block.tensor
        def _(t):
            run_stream(t, "pe")

        @block.scalar
        def _(s):
            run_stream(s, "act")

        @block.vector
        def _(v):
            run_stream(v, "dve")

        @block.gpsimd
        def _(g):
            run_stream(g, "pool")



class H:
    pass


_ALLOC_N = [0]

def alloc_sb(ctx, nc, name, shape, dtype):
    _ALLOC_N[0] += 1
    t = ctx.enter_context(nc.sbuf_tensor(f"{name}_{_ALLOC_N[0]}", shape, dtype))
    return t, Res(name)


def alloc_ps(ctx, nc, name, shape, dtype):
    _ALLOC_N[0] += 1
    t = ctx.enter_context(nc.psum_tensor(f"{name}_{_ALLOC_N[0]}", shape, dtype))
    return t, Res(name)


def emit_transpose(em, h, src_ap, dst_ap, psum, psum_res, src_res, dst_res, evac="dve"):
    n = src_ap.shape[1]
    p = src_ap.shape[0]
    par = h.tr_par = (getattr(h, "tr_par", 0) + 1) % 2
    pt = psum[0:n, 1024 * par:1024 * par + p]
    idn = h.ident[0:p, 0:p]
    pres = psum_res[par]
    em.c("pe", lambda t, o=pt, i=src_ap, e=idn: t.transpose(o, i, e),
         reads=[src_res, h.ident_res], writes=[pres], tag="tr")
    em.c(evac, lambda v, o=dst_ap, i=pt: v.tensor_copy(o, i),
         reads=[pres], writes=[dst_res], tag="tr_ev")


def emit_ln(em, h, src_ap, src_res, outs, scratch, tag="ln"):
    sq, sq_res = scratch["sq"]
    cols, cols_res = scratch["cols"]
    ssum = cols[:, 0:1]; ssq = cols[:, 1:2]; mean = cols[:, 2:3]
    varr = cols[:, 3:4]; rstd = cols[:, 4:5]; negmr = cols[:, 5:6]
    em.c("dve", lambda v: v.reduce_sum(ssum, src_ap, axis=AX.X),
         reads=[src_res], writes=[cols_res], tag=tag + "_sum")
    em.c("act", lambda s: s.activation(sq[:, :], src_ap, AF.Square, accum_out=ssq),
         reads=[src_res], writes=[sq_res, cols_res], tag=tag + "_sq")
    em.c("dve", lambda v: v.tensor_scalar(mean, ssum, 1.0 / 1024.0, None, ALU.mult),
         reads=[cols_res], writes=[cols_res], tag=tag)
    em.c("dve", lambda v: v.tensor_scalar(varr, ssq, 1.0 / 1024.0, EPS, ALU.mult, ALU.add),
         reads=[cols_res], writes=[cols_res], tag=tag)
    tmp = cols[:, 6:7]
    em.c("dve", lambda v: v.tensor_tensor(tmp, mean, mean, ALU.mult),
         reads=[cols_res], writes=[cols_res], tag=tag)
    em.c("dve", lambda v: v.tensor_tensor(varr, varr, tmp, ALU.subtract),
         reads=[cols_res], writes=[cols_res], tag=tag)
    em.c("act", lambda s: s.activation(rstd, varr, AF.Ln),
         reads=[cols_res], writes=[cols_res], tag=tag + "_ln")
    em.c("act", lambda s: s.activation(rstd, rstd, AF.Exp, scale=-0.5),
         reads=[cols_res], writes=[cols_res], tag=tag + "_exp")
    em.c("dve", lambda v: v.scalar_tensor_tensor(negmr, mean, -1.0, rstd, ALU.mult, ALU.mult),
         reads=[cols_res], writes=[cols_res], tag=tag)
    for dst_ap, dst_res in outs:
        em.c("dve", lambda v, o=dst_ap: v.tensor_scalar(o, src_ap, rstd, negmr, ALU.mult, ALU.add),
             reads=[src_res, cols_res], writes=[dst_res], tag=tag + "_app")



def host_consts():
    c = {}
    c["ident"] = np.eye(128, dtype=BFDT)
    c["ones_row"] = np.ones((1, 512), dtype=BFDT)
    c["ones64c"] = np.ones((64, 1), dtype=BFDT)
    c["ones1x64f"] = np.ones((1, 64), dtype=np.float32)
    c["ones1x128f"] = np.ones((1, 128), dtype=np.float32)
    halfA = np.zeros((128, 1), dtype=BFDT); halfA[0:64] = 1
    halfB = np.zeros((128, 1), dtype=BFDT); halfB[64:128] = 1
    c["halfA"] = halfA
    c["halfB"] = halfB
    return c


def declare_consts(nc, ctx, h):
    specs = [
        ("ident", [128, 128], BF16),
        ("ones_row", [1, 512], BF16),
        ("ones64c", [64, 1], BF16),
        ("ones1x64f", [1, 64], F32),
        ("ones1x128f", [1, 128], F32),
        ("halfA", [128, 1], BF16),
        ("halfB", [128, 1], BF16),
    ]
    h.const_dram = {}
    for name, shape, dt in specs:
        dram = nc.dram_tensor("c_" + name, shape, dt, kind="ExternalInput")
        sb, res = alloc_sb(ctx, nc, "s_" + name, shape, dt)
        h.const_dram[name] = dram
        setattr(h, name, sb)
        setattr(h, name + "_res", res)

def load_consts(em, h):
    for name in h.const_dram:
        dram = h.const_dram[name]
        sb = getattr(h, name)
        res = getattr(h, name + "_res")
        em.d("sk", sb[:, :], dram[:, :], writes=[res], tag="const")



A_PHASES = {0, 1, 2, 3, 4}

def emit_blockA(em, h, ctx, nc, outer_ctx):
    import contextlib

    XHAT, XHAT_res = alloc_sb(outer_ctx, nc, "XHAT", [128, NT * 1024], BF16)
    WOUT, WOUT_res = alloc_sb(outer_ctx, nc, "WOUT", [128, 8 * 1024], BF16)
    BOUT, BOUT_res = alloc_sb(outer_ctx, nc, "BOUT", [1, 1024], BF16)
    SQ, SQ_res = alloc_sb(outer_ctx, nc, "SQln", [128, 1024], F32)
    COLS, COLS_res = alloc_sb(outer_ctx, nc, "COLS", [128, 16], F32)
    ln_scratch = {"sq": (SQ, SQ_res), "cols": (COLS, COLS_res)}
    QMX, QMX_res = alloc_sb(outer_ctx, nc, "QMX", [1, 16], F32)
    KMX, KMX_res = alloc_sb(outer_ctx, nc, "KMX", [1, 16], F32)
    NTMP, NTMP_res = alloc_sb(outer_ctx, nc, "NTMP", [1, 4], F32)
    em.c("dve", lambda v: v.memset(QMX[:, :], 0.0), writes=[QMX_res])
    em.c("dve", lambda v: v.memset(KMX[:, :], 0.0), writes=[KMX_res])

    PS0, PS0_res = alloc_ps(outer_ctx, nc, "PS0", [128, 1024], F32)
    PS1, PS1_res = alloc_ps(outer_ctx, nc, "PS1", [128, 1024], F32)
    PS2, PS2_res = alloc_ps(outer_ctx, nc, "PS2", [128, 1024], F32)
    PSt, _pstres = alloc_ps(outer_ctx, nc, "PSt", [128, 2048], BF16)
    PSt_res = [_pstres, Res("PSt1")]

    for f in range(NF):
        em.d("pw", WOUT[:, 1024 * f:1024 * (f + 1)], h.out_wT[128 * f:128 * (f + 1), :],
             writes=[WOUT_res], tag="woutld")
    em.d("pw", BOUT[:, :], h.out_b[:, :], writes=[BOUT_res])

    with contextlib.ExitStack() as ph12:
        XT, XT_res = alloc_sb(ph12, nc, "XT", [128, NF * 2048], BF16)
        WQKV, WQKV_res = alloc_sb(ph12, nc, "WQKV", [128, 8 * 3072], BF16)
        BQKV, BQKV_res = alloc_sb(ph12, nc, "BQKV", [1, 3072], BF16)
        XBH0, XBH0_res = alloc_sb(ph12, nc, "XBH0", [128, 4 * 1024], BF16)
        XBH1, XBH1_res = alloc_sb(ph12, nc, "XBH1", [128, 4 * 1024], BF16)
        XBT0, XBT0_res = alloc_sb(ph12, nc, "XBT0", [128, 8 * 512], BF16)
        XBT1, XBT1_res = alloc_sb(ph12, nc, "XBT1", [128, 8 * 512], BF16)
        KVE, KVE_res = alloc_sb(ph12, nc, "KVE", [128, 4 * 512], BF16)
        SQ2, SQ2_res = alloc_sb(ph12, nc, "SQ2", [128, 512], BF16)
        XBHs = [(XBH0, XBH0_res), (XBH1, XBH1_res)]
        XBTs = [(XBT0, XBT0_res), (XBT1, XBT1_res)]

        for kk in range(NF):
            em.d("pw", WQKV[:, 3072 * kk:3072 * (kk + 1)],
                 h.qkv_wT[128 * kk:128 * (kk + 1), :], writes=[WQKV_res], tag="wqkvld")
        em.d("pw", BQKV[:, :], h.qkv_b[:, :], writes=[BQKV_res])

        for t in range(NT):
            em.d("px", XHAT[:, 1024 * t:1024 * (t + 1)],
                 h.x_own[128 * t:128 * (t + 1), :], writes=[XHAT_res], tag="xload")
        for t in range(NT):
            for f in range(NF):
                src = XHAT[:, 1024 * t + 128 * f:1024 * t + 128 * (f + 1)]
                dst = XT[:, 2048 * f + 128 * t:2048 * f + 128 * (t + 1)]
                emit_transpose(em, h, src, dst, PSt, PSt_res, XHAT_res, XT_res)

        pspool = [(PS0, PS0_res), (PS1, PS1_res), (PS2, PS2_res)]
        pi = 0
        for f in range(NF if 1 in A_PHASES else 0):
            for s in range(4):
                ps, ps_res = pspool[pi % 3]; pi += 1
                pp = ps[:, 0:512]
                for kk in range(NF):
                    lhsT = WQKV[:, 3072 * kk + 128 * f:3072 * kk + 128 * (f + 1)]
                    rhs = XT[:, 2048 * kk + 512 * s:2048 * kk + 512 * (s + 1)]
                    em.c("pe", lambda t, o=pp, l=lhsT, r=rhs, st=(kk == 0): t.matmul(
                        o, l, r, start=st, stop=False),
                        reads=[WQKV_res, XT_res], writes=[ps_res], tag="qproj")
                brow = BQKV[0:1, 128 * f:128 * (f + 1)]
                em.c("pe", lambda t, o=pp, l=brow: t.matmul(
                    o, l, h.ones_row[0:1, :], start=False, stop=True),
                    reads=[BQKV_res, h.ones_row_res], writes=[ps_res], tag="qbias")
                qe = KVE[:, 0:512]
                em.c("act", lambda s_, o=qe, i=pp: s_.copy(o, i),
                     reads=[ps_res], writes=[KVE_res], tag="qevac")
                em.d("sq", h.Q_dram[128 * f:128 * (f + 1), 512 * s:512 * (s + 1)],
                     qe, reads=[KVE_res], writes=[h.Q_dram_res], tag="qstore")
                em.c("act", lambda s_, i=qe: s_.activation(SQ2[:, :], i, AF.Square),
                     reads=[KVE_res], writes=[SQ2_res], tag="qsq")
                for half, hh in ((h.halfA[:, :], 0), (h.halfB[:, :], 1)):
                    head = 2 * f + hh
                    pn = ps[0:1, 0:512]
                    em.c("pe", lambda t, l=half, o=pn: t.matmul(o, l, SQ2[:, :], start=True, stop=True),
                         reads=[SQ2_res, h.halfA_res, h.halfB_res], writes=[ps_res], tag="qn")
                    em.c("dve", lambda v, i=pn: v.reduce_max(NTMP[0:1, 0:1], i, axis=AX.X),
                         reads=[ps_res], writes=[NTMP_res], tag="qnev")
                    qmc = QMX[0:1, head:head + 1]
                    em.c("dve", lambda v, o=qmc: v.tensor_tensor(o, o, NTMP[0:1, 0:1], ALU.max),
                         reads=[NTMP_res, QMX_res], writes=[QMX_res], tag="qnmax")

        for s16 in range(16 if 2 in A_PHASES else 0):
            par = s16 % 2
            XBH, XBH_res = XBHs[par]
            XBT, XBT_res = XBTs[par]
            for j in range(4):
                tok0 = 512 * s16 + 128 * j
                em.d("px", XBH[:, 1024 * j:1024 * (j + 1)],
                     h.x_batch[tok0:tok0 + 128, :], writes=[XBH_res], tag="xbload")
            for j in range(4):
                for f in range(NF):
                    src = XBH[:, 1024 * j + 128 * f:1024 * j + 128 * (f + 1)]
                    dst = XBT[:, 512 * f + 128 * j:512 * f + 128 * (j + 1)]
                    emit_transpose(em, h, src, dst, PSt, PSt_res, XBH_res, XBT_res)
            for kvi, (dram, dram_res) in ((1, (h.K_dram, h.K_dram_res)),
                                          (2, (h.V_dram, h.V_dram_res))):
                for f in range(NF):
                    ps, ps_res = pspool[pi % 3]; pi += 1
                    pp = ps[:, 0:512]
                    outc = 1024 * kvi + 128 * f
                    for kk in range(NF):
                        lhsT = WQKV[:, 3072 * kk + outc:3072 * kk + outc + 128]
                        rhs = XBT[:, 512 * kk:512 * (kk + 1)]
                        em.c("pe", lambda t, o=pp, l=lhsT, r=rhs, st=(kk == 0): t.matmul(
                            o, l, r, start=st, stop=False),
                            reads=[WQKV_res, XBT_res], writes=[ps_res], tag="kvproj")
                    brow = BQKV[0:1, outc:outc + 128]
                    em.c("pe", lambda t, o=pp, l=brow: t.matmul(
                        o, l, h.ones_row[0:1, :], start=False, stop=True),
                        reads=[BQKV_res, h.ones_row_res], writes=[ps_res], tag="kvbias")
                    kve = KVE[:, 512 * (kvi - 1 + 2 * (f % 2)):512 * (kvi + 2 * (f % 2))]
                    em.c("act", lambda s_, o=kve, i=pp: s_.copy(o, i),
                         reads=[ps_res], writes=[KVE_res], tag="kvevac")
                    em.d("sk", dram[128 * f:128 * (f + 1), 512 * s16:512 * (s16 + 1)],
                         kve, reads=[KVE_res], writes=[dram_res], tag="kvstore")
                    if kvi == 1:
                        em.c("act", lambda s_, i=kve: s_.activation(SQ2[:, :], i, AF.Square),
                             reads=[KVE_res], writes=[SQ2_res], tag="ksq")
                        for half, hh in ((h.halfA[:, :], 0), (h.halfB[:, :], 1)):
                            head = 2 * f + hh
                            pn = ps[0:1, 0:512]
                            em.c("pe", lambda t, l=half, o=pn: t.matmul(o, l, SQ2[:, :],
                                                                        start=True, stop=True),
                                 reads=[SQ2_res, h.halfA_res, h.halfB_res],
                                 writes=[ps_res], tag="kn")
                            em.c("dve", lambda v, i=pn: v.reduce_max(NTMP[0:1, 1:2], i, axis=AX.X),
                                 reads=[ps_res], writes=[NTMP_res], tag="knev")
                            kmc = KMX[0:1, head:head + 1]
                            em.c("dve", lambda v, o=kmc: v.tensor_tensor(o, o, NTMP[0:1, 1:2], ALU.max),
                                 reads=[NTMP_res, KMX_res], writes=[KMX_res], tag="knmax")

    em.barrier()

    with contextlib.ExitStack() as ph34:
        ATT, ATT_res = alloc_sb(ph34, nc, "ATT", [128, 8 * 2048], BF16)
        KH0, KH0_res = alloc_sb(ph34, nc, "KH0", [64, 8192], BF16)
        KH1, KH1_res = alloc_sb(ph34, nc, "KH1", [64, 8192], BF16)
        VH0, VH0_res = alloc_sb(ph34, nc, "VH0", [64, 8192], BF16)
        VH1, VH1_res = alloc_sb(ph34, nc, "VH1", [64, 8192], BF16)
        QH0, QH0_res = alloc_sb(ph34, nc, "QH0", [64, 2048], BF16)
        QH1, QH1_res = alloc_sb(ph34, nc, "QH1", [64, 2048], BF16)
        VA, VA_res = alloc_sb(ph34, nc, "VA", [128, 64 * 65], BF16)
        PT0, PT0_res = alloc_sb(ph34, nc, "PTb0", [128, 1024], BF16)
        PT1, PT1_res = alloc_sb(ph34, nc, "PTb1", [128, 1024], BF16)
        RB, RB_res = alloc_sb(ph34, nc, "RB", [64, 1024], F32)
        RCP, RCP_res = alloc_sb(ph34, nc, "RCP", [1, 1024], F32)
        BCOL, BCOL_res = alloc_sb(ph34, nc, "BCOL", [128, 16], F32)
        SCL, SCL_res = alloc_sb(ph34, nc, "SCL", [1, 16], F32)
        KHs = [(KH0, KH0_res), (KH1, KH1_res)]
        VHs = [(VH0, VH0_res), (VH1, VH1_res)]
        QHs = [(QH0, QH0_res), (QH1, QH1_res)]
        PTs = [(PT0, PT0_res), (PT1, PT1_res)]

        em.c("pool", lambda g: g.memset(VA[:, :], 1.0), writes=[VA_res], tag="vaones")

        def issue_head_loads(hd):
            KH, KH_res = KHs[hd % 2]
            VH, VH_res = VHs[hd % 2]
            QH, QH_res = QHs[hd % 2]
            em.d("sk", KH[:, :], h.K_dram[64 * hd:64 * (hd + 1), :],
                 reads=[h.K_dram_res], writes=[KH_res], tag="khld")
            em.d("sk", VH[:, :], h.V_dram[64 * hd:64 * (hd + 1), :],
                 reads=[h.V_dram_res], writes=[VH_res], tag="vhld")
            em.d("sq", QH[:, :], h.Q_dram[64 * hd:64 * (hd + 1), :],
                 reads=[h.Q_dram_res], writes=[QH_res], tag="qhld")

        nheads = 16 if 3 in A_PHASES else 0
        if nheads:
            issue_head_loads(0)
            issue_head_loads(1)
        for hd in range(nheads):
            KH, KH_res = KHs[hd % 2]
            VH, VH_res = VHs[hd % 2]
            QH, QH_res = QHs[hd % 2]
            pr = SCL[0:1, 2:3]; mm_ = SCL[0:1, 3:4]
            em.c("dve", lambda v, a=QMX[0:1, hd:hd + 1], b=KMX[0:1, hd:hd + 1]:
                 v.tensor_tensor(pr, a, b, ALU.mult),
                 reads=[QMX_res, KMX_res], writes=[SCL_res])
            em.c("act", lambda s: s.activation(mm_, pr, AF.Ln),
                 reads=[SCL_res], writes=[SCL_res])
            em.c("act", lambda s: s.activation(mm_, mm_, AF.Exp, scale=0.5),
                 reads=[SCL_res], writes=[SCL_res])
            pb = PS1[0:128, 0:1]
            em.c("pe", lambda t: t.matmul(pb, h.ones1x128f[0:1, :], mm_, start=True, stop=True),
                 reads=[SCL_res, h.ones1x128f_res], writes=[PS1_res], tag="bcolmm")
            bcol = BCOL[:, hd:hd + 1]
            em.c("act", lambda s, o=bcol: s.activation(o, pb, AF.Copy, scale=-0.125),
                 reads=[PS1_res], writes=[BCOL_res], tag="bcol")

            for kt in range(64):
                src = VH[:, 128 * kt:128 * (kt + 1)]
                dst = VA[:, 65 * kt:65 * kt + 64]
                emit_transpose(em, h, src, dst, PSt, PSt_res, VH_res, VA_res)

            for qh in range(2):
                PA, PA_res = PS0, PS0_res
                for kt in range(64):
                    PB, PB_res = (PS1, PS1_res) if kt % 2 == 0 else (PS2, PS2_res)
                    PT, PT_res = PTs[kt % 2]
                    lhsT = KH[:, 128 * kt:128 * (kt + 1)]
                    for j in range(2):
                        rhs = QH[:, 1024 * qh + 512 * j:1024 * qh + 512 * (j + 1)]
                        em.c("pe", lambda t, o=PB[:, 512 * j:512 * (j + 1)], l=lhsT, r=rhs:
                             t.matmul(o, l, r, start=True, stop=True),
                             reads=[KH_res, QH_res], writes=[PB_res], tag="scores")
                    em.c("act", lambda s, o=PT[:, :], i=PB[:, :], b=BCOL[:, hd:hd + 1]:
                         s.activation(o, i, AF.Exp, bias=b, scale=0.125),
                         reads=[PB_res, BCOL_res], writes=[PT_res], tag="exp")
                    va = VA[:, 65 * kt:65 * (kt + 1)]
                    for j in range(2):
                        em.c("pe", lambda t, o=PA[0:65, 512 * j:512 * (j + 1)], l=va,
                             r=PT[:, 512 * j:512 * (j + 1)], st=(kt == 0), sp=(kt == 63):
                             t.matmul(o, l, r, start=st, stop=sp),
                             reads=[VA_res, PT_res], writes=[PA_res], tag="pv")
                em.c("dve", lambda v: v.reciprocal(RCP[0:1, :], PA[64:65, :]),
                     reads=[PA_res], writes=[RCP_res], tag="recip")
                for j in range(2):
                    PBX, PBX_res = (PS1, PS1_res) if j == 0 else (PS2, PS2_res)
                    pbc = PBX[0:64, 0:512]
                    em.c("pe", lambda t, r=RCP[0:1, 512 * j:512 * (j + 1)], o=pbc:
                         t.matmul(o, h.ones1x64f[0:1, :], r, start=True, stop=True),
                         reads=[RCP_res, h.ones1x64f_res], writes=[PBX_res], tag="rbmm")
                    em.c("dve", lambda v, o=RB[:, 512 * j:512 * (j + 1)], i=pbc: v.tensor_copy(o, i),
                         reads=[PBX_res], writes=[RB_res], tag="rbev")
                attd = ATT[64 * (hd % 2):64 * (hd % 2) + 64,
                           2048 * (hd // 2) + 1024 * qh:2048 * (hd // 2) + 1024 * (qh + 1)]
                em.c("dve", lambda v, o=attd: v.tensor_tensor(o, PA[0:64, :], RB[:, :], ALU.mult),
                     reads=[PA_res, RB_res], writes=[ATT_res], tag="attnorm")
            if hd + 2 < nheads:
                issue_head_loads(hd + 2)

        T1, T1_res = alloc_sb(ph34, nc, "T1", [128, 1024], F32)
        XO0, XO0_res = alloc_sb(ph34, nc, "XO0", [128, 1024], F32)
        XO1, XO1_res = alloc_sb(ph34, nc, "XO1", [128, 1024], F32)
        XOs = [(XO0, XO0_res), (XO1, XO1_res)]
        for tt in range(NT if 4 in A_PHASES else 0):
            ps, ps_res = (PS1, PS1_res) if tt % 2 == 0 else (PS2, PS2_res)
            for f in range(NF):
                lhsT = ATT[:, 2048 * f + 128 * tt:2048 * f + 128 * (tt + 1)]
                for j in range(2):
                    rhs = WOUT[:, 1024 * f + 512 * j:1024 * f + 512 * (j + 1)]
                    em.c("pe", lambda t, o=ps[:, 512 * j:512 * (j + 1)], l=lhsT, r=rhs,
                         st=(f == 0): t.matmul(o, l, r, start=st, stop=False),
                         reads=[ATT_res, WOUT_res], writes=[ps_res], tag="oproj")
            for j in range(2):
                em.c("pe", lambda t, o=ps[:, 512 * j:512 * (j + 1)],
                     r=BOUT[0:1, 512 * j:512 * (j + 1)]: t.matmul(
                     o, h.ones_row[0:1, 0:128], r, start=False, stop=True),
                     reads=[BOUT_res, h.ones_row_res], writes=[ps_res], tag="obias")
            xh = XHAT[:, 1024 * tt:1024 * (tt + 1)]
            em.c("dve", lambda v, i=ps[:, :], x=xh: v.tensor_tensor(T1[:, :], i, x, ALU.add),
                 reads=[ps_res, XHAT_res], writes=[T1_res], tag="resid")
            XO, XO_res = XOs[tt % 2]
            emit_ln(em, h, T1[:, :], T1_res, [(XO[:, :], XO_res)], ln_scratch, tag="lnA")
            em.d("so", h.xa[128 * tt:128 * (tt + 1), :], XO[:, :],
                 reads=[XO_res], writes=[h.xa_res], tag="xastore")
        if 4 not in A_PHASES:
            for tt in range(NT):
                em.d("px", h.xa[128 * tt:128 * (tt + 1), :],
                     XHAT[:, 1024 * tt:1024 * (tt + 1)],
                     reads=[XHAT_res], writes=[h.xa_res], tag="xadump")


def build_A(core):
    import contextlib
    nc = bass.Bass("TRN2", target_bir_lowering=False)
    h = H()
    h.x_own = nc.dram_tensor("x_own", [TOK, D], F32, kind="ExternalInput")
    h.x_batch = nc.dram_tensor("x_batch", [SEQ, D], F32, kind="ExternalInput")
    h.qkv_wT = nc.dram_tensor("qkv_wT", [D, 3 * D], F32, kind="ExternalInput")
    h.qkv_b = nc.dram_tensor("qkv_b", [1, 3 * D], F32, kind="ExternalInput")
    h.out_wT = nc.dram_tensor("out_wT", [D, D], F32, kind="ExternalInput")
    h.out_b = nc.dram_tensor("out_b", [1, D], F32, kind="ExternalInput")
    h.xa = nc.dram_tensor("xa", [TOK, D], F32, kind="ExternalOutput")
    h.xa_res = Res("xa")
    h.K_dram = nc.dram_tensor("K_dram", [D, SEQ], BF16)
    h.V_dram = nc.dram_tensor("V_dram", [D, SEQ], BF16)
    h.Q_dram = nc.dram_tensor("Q_dram", [D, TOK], BF16)
    h.K_dram_res = Res("K_dram")
    h.V_dram_res = Res("V_dram")
    h.Q_dram_res = Res("Q_dram")

    chans = {"pw": "pool", "px": "pool", "sk": "sp", "sq": "sp", "so": "sp"}
    em = Em(nc, chans)
    with contextlib.ExitStack() as outer:
        declare_consts(nc, outer, h)
        load_consts(em, h)
        emit_blockA(em, h, None, nc, outer)
        em.barrier()
        with nc.Block() as block:
            em.finalize_and_emit(block, None)
    return nc


def prep_A_inputs(inputs, core):
    b = core // 4
    q0 = 2048 * (core % 4)
    x = np.asarray(inputs["L0_attn_x"])
    cc = host_consts()
    m = {
        "x_own": np.ascontiguousarray(x[b, q0:q0 + 2048, :]),
        "x_batch": np.ascontiguousarray(x[b]),
        "qkv_wT": np.ascontiguousarray(np.asarray(inputs["L0_attn_qkv_w"]).T),
        "qkv_b": np.asarray(inputs["L0_attn_qkv_b"]).reshape(1, -1),
        "out_wT": np.ascontiguousarray(np.asarray(inputs["L0_attn_out_w"]).T),
        "out_b": np.asarray(inputs["L0_attn_out_b"]).reshape(1, -1),
    }
    for k, v in cc.items():
        m["c_" + k] = v
    return m



NE = 8
NI = 22
NIS = 11
B_SLAB = 512
NSL = TOK // B_SLAB
NTT = B_SLAB // 128


def emit_blockB(em, h, nc, outer_ctx):
    import contextlib
    XTOK, XTOK_res = alloc_sb(outer_ctx, nc, "XTOK", [128, NT * 1024], BF16)
    XT, XT_res = alloc_sb(outer_ctx, nc, "XT_B", [128, NF * 2048], BF16)
    SQ, SQ_res = alloc_sb(outer_ctx, nc, "SQln", [128, 1024], F32)
    COLS, COLS_res = alloc_sb(outer_ctx, nc, "COLS", [128, 16], F32)
    ln_scratch = {"sq": (SQ, SQ_res), "cols": (COLS, COLS_res)}

    PS0, PS0_res = alloc_ps(outer_ctx, nc, "PS0", [128, 1024], F32)
    PS1, PS1_res = alloc_ps(outer_ctx, nc, "PS1", [128, 1024], F32)
    PS2, PS2_res = alloc_ps(outer_ctx, nc, "PS2", [128, 1024], F32)
    PSt, _pstres = alloc_ps(outer_ctx, nc, "PSt", [128, 2048], BF16)
    PSt_res = [_pstres, Res("PSt1")]

    ACC, ACC_res = alloc_sb(outer_ctx, nc, "ACC", [128, NTT * 1024], F32)
    AES, AES_res = alloc_sb(outer_ctx, nc, "AES", [128, NI * B_SLAB], BF16)
    WGB0, WGB0_res = alloc_sb(outer_ctx, nc, "WGB0", [128, 2048], BF16)
    WGB1, WGB1_res = alloc_sb(outer_ctx, nc, "WGB1", [128, 2048], BF16)
    WGB2, WGB2_res = alloc_sb(outer_ctx, nc, "WGB2", [128, 2048], BF16)
    WGB3, WGB3_res = alloc_sb(outer_ctx, nc, "WGB3", [128, 2048], BF16)
    WDE, WDE_res = alloc_sb(outer_ctx, nc, "WDE", [128, NI * 1024], BF16)
    SDB, SDB_res = alloc_sb(outer_ctx, nc, "SDB", [128, NIS * 1024], BF16)
    GWB, GWB_res = alloc_sb(outer_ctx, nc, "GWB", [128, 64], BF16)
    LG, LG_res = alloc_sb(outer_ctx, nc, "LG", [128, 8], F32)
    GCOL, GCOL_res = alloc_sb(outer_ctx, nc, "GCOL", [128, 8], F32)
    CMB, CMB_res = alloc_sb(outer_ctx, nc, "CMB", [128, NT * 8], F32)
    TMP, TMP_res = alloc_sb(outer_ctx, nc, "TMPB", [128, 512], BF16)
    T1, T1_res = alloc_sb(outer_ctx, nc, "T1", [128, 1024], F32)
    SAS, SAS_res = AES, AES_res
    XO0, XO0_res = alloc_sb(outer_ctx, nc, "XO0", [128, 1024], F32)
    XO1, XO1_res = alloc_sb(outer_ctx, nc, "XO1", [128, 1024], F32)
    XOs = [(XO0, XO0_res), (XO1, XO1_res)]
    WGBs = [(WGB0, WGB0_res), (WGB1, WGB1_res), (WGB2, WGB2_res), (WGB3, WGB3_res)]
    WCH = ["p0", "p1", "p2", "p3"]

    for t in range(NT):
        em.d(["px", "p0"], XTOK[:, 1024 * t:1024 * (t + 1)],
             h.xb_in[128 * t:128 * (t + 1), :], reads=[h.xb_in_res],
             writes=[XTOK_res], tag="xload")
    for t in range(NT):
        for f in range(NF):
            src = XTOK[:, 1024 * t + 128 * f:1024 * t + 128 * (f + 1)]
            dst = XT[:, 2048 * f + 128 * t:2048 * f + 128 * (t + 1)]
            emit_transpose(em, h, src, dst, PSt, PSt_res, XTOK_res, XT_res)
    for kk in range(NF):
        em.d("p1", GWB[:, 8 * kk:8 * (kk + 1)], h.gate_w[128 * kk:128 * (kk + 1), :],
             writes=[GWB_res], tag="gwld")
    for i in range(NIS):
        em.d("p2", SDB[:, 1024 * i:1024 * (i + 1)], h.Sd[128 * i:128 * (i + 1), :],
             writes=[SDB_res])

    for tt in range(NT):
        pg = PS0[:, 0:8]
        for kk in range(NF):
            lhsT = XT[:, 2048 * kk + 128 * tt:2048 * kk + 128 * (tt + 1)]
            gr = GWB[:, 8 * kk:8 * (kk + 1)]
            em.c("pe", lambda t_, l=lhsT, r=gr, st=(kk == 0), sp=(kk == NF - 1):
                 t_.matmul(pg, l, r, start=st, stop=sp),
                 reads=[XT_res, GWB_res], writes=[PS0_res], tag="gate")
        em.c("dve", lambda v: v.tensor_copy(LG[:, :], pg),
             reads=[PS0_res], writes=[LG_res], tag="gateev")
        c0 = COLS[:, 8:9]; c1 = COLS[:, 9:10]; c2 = COLS[:, 10:11]
        c3 = COLS[:, 11:12]; c4 = COLS[:, 12:13]
        em.c("dve", lambda v: v.reduce_max(c0, LG[:, :], axis=AX.X),
             reads=[LG_res], writes=[COLS_res], tag="g_v1")
        em.c("dve", lambda v: v.tensor_scalar(GCOL[:, :], LG[:, :], c0, None, ALU.is_equal),
             reads=[LG_res, COLS_res], writes=[GCOL_res], tag="g_m1")
        em.c("dve", lambda v: v.scalar_tensor_tensor(LG[:, :], GCOL[:, :], -1e30, LG[:, :],
                                                     ALU.mult, ALU.add),
             reads=[GCOL_res, LG_res], writes=[LG_res], tag="g_l2")
        em.c("dve", lambda v: v.reduce_max(c1, LG[:, :], axis=AX.X),
             reads=[LG_res], writes=[COLS_res], tag="g_v2")
        em.c("dve", lambda v: v.tensor_tensor(c2, c1, c0, ALU.subtract),
             reads=[COLS_res], writes=[COLS_res], tag="g_d21")
        em.c("act", lambda s: s.activation(c2, c2, AF.Exp),
             reads=[COLS_res], writes=[COLS_res], tag="g_exp")
        em.c("dve", lambda v: v.tensor_scalar(c2, c2, 1.0, None, ALU.add),
             reads=[COLS_res], writes=[COLS_res], tag="g_1p")
        em.c("dve", lambda v: v.reciprocal(c3, c2),
             reads=[COLS_res], writes=[COLS_res], tag="g_p1")
        em.c("dve", lambda v: v.tensor_scalar(c4, c3, -1.0, 1.0, ALU.mult, ALU.add),
             reads=[COLS_res], writes=[COLS_res], tag="g_p2")
        em.c("dve", lambda v: v.tensor_scalar(LG[:, :], LG[:, :], c1, None, ALU.is_equal),
             reads=[LG_res, COLS_res], writes=[LG_res], tag="g_m2")
        cmb = CMB[:, 8 * tt:8 * (tt + 1)]
        em.c("dve", lambda v, o=cmb: v.tensor_scalar(o, GCOL[:, :], c3, None, ALU.mult),
             reads=[GCOL_res, COLS_res], writes=[CMB_res], tag="g_c1")
        em.c("dve", lambda v, o=cmb: v.scalar_tensor_tensor(o, LG[:, :], c4, o,
                                                            ALU.mult, ALU.add),
             reads=[LG_res, COLS_res, CMB_res], writes=[CMB_res], tag="g_c2")

    first_wde_load = [True]

    def load_wde(e):
        for i in range(NI):
            em.d(WCH[i % 4], WDE[:, 1024 * i:1024 * (i + 1)],
                 h.Wd[e, 128 * i:128 * (i + 1), :], writes=[WDE_res], tag="wdld")

    for s in range(NSL):
        for i in range(NIS):
            WGB, WGB_res_ = WGBs[i % 4]
            em.d(WCH[i % 4], WGB[:, :], h.SGU_pre[i],
                 writes=[WGB_res_], tag="sgld")
            for which, (ps, ps_res) in enumerate(((PS0, PS0_res), (PS1, PS1_res))):
                for kk in range(NF):
                    lhsT = WGB[:, 1024 * which + 128 * kk:1024 * which + 128 * (kk + 1)]
                    rhs = XT[:, 2048 * kk + B_SLAB * s:2048 * kk + B_SLAB * (s + 1)]
                    em.c("pe", lambda t_, o=ps[:, 0:B_SLAB], l=lhsT, r=rhs, st=(kk == 0),
                         sp=(kk == NF - 1): t_.matmul(o, l, r, start=st, stop=sp),
                         reads=[WGB_res_, XT_res], writes=[ps_res], tag="sgu")
            em.c("act", lambda s_, i_=PS0[:, 0:B_SLAB]: s_.activation(
                TMP[:, :], i_, AF.Silu), reads=[PS0_res], writes=[TMP_res], tag="ssilu")
            dst = AES[:, B_SLAB * 0:B_SLAB] if False else SAS[:, B_SLAB * i:B_SLAB * (i + 1)]
            em.c("dve", lambda v, o=dst: v.tensor_tensor(o, TMP[:, :], PS1[:, 0:B_SLAB],
                                                         ALU.mult),
                 reads=[TMP_res, PS1_res], writes=[SAS_res], tag="smul")
        for tt in range(NTT):
            ps, ps_res = (PS2, PS2_res) if tt % 2 == 0 else (PS0, PS0_res)
            for i in range(NIS):
                lhsT = SAS[:, B_SLAB * i + 128 * tt:B_SLAB * i + 128 * (tt + 1)]
                for j in range(2):
                    rhs = SDB[:, 1024 * i + 512 * j:1024 * i + 512 * (j + 1)]
                    em.c("pe", lambda t_, o=ps[:, 512 * j:512 * (j + 1)], l=lhsT, r=rhs,
                         st=(i == 0), sp=(i == NIS - 1): t_.matmul(o, l, r, start=st, stop=sp),
                         reads=[SAS_res, SDB_res], writes=[ps_res], tag="sd")
            acc = ACC[:, 1024 * tt:1024 * (tt + 1)]
            em.c("act", lambda s_, o=acc, i_=ps[:, :]: s_.copy(o, i_),
                 reads=[ps_res], writes=[ACC_res], tag="accinit")

        for e in range(NE):
            if first_wde_load[0]:
                load_wde(e)
                first_wde_load[0] = False
            for i in range(NI):
                WGB, WGB_res_ = WGBs[i % 4]
                em.d(WCH[i % 4], WGB[:, :], h.WGU_pre[e, i],
                     writes=[WGB_res_], tag="wgld")
                for which, ps, ps_res in ((0, PS0, PS0_res), (1, PS1, PS1_res)):
                    for kk in range(NF):
                        lhsT = WGB[:, 1024 * which + 128 * kk:1024 * which + 128 * (kk + 1)]
                        rhs = XT[:, 2048 * kk + B_SLAB * s:2048 * kk + B_SLAB * (s + 1)]
                        em.c("pe", lambda t_, o=ps[:, 0:B_SLAB], l=lhsT, r=rhs, st=(kk == 0),
                             sp=(kk == NF - 1): t_.matmul(o, l, r, start=st, stop=sp),
                             reads=[WGB_res_, XT_res], writes=[ps_res], tag="gu")
                em.c("act", lambda s_, i_=PS0[:, 0:B_SLAB]: s_.activation(
                    TMP[:, :], i_, AF.Silu), reads=[PS0_res], writes=[TMP_res], tag="esilu")
                dst = AES[:, B_SLAB * i:B_SLAB * (i + 1)]
                em.c("dve", lambda v, o=dst: v.tensor_tensor(o, TMP[:, :], PS1[:, 0:B_SLAB],
                                                             ALU.mult),
                     reads=[TMP_res, PS1_res], writes=[AES_res], tag="emul")
            for tt in range(NTT):
                ps, ps_res = (PS2, PS2_res) if tt % 2 == 0 else (PS1, PS1_res)
                for i in range(NI):
                    lhsT = AES[:, B_SLAB * i + 128 * tt:B_SLAB * i + 128 * (tt + 1)]
                    for j in range(2):
                        rhs = WDE[:, 1024 * i + 512 * j:1024 * i + 512 * (j + 1)]
                        em.c("pe", lambda t_, o=ps[:, 512 * j:512 * (j + 1)], l=lhsT, r=rhs,
                             st=(i == 0), sp=(i == NI - 1): t_.matmul(o, l, r, start=st,
                                                                      stop=sp),
                             reads=[AES_res, WDE_res], writes=[ps_res], tag="d")
                gtt = NTT * s + tt
                ccol = CMB[:, 8 * gtt + e:8 * gtt + e + 1]
                acc = ACC[:, 1024 * tt:1024 * (tt + 1)]
                em.c("dve", lambda v, o=acc, i_=ps[:, :], sc=ccol: v.scalar_tensor_tensor(
                    o, i_, sc, o, ALU.mult, ALU.add),
                    reads=[ps_res, CMB_res, ACC_res], writes=[ACC_res], tag="dmerge")
            if e + 1 < NE:
                load_wde(e + 1)
        first_wde_load[0] = True

        for tt in range(NTT):
            gtt = NTT * s + tt
            xtk = XTOK[:, 1024 * gtt:1024 * (gtt + 1)]
            acc = ACC[:, 1024 * tt:1024 * (tt + 1)]
            em.c("dve", lambda v, x=xtk, a=acc: v.tensor_tensor(T1[:, :], a, x, ALU.add),
                 reads=[ACC_res, XTOK_res], writes=[T1_res], tag="bresid")
            XO, XO_res = XOs[tt % 2]
            emit_ln(em, h, T1[:, :], T1_res, [(XO[:, :], XO_res)], ln_scratch, tag="lnB")
            em.d("so", h.xb[128 * gtt:128 * (gtt + 1), :], XO[:, :],
                 reads=[XO_res], writes=[h.xb_res], tag="xbstore")


def build_B():
    import contextlib
    nc = bass.Bass("TRN2", target_bir_lowering=False)
    h = H()
    h.xb_in = nc.dram_tensor("xb_in", [TOK, D], F32, kind="ExternalInput")
    h.gate_w = nc.dram_tensor("gate_w", [D, NE], F32, kind="ExternalInput")
    h.WGU_pre = nc.dram_tensor("WGU_pre", [NE, NI, 128, 2048], F32, kind="ExternalInput")
    h.Wd = nc.dram_tensor("Wd", [NE, 2816, 1024], F32, kind="ExternalInput")
    h.SGU_pre = nc.dram_tensor("SGU_pre", [NIS, 128, 2048], F32, kind="ExternalInput")
    h.Sd = nc.dram_tensor("Sd", [1408, 1024], F32, kind="ExternalInput")
    h.xb = nc.dram_tensor("xb", [TOK, D], F32, kind="ExternalOutput")
    h.xb_res = Res("xb")
    h.xb_in_res = Res("xb_in")

    chans = {"p0": "pool", "p1": "pool", "p2": "pool", "p3": "pool", "px": "pool",
             "sk": "sp", "so": "sp"}
    em = Em(nc, chans)
    with contextlib.ExitStack() as outer:
        declare_consts(nc, outer, h)
        load_consts(em, h)
        emit_blockB(em, h, nc, outer)
        em.barrier()
        with nc.Block() as block:
            em.finalize_and_emit(block, None)
    return nc


def prep_B_inputs(inputs, core, xa_full):
    b = core // 4
    q0 = 2048 * (core % 4)
    cc = host_consts()
    Wg = np.asarray(inputs["L0_ffn_Wg"])
    Wu = np.asarray(inputs["L0_ffn_Wu"])
    Sg = np.asarray(inputs["L0_ffn_Sg"])
    Su = np.asarray(inputs["L0_ffn_Su"])
    pre = lambda W: np.ascontiguousarray(
        W.reshape(NE, 8, 128, NI, 128).transpose(0, 3, 2, 1, 4).reshape(NE, NI, 128, 1024))
    pres = lambda W: np.ascontiguousarray(
        W.reshape(8, 128, NIS, 128).transpose(2, 1, 0, 3).reshape(NIS, 128, 1024))
    m = {
        "xb_in": np.ascontiguousarray(xa_full[b, q0:q0 + 2048, :]),
        "gate_w": np.asarray(inputs["L0_ffn_gate_w"]),
        "WGU_pre": np.ascontiguousarray(np.concatenate([pre(Wg), pre(Wu)], axis=3)),
        "Wd": np.asarray(inputs["L0_ffn_Wd"]),
        "SGU_pre": np.ascontiguousarray(np.concatenate([pres(Sg), pres(Su)], axis=2)),
        "Sd": np.asarray(inputs["L0_ffn_Sd"]),
    }
    for k, v in cc.items():
        m["c_" + k] = v
    return m



NH = 16
HD = 64
NPRE = 48
C_SLAB = 512
NIC = 22
GAMMAS = [1.0 - 2.0 ** (-5.0 - hh) for hh in range(NH)]


def emit_ln_small(em, h, src_ap, src_res, n, outs, scratch, tag="gn"):
    sq, sq_res = scratch["sq"]
    cols, cols_res = scratch["cols"]
    ssum = cols[:, 0:1]; ssq = cols[:, 1:2]; mean = cols[:, 2:3]
    varr = cols[:, 3:4]; rstd = cols[:, 4:5]; negmr = cols[:, 5:6]; tmp = cols[:, 6:7]
    em.c("dve", lambda v: v.reduce_sum(ssum, src_ap, axis=AX.X),
         reads=[src_res], writes=[cols_res], tag=tag)
    em.c("act", lambda s: s.activation(sq[:, 0:n], src_ap, AF.Square, accum_out=ssq),
         reads=[src_res], writes=[sq_res, cols_res], tag=tag)
    em.c("dve", lambda v: v.tensor_scalar(mean, ssum, 1.0 / n, None, ALU.mult),
         reads=[cols_res], writes=[cols_res], tag=tag)
    em.c("dve", lambda v: v.tensor_scalar(varr, ssq, 1.0 / n, EPS, ALU.mult, ALU.add),
         reads=[cols_res], writes=[cols_res], tag=tag)
    em.c("dve", lambda v: v.tensor_tensor(tmp, mean, mean, ALU.mult),
         reads=[cols_res], writes=[cols_res], tag=tag)
    em.c("dve", lambda v: v.tensor_tensor(varr, varr, tmp, ALU.subtract),
         reads=[cols_res], writes=[cols_res], tag=tag)
    em.c("act", lambda s: s.activation(rstd, varr, AF.Ln),
         reads=[cols_res], writes=[cols_res], tag=tag)
    em.c("act", lambda s: s.activation(rstd, rstd, AF.Exp, scale=-0.5),
         reads=[cols_res], writes=[cols_res], tag=tag)
    em.c("dve", lambda v: v.scalar_tensor_tensor(negmr, mean, -1.0, rstd, ALU.mult, ALU.mult),
         reads=[cols_res], writes=[cols_res], tag=tag)
    for dst_ap, dst_res in outs:
        em.c("dve", lambda v, o=dst_ap: v.tensor_scalar(o, src_ap, rstd, negmr,
                                                        ALU.mult, ALU.add),
             reads=[src_res, cols_res], writes=[dst_res], tag=tag + "_app")


def emit_blockC(em, h, nc, outer_ctx):
    import contextlib
    XTOK, XTOK_res = alloc_sb(outer_ctx, nc, "XTOKC", [128, NT * 1024], BF16)
    XNA, XNA_res = alloc_sb(outer_ctx, nc, "XNA", [128, NT * 1024], BF16)
    SQ, SQ_res = alloc_sb(outer_ctx, nc, "SQln", [128, 1024], F32)
    COLS, COLS_res = alloc_sb(outer_ctx, nc, "COLS", [128, 16], F32)
    ln_scratch = {"sq": (SQ, SQ_res), "cols": (COLS, COLS_res)}
    T1, T1_res = alloc_sb(outer_ctx, nc, "T1", [128, 1024], F32)
    XO0, XO0_res = alloc_sb(outer_ctx, nc, "XO0", [128, 1024], F32)
    XOs = [(XO0, XO0_res), (XO0, XO0_res)]
    BCT, BCT_res = alloc_sb(outer_ctx, nc, "BCT", [128, 4 * 1024], BF16)
    GQ, GQ_res = alloc_sb(outer_ctx, nc, "GQ", [128, NH], F32)
    GKN, GKN_res = alloc_sb(outer_ctx, nc, "GKN", [128, NH], F32)
    GKS, GKS_res = alloc_sb(outer_ctx, nc, "GKS", [128, NH], F32)
    GKSP, GKSP_res = alloc_sb(outer_ctx, nc, "GKSP", [128, NPRE * NH], F32)
    GDEC, GDEC_res = alloc_sb(outer_ctx, nc, "GDEC", [64, NPRE * NH], F32)
    MASKT, MASKT_res = alloc_sb(outer_ctx, nc, "MASKT", [128, 128], BF16)
    SST, SST_res = alloc_sb(outer_ctx, nc, "SST", [64, NH * HD], F32)
    SSTB, SSTB_res = alloc_sb(outer_ctx, nc, "SSTB", [64, NH * HD], BF16)
    BR, BR_res = alloc_sb(outer_ctx, nc, "BR", [1, 5 * 1024], BF16)

    PS0, PS0_res = alloc_ps(outer_ctx, nc, "PS0", [128, 1024], F32)
    PS1, PS1_res = alloc_ps(outer_ctx, nc, "PS1", [128, 1024], F32)
    PSC, PSC_res = alloc_ps(outer_ctx, nc, "PSC", [128, 512], F32)
    PSO, PSO_res = alloc_ps(outer_ctx, nc, "PSO", [128, 512], F32)
    PSS, PSS_res = PSC, PSC_res
    PSt, _pstres = alloc_ps(outer_ctx, nc, "PSt", [128, 2048], BF16)
    PSt_res = [_pstres, Res("PSt1")]

    for t, d, r in [(GQ, h.GQ_d, GQ_res), (GKN, h.GKN_d, GKN_res), (GKS, h.GKS_d, GKS_res),
                    (GKSP, h.GKSP_d, GKSP_res), (GDEC, h.GDEC_d, GDEC_res),
                    (MASKT, h.MASKT_d, MASKT_res)]:
        em.d("sk", t[:, :], d[:, :], writes=[r], tag="gld")
    off = 0
    bias_off = {}
    for nm, dd, ln in [("bq", h.bq, 1024), ("bk", h.bk, 1024), ("bv", h.bv, 1024),
                       ("bg", h.bg, 1024), ("bo", h.bo, 1024)]:
        em.d("p1", BR[0:1, off:off + ln], dd[:, :], writes=[BR_res], tag="bld")
        bias_off[nm] = off
        off += ln
    NRMR, NRMR_res = alloc_sb(outer_ctx, nc, "NRMR", [1, 1024], F32)
    for j, dd in enumerate([h.norm1_w, h.norm1_b, h.gn_w, h.gn_b]):
        em.d("sk", NRMR[0:1, :], dd[:, :], reads=[NRMR_res], writes=[NRMR_res])
        for jj in range(2):
            pb = PS0[:, 0:512]
            em.c("pe", lambda t, r=NRMR[0:1, 512 * jj:512 * (jj + 1)]:
                 t.matmul(pb, h.ones1x128f[0:1, :], r, start=True, stop=True),
                 reads=[NRMR_res, h.ones1x128f_res], writes=[PS0_res], tag="bcmm")
            dst = BCT[:, 1024 * j + 512 * jj:1024 * j + 512 * (jj + 1)]
            em.c("dve", lambda v, o=dst: v.tensor_copy(o, pb),
                 reads=[PS0_res], writes=[BCT_res], tag="bcev")
    W1B = BCT[:, 0:1024]; B1B = BCT[:, 1024:2048]
    GNW = BCT[:, 2048:3072]; GNB = BCT[:, 3072:4096]

    em.c("dve", lambda v: v.memset(SST[:, :], 0.0), writes=[SST_res])
    em.c("dve", lambda v: v.memset(SSTB[:, :], 0.0), writes=[SSTB_res])

    for t in range(NT):
        em.d(["px", "p0"], XTOK[:, 1024 * t:1024 * (t + 1)],
             h.xc_in[128 * t:128 * (t + 1), :], reads=[h.xc_in_res],
             writes=[XTOK_res], tag="xload")

    for tt in range(NT):
        src = XTOK[:, 1024 * tt:1024 * (tt + 1)]
        emit_ln(em, h, src, XTOK_res, [(T1[:, :], T1_res)], ln_scratch, tag="ln1")
        em.c("dve", lambda v: v.tensor_tensor(T1[:, :], T1[:, :], W1B, ALU.mult),
             reads=[T1_res, BCT_res], writes=[T1_res], tag="xnw")
        dst = XNA[:, 1024 * tt:1024 * (tt + 1)]
        em.c("dve", lambda v, o=dst: v.tensor_tensor(o, T1[:, :], B1B, ALU.add),
             reads=[T1_res, BCT_res], writes=[XNA_res], tag="xnb")

    with contextlib.ExitStack() as ph2:
        WQB, WQB_res = alloc_sb(ph2, nc, "WQB", [128, 8 * 1024], BF16)
        WKB, WKB_res = alloc_sb(ph2, nc, "WKB", [128, 8 * 1024], BF16)
        WVB, WVB_res = alloc_sb(ph2, nc, "WVB", [128, 8 * 1024], BF16)
        WGB_, WGB__res = alloc_sb(ph2, nc, "WGB_", [128, 8 * 1024], BF16)
        WOB, WOB_res = alloc_sb(ph2, nc, "WOB", [128, 8 * 1024], BF16)
        XNC, XNC_res = alloc_sb(ph2, nc, "XNC", [128, 8 * 128], BF16)
        QTC, QTC_res = alloc_sb(ph2, nc, "QTC", [128, 1024], BF16)
        KTC, KTC_res = alloc_sb(ph2, nc, "KTC", [128, 1024], BF16)
        VTC, VTC_res = alloc_sb(ph2, nc, "VTC", [128, 1024], BF16)
        GTC, GTC_res = alloc_sb(ph2, nc, "GTC", [128, 1024], BF16)
        QS, QS_res = alloc_sb(ph2, nc, "QS", [128, 64], BF16)
        KS1, KS1_res = alloc_sb(ph2, nc, "KS1", [128, 64], BF16)
        KS2, KS2_res = alloc_sb(ph2, nc, "KS2", [128, 64], BF16)
        QSF, QSF_res = alloc_sb(ph2, nc, "QSF", [64, 128], BF16)
        KSF, KSF_res = alloc_sb(ph2, nc, "KSF", [64, 128], BF16)
        AIN, AIN_res = alloc_sb(ph2, nc, "AIN", [128, 128], BF16)
        TGN, TGN_res = alloc_sb(ph2, nc, "TGN", [128, 64], F32)
        QSb, QSb_res = alloc_sb(ph2, nc, "QSb", [128, 64], BF16)
        KS1b, KS1b_res = alloc_sb(ph2, nc, "KS1b", [128, 64], BF16)
        QSFb, QSFb_res = alloc_sb(ph2, nc, "QSFb", [64, 128], BF16)
        KSFb, KSFb_res = alloc_sb(ph2, nc, "KSFb", [64, 128], BF16)
        AINb, AINb_res = alloc_sb(ph2, nc, "AINb", [128, 128], BF16)
        TGNb, TGNb_res = alloc_sb(ph2, nc, "TGNb", [128, 64], F32)
        SQg, SQg_res = alloc_sb(ph2, nc, "SQg", [128, 64], F32)
        COLSg, COLSg_res = alloc_sb(ph2, nc, "COLSg", [128, 16], F32)
        gn_scr0 = {"sq": (SQg, SQg_res), "cols": (COLSg, COLSg_res)}
        COLSg2, COLSg2_res = alloc_sb(ph2, nc, "COLSg2", [128, 16], F32)
        gn_scr1 = {"sq": (SQg, SQg_res), "cols": (COLSg2, COLSg2_res)}
        RGC, RGC_res = alloc_sb(ph2, nc, "RGC", [128, 1024], BF16)
        RGT, RGT_res = XNC, XNC_res
        XPH, XPH_res = QTC, QTC_res

        for kk in range(NF):
            em.d("p1", WQB[:, 1024 * kk:1024 * (kk + 1)], h.wq[128 * kk:128 * (kk + 1), :],
                 writes=[WQB_res])
            em.d("p2", WKB[:, 1024 * kk:1024 * (kk + 1)], h.wk[128 * kk:128 * (kk + 1), :],
                 writes=[WKB_res])
            em.d("p3", WVB[:, 1024 * kk:1024 * (kk + 1)], h.wv[128 * kk:128 * (kk + 1), :],
                 writes=[WVB_res])
            em.d("p1", WGB_[:, 1024 * kk:1024 * (kk + 1)], h.wg[128 * kk:128 * (kk + 1), :],
                 writes=[WGB__res])
            em.d("p2", WOB[:, 1024 * kk:1024 * (kk + 1)], h.wo[128 * kk:128 * (kk + 1), :],
                 writes=[WOB_res])

        def proj(dst, dst_res, wsb, wsb_res, bias_name, ps, ps_res, act=None):
            for j in range(2):
                for kk in range(NF):
                    lhsT = XNC[:, 128 * kk:128 * (kk + 1)]
                    rhs = wsb[:, 1024 * kk + 512 * j:1024 * kk + 512 * (j + 1)]
                    em.c("pe", lambda t, o=ps[:, 512 * j:512 * (j + 1)], l=lhsT, r=rhs,
                         st=(kk == 0): t.matmul(o, l, r, start=st, stop=False),
                         reads=[XNC_res, wsb_res], writes=[ps_res], tag="cproj")
                boff = bias_off[bias_name]
                em.c("pe", lambda t, o=ps[:, 512 * j:512 * (j + 1)],
                     r=BR[0:1, boff + 512 * j:boff + 512 * (j + 1)]:
                     t.matmul(o, h.ones_row[0:1, 0:128], r, start=False, stop=True),
                     reads=[BR_res, h.ones_row_res], writes=[ps_res], tag="cbias")
            if act is None:
                em.c("act", lambda s, o=dst, i=ps[:, :]: s.copy(o, i),
                     reads=[ps_res], writes=[dst_res], tag="cevac")
            else:
                em.c("act", lambda s, o=dst, i=ps[:, :], a=act: s.activation(o, i, a),
                     reads=[ps_res], writes=[dst_res], tag="cevac")

        def state_update(hd_, scale_col, scale_res, dec, dec_res):
            em.c("act", lambda s, sc=scale_col: s.activation(
                KS2[:, :], KTC[:, 64 * hd_:64 * (hd_ + 1)], AF.Copy, scale=sc),
                reads=[KTC_res, scale_res], writes=[KS2_res], tag="ks2")
            pss = PSS[0:64, 128:192]
            em.c("pe", lambda t: t.matmul(pss, KS2[:, :], VTC[:, 64 * hd_:64 * (hd_ + 1)],
                                          start=True, stop=True),
                 reads=[KS2_res, VTC_res], writes=[PSS_res], tag="smm")
            scol = SST[:, 64 * hd_:64 * (hd_ + 1)]
            if dec_res is None:
                em.c("dve", lambda v, o=scol, d=dec: v.scalar_tensor_tensor(
                    o, o, d, pss, ALU.mult, ALU.add),
                    reads=[SST_res, PSS_res], writes=[SST_res], tag="sdec")
            else:
                em.c("dve", lambda v, o=scol, d=dec: v.scalar_tensor_tensor(
                    o, o, d, pss, ALU.mult, ALU.add),
                    reads=[SST_res, PSS_res, dec_res], writes=[SST_res], tag="sdec")
            sbcol = SSTB[:, 64 * hd_:64 * (hd_ + 1)]
            em.c("dve", lambda v, o=sbcol, i=scol: v.tensor_copy(o, i),
                 reads=[SST_res], writes=[SSTB_res], tag="sbf")

        for c in range(NPRE):
            em.d(["px", "p0"], XPH[:, :], h.xprefix[128 * c:128 * (c + 1), :],
                 reads=[h.xprefix_res], writes=[XPH_res], tag="xpload")
            emit_ln(em, h, XPH[:, :], XPH_res, [(T1[:, :], T1_res)], ln_scratch, tag="lnp")
            em.c("dve", lambda v: v.tensor_tensor(T1[:, :], T1[:, :], W1B, ALU.mult),
                 reads=[T1_res, BCT_res], writes=[T1_res], tag="xnpw")
            em.c("dve", lambda v: v.tensor_tensor(T1[:, :], T1[:, :], B1B, ALU.add),
                 reads=[T1_res, BCT_res], writes=[T1_res], tag="xnpb")
            em.c("dve", lambda v: v.tensor_copy(GTC[:, :], T1[:, :]),
                 reads=[T1_res], writes=[GTC_res], tag="xnpcast")
            for f in range(NF):
                src = GTC[:, 128 * f:128 * (f + 1)]
                dst = XNC[:, 128 * f:128 * (f + 1)]
                emit_transpose(em, h, src, dst, PSt, PSt_res, GTC_res, XNC_res)
            proj(KTC[:, :], KTC_res, WKB, WKB_res, "bk", PS0, PS0_res)
            proj(VTC[:, :], VTC_res, WVB, WVB_res, "bv", PS1, PS1_res)
            for hd_ in range(NH):
                state_update(hd_, GKSP[:, NH * c + hd_:NH * c + hd_ + 1], GKSP_res,
                             GDEC[:, NH * c + hd_:NH * c + hd_ + 1], GDEC_res)

        for c in range(NT):
            for f in range(NF):
                src = XNA[:, 1024 * c + 128 * f:1024 * c + 128 * (f + 1)]
                dst = XNC[:, 128 * f:128 * (f + 1)]
                emit_transpose(em, h, src, dst, PSt, PSt_res, XNA_res, XNC_res)
            proj(QTC[:, :], QTC_res, WQB, WQB_res, "bq", PS0, PS0_res)
            proj(KTC[:, :], KTC_res, WKB, WKB_res, "bk", PS1, PS1_res)
            proj(VTC[:, :], VTC_res, WVB, WVB_res, "bv", PS0, PS0_res)
            proj(GTC[:, :], GTC_res, WGB_, WGB__res, "bg", PS1, PS1_res, act=AF.Silu)
            if c == 0 and getattr(h, "dbg", None) is not None:
                for nm, t_, r_ in [("q", QTC, QTC_res), ("k", KTC, KTC_res),
                                   ("v", VTC, VTC_res), ("g", GTC, GTC_res)]:
                    em.d("so", h.dbg[nm][:, :], t_[:, :], reads=[r_],
                         writes=[h.dbg_res[nm]], tag="dbg")
            for hd_ in range(NH):
                qslice = QTC[:, 64 * hd_:64 * (hd_ + 1)]
                kslice = KTC[:, 64 * hd_:64 * (hd_ + 1)]
                vslice = VTC[:, 64 * hd_:64 * (hd_ + 1)]
                sbslice = SSTB[:, 64 * hd_:64 * (hd_ + 1)]
                em.c("act", lambda s, sc=GQ[:, hd_:hd_ + 1], i_=qslice: s.activation(
                    QS[:, :], i_, AF.Copy, scale=sc),
                    reads=[QTC_res, GQ_res], writes=[QS_res], tag="qsc")
                em.c("act", lambda s, sc=GKN[:, hd_:hd_ + 1], i_=kslice: s.activation(
                    KS1[:, :], i_, AF.Copy, scale=sc),
                    reads=[KTC_res, GKN_res], writes=[KS1_res], tag="ksc")
                emit_transpose(em, h, QS[:, :], QSF[:, :], PSt, PSt_res, QS_res, QSF_res)
                emit_transpose(em, h, KS1[:, :], KSF[:, :], PSt, PSt_res, KS1_res, KSF_res)
                psc = PSC[0:128, 0:128]
                em.c("pe", lambda t: t.matmul(psc, KSF[:, :], QSF[:, :], start=True,
                                              stop=True),
                     reads=[KSF_res, QSF_res], writes=[PSC_res], tag="sc_mm")
                em.c("dve", lambda v: v.tensor_tensor(AIN[:, :], psc, MASKT[:, :], ALU.mult),
                     reads=[PSC_res, MASKT_res], writes=[AIN_res], tag="mask")
                pso = PSO[0:128, 0:64]
                em.c("pe", lambda t, r=vslice: t.matmul(pso, AIN[:, :], r,
                                                        start=True, stop=False),
                     reads=[AIN_res, VTC_res], writes=[PSO_res], tag="o_intra")
                em.c("pe", lambda t, r=sbslice: t.matmul(pso, QSF[:, :], r,
                                                         start=False, stop=True),
                     reads=[QSF_res, SSTB_res], writes=[PSO_res], tag="o_inter")
                if c == 0 and hd_ == 0 and getattr(h, "dbg", None) is not None:
                    em.d("px", h.dbg["qs"][:, 0:64], QS[:, :], reads=[QS_res],
                         writes=[h.dbg_res["qs"]], tag="dbg")
                    em.d("px", h.dbg["ks"][:, 0:64], KS1[:, :], reads=[KS1_res],
                         writes=[h.dbg_res["ks"]], tag="dbg")
                    em.d("px", h.dbg["ain"][:, 0:128], AIN[:, :], reads=[AIN_res],
                         writes=[h.dbg_res["ain"]], tag="dbg")
                    em.c("dve", lambda v: v.tensor_copy(TGN[:, :], pso),
                         reads=[PSO_res], writes=[TGN_res], tag="dbgret")
                    em.d("so", h.dbg["ret"][:, 0:64], TGN[:, :], reads=[TGN_res],
                         writes=[h.dbg_res["ret"]], tag="dbg")
                state_update(hd_, GKS[:, hd_:hd_ + 1], GKS_res, GAMMAS[hd_] ** 128, None)
                emit_ln_small(em, h, pso, PSO_res, 64, [(TGN[:, :], TGN_res)], ln_scratch)
                em.c("dve", lambda v, w=GNW[:, 64 * hd_:64 * (hd_ + 1)]: v.tensor_tensor(
                    TGN[:, :], TGN[:, :], w, ALU.mult),
                    reads=[TGN_res, BCT_res], writes=[TGN_res], tag="gnw")
                em.c("dve", lambda v, b=GNB[:, 64 * hd_:64 * (hd_ + 1)]: v.tensor_tensor(
                    TGN[:, :], TGN[:, :], b, ALU.add),
                    reads=[TGN_res, BCT_res], writes=[TGN_res], tag="gnb")
                dst = RGC[:, 64 * hd_:64 * (hd_ + 1)]
                em.c("dve", lambda v, o=dst, g=GTC[:, 64 * hd_:64 * (hd_ + 1)]:
                     v.tensor_tensor(o, TGN[:, :], g, ALU.mult),
                     reads=[TGN_res, GTC_res], writes=[RGC_res], tag="gmul")
            if c == 0 and getattr(h, "dbg", None) is not None:
                em.d("so", h.dbg["rg"][:, :], RGC[:, :], reads=[RGC_res],
                     writes=[h.dbg_res["rg"]], tag="dbg")
            for f in range(NF):
                src = RGC[:, 128 * f:128 * (f + 1)]
                dst = RGT[:, 128 * f:128 * (f + 1)]
                emit_transpose(em, h, src, dst, PSt, PSt_res, RGC_res, RGT_res)
            for j in range(2):
                for kk in range(NF):
                    lhsT = RGT[:, 128 * kk:128 * (kk + 1)]
                    rhs = WOB[:, 1024 * kk + 512 * j:1024 * kk + 512 * (j + 1)]
                    em.c("pe", lambda t, o=PS0[:, 512 * j:512 * (j + 1)], l=lhsT, r=rhs,
                         st=(kk == 0): t.matmul(o, l, r, start=st, stop=False),
                         reads=[RGT_res, WOB_res], writes=[PS0_res], tag="oproj")
                boff = bias_off["bo"]
                em.c("pe", lambda t, o=PS0[:, 512 * j:512 * (j + 1)],
                     r=BR[0:1, boff + 512 * j:boff + 512 * (j + 1)]:
                     t.matmul(o, h.ones_row[0:1, 0:128], r, start=False, stop=True),
                     reads=[BR_res, h.ones_row_res], writes=[PS0_res], tag="obias")
            xt = XTOK[:, 1024 * c:1024 * (c + 1)]
            em.c("dve", lambda v, x=xt: v.tensor_tensor(T1[:, :], PS0[:, :], x, ALU.add),
                 reads=[PS0_res, XTOK_res], writes=[T1_res], tag="midres")
            XO, XO_res = XOs[c % 2]
            em.c("dve", lambda v, o=XO[:, :]: v.tensor_copy(o, T1[:, :]),
                 reads=[T1_res], writes=[XO_res], tag="midcp")
            em.d("so", h.xmid_dram[128 * c:128 * (c + 1), :], XO[:, :],
                 reads=[XO_res], writes=[h.xmid_res], tag="midstore")

    em.barrier()

    with contextlib.ExitStack() as ph3:
        XMH, XMH_res = alloc_sb(ph3, nc, "XMH", [128, 1024], F32)
        T2, T2_res = alloc_sb(ph3, nc, "T2", [128, 1024], F32)
        BR3, BR3_res = alloc_sb(ph3, nc, "BR3", [1, 2 * 2816 + 1024], BF16)
        AFS, AFS_res = alloc_sb(ph3, nc, "AFS", [128, NIC * C_SLAB], BF16)
        WIN0, WIN0_res = alloc_sb(ph3, nc, "WIN0", [128, 2048], BF16)
        WIN1, WIN1_res = alloc_sb(ph3, nc, "WIN1", [128, 2048], BF16)
        WOU0, WOU0_res = alloc_sb(ph3, nc, "WOU0", [128, 1024], BF16)
        WOU1, WOU1_res = alloc_sb(ph3, nc, "WOU1", [128, 1024], BF16)
        TMP, TMP_res = alloc_sb(ph3, nc, "TMPC", [128, 512], BF16)
        X2C, X2C_res = alloc_sb(ph3, nc, "X2C", [128, 1024], BF16)
        WINs = [(WIN0, WIN0_res), (WIN1, WIN1_res)]
        WOUs = [(WOU0, WOU0_res), (WOU1, WOU1_res)]
        for nm, dd, ln in [("biu", h.b_in_u, 2816), ("big", h.b_in_g, 2816),
                           ("bout", h.b_out, 1024)]:
            em.d("p1", BR3[0:1, off - 5120:off - 5120 + ln], dd[:, :], writes=[BR3_res])
            bias_off[nm] = off
            off += ln
        BCT3, BCT3_res = alloc_sb(ph3, nc, "BCT3", [128, 2048], BF16)
        for j, dd in enumerate([h.norm2_w, h.norm2_b]):
            em.d("sk", NRMR[0:1, :], dd[:, :], reads=[NRMR_res], writes=[NRMR_res])
            for jj in range(2):
                pb = PS0[:, 0:512]
                em.c("pe", lambda t, r=NRMR[0:1, 512 * jj:512 * (jj + 1)]:
                     t.matmul(pb, h.ones1x128f[0:1, :], r, start=True, stop=True),
                     reads=[NRMR_res, h.ones1x128f_res], writes=[PS0_res], tag="bcmm3")
                dst = BCT3[:, 1024 * j + 512 * jj:1024 * j + 512 * (jj + 1)]
                em.c("dve", lambda v, o=dst: v.tensor_copy(o, pb),
                     reads=[PS0_res], writes=[BCT3_res], tag="bcev3")
        W2B = BCT3[:, 0:1024]
        B2B = BCT3[:, 1024:2048]

        for tt in range(NT):
            em.d(["px", "p0"], XMH[:, :], h.xmid_dram[128 * tt:128 * (tt + 1), :],
                 reads=[h.xmid_res], writes=[XMH_res], tag="xmload")
            emit_ln(em, h, XMH[:, :], XMH_res, [(T1[:, :], T1_res)], ln_scratch, tag="ln2")
            em.c("dve", lambda v: v.tensor_tensor(T1[:, :], T1[:, :], W2B, ALU.mult),
                 reads=[T1_res, BCT3_res], writes=[T1_res], tag="x2w")
            em.c("dve", lambda v: v.tensor_tensor(T2[:, :], T1[:, :], B2B, ALU.add),
                 reads=[T1_res, BCT3_res], writes=[T2_res], tag="x2b")
            em.c("dve", lambda v: v.tensor_copy(X2C[:, :], T2[:, :]),
                 reads=[T2_res], writes=[X2C_res], tag="x2cast")
            for f in range(NF):
                src = X2C[:, 128 * f:128 * (f + 1)]
                dst = XNA[:, 2048 * f + 128 * tt:2048 * f + 128 * (tt + 1)]
                emit_transpose(em, h, src, dst, PSt, PSt_res, X2C_res, XNA_res)
        X2T, X2T_res = XNA, XNA_res

        for s in range(4):
            for i in range(NIC):
                WIN, WIN_res = WINs[i % 2]
                em.d(WCH2[(2 * i) % 4], WIN[:, 0:1024], h.win_u_pre[i],
                     writes=[WIN_res], tag="wiuld")
                em.d(WCH2[(2 * i + 1) % 4], WIN[:, 1024:2048], h.win_g_pre[i],
                     writes=[WIN_res], tag="wigld")
                for which, ps, ps_res, bias_name in ((0, PS0, PS0_res, "biu"),
                                                     (1, PS1, PS1_res, "big")):
                    for kk in range(NF):
                        lhsT = WIN[:, 1024 * which + 128 * kk:1024 * which + 128 * (kk + 1)]
                        rhs = X2T[:, 2048 * kk + C_SLAB * s:2048 * kk + C_SLAB * (s + 1)]
                        em.c("pe", lambda t, o=ps[:, 0:C_SLAB], l=lhsT, r=rhs, st=(kk == 0):
                             t.matmul(o, l, r, start=st, stop=False),
                             reads=[WIN_res, X2T_res], writes=[ps_res], tag="ug")
                    boff = bias_off[bias_name] - 5120 + 128 * i
                    em.c("pe", lambda t, o=ps[:, 0:C_SLAB], l=BR3[0:1, boff:boff + 128]:
                         t.matmul(o, l, h.ones_row[0:1, :], start=False, stop=True),
                         reads=[BR3_res, h.ones_row_res], writes=[ps_res], tag="ugbias")
                em.c("act", lambda s_, i_=PS0[:, 0:C_SLAB]: s_.activation(
                    TMP[:, :], i_, AF.Silu), reads=[PS0_res], writes=[TMP_res], tag="usilu")
                dst = AFS[:, C_SLAB * i:C_SLAB * (i + 1)]
                em.c("dve", lambda v, o=dst: v.tensor_tensor(o, TMP[:, :], PS1[:, 0:C_SLAB],
                                                             ALU.mult),
                     reads=[TMP_res, PS1_res], writes=[AFS_res], tag="umul")
            for half in range(2):
                pss = [(PS0, PS0_res), (PS1, PS1_res)]
                for i in range(NIC):
                    WOU, WOU_res = WOUs[i % 2]
                    em.d(WCH2[i % 4], WOU[:, :], h.w_out_pad[128 * i:128 * (i + 1), :],
                         writes=[WOU_res], tag="would")
                    for tt2 in range(2):
                        tt = 2 * half + tt2
                        ps, ps_res = pss[tt2]
                        lhsT = AFS[:, C_SLAB * i + 128 * tt:C_SLAB * i + 128 * (tt + 1)]
                        for j in range(2):
                            rhs = WOU[:, 512 * j:512 * (j + 1)]
                            em.c("pe", lambda t, o=ps[:, 512 * j:512 * (j + 1)], l=lhsT,
                                 r=rhs, st=(i == 0): t.matmul(o, l, r, start=st, stop=False),
                                 reads=[AFS_res, WOU_res], writes=[ps_res], tag="wout")
                for tt2 in range(2):
                    tt = 2 * half + tt2
                    ps, ps_res = pss[tt2]
                    boff = bias_off["bout"] - 5120
                    for j in range(2):
                        em.c("pe", lambda t, o=ps[:, 512 * j:512 * (j + 1)],
                             r=BR3[0:1, boff + 512 * j:boff + 512 * (j + 1)]:
                             t.matmul(o, h.ones_row[0:1, 0:128], r, start=False, stop=True),
                             reads=[BR3_res, h.ones_row_res], writes=[ps_res], tag="woutb")
                    gtt = 4 * s + tt
                    em.d(["px", "p0"], XMH[:, :], h.xmid_dram[128 * gtt:128 * (gtt + 1), :],
                         reads=[h.xmid_res], writes=[XMH_res], tag="xm2")
                    em.c("dve", lambda v, i_=ps[:, :]: v.tensor_tensor(T1[:, :], i_,
                                                                       XMH[:, :], ALU.add),
                         reads=[ps_res, XMH_res], writes=[T1_res], tag="finres1")
                    xt = XTOK[:, 1024 * gtt:1024 * (gtt + 1)]
                    em.c("dve", lambda v, x=xt: v.tensor_tensor(T1[:, :], T1[:, :], x,
                                                                ALU.add),
                         reads=[T1_res, XTOK_res], writes=[T1_res], tag="finres2")
                    XO, XO_res = XOs[tt2]
                    emit_ln(em, h, T1[:, :], T1_res, [(XO[:, :], XO_res)], ln_scratch,
                            tag="lnC")
                    em.d("so", h.xcout[128 * gtt:128 * (gtt + 1), :], XO[:, :],
                         reads=[XO_res], writes=[h.xcout_res], tag="xcstore")


WCH2 = ["p0", "p1", "p2", "p3"]


def build_C(core, debug=False):
    import contextlib
    nc = bass.Bass("TRN2", target_bir_lowering=False)
    h = H()
    if debug:
        h.dbg = {}
        h.dbg_res = {}
        for nm in ["q", "k", "v", "g", "rg"]:
            h.dbg[nm] = nc.dram_tensor("dbg_" + nm, [128, 1024], BF16, kind="ExternalOutput")
            h.dbg_res[nm] = Res("dbg_" + nm)
        for nm in ["qs", "ks", "ain", "ret"]:
            h.dbg[nm] = nc.dram_tensor("dbg_" + nm, [128, 128], F32, kind="ExternalOutput")
            h.dbg_res[nm] = Res("dbg_" + nm)
    else:
        h.dbg = None
    h.xc_in = nc.dram_tensor("xc_in", [TOK, D], F32, kind="ExternalInput")
    h.xprefix = nc.dram_tensor("xprefix", [NPRE * 128, D], F32, kind="ExternalInput")
    for nm in ["wq", "wk", "wv", "wg", "wo"]:
        setattr(h, nm, nc.dram_tensor(nm, [D, D], F32, kind="ExternalInput"))
    for nm in ["bq", "bk", "bv", "bg", "bo", "b_out"]:
        setattr(h, nm, nc.dram_tensor(nm, [1, D], F32, kind="ExternalInput"))
    for nm in ["norm1_w", "norm1_b", "gn_w", "gn_b", "norm2_w", "norm2_b"]:
        setattr(h, nm, nc.dram_tensor(nm, [1, D], F32, kind="ExternalInput"))
    h.b_in_u = nc.dram_tensor("b_in_u", [1, 2816], F32, kind="ExternalInput")
    h.b_in_g = nc.dram_tensor("b_in_g", [1, 2816], F32, kind="ExternalInput")
    h.win_u_pre = nc.dram_tensor("win_u_pre", [NIC, 128, 1024], F32, kind="ExternalInput")
    h.win_g_pre = nc.dram_tensor("win_g_pre", [NIC, 128, 1024], F32, kind="ExternalInput")
    h.w_out_pad = nc.dram_tensor("w_out_pad", [NIC * 128, 1024], F32, kind="ExternalInput")
    h.GQ_d = nc.dram_tensor("GQ_d", [128, NH], F32, kind="ExternalInput")
    h.GKN_d = nc.dram_tensor("GKN_d", [128, NH], F32, kind="ExternalInput")
    h.GKS_d = nc.dram_tensor("GKS_d", [128, NH], F32, kind="ExternalInput")
    h.GKSP_d = nc.dram_tensor("GKSP_d", [128, NPRE * NH], F32, kind="ExternalInput")
    h.GDEC_d = nc.dram_tensor("GDEC_d", [64, NPRE * NH], F32, kind="ExternalInput")
    h.MASKT_d = nc.dram_tensor("MASKT_d", [128, 128], BF16, kind="ExternalInput")
    h.xcout = nc.dram_tensor("xcout", [TOK, D], F32, kind="ExternalOutput")
    h.xcout_res = Res("xcout")
    h.xc_in_res = Res("xc_in")
    h.xprefix_res = Res("xprefix")
    h.xmid_dram = nc.dram_tensor("xmid_dram", [TOK, D], F32)
    h.xmid_res = Res("xmid")

    chans = {"p0": "pool", "p1": "pool", "p2": "pool", "p3": "pool", "px": "pool",
             "sk": "sp", "so": "sp"}
    em = Em(nc, chans)
    with contextlib.ExitStack() as outer:
        declare_consts(nc, outer, h)
        load_consts(em, h)
        emit_blockC(em, h, nc, outer)
        em.barrier()
        with nc.Block() as block:
            em.finalize_and_emit(block, None)
    return nc


def prep_C_inputs(inputs, core, xb_full):
    b = core // 4
    q0 = 2048 * (core % 4)
    cc = host_consts()
    xpre = np.zeros((NPRE * 128, D), np.float32)
    xpre[:q0] = xb_full[b, :q0]
    nvalid = q0 // 128
    ii = np.arange(128, dtype=np.float64)
    GQ = np.stack([(g ** ii) / 8.0 for g in GAMMAS], 1).astype(np.float32)
    GKN = np.stack([g ** (-ii) for g in GAMMAS], 1).astype(np.float32)
    GKS = np.stack([g ** (128 - ii) for g in GAMMAS], 1).astype(np.float32)
    GKSP = np.zeros((128, NPRE * NH), np.float32)
    GDEC = np.ones((64, NPRE * NH), np.float32)
    for c in range(NPRE):
        for hh in range(NH):
            if c < nvalid:
                GKSP[:, NH * c + hh] = GKS[:, hh]
                GDEC[:, NH * c + hh] = GAMMAS[hh] ** 128
    maskT = np.zeros((128, 128), BFDT)
    for m_ in range(128):
        maskT[m_, m_:] = 1
    w_in = np.asarray(inputs["L1_attn_w_in"])
    b_in = np.asarray(inputs["L1_attn_b_in"])
    w_out = np.asarray(inputs["L1_attn_w_out"])
    wu = np.zeros((D, NIC * 128), np.float32); wu[:, :2730] = w_in[:, :2730]
    wgg = np.zeros((D, NIC * 128), np.float32); wgg[:, :2730] = w_in[:, 2730:]
    biu = np.zeros((1, NIC * 128), np.float32); biu[0, :2730] = b_in[:2730]
    big = np.zeros((1, NIC * 128), np.float32); big[0, :2730] = b_in[2730:]
    wop = np.zeros((NIC * 128, D), np.float32); wop[:2730] = w_out
    pre = lambda W: np.ascontiguousarray(
        W.reshape(8, 128, NIC, 128).transpose(2, 1, 0, 3).reshape(NIC, 128, 1024))
    m = {
        "xc_in": np.ascontiguousarray(xb_full[b, q0:q0 + 2048, :]),
        "xprefix": xpre,
        "wq": np.asarray(inputs["L1_attn_wq"]), "wk": np.asarray(inputs["L1_attn_wk"]),
        "wv": np.asarray(inputs["L1_attn_wv"]), "wg": np.asarray(inputs["L1_attn_wg"]),
        "wo": np.asarray(inputs["L1_attn_wo"]),
        "bq": np.asarray(inputs["L1_attn_bq"]).reshape(1, -1),
        "bk": np.asarray(inputs["L1_attn_bk"]).reshape(1, -1),
        "bv": np.asarray(inputs["L1_attn_bv"]).reshape(1, -1),
        "bg": np.asarray(inputs["L1_attn_bg"]).reshape(1, -1),
        "bo": np.asarray(inputs["L1_attn_bo"]).reshape(1, -1),
        "b_out": np.asarray(inputs["L1_attn_b_out"]).reshape(1, -1),
        "norm1_w": np.asarray(inputs["L1_attn_norm1_w"]).reshape(1, -1),
        "norm1_b": np.asarray(inputs["L1_attn_norm1_b"]).reshape(1, -1),
        "gn_w": np.asarray(inputs["L1_attn_gn_w"]).reshape(1, -1),
        "gn_b": np.asarray(inputs["L1_attn_gn_b"]).reshape(1, -1),
        "norm2_w": np.asarray(inputs["L1_attn_norm2_w"]).reshape(1, -1),
        "norm2_b": np.asarray(inputs["L1_attn_norm2_b"]).reshape(1, -1),
        "b_in_u": biu, "b_in_g": big,
        "win_u_pre": pre(wu), "win_g_pre": pre(wgg),
        "w_out_pad": wop,
        "GQ_d": GQ, "GKN_d": GKN, "GKS_d": GKS, "GKSP_d": GKSP, "GDEC_d": GDEC,
        "MASKT_d": maskT,
    }
    for k, v in cc.items():
        m["c_" + k] = v
    return m



NEXP = 64
PSLOT = 32
NLOC = 8


def emit_blockD(em, h, nc, outer_ctx):
    import contextlib
    XTOK, XTOK_res = alloc_sb(outer_ctx, nc, "XTOKD", [128, NT * 1024], BF16)
    SQ, SQ_res = alloc_sb(outer_ctx, nc, "SQln", [128, 1024], F32)
    COLS, COLS_res = alloc_sb(outer_ctx, nc, "COLS", [128, 16], F32)
    ln_scratch = {"sq": (SQ, SQ_res), "cols": (COLS, COLS_res)}
    T1, T1_res = alloc_sb(outer_ctx, nc, "T1", [128, 1024], F32)
    XO0, XO0_res = alloc_sb(outer_ctx, nc, "XO0", [128, 1024], F32)
    XOs = [(XO0, XO0_res), (XO0, XO0_res)]
    XPE, XPE_res = alloc_sb(outer_ctx, nc, "XPE", [128, 1024], F32)
    XS0, XS0_res = XPE, XPE_res
    XTN, XTN_res = alloc_sb(outer_ctx, nc, "XTN", [128, NF * 256], BF16)
    X2C, X2C_res = alloc_sb(outer_ctx, nc, "X2CD", [128, 1024], BF16)
    ph12 = contextlib.ExitStack()
    XT, XT_res = alloc_sb(ph12, nc, "XT_D", [128, NF * 2048], BF16)
    PHB, PHB_res = alloc_sb(ph12, nc, "PHB", [128, NF * 2048], BF16)
    DHB, DHB_res = alloc_sb(ph12, nc, "DHB", [128, NT * 2048], BF16)
    ESB, ESB_res = alloc_sb(ph12, nc, "ESB", [128, 2048], F32)
    CHT, CHT_res = alloc_sb(ph12, nc, "CHT", [128, 2048], BF16)
    RSP, RSP_res = alloc_sb(ph12, nc, "RSP", [128, 64], F32)
    RSN, RSN_res = alloc_sb(ph12, nc, "RSN", [128, 32], F32)
    PBR, PBR_res = alloc_sb(ph12, nc, "PBR", [1, 2048], BF16)

    PSL, PSL_res = alloc_ps(outer_ctx, nc, "PSL", [128, 2048], F32)
    PS0, PS0_res = alloc_ps(outer_ctx, nc, "PS0", [128, 1024], F32)
    PSt, _pstres = alloc_ps(outer_ctx, nc, "PSt", [128, 2048], BF16)
    PSt_res = [_pstres, Res("PSt1")]

    for t in range(NT):
        em.d(["px", "p0"], XTOK[:, 1024 * t:1024 * (t + 1)],
             h.xd_in[128 * t:128 * (t + 1), :], reads=[h.xd_in_res],
             writes=[XTOK_res], tag="xload")
    for t in range(NT):
        for f in range(NF):
            src = XTOK[:, 1024 * t + 128 * f:1024 * t + 128 * (f + 1)]
            dst = XT[:, 2048 * f + 128 * t:2048 * f + 128 * (t + 1)]
            emit_transpose(em, h, src, dst, PSt, PSt_res, XTOK_res, XT_res)
    for kk in range(NF):
        em.d("p1", PHB[:, 2048 * kk:2048 * (kk + 1)],
             h.phi_wT[128 * kk:128 * (kk + 1), :], writes=[PHB_res], tag="phld")
    em.d("p2", PBR[:, :], h.phi_b[:, :], writes=[PBR_res])

    for tt in range(NT):
        for j in range(4):
            pl = PSL[:, 512 * j:512 * (j + 1)]
            for kk in range(NF):
                lhsT = XT[:, 2048 * kk + 128 * tt:2048 * kk + 128 * (tt + 1)]
                rhs = PHB[:, 2048 * kk + 512 * j:2048 * kk + 512 * (j + 1)]
                em.c("pe", lambda t, o=pl, l=lhsT, r=rhs, st=(kk == 0):
                     t.matmul(o, l, r, start=st, stop=False),
                     reads=[XT_res, PHB_res], writes=[PSL_res], tag="phi")
            em.c("pe", lambda t, o=pl, r=PBR[0:1, 512 * j:512 * (j + 1)]:
                 t.matmul(o, h.ones_row[0:1, 0:128], r, start=False, stop=True),
                 reads=[PBR_res, h.ones_row_res], writes=[PSL_res], tag="phib")
        nmx = COLS[:, 8:9]
        em.c("dve", lambda v: v.reduce_max(nmx, PSL[:, :], axis=AX.X, negate=True),
             reads=[PSL_res], writes=[COLS_res], tag="dmax")
        em.c("act", lambda s: s.activation(ESB[:, :], PSL[:, :], AF.Exp, bias=nmx),
             reads=[PSL_res, COLS_res], writes=[ESB_res], tag="dexp")
        ev_np = ESB[:, :].rearrange("p (n q) -> p n q", q=PSLOT)
        em.c("dve", lambda v: v.reduce_sum(RSP[:, :], ev_np, axis=AX.X),
             reads=[ESB_res], writes=[RSP_res], tag="dsp")
        ev_pn = ESB[:, :].rearrange("p (n q) -> p q n", q=PSLOT)
        em.c("dve", lambda v: v.reduce_sum(RSN[:, :], ev_pn, axis=AX.X),
             reads=[ESB_res], writes=[RSN_res], tag="dsn")
        em.c("dve", lambda v: v.reciprocal(RSP[:, :], RSP[:, :]),
             reads=[RSP_res], writes=[RSP_res], tag="drsp")
        em.c("dve", lambda v: v.reciprocal(RSN[:, :], RSN[:, :]),
             reads=[RSN_res], writes=[RSN_res], tag="drsn")
        for n_ in range(NEXP):
            dst = DHB[:, 2048 * tt + PSLOT * n_:2048 * tt + PSLOT * (n_ + 1)]
            esl = ESB[:, PSLOT * n_:PSLOT * (n_ + 1)]
            em.c("dve", lambda v, o=dst, i_=esl, sc=RSP[:, n_:n_ + 1]:
                 v.tensor_scalar(o, i_, sc, None, ALU.mult),
                 reads=[ESB_res, RSP_res], writes=[DHB_res], tag="dhat")
        for p_ in range(PSLOT):
            dst = CHT[:, :].rearrange("p (n q) -> p q n", q=PSLOT)[:, p_, :]
            esl = ESB[:, :].rearrange("p (n q) -> p q n", q=PSLOT)[:, p_, :]
            em.c("dve", lambda v, o=dst, i_=esl, sc=RSN[:, p_:p_ + 1]:
                 v.tensor_scalar(o, i_, sc, None, ALU.mult),
                 reads=[ESB_res, RSN_res], writes=[CHT_res], tag="chat")
        for j in range(16):
            src = CHT[:, 128 * j:128 * (j + 1)]
            dst = X2C[:, 0:128]
            emit_transpose(em, h, src, dst, PSt, PSt_res, CHT_res, X2C_res)
            em.d("sq", h.cht_dram[128 * j:128 * (j + 1), 128 * tt:128 * (tt + 1)],
                 X2C[:, 0:128], reads=[X2C_res], writes=[h.cht_res], tag="chtst")

    for j in range(16):
        for jj in range(2):
            ps = PS0[:, 512 * jj:512 * (jj + 1)]
            for tt in range(NT):
                lhsT = DHB[:, 2048 * tt + 128 * j:2048 * tt + 128 * (j + 1)]
                rhs = XTOK[:, 1024 * tt + 512 * jj:1024 * tt + 512 * (jj + 1)]
                em.c("pe", lambda t, o=ps, l=lhsT, r=rhs, st=(tt == 0), sp=(tt == NT - 1):
                     t.matmul(o, l, r, start=st, stop=sp),
                     reads=[DHB_res, XTOK_res], writes=[PS0_res], tag="xtp")
        em.c("act", lambda s: s.copy(XPE[:, :], PS0[:, :]),
             reads=[PS0_res], writes=[XPE_res], tag="xtpev")
        em.d("sq", h.rs_in[128 * j:128 * (j + 1), :], XPE[:, :],
             reads=[XPE_res], writes=[h.rs_in_res], tag="xtpst")
    cc1 = em.c("pool", lambda g: g.collective_compute(
        "ReduceScatter", ALU.add, replica_groups=[list(range(8))],
        ins=[h.rs_in[:, :]], outs=[h.rs_out[:, :]]),
        reads=[h.rs_in_res], writes=[h.rs_out_res], tag="rs", async_=True)
    em.barrier()
    ph12.close()
    WEB0, WEB0_res = alloc_sb(outer_ctx, nc, "WEB0", [128, 8 * 1024], BF16)
    WEB1, WEB1_res = alloc_sb(outer_ctx, nc, "WEB1", [128, 8 * 1024], BF16)
    EBR, EBR_res = alloc_sb(outer_ctx, nc, "EBR", [1, 1024], BF16)
    Y32, Y32_res = alloc_sb(outer_ctx, nc, "Y32", [32, 1024], F32)
    YTB, YTB_res = alloc_sb(outer_ctx, nc, "YTB", [128, 16 * 1024], BF16)
    CTB0, CTB0_res = alloc_sb(outer_ctx, nc, "CTB0", [128, 128], BF16)
    CTB1, CTB1_res = alloc_sb(outer_ctx, nc, "CTB1", [128, 128], BF16)
    WEBs = [(WEB0, WEB0_res), (WEB1, WEB1_res)]
    CTBs = [(CTB0, CTB0_res), (CTB1, CTB1_res)]

    for half in range(2):
        em.d(["px", "p0"], XS0[:, :], h.rs_out[128 * half:128 * (half + 1), :],
             reads=[h.rs_out_res], writes=[XS0_res], tag="rsld")
        em.c("dve", lambda v: v.tensor_scalar_max(XS0[:, :], XS0[:, :], -33000.0),
             reads=[XS0_res], writes=[XS0_res], tag="clip1")
        em.c("dve", lambda v: v.tensor_scalar_min(XS0[:, :], XS0[:, :], 65000.0),
             reads=[XS0_res], writes=[XS0_res], tag="clip2")
        emit_ln(em, h, XS0[:, :], XS0_res, [(T1[:, :], T1_res)], ln_scratch, tag="lnD")
        em.c("act", lambda s: s.activation(X2C[:, :], T1[:, :], AF.Tanh),
             reads=[T1_res], writes=[X2C_res], tag="dtanh")
        for f in range(NF):
            src = X2C[:, 128 * f:128 * (f + 1)]
            dst = XTN[:, 256 * f + 128 * half:256 * f + 128 * (half + 1)]
            emit_transpose(em, h, src, dst, PSt, PSt_res, X2C_res, XTN_res)
    for e in range(NLOC):
        WEB, WEB_res = WEBs[e % 2]
        if e < 2:
            for kk in range(NF):
                em.d(["p1", "p2", "p3"][kk % 3], WEBs[e][0][:, 1024 * kk:1024 * (kk + 1)],
                     h.expert_w[e, 128 * kk:128 * (kk + 1), :],
                     writes=[WEBs[e][1]], tag="ewld")
        em.d("p3", EBR[:, :], h.expert_b[e:e + 1, :], reads=[EBR_res], writes=[EBR_res],
             tag="ebld")
        pp = PS0[0:32, :]
        for jj in range(2):
            for kk in range(NF):
                lhsT = XTN[:, 256 * kk + 32 * e:256 * kk + 32 * (e + 1)]
                rhs = WEB[:, 1024 * kk + 512 * jj:1024 * kk + 512 * (jj + 1)]
                em.c("pe", lambda t, o=pp[:, 512 * jj:512 * (jj + 1)], l=lhsT, r=rhs,
                     st=(kk == 0): t.matmul(o, l, r, start=st, stop=False),
                     reads=[XTN_res, WEB_res], writes=[PS0_res], tag="yt")
            em.c("pe", lambda t, o=pp[:, 512 * jj:512 * (jj + 1)],
                 r=EBR[0:1, 512 * jj:512 * (jj + 1)]: t.matmul(
                 o, h.ones_row[0:1, 0:32], r, start=False, stop=True),
                 reads=[EBR_res, h.ones_row_res], writes=[PS0_res], tag="ytb")
        em.c("dve", lambda v: v.tensor_copy(Y32[:, :], pp),
             reads=[PS0_res], writes=[Y32_res], tag="ytev")
        em.d("sq", h.ag_in[32 * e:32 * (e + 1), :], Y32[:, :],
             reads=[Y32_res], writes=[h.ag_in_res], tag="ytst")
        if e + 2 < NLOC:
            WEBn, WEBn_res = WEBs[e % 2]
            for kk in range(NF):
                em.d(["p1", "p2", "p3"][kk % 3], WEBn[:, 1024 * kk:1024 * (kk + 1)],
                     h.expert_w[e + 2, 128 * kk:128 * (kk + 1), :],
                     writes=[WEBn_res], tag="ewld2")
    cc2 = em.c("pool", lambda g: g.collective_compute(
        "AllGather", ALU.bypass, replica_groups=[list(range(8))],
        ins=[h.ag_in[:, :]], outs=[h.ag_out[:, :]]),
        reads=[h.ag_in_res], writes=[h.ag_out_res], tag="ag", async_=True)

    for j in range(16):
        em.d(["px", "p0"], YTB[:, 1024 * j:1024 * (j + 1)],
             h.ag_out[128 * j:128 * (j + 1), :], reads=[h.ag_out_res],
             writes=[YTB_res], tag="ytld")
    for tt in range(NT):
        if tt == 0:
            for j in range(2):
                CTB, CTB_res = CTBs[j % 2]
                em.d("sk", CTB[:, :], h.cht_dram[128 * j:128 * (j + 1), 0:128],
                     reads=[h.cht_res], writes=[CTB_res], tag="ctld")
        ps = PS0
        for j in range(16):
            CTB, CTB_res = CTBs[j % 2]
            for jj in range(2):
                rhs = YTB[:, 1024 * j + 512 * jj:1024 * j + 512 * (jj + 1)]
                em.c("pe", lambda t, o=ps[:, 512 * jj:512 * (jj + 1)], l=CTB[:, :], r=rhs,
                     st=(j == 0), sp=(j == 15), cr=CTB_res: t.matmul(o, l, r, start=st,
                                                                    stop=sp),
                     reads=[CTB_res, YTB_res], writes=[PS0_res], tag="ymm")
            nj, ntt = (j + 2, tt) if j + 2 < 16 else ((j + 2) % 16, tt + 1)
            if ntt < NT:
                CTBn, CTBn_res = CTBs[j % 2]
                em.d("sk", CTBn[:, :],
                     h.cht_dram[128 * nj:128 * (nj + 1), 128 * ntt:128 * (ntt + 1)],
                     reads=[h.cht_res], writes=[CTBn_res], tag="ctld2")
        xt = XTOK[:, 1024 * tt:1024 * (tt + 1)]
        em.c("dve", lambda v, x=xt: v.tensor_tensor(T1[:, :], PS0[:, :], x, ALU.add),
             reads=[PS0_res, XTOK_res], writes=[T1_res], tag="dres")
        XO, XO_res = XOs[tt % 2]
        emit_ln(em, h, T1[:, :], T1_res, [(XO[:, :], XO_res)], ln_scratch, tag="lnF")
        em.d("so", h.xout[128 * tt:128 * (tt + 1), :], XO[:, :],
             reads=[XO_res], writes=[h.xout_res], tag="xostore")


def build_D():
    import contextlib
    nc = bass.Bass("TRN2", target_bir_lowering=False)
    h = H()
    h.xd_in = nc.dram_tensor("xd_in", [TOK, D], F32, kind="ExternalInput")
    h.phi_wT = nc.dram_tensor("phi_wT", [D, 2048], F32, kind="ExternalInput")
    h.phi_b = nc.dram_tensor("phi_b", [1, 2048], F32, kind="ExternalInput")
    h.expert_w = nc.dram_tensor("expert_w", [NLOC, D, D], F32, kind="ExternalInput")
    h.expert_b = nc.dram_tensor("expert_b", [NLOC, D], F32, kind="ExternalInput")
    h.xout = nc.dram_tensor("xout", [TOK, D], F32, kind="ExternalOutput")
    h.xout_res = Res("xout")
    h.xd_in_res = Res("xd_in")
    h.cht_dram = nc.dram_tensor("cht_dram", [2048, 2048], BF16)
    h.cht_res = Res("cht")
    h.rs_in = nc.dram_tensor("rs_in", [2048, 1024], F32)
    h.rs_in_res = Res("rs_in")
    h.rs_out = nc.dram_tensor("rs_out", [256, 1024], F32)
    h.rs_out_res = Res("rs_out")
    h.ag_in = nc.dram_tensor("ag_in", [256, 1024], F32)
    h.ag_in_res = Res("ag_in")
    h.ag_out = nc.dram_tensor("ag_out", [2048, 1024], F32, addr_space="Shared")
    h.ag_out_res = Res("ag_out")

    chans = {"p0": "pool", "p1": "pool", "p2": "pool", "p3": "pool", "px": "pool",
             "sk": "sp", "sq": "sp", "so": "sp"}
    em = Em(nc, chans)
    with contextlib.ExitStack() as outer:
        declare_consts(nc, outer, h)
        load_consts(em, h)
        emit_blockD(em, h, nc, outer)
        em.barrier()
        with nc.Block() as block:
            em.finalize_and_emit(block, None)
    return nc


def prep_D_inputs(inputs, core, xc_full):
    b = core // 4
    q0 = 2048 * (core % 4)
    cc = host_consts()
    m = {
        "xd_in": np.ascontiguousarray(xc_full[b, q0:q0 + 2048, :]),
        "phi_wT": np.ascontiguousarray(np.asarray(inputs["L1_ffn_phi_w"]).T),
        "phi_b": np.asarray(inputs["L1_ffn_phi_b"]).reshape(1, -1),
        "expert_w": np.ascontiguousarray(np.asarray(inputs["L1_ffn_expert_w"])[8 * core:8 * core + 8]),
        "expert_b": np.ascontiguousarray(np.asarray(inputs["L1_ffn_expert_b"])[8 * core:8 * core + 8]),
    }
    for k, v in cc.items():
        m["c_" + k] = v
    return m



def _gather(res, key):
    out = np.zeros((2, 8192, 1024), np.float32)
    for c in range(NCORE):
        b, q0 = c // 4, 2048 * (c % 4)
        out[b, q0:q0 + 2048] = res.results[c][key]
    return out


def kernel(**inputs):
    nc = build_fused()
    res = run_bass_kernel_spmd(nc, [prep_fused_inputs(inputs, c) for c in range(NCORE)],
                               core_ids=list(range(NCORE)))
    return _gather(res, "xout")



def build_fused():
    import contextlib
    nc = bass.Bass("TRN2", target_bir_lowering=False)
    h = H()
    h.x_own = nc.dram_tensor("x_own", [TOK, D], F32, kind="ExternalInput")
    h.x_batch = nc.dram_tensor("x_batch", [SEQ, D], F32, kind="ExternalInput")
    h.qkv_wT = nc.dram_tensor("qkv_wT", [D, 3 * D], F32, kind="ExternalInput")
    h.qkv_b = nc.dram_tensor("qkv_b", [1, 3 * D], F32, kind="ExternalInput")
    h.out_wT = nc.dram_tensor("out_wT", [D, D], F32, kind="ExternalInput")
    h.out_b = nc.dram_tensor("out_b", [1, D], F32, kind="ExternalInput")
    h.gate_w = nc.dram_tensor("gate_w", [D, NE], F32, kind="ExternalInput")
    h.WGU_pre = nc.dram_tensor("WGU_pre", [NE, NI, 128, 2048], F32, kind="ExternalInput")
    h.Wd = nc.dram_tensor("Wd", [NE, 2816, 1024], F32, kind="ExternalInput")
    h.SGU_pre = nc.dram_tensor("SGU_pre", [NIS, 128, 2048], F32, kind="ExternalInput")
    h.Sd = nc.dram_tensor("Sd", [1408, 1024], F32, kind="ExternalInput")
    for nm in ["wq", "wk", "wv", "wg", "wo"]:
        setattr(h, nm, nc.dram_tensor(nm, [D, D], F32, kind="ExternalInput"))
    for nm in ["bq", "bk", "bv", "bg", "bo", "b_out"]:
        setattr(h, nm, nc.dram_tensor(nm, [1, D], F32, kind="ExternalInput"))
    for nm in ["norm1_w", "norm1_b", "gn_w", "gn_b", "norm2_w", "norm2_b"]:
        setattr(h, nm, nc.dram_tensor(nm, [1, D], F32, kind="ExternalInput"))
    h.b_in_u = nc.dram_tensor("b_in_u", [1, 2816], F32, kind="ExternalInput")
    h.b_in_g = nc.dram_tensor("b_in_g", [1, 2816], F32, kind="ExternalInput")
    h.win_u_pre = nc.dram_tensor("win_u_pre", [NIC, 128, 1024], F32, kind="ExternalInput")
    h.win_g_pre = nc.dram_tensor("win_g_pre", [NIC, 128, 1024], F32, kind="ExternalInput")
    h.w_out_pad = nc.dram_tensor("w_out_pad", [NIC * 128, 1024], F32, kind="ExternalInput")
    h.GQ_d = nc.dram_tensor("GQ_d", [128, NH], F32, kind="ExternalInput")
    h.GKN_d = nc.dram_tensor("GKN_d", [128, NH], F32, kind="ExternalInput")
    h.GKS_d = nc.dram_tensor("GKS_d", [128, NH], F32, kind="ExternalInput")
    h.GKSP_d = nc.dram_tensor("GKSP_d", [128, NPRE * NH], F32, kind="ExternalInput")
    h.GDEC_d = nc.dram_tensor("GDEC_d", [64, NPRE * NH], F32, kind="ExternalInput")
    h.MASKT_d = nc.dram_tensor("MASKT_d", [128, 128], BF16, kind="ExternalInput")
    h.phi_wT = nc.dram_tensor("phi_wT", [D, 2048], F32, kind="ExternalInput")
    h.phi_b = nc.dram_tensor("phi_b", [1, 2048], F32, kind="ExternalInput")
    h.expert_w = nc.dram_tensor("expert_w", [NLOC, D, D], F32, kind="ExternalInput")
    h.expert_b = nc.dram_tensor("expert_b", [NLOC, D], F32, kind="ExternalInput")
    h.xout = nc.dram_tensor("xout", [TOK, D], F32, kind="ExternalOutput")
    h.xout_res = Res("xout")
    h.K_dram = nc.dram_tensor("K_dram", [D, SEQ], BF16); h.K_dram_res = Res("K")
    h.V_dram = nc.dram_tensor("V_dram", [D, SEQ], BF16); h.V_dram_res = Res("V")
    h.Q_dram = nc.dram_tensor("Q_dram", [D, TOK], BF16); h.Q_dram_res = Res("Q")
    h.xa = nc.dram_tensor("xa_i", [TOK, D], F32); h.xa_res = Res("xa")
    h.xb = nc.dram_tensor("xb_i", [TOK, D], F32); h.xb_res = Res("xb")
    h.xcout = nc.dram_tensor("xc_i", [TOK, D], F32); h.xcout_res = Res("xc")
    h.xb_all = nc.dram_tensor("xb_all", [SEQ, D], F32); h.xb_all_res = Res("xb_all")
    h.xmid_dram = nc.dram_tensor("xmid_dram", [TOK, D], F32); h.xmid_res = Res("xmid")
    h.cht_dram = nc.dram_tensor("cht_dram", [2048, 2048], BF16); h.cht_res = Res("cht")
    h.rs_in = nc.dram_tensor("rs_in", [2048, 1024], F32); h.rs_in_res = Res("rs_in")
    h.rs_out = nc.dram_tensor("rs_out", [256, 1024], F32); h.rs_out_res = Res("rs_out")
    h.ag_in = nc.dram_tensor("ag_in", [256, 1024], F32); h.ag_in_res = Res("ag_in")
    h.ag_out = nc.dram_tensor("ag_out", [2048, 1024], F32, addr_space="Shared")
    h.ag_out_res = Res("ag_out")
    h.xb_in = h.xa
    h.xb_in_res = h.xa_res
    h.xc_in = h.xb
    h.xc_in_res = h.xb_res
    h.xprefix = h.xb_all
    h.xprefix_res = h.xb_all_res
    h.xd_in = h.xcout
    h.xd_in_res = h.xcout_res
    h.dbg = None

    chans = {"p0": "pool", "p1": "pool", "p2": "pool", "p3": "pool", "px": "pool",
             "pw": "pool", "sk": "sp", "sq": "sp", "so": "sp"}
    em = Em(nc, chans)
    with contextlib.ExitStack() as outer:
        declare_consts(nc, outer, h)
        load_consts(em, h)
        with contextlib.ExitStack() as sA:
            emit_blockA(em, h, None, nc, sA)
        em.barrier()
        with contextlib.ExitStack() as sB:
            emit_blockB_fused(em, h, nc, sB)
        em.barrier()
        em.c("pool", lambda g: g.collective_compute(
            "AllGather", ALU.bypass, replica_groups=[[0, 1, 2, 3], [4, 5, 6, 7]],
            ins=[h.xb[:, :]], outs=[h.xb_all[:, :]]),
            reads=[h.xb_res], writes=[h.xb_all_res], tag="agxb", async_=True)
        with contextlib.ExitStack() as sC:
            emit_blockC_fused(em, h, nc, sC)
        em.barrier()
        with contextlib.ExitStack() as sD:
            emit_blockD(em, h, nc, sD)
        em.barrier()
        with nc.Block() as block:
            em.finalize_and_emit(block, None)
    return nc


def emit_blockB_fused(em, h, nc, ctx):
    emit_blockB(em, h, nc, ctx)


def emit_blockC_fused(em, h, nc, ctx):
    emit_blockC(em, h, nc, ctx)


def prep_fused_inputs(inputs, core):
    m = {}
    m.update(prep_A_inputs(inputs, core))
    dummy = np.zeros((2, 8192, 1024), np.float32)
    mb = prep_B_inputs(inputs, core, dummy); mb.pop("xb_in")
    m.update(mb)
    mc = prep_C_inputs(inputs, core, dummy); mc.pop("xc_in"); mc.pop("xprefix")
    m.update(mc)
    md = prep_D_inputs(inputs, core, dummy); md.pop("xd_in")
    m.update(md)
    return m


# version 04
# speedup vs baseline: 1.2723x; 1.1107x over previous
import math
import numpy as np
import ml_dtypes
import concourse.bass as bass
import concourse.mybir as mybir
from concourse.bass_utils import run_bass_kernel_spmd

F32 = mybir.dt.float32
BF16 = mybir.dt.bfloat16
AF = mybir.ActivationFunctionType
ALU = mybir.AluOpType
AX = mybir.AxisListType
BFDT = ml_dtypes.bfloat16

NCORE = 8
TOK = 2048
NT = TOK // 128
D = 1024
NF = D // 128
SEQ = 8192
EPS = 1e-5


class Res:
    __slots__ = ("name", "last_w", "readers")
    def __init__(self, name=""):
        self.name = name
        self.last_w = None
        self.readers = []

class Op:
    __slots__ = ("idx", "stream", "timeline", "kind", "fn", "deps", "inc", "val", "tag",
                 "async_")
    def __init__(self, idx, stream, timeline, kind, fn, tag=""):
        self.async_ = False
        self.idx = idx
        self.stream = stream
        self.timeline = timeline
        self.kind = kind
        self.fn = fn
        self.deps = []
        self.inc = False
        self.val = None
        self.tag = tag

COMPUTE_STREAMS = ("pe", "act", "dve", "pool", "sp")

class Em:
    def __init__(self, nc, chans):
        self.nc = nc
        self.ops = []
        self.streams = {s: [] for s in COMPUTE_STREAMS}
        self.chan_stream = dict(chans)
        self.chan_count = {c: 0 for c in chans}
        self.chan_last = {c: None for c in chans}
        self.grp_rr = {}

    def _mkop(self, stream, timeline, kind, fn, reads, writes, deps, tag):
        op = Op(len(self.ops), stream, timeline, kind, fn, tag)
        dd = []
        for r in reads:
            if r.last_w is not None:
                dd.append(r.last_w)
        for w in writes:
            if w.last_w is not None:
                dd.append(w.last_w)
            dd.extend(w.readers)
        dd.extend(deps)
        keep = []
        seen = set()
        for d in dd:
            if d is op:
                continue
            if (d.timeline == op.timeline and d.kind == "c" and op.kind == "c"
                    and d.timeline == "pe"):
                continue
            if (d.kind == "c" and op.kind == "d" and d.stream == op.stream
                    and not d.async_):
                continue
            if id(d) not in seen:
                seen.add(id(d))
                keep.append(d)
        op.deps = keep
        for r in reads:
            r.readers.append(op)
        for w in writes:
            w.last_w = op
            w.readers = []
        self.ops.append(op)
        self.streams[stream].append(op)
        return op

    def c(self, stream, fn, reads=(), writes=(), deps=(), tag="", async_=False):
        op = self._mkop(stream, stream, "c", fn, reads, writes, deps, tag)
        op.async_ = async_
        return op

    def d(self, chan, out_ap, in_ap, reads=(), writes=(), deps=(), tag="", **kw):
        if isinstance(chan, (list, tuple)):
            key = id(chan)
            i = self.grp_rr.get(key, 0)
            self.grp_rr[key] = i + 1
            chan = chan[i % len(chan)]
        stream = self.chan_stream[chan]
        fn = lambda eng, o=out_ap, i=in_ap, kw=kw: eng.dma_start(out=o, in_=i, **kw)
        deps = list(deps)
        if self.chan_last[chan] is not None:
            deps.append(self.chan_last[chan])
        op = self._mkop(stream, chan, "d", fn, reads, writes, deps, tag)
        self.chan_last[chan] = op
        return op

    def barrier(self):
        lasts = {}
        for op in self.ops:
            if op.kind != "w":
                lasts[op.timeline] = op
        ops = list(lasts.values())
        for s in COMPUTE_STREAMS:
            def mk(eng):
                return None
            o = Op(len(self.ops), s, s, "w", mk, "barrier")
            o.deps = [d for d in ops if not (d.timeline == s and d.kind == "c")]
            self.ops.append(o)
            self.streams[s].append(o)

    def finalize_and_emit(self, block, engines):
        for op in self.ops:
            for dpp in op.deps:
                dpp.inc = True
        cnt = {}
        for op in self.ops:
            if op.kind == "d":
                c = cnt.get(op.timeline, 0) + 16
                cnt[op.timeline] = c
                op.val = c
                op.inc = True
            elif op.inc and op.kind != "w":
                c = cnt.get(op.timeline, 0) + 1
                cnt[op.timeline] = c
                op.val = c
        used = sorted(set(op.timeline for op in self.ops if op.inc))
        sems = {t: self.nc.alloc_semaphore(name=f"sem_{t}") for t in used}
        self.sems = sems
        self.final_counts = dict(cnt)

        def run_stream(eng, stream_name):
            waited = {}
            for op in self.streams[stream_name]:
                for dpp in op.deps:
                    sem = sems[dpp.timeline]
                    if waited.get(dpp.timeline, 0) < dpp.val:
                        eng.wait_ge(sem, dpp.val)
                        waited[dpp.timeline] = dpp.val
                if op.kind == "w":
                    continue
                instr = op.fn(eng)
                if op.inc:
                    instr.then_inc(sems[op.timeline], 16 if op.kind == "d" else 1)

        @block.sync
        def _(sync):
            run_stream(sync, "sp")

        @block.tensor
        def _(t):
            run_stream(t, "pe")

        @block.scalar
        def _(s):
            run_stream(s, "act")

        @block.vector
        def _(v):
            run_stream(v, "dve")

        @block.gpsimd
        def _(g):
            run_stream(g, "pool")



class H:
    pass


_ALLOC_N = [0]

def alloc_sb(ctx, nc, name, shape, dtype):
    _ALLOC_N[0] += 1
    t = ctx.enter_context(nc.sbuf_tensor(f"{name}_{_ALLOC_N[0]}", shape, dtype))
    return t, Res(name)


def alloc_ps(ctx, nc, name, shape, dtype):
    _ALLOC_N[0] += 1
    t = ctx.enter_context(nc.psum_tensor(f"{name}_{_ALLOC_N[0]}", shape, dtype))
    return t, Res(name)


def emit_transpose(em, h, src_ap, dst_ap, psum, psum_res, src_res, dst_res, evac="dve"):
    n = src_ap.shape[1]
    p = src_ap.shape[0]
    par = h.tr_par = (getattr(h, "tr_par", 0) + 1) % 2
    pt = psum[0:n, 1024 * par:1024 * par + p]
    idn = h.ident[0:p, 0:p]
    pres = psum_res[par]
    em.c("pe", lambda t, o=pt, i=src_ap, e=idn: t.transpose(o, i, e),
         reads=[src_res, h.ident_res], writes=[pres], tag="tr")
    em.c(evac, lambda v, o=dst_ap, i=pt: v.tensor_copy(o, i),
         reads=[pres], writes=[dst_res], tag="tr_ev")


def emit_ln(em, h, src_ap, src_res, outs, scratch, tag="ln"):
    sq, sq_res = scratch["sq"]
    cols, cols_res = scratch["cols"]
    ssum = cols[:, 0:1]; ssq = cols[:, 1:2]; mean = cols[:, 2:3]
    varr = cols[:, 3:4]; rstd = cols[:, 4:5]; negmr = cols[:, 5:6]
    em.c("dve", lambda v: v.reduce_sum(ssum, src_ap, axis=AX.X),
         reads=[src_res], writes=[cols_res], tag=tag + "_sum")
    em.c("act", lambda s: s.activation(sq[:, :], src_ap, AF.Square, accum_out=ssq),
         reads=[src_res], writes=[sq_res, cols_res], tag=tag + "_sq")
    em.c("dve", lambda v: v.tensor_scalar(mean, ssum, 1.0 / 1024.0, None, ALU.mult),
         reads=[cols_res], writes=[cols_res], tag=tag)
    em.c("dve", lambda v: v.tensor_scalar(varr, ssq, 1.0 / 1024.0, EPS, ALU.mult, ALU.add),
         reads=[cols_res], writes=[cols_res], tag=tag)
    tmp = cols[:, 6:7]
    em.c("dve", lambda v: v.tensor_tensor(tmp, mean, mean, ALU.mult),
         reads=[cols_res], writes=[cols_res], tag=tag)
    em.c("dve", lambda v: v.tensor_tensor(varr, varr, tmp, ALU.subtract),
         reads=[cols_res], writes=[cols_res], tag=tag)
    em.c("act", lambda s: s.activation(rstd, varr, AF.Ln),
         reads=[cols_res], writes=[cols_res], tag=tag + "_ln")
    em.c("act", lambda s: s.activation(rstd, rstd, AF.Exp, scale=-0.5),
         reads=[cols_res], writes=[cols_res], tag=tag + "_exp")
    em.c("dve", lambda v: v.scalar_tensor_tensor(negmr, mean, -1.0, rstd, ALU.mult, ALU.mult),
         reads=[cols_res], writes=[cols_res], tag=tag)
    for dst_ap, dst_res in outs:
        em.c("dve", lambda v, o=dst_ap: v.tensor_scalar(o, src_ap, rstd, negmr, ALU.mult, ALU.add),
             reads=[src_res, cols_res], writes=[dst_res], tag=tag + "_app")



def host_consts():
    c = {}
    c["ident"] = np.eye(128, dtype=BFDT)
    c["ones_row"] = np.ones((1, 512), dtype=BFDT)
    c["ones64c"] = np.ones((64, 1), dtype=BFDT)
    c["ones1x64f"] = np.ones((1, 64), dtype=np.float32)
    c["ones1x128f"] = np.ones((1, 128), dtype=np.float32)
    halfA = np.zeros((128, 1), dtype=BFDT); halfA[0:64] = 1
    halfB = np.zeros((128, 1), dtype=BFDT); halfB[64:128] = 1
    c["halfA"] = halfA
    c["halfB"] = halfB
    return c


def declare_consts(nc, ctx, h):
    specs = [
        ("ident", [128, 128], BF16),
        ("ones_row", [1, 512], BF16),
        ("ones64c", [64, 1], BF16),
        ("ones1x64f", [1, 64], F32),
        ("ones1x128f", [1, 128], F32),
        ("halfA", [128, 1], BF16),
        ("halfB", [128, 1], BF16),
    ]
    h.const_dram = {}
    for name, shape, dt in specs:
        dram = nc.dram_tensor("c_" + name, shape, dt, kind="ExternalInput")
        sb, res = alloc_sb(ctx, nc, "s_" + name, shape, dt)
        h.const_dram[name] = dram
        setattr(h, name, sb)
        setattr(h, name + "_res", res)

def load_consts(em, h):
    for name in h.const_dram:
        dram = h.const_dram[name]
        sb = getattr(h, name)
        res = getattr(h, name + "_res")
        em.d("sk", sb[:, :], dram[:, :], writes=[res], tag="const")



A_PHASES = {0, 1, 2, 3, 4}

def emit_blockA(em, h, ctx, nc, outer_ctx):
    import contextlib

    XHAT, XHAT_res = alloc_sb(outer_ctx, nc, "XHAT", [128, NT * 1024], BF16)
    WOUT, WOUT_res = alloc_sb(outer_ctx, nc, "WOUT", [128, 8 * 1024], BF16)
    BOUT, BOUT_res = alloc_sb(outer_ctx, nc, "BOUT", [1, 1024], BF16)
    SQ, SQ_res = alloc_sb(outer_ctx, nc, "SQln", [128, 1024], F32)
    COLS, COLS_res = alloc_sb(outer_ctx, nc, "COLS", [128, 16], F32)
    ln_scratch = {"sq": (SQ, SQ_res), "cols": (COLS, COLS_res)}
    QMX, QMX_res = alloc_sb(outer_ctx, nc, "QMX", [1, 16], F32)
    KMX, KMX_res = alloc_sb(outer_ctx, nc, "KMX", [1, 16], F32)
    NTMP, NTMP_res = alloc_sb(outer_ctx, nc, "NTMP", [1, 4], F32)
    em.c("dve", lambda v: v.memset(QMX[:, :], 0.0), writes=[QMX_res])
    em.c("dve", lambda v: v.memset(KMX[:, :], 0.0), writes=[KMX_res])

    PS0, PS0_res = alloc_ps(outer_ctx, nc, "PS0", [128, 1024], F32)
    PS1, PS1_res = alloc_ps(outer_ctx, nc, "PS1", [128, 1024], F32)
    PS2, PS2_res = alloc_ps(outer_ctx, nc, "PS2", [128, 1024], F32)
    PSt, _pstres = alloc_ps(outer_ctx, nc, "PSt", [128, 2048], BF16)
    PSt_res = [_pstres, Res("PSt1")]

    for f in range(NF):
        em.d("pw", WOUT[:, 1024 * f:1024 * (f + 1)], h.out_wT[128 * f:128 * (f + 1), :],
             writes=[WOUT_res], tag="woutld")
    em.d("pw", BOUT[:, :], h.out_b[:, :], writes=[BOUT_res])

    with contextlib.ExitStack() as ph12:
        XT, XT_res = alloc_sb(ph12, nc, "XT", [128, NF * 2048], BF16)
        WQKV, WQKV_res = alloc_sb(ph12, nc, "WQKV", [128, 8 * 3072], BF16)
        BQKV, BQKV_res = alloc_sb(ph12, nc, "BQKV", [1, 3072], BF16)
        XBH0, XBH0_res = alloc_sb(ph12, nc, "XBH0", [128, 4 * 1024], BF16)
        XBH1, XBH1_res = alloc_sb(ph12, nc, "XBH1", [128, 4 * 1024], BF16)
        XBT0, XBT0_res = alloc_sb(ph12, nc, "XBT0", [128, 8 * 512], BF16)
        XBT1, XBT1_res = alloc_sb(ph12, nc, "XBT1", [128, 8 * 512], BF16)
        KVE, KVE_res = alloc_sb(ph12, nc, "KVE", [128, 4 * 512], BF16)
        SQ2, SQ2_res = alloc_sb(ph12, nc, "SQ2", [128, 512], BF16)
        XBHs = [(XBH0, XBH0_res), (XBH1, XBH1_res)]
        XBTs = [(XBT0, XBT0_res), (XBT1, XBT1_res)]

        for kk in range(NF):
            em.d("pw", WQKV[:, 3072 * kk:3072 * (kk + 1)],
                 h.qkv_wT[128 * kk:128 * (kk + 1), :], writes=[WQKV_res], tag="wqkvld")
        em.d("pw", BQKV[:, :], h.qkv_b[:, :], writes=[BQKV_res])

        for t in range(NT):
            em.d("px", XHAT[:, 1024 * t:1024 * (t + 1)],
                 h.x_own[128 * t:128 * (t + 1), :], writes=[XHAT_res], tag="xload")
        for t in range(NT):
            for f in range(NF):
                src = XHAT[:, 1024 * t + 128 * f:1024 * t + 128 * (f + 1)]
                dst = XT[:, 2048 * f + 128 * t:2048 * f + 128 * (t + 1)]
                emit_transpose(em, h, src, dst, PSt, PSt_res, XHAT_res, XT_res)

        pspool = [(PS0, PS0_res), (PS1, PS1_res), (PS2, PS2_res)]
        pi = 0
        for f in range(NF if 1 in A_PHASES else 0):
            for s in range(4):
                ps, ps_res = pspool[pi % 3]; pi += 1
                pp = ps[:, 0:512]
                for kk in range(NF):
                    lhsT = WQKV[:, 3072 * kk + 128 * f:3072 * kk + 128 * (f + 1)]
                    rhs = XT[:, 2048 * kk + 512 * s:2048 * kk + 512 * (s + 1)]
                    em.c("pe", lambda t, o=pp, l=lhsT, r=rhs, st=(kk == 0): t.matmul(
                        o, l, r, start=st, stop=False),
                        reads=[WQKV_res, XT_res], writes=[ps_res], tag="qproj")
                brow = BQKV[0:1, 128 * f:128 * (f + 1)]
                em.c("pe", lambda t, o=pp, l=brow: t.matmul(
                    o, l, h.ones_row[0:1, :], start=False, stop=True),
                    reads=[BQKV_res, h.ones_row_res], writes=[ps_res], tag="qbias")
                qe = KVE[:, 0:512]
                em.c("act", lambda s_, o=qe, i=pp: s_.copy(o, i),
                     reads=[ps_res], writes=[KVE_res], tag="qevac")
                em.d("sq", h.Q_dram[128 * f:128 * (f + 1), 512 * s:512 * (s + 1)],
                     qe, reads=[KVE_res], writes=[h.Q_dram_res], tag="qstore")
                em.c("act", lambda s_, i=qe: s_.activation(SQ2[:, :], i, AF.Square),
                     reads=[KVE_res], writes=[SQ2_res], tag="qsq")
                for half, hh in ((h.halfA[:, :], 0), (h.halfB[:, :], 1)):
                    head = 2 * f + hh
                    pn = ps[0:1, 0:512]
                    em.c("pe", lambda t, l=half, o=pn: t.matmul(o, l, SQ2[:, :], start=True, stop=True),
                         reads=[SQ2_res, h.halfA_res, h.halfB_res], writes=[ps_res], tag="qn")
                    em.c("dve", lambda v, i=pn: v.reduce_max(NTMP[0:1, 0:1], i, axis=AX.X),
                         reads=[ps_res], writes=[NTMP_res], tag="qnev")
                    qmc = QMX[0:1, head:head + 1]
                    em.c("dve", lambda v, o=qmc: v.tensor_tensor(o, o, NTMP[0:1, 0:1], ALU.max),
                         reads=[NTMP_res, QMX_res], writes=[QMX_res], tag="qnmax")

        for s16 in range(16 if 2 in A_PHASES else 0):
            par = s16 % 2
            XBH, XBH_res = XBHs[par]
            XBT, XBT_res = XBTs[par]
            for j in range(4):
                tok0 = 512 * s16 + 128 * j
                em.d("px", XBH[:, 1024 * j:1024 * (j + 1)],
                     h.x_batch[tok0:tok0 + 128, :], writes=[XBH_res], tag="xbload")
            for j in range(4):
                for f in range(NF):
                    src = XBH[:, 1024 * j + 128 * f:1024 * j + 128 * (f + 1)]
                    dst = XBT[:, 512 * f + 128 * j:512 * f + 128 * (j + 1)]
                    emit_transpose(em, h, src, dst, PSt, PSt_res, XBH_res, XBT_res)
            for kvi, (dram, dram_res) in ((1, (h.K_dram, h.K_dram_res)),
                                          (2, (h.V_dram, h.V_dram_res))):
                for f in range(NF):
                    ps, ps_res = pspool[pi % 3]; pi += 1
                    pp = ps[:, 0:512]
                    outc = 1024 * kvi + 128 * f
                    for kk in range(NF):
                        lhsT = WQKV[:, 3072 * kk + outc:3072 * kk + outc + 128]
                        rhs = XBT[:, 512 * kk:512 * (kk + 1)]
                        em.c("pe", lambda t, o=pp, l=lhsT, r=rhs, st=(kk == 0): t.matmul(
                            o, l, r, start=st, stop=False),
                            reads=[WQKV_res, XBT_res], writes=[ps_res], tag="kvproj")
                    brow = BQKV[0:1, outc:outc + 128]
                    em.c("pe", lambda t, o=pp, l=brow: t.matmul(
                        o, l, h.ones_row[0:1, :], start=False, stop=True),
                        reads=[BQKV_res, h.ones_row_res], writes=[ps_res], tag="kvbias")
                    kve = KVE[:, 512 * (kvi - 1 + 2 * (f % 2)):512 * (kvi + 2 * (f % 2))]
                    em.c("act", lambda s_, o=kve, i=pp: s_.copy(o, i),
                         reads=[ps_res], writes=[KVE_res], tag="kvevac")
                    em.d("sk", dram[128 * f:128 * (f + 1), 512 * s16:512 * (s16 + 1)],
                         kve, reads=[KVE_res], writes=[dram_res], tag="kvstore")
                    if kvi == 1:
                        em.c("act", lambda s_, i=kve: s_.activation(SQ2[:, :], i, AF.Square),
                             reads=[KVE_res], writes=[SQ2_res], tag="ksq")
                        for half, hh in ((h.halfA[:, :], 0), (h.halfB[:, :], 1)):
                            head = 2 * f + hh
                            pn = ps[0:1, 0:512]
                            em.c("pe", lambda t, l=half, o=pn: t.matmul(o, l, SQ2[:, :],
                                                                        start=True, stop=True),
                                 reads=[SQ2_res, h.halfA_res, h.halfB_res],
                                 writes=[ps_res], tag="kn")
                            em.c("dve", lambda v, i=pn: v.reduce_max(NTMP[0:1, 1:2], i, axis=AX.X),
                                 reads=[ps_res], writes=[NTMP_res], tag="knev")
                            kmc = KMX[0:1, head:head + 1]
                            em.c("dve", lambda v, o=kmc: v.tensor_tensor(o, o, NTMP[0:1, 1:2], ALU.max),
                                 reads=[NTMP_res, KMX_res], writes=[KMX_res], tag="knmax")

    em.barrier()

    with contextlib.ExitStack() as ph34:
        ATT, ATT_res = alloc_sb(ph34, nc, "ATT", [128, 8 * 2048], BF16)
        KH0, KH0_res = alloc_sb(ph34, nc, "KH0", [64, 8192], BF16)
        KH1, KH1_res = alloc_sb(ph34, nc, "KH1", [64, 8192], BF16)
        VH0, VH0_res = alloc_sb(ph34, nc, "VH0", [64, 8192], BF16)
        VH1, VH1_res = alloc_sb(ph34, nc, "VH1", [64, 8192], BF16)
        QH0, QH0_res = alloc_sb(ph34, nc, "QH0", [64, 2048], BF16)
        QH1, QH1_res = alloc_sb(ph34, nc, "QH1", [64, 2048], BF16)
        VA, VA_res = alloc_sb(ph34, nc, "VA", [128, 64 * 65], BF16)
        PT0, PT0_res = alloc_sb(ph34, nc, "PTb0", [128, 1024], BF16)
        PT1, PT1_res = alloc_sb(ph34, nc, "PTb1", [128, 1024], BF16)
        RB, RB_res = alloc_sb(ph34, nc, "RB", [64, 1024], F32)
        RCP, RCP_res = alloc_sb(ph34, nc, "RCP", [1, 1024], F32)
        BCOL, BCOL_res = alloc_sb(ph34, nc, "BCOL", [128, 16], F32)
        SCL, SCL_res = alloc_sb(ph34, nc, "SCL", [1, 16], F32)
        KHs = [(KH0, KH0_res), (KH1, KH1_res)]
        VHs = [(VH0, VH0_res), (VH1, VH1_res)]
        QHs = [(QH0, QH0_res), (QH1, QH1_res)]
        PTs = [(PT0, PT0_res), (PT1, PT1_res)]

        em.c("pool", lambda g: g.memset(VA[:, :], 1.0), writes=[VA_res], tag="vaones")

        def issue_head_loads(hd):
            KH, KH_res = KHs[hd % 2]
            VH, VH_res = VHs[hd % 2]
            QH, QH_res = QHs[hd % 2]
            em.d("sk", KH[:, :], h.K_dram[64 * hd:64 * (hd + 1), :],
                 reads=[h.K_dram_res], writes=[KH_res], tag="khld")
            em.d("sk", VH[:, :], h.V_dram[64 * hd:64 * (hd + 1), :],
                 reads=[h.V_dram_res], writes=[VH_res], tag="vhld")
            em.d("sq", QH[:, :], h.Q_dram[64 * hd:64 * (hd + 1), :],
                 reads=[h.Q_dram_res], writes=[QH_res], tag="qhld")

        nheads = 16 if 3 in A_PHASES else 0
        if nheads:
            issue_head_loads(0)
            issue_head_loads(1)
        for hd in range(nheads):
            KH, KH_res = KHs[hd % 2]
            VH, VH_res = VHs[hd % 2]
            QH, QH_res = QHs[hd % 2]
            pr = SCL[0:1, 2:3]; mm_ = SCL[0:1, 3:4]
            em.c("dve", lambda v, a=QMX[0:1, hd:hd + 1], b=KMX[0:1, hd:hd + 1]:
                 v.tensor_tensor(pr, a, b, ALU.mult),
                 reads=[QMX_res, KMX_res], writes=[SCL_res])
            em.c("act", lambda s: s.activation(mm_, pr, AF.Ln),
                 reads=[SCL_res], writes=[SCL_res])
            em.c("act", lambda s: s.activation(mm_, mm_, AF.Exp, scale=0.5),
                 reads=[SCL_res], writes=[SCL_res])
            pb = PS1[0:128, 0:1]
            em.c("pe", lambda t: t.matmul(pb, h.ones1x128f[0:1, :], mm_, start=True, stop=True),
                 reads=[SCL_res, h.ones1x128f_res], writes=[PS1_res], tag="bcolmm")
            bcol = BCOL[:, hd:hd + 1]
            em.c("act", lambda s, o=bcol: s.activation(o, pb, AF.Copy, scale=-0.125),
                 reads=[PS1_res], writes=[BCOL_res], tag="bcol")

            for kt in range(64):
                src = VH[:, 128 * kt:128 * (kt + 1)]
                dst = VA[:, 65 * kt:65 * kt + 64]
                emit_transpose(em, h, src, dst, PSt, PSt_res, VH_res, VA_res)

            for qh in range(2):
                PA, PA_res = PS0, PS0_res

                def emit_pv(kt):
                    PT, PT_res = PTs[kt % 2]
                    va = VA[:, 65 * kt:65 * (kt + 1)]
                    for j in range(2):
                        em.c("pe", lambda t, o=PA[0:65, 512 * j:512 * (j + 1)], l=va,
                             r=PT[:, 512 * j:512 * (j + 1)], st=(kt == 0), sp=(kt == 63):
                             t.matmul(o, l, r, start=st, stop=sp),
                             reads=[VA_res, PT_res], writes=[PA_res], tag="pv")

                for kt in range(64):
                    PB, PB_res = (PS1, PS1_res) if kt % 2 == 0 else (PS2, PS2_res)
                    PT, PT_res = PTs[kt % 2]
                    lhsT = KH[:, 128 * kt:128 * (kt + 1)]
                    for j in range(2):
                        rhs = QH[:, 1024 * qh + 512 * j:1024 * qh + 512 * (j + 1)]
                        em.c("pe", lambda t, o=PB[:, 512 * j:512 * (j + 1)], l=lhsT, r=rhs:
                             t.matmul(o, l, r, start=True, stop=True),
                             reads=[KH_res, QH_res], writes=[PB_res], tag="scores")
                    em.c("act", lambda s, o=PT[:, :], i=PB[:, :], b=BCOL[:, hd:hd + 1]:
                         s.activation(o, i, AF.Exp, bias=b, scale=0.125),
                         reads=[PB_res, BCOL_res], writes=[PT_res], tag="exp")
                    if kt > 0:
                        emit_pv(kt - 1)
                emit_pv(63)
                em.c("dve", lambda v: v.reciprocal(RCP[0:1, :], PA[64:65, :]),
                     reads=[PA_res], writes=[RCP_res], tag="recip")
                for j in range(2):
                    PBX, PBX_res = (PS1, PS1_res) if j == 0 else (PS2, PS2_res)
                    pbc = PBX[0:64, 0:512]
                    em.c("pe", lambda t, r=RCP[0:1, 512 * j:512 * (j + 1)], o=pbc:
                         t.matmul(o, h.ones1x64f[0:1, :], r, start=True, stop=True),
                         reads=[RCP_res, h.ones1x64f_res], writes=[PBX_res], tag="rbmm")
                    em.c("dve", lambda v, o=RB[:, 512 * j:512 * (j + 1)], i=pbc: v.tensor_copy(o, i),
                         reads=[PBX_res], writes=[RB_res], tag="rbev")
                attd = ATT[64 * (hd % 2):64 * (hd % 2) + 64,
                           2048 * (hd // 2) + 1024 * qh:2048 * (hd // 2) + 1024 * (qh + 1)]
                em.c("dve", lambda v, o=attd: v.tensor_tensor(o, PA[0:64, :], RB[:, :], ALU.mult),
                     reads=[PA_res, RB_res], writes=[ATT_res], tag="attnorm")
            if hd + 2 < nheads:
                issue_head_loads(hd + 2)

        T1, T1_res = alloc_sb(ph34, nc, "T1", [128, 1024], F32)
        XO0, XO0_res = alloc_sb(ph34, nc, "XO0", [128, 1024], F32)
        XO1, XO1_res = alloc_sb(ph34, nc, "XO1", [128, 1024], F32)
        XOs = [(XO0, XO0_res), (XO1, XO1_res)]
        for tt in range(NT if 4 in A_PHASES else 0):
            ps, ps_res = (PS1, PS1_res) if tt % 2 == 0 else (PS2, PS2_res)
            for f in range(NF):
                lhsT = ATT[:, 2048 * f + 128 * tt:2048 * f + 128 * (tt + 1)]
                for j in range(2):
                    rhs = WOUT[:, 1024 * f + 512 * j:1024 * f + 512 * (j + 1)]
                    em.c("pe", lambda t, o=ps[:, 512 * j:512 * (j + 1)], l=lhsT, r=rhs,
                         st=(f == 0): t.matmul(o, l, r, start=st, stop=False),
                         reads=[ATT_res, WOUT_res], writes=[ps_res], tag="oproj")
            for j in range(2):
                em.c("pe", lambda t, o=ps[:, 512 * j:512 * (j + 1)],
                     r=BOUT[0:1, 512 * j:512 * (j + 1)]: t.matmul(
                     o, h.ones_row[0:1, 0:128], r, start=False, stop=True),
                     reads=[BOUT_res, h.ones_row_res], writes=[ps_res], tag="obias")
            xh = XHAT[:, 1024 * tt:1024 * (tt + 1)]
            em.c("dve", lambda v, i=ps[:, :], x=xh: v.tensor_tensor(T1[:, :], i, x, ALU.add),
                 reads=[ps_res, XHAT_res], writes=[T1_res], tag="resid")
            XO, XO_res = XOs[tt % 2]
            emit_ln(em, h, T1[:, :], T1_res, [(XO[:, :], XO_res)], ln_scratch, tag="lnA")
            em.d("so", h.xa[128 * tt:128 * (tt + 1), :], XO[:, :],
                 reads=[XO_res], writes=[h.xa_res], tag="xastore")
        if 4 not in A_PHASES:
            for tt in range(NT):
                em.d("px", h.xa[128 * tt:128 * (tt + 1), :],
                     XHAT[:, 1024 * tt:1024 * (tt + 1)],
                     reads=[XHAT_res], writes=[h.xa_res], tag="xadump")


def build_A(core):
    import contextlib
    nc = bass.Bass("TRN2", target_bir_lowering=False)
    h = H()
    h.x_own = nc.dram_tensor("x_own", [TOK, D], F32, kind="ExternalInput")
    h.x_batch = nc.dram_tensor("x_batch", [SEQ, D], F32, kind="ExternalInput")
    h.qkv_wT = nc.dram_tensor("qkv_wT", [D, 3 * D], F32, kind="ExternalInput")
    h.qkv_b = nc.dram_tensor("qkv_b", [1, 3 * D], F32, kind="ExternalInput")
    h.out_wT = nc.dram_tensor("out_wT", [D, D], F32, kind="ExternalInput")
    h.out_b = nc.dram_tensor("out_b", [1, D], F32, kind="ExternalInput")
    h.xa = nc.dram_tensor("xa", [TOK, D], F32, kind="ExternalOutput")
    h.xa_res = Res("xa")
    h.K_dram = nc.dram_tensor("K_dram", [D, SEQ], BF16)
    h.V_dram = nc.dram_tensor("V_dram", [D, SEQ], BF16)
    h.Q_dram = nc.dram_tensor("Q_dram", [D, TOK], BF16)
    h.K_dram_res = Res("K_dram")
    h.V_dram_res = Res("V_dram")
    h.Q_dram_res = Res("Q_dram")

    chans = {"pw": "pool", "px": "pool", "sk": "sp", "sq": "sp", "so": "sp"}
    em = Em(nc, chans)
    with contextlib.ExitStack() as outer:
        declare_consts(nc, outer, h)
        load_consts(em, h)
        emit_blockA(em, h, None, nc, outer)
        em.barrier()
        with nc.Block() as block:
            em.finalize_and_emit(block, None)
    return nc


def prep_A_inputs(inputs, core):
    b = core // 4
    q0 = 2048 * (core % 4)
    x = np.asarray(inputs["L0_attn_x"])
    cc = host_consts()
    m = {
        "x_own": np.ascontiguousarray(x[b, q0:q0 + 2048, :]),
        "x_batch": np.ascontiguousarray(x[b]),
        "qkv_wT": np.ascontiguousarray(np.asarray(inputs["L0_attn_qkv_w"]).T),
        "qkv_b": np.asarray(inputs["L0_attn_qkv_b"]).reshape(1, -1),
        "out_wT": np.ascontiguousarray(np.asarray(inputs["L0_attn_out_w"]).T),
        "out_b": np.asarray(inputs["L0_attn_out_b"]).reshape(1, -1),
    }
    for k, v in cc.items():
        m["c_" + k] = v
    return m



NE = 8
NI = 22
NIS = 11
B_SLAB = 512
NSL = TOK // B_SLAB
NTT = B_SLAB // 128


def emit_blockB(em, h, nc, outer_ctx):
    import contextlib
    XTOK, XTOK_res = alloc_sb(outer_ctx, nc, "XTOK", [128, NT * 1024], BF16)
    XT, XT_res = alloc_sb(outer_ctx, nc, "XT_B", [128, NF * 2048], BF16)
    SQ, SQ_res = alloc_sb(outer_ctx, nc, "SQln", [128, 1024], F32)
    COLS, COLS_res = alloc_sb(outer_ctx, nc, "COLS", [128, 16], F32)
    ln_scratch = {"sq": (SQ, SQ_res), "cols": (COLS, COLS_res)}

    PS0, PS0_res = alloc_ps(outer_ctx, nc, "PS0", [128, 1024], F32)
    PS1, PS1_res = alloc_ps(outer_ctx, nc, "PS1", [128, 1024], F32)
    PS2, PS2_res = alloc_ps(outer_ctx, nc, "PS2", [128, 1024], F32)
    PSt, _pstres = alloc_ps(outer_ctx, nc, "PSt", [128, 2048], BF16)
    PSt_res = [_pstres, Res("PSt1")]

    ACC, ACC_res = alloc_sb(outer_ctx, nc, "ACC", [128, NTT * 1024], F32)
    AES, AES_res = alloc_sb(outer_ctx, nc, "AES", [128, NI * B_SLAB], BF16)
    WGB0, WGB0_res = alloc_sb(outer_ctx, nc, "WGB0", [128, 2048], BF16)
    WGB1, WGB1_res = alloc_sb(outer_ctx, nc, "WGB1", [128, 2048], BF16)
    WGB2, WGB2_res = alloc_sb(outer_ctx, nc, "WGB2", [128, 2048], BF16)
    WGB3, WGB3_res = alloc_sb(outer_ctx, nc, "WGB3", [128, 2048], BF16)
    WDE, WDE_res = alloc_sb(outer_ctx, nc, "WDE", [128, NI * 1024], BF16)
    SDB, SDB_res = alloc_sb(outer_ctx, nc, "SDB", [128, NIS * 1024], BF16)
    GWB, GWB_res = alloc_sb(outer_ctx, nc, "GWB", [128, 64], BF16)
    LG, LG_res = alloc_sb(outer_ctx, nc, "LG", [128, 8], F32)
    GCOL, GCOL_res = alloc_sb(outer_ctx, nc, "GCOL", [128, 8], F32)
    CMB, CMB_res = alloc_sb(outer_ctx, nc, "CMB", [128, NT * 8], F32)
    TMP, TMP_res = alloc_sb(outer_ctx, nc, "TMPB", [128, 512], BF16)
    TMPb, TMPb_res = alloc_sb(outer_ctx, nc, "TMPBb", [128, 512], BF16)
    TMPs = [(TMP, TMP_res), (TMPb, TMPb_res)]
    T1, T1_res = alloc_sb(outer_ctx, nc, "T1", [128, 1024], F32)
    SAS, SAS_res = AES, AES_res
    XO0, XO0_res = alloc_sb(outer_ctx, nc, "XO0", [128, 1024], F32)
    XO1, XO1_res = alloc_sb(outer_ctx, nc, "XO1", [128, 1024], F32)
    XOs = [(XO0, XO0_res), (XO1, XO1_res)]
    WGBs = [(WGB0, WGB0_res), (WGB1, WGB1_res), (WGB2, WGB2_res), (WGB3, WGB3_res)]
    WCH = ["p0", "p1", "p2", "p3"]

    for t in range(NT):
        em.d(["px", "p0"], XTOK[:, 1024 * t:1024 * (t + 1)],
             h.xb_in[128 * t:128 * (t + 1), :], reads=[h.xb_in_res],
             writes=[XTOK_res], tag="xload")
    for t in range(NT):
        for f in range(NF):
            src = XTOK[:, 1024 * t + 128 * f:1024 * t + 128 * (f + 1)]
            dst = XT[:, 2048 * f + 128 * t:2048 * f + 128 * (t + 1)]
            emit_transpose(em, h, src, dst, PSt, PSt_res, XTOK_res, XT_res)
    for kk in range(NF):
        em.d("p1", GWB[:, 8 * kk:8 * (kk + 1)], h.gate_w[128 * kk:128 * (kk + 1), :],
             writes=[GWB_res], tag="gwld")
    for i in range(NIS):
        em.d("p2", SDB[:, 1024 * i:1024 * (i + 1)], h.Sd[128 * i:128 * (i + 1), :],
             writes=[SDB_res])

    for tt in range(NT):
        pg = PS0[:, 0:8]
        for kk in range(NF):
            lhsT = XT[:, 2048 * kk + 128 * tt:2048 * kk + 128 * (tt + 1)]
            gr = GWB[:, 8 * kk:8 * (kk + 1)]
            em.c("pe", lambda t_, l=lhsT, r=gr, st=(kk == 0), sp=(kk == NF - 1):
                 t_.matmul(pg, l, r, start=st, stop=sp),
                 reads=[XT_res, GWB_res], writes=[PS0_res], tag="gate")
        em.c("dve", lambda v: v.tensor_copy(LG[:, :], pg),
             reads=[PS0_res], writes=[LG_res], tag="gateev")
        c0 = COLS[:, 8:9]; c1 = COLS[:, 9:10]; c2 = COLS[:, 10:11]
        c3 = COLS[:, 11:12]; c4 = COLS[:, 12:13]
        em.c("dve", lambda v: v.reduce_max(c0, LG[:, :], axis=AX.X),
             reads=[LG_res], writes=[COLS_res], tag="g_v1")
        em.c("dve", lambda v: v.tensor_scalar(GCOL[:, :], LG[:, :], c0, None, ALU.is_equal),
             reads=[LG_res, COLS_res], writes=[GCOL_res], tag="g_m1")
        em.c("dve", lambda v: v.scalar_tensor_tensor(LG[:, :], GCOL[:, :], -1e30, LG[:, :],
                                                     ALU.mult, ALU.add),
             reads=[GCOL_res, LG_res], writes=[LG_res], tag="g_l2")
        em.c("dve", lambda v: v.reduce_max(c1, LG[:, :], axis=AX.X),
             reads=[LG_res], writes=[COLS_res], tag="g_v2")
        em.c("dve", lambda v: v.tensor_tensor(c2, c1, c0, ALU.subtract),
             reads=[COLS_res], writes=[COLS_res], tag="g_d21")
        em.c("act", lambda s: s.activation(c2, c2, AF.Exp),
             reads=[COLS_res], writes=[COLS_res], tag="g_exp")
        em.c("dve", lambda v: v.tensor_scalar(c2, c2, 1.0, None, ALU.add),
             reads=[COLS_res], writes=[COLS_res], tag="g_1p")
        em.c("dve", lambda v: v.reciprocal(c3, c2),
             reads=[COLS_res], writes=[COLS_res], tag="g_p1")
        em.c("dve", lambda v: v.tensor_scalar(c4, c3, -1.0, 1.0, ALU.mult, ALU.add),
             reads=[COLS_res], writes=[COLS_res], tag="g_p2")
        em.c("dve", lambda v: v.tensor_scalar(LG[:, :], LG[:, :], c1, None, ALU.is_equal),
             reads=[LG_res, COLS_res], writes=[LG_res], tag="g_m2")
        cmb = CMB[:, 8 * tt:8 * (tt + 1)]
        em.c("dve", lambda v, o=cmb: v.tensor_scalar(o, GCOL[:, :], c3, None, ALU.mult),
             reads=[GCOL_res, COLS_res], writes=[CMB_res], tag="g_c1")
        em.c("dve", lambda v, o=cmb: v.scalar_tensor_tensor(o, LG[:, :], c4, o,
                                                            ALU.mult, ALU.add),
             reads=[LG_res, COLS_res, CMB_res], writes=[CMB_res], tag="g_c2")

    first_wde_load = [True]

    def load_wde(e):
        for i in range(NI):
            em.d(WCH[i % 4], WDE[:, 1024 * i:1024 * (i + 1)],
                 h.Wd[e, 128 * i:128 * (i + 1), :], writes=[WDE_res], tag="wdld")

    for s in range(NSL):
        for i in range(NIS):
            WGB, WGB_res_ = WGBs[i % 4]
            em.d(WCH[i % 4], WGB[:, :], h.SGU_pre[i],
                 writes=[WGB_res_], tag="sgld")
            PSp, PSp_res = (PS0, PS0_res) if i % 2 == 0 else (PS1, PS1_res)
            for which in (0, 1):
                ps = PSp[:, 512 * which:512 * (which + 1)]
                for kk in range(NF):
                    lhsT = WGB[:, 1024 * which + 128 * kk:1024 * which + 128 * (kk + 1)]
                    rhs = XT[:, 2048 * kk + B_SLAB * s:2048 * kk + B_SLAB * (s + 1)]
                    em.c("pe", lambda t_, o=ps, l=lhsT, r=rhs, st=(kk == 0),
                         sp=(kk == NF - 1): t_.matmul(o, l, r, start=st, stop=sp),
                         reads=[WGB_res_, XT_res], writes=[PSp_res], tag="sgu")
            em.c("act", lambda s_, i_=PSp[:, 0:512], tp=TMPs[i % 2]: s_.activation(
                tp[0][:, :], i_, AF.Silu), reads=[PSp_res], writes=[TMPs[i % 2][1]],
                tag="ssilu")
            dst = SAS[:, B_SLAB * i:B_SLAB * (i + 1)]
            em.c("dve", lambda v, o=dst, tp=TMPs[i % 2], i_=PSp[:, 512:1024]:
                 v.tensor_tensor(o, tp[0][:, :], i_, ALU.mult),
                 reads=[TMPs[i % 2][1], PSp_res], writes=[SAS_res], tag="smul")
        for tt in range(NTT):
            ps, ps_res = (PS2, PS2_res) if tt % 2 == 0 else (PS0, PS0_res)
            for i in range(NIS):
                lhsT = SAS[:, B_SLAB * i + 128 * tt:B_SLAB * i + 128 * (tt + 1)]
                for j in range(2):
                    rhs = SDB[:, 1024 * i + 512 * j:1024 * i + 512 * (j + 1)]
                    em.c("pe", lambda t_, o=ps[:, 512 * j:512 * (j + 1)], l=lhsT, r=rhs,
                         st=(i == 0), sp=(i == NIS - 1): t_.matmul(o, l, r, start=st, stop=sp),
                         reads=[SAS_res, SDB_res], writes=[ps_res], tag="sd")
            acc = ACC[:, 1024 * tt:1024 * (tt + 1)]
            em.c("act", lambda s_, o=acc, i_=ps[:, :]: s_.copy(o, i_),
                 reads=[ps_res], writes=[ACC_res], tag="accinit")

        for e in range(NE):
            if first_wde_load[0]:
                load_wde(e)
                first_wde_load[0] = False
            for i in range(NI):
                WGB, WGB_res_ = WGBs[i % 4]
                em.d(WCH[i % 4], WGB[:, :], h.WGU_pre[e, i],
                     writes=[WGB_res_], tag="wgld")
                PSp, PSp_res = (PS0, PS0_res) if i % 2 == 0 else (PS1, PS1_res)
                for which in (0, 1):
                    ps = PSp[:, 512 * which:512 * (which + 1)]
                    for kk in range(NF):
                        lhsT = WGB[:, 1024 * which + 128 * kk:1024 * which + 128 * (kk + 1)]
                        rhs = XT[:, 2048 * kk + B_SLAB * s:2048 * kk + B_SLAB * (s + 1)]
                        em.c("pe", lambda t_, o=ps, l=lhsT, r=rhs, st=(kk == 0),
                             sp=(kk == NF - 1): t_.matmul(o, l, r, start=st, stop=sp),
                             reads=[WGB_res_, XT_res], writes=[PSp_res], tag="gu")
                em.c("act", lambda s_, i_=PSp[:, 0:512], tp=TMPs[i % 2]: s_.activation(
                    tp[0][:, :], i_, AF.Silu), reads=[PSp_res], writes=[TMPs[i % 2][1]],
                    tag="esilu")
                dst = AES[:, B_SLAB * i:B_SLAB * (i + 1)]
                em.c("dve", lambda v, o=dst, tp=TMPs[i % 2], i_=PSp[:, 512:1024]:
                     v.tensor_tensor(o, tp[0][:, :], i_, ALU.mult),
                     reads=[TMPs[i % 2][1], PSp_res], writes=[AES_res], tag="emul")
            for tt in range(NTT):
                ps, ps_res = (PS2, PS2_res) if tt % 2 == 0 else (PS1, PS1_res)
                for i in range(NI):
                    lhsT = AES[:, B_SLAB * i + 128 * tt:B_SLAB * i + 128 * (tt + 1)]
                    for j in range(2):
                        rhs = WDE[:, 1024 * i + 512 * j:1024 * i + 512 * (j + 1)]
                        em.c("pe", lambda t_, o=ps[:, 512 * j:512 * (j + 1)], l=lhsT, r=rhs,
                             st=(i == 0), sp=(i == NI - 1): t_.matmul(o, l, r, start=st,
                                                                      stop=sp),
                             reads=[AES_res, WDE_res], writes=[ps_res], tag="d")
                gtt = NTT * s + tt
                ccol = CMB[:, 8 * gtt + e:8 * gtt + e + 1]
                acc = ACC[:, 1024 * tt:1024 * (tt + 1)]
                em.c("dve", lambda v, o=acc, i_=ps[:, :], sc=ccol: v.scalar_tensor_tensor(
                    o, i_, sc, o, ALU.mult, ALU.add),
                    reads=[ps_res, CMB_res, ACC_res], writes=[ACC_res], tag="dmerge")
            if e + 1 < NE:
                load_wde(e + 1)
        first_wde_load[0] = True

        for tt in range(NTT):
            gtt = NTT * s + tt
            xtk = XTOK[:, 1024 * gtt:1024 * (gtt + 1)]
            acc = ACC[:, 1024 * tt:1024 * (tt + 1)]
            em.c("dve", lambda v, x=xtk, a=acc: v.tensor_tensor(T1[:, :], a, x, ALU.add),
                 reads=[ACC_res, XTOK_res], writes=[T1_res], tag="bresid")
            XO, XO_res = XOs[tt % 2]
            emit_ln(em, h, T1[:, :], T1_res, [(XO[:, :], XO_res)], ln_scratch, tag="lnB")
            em.d("so", h.xb[128 * gtt:128 * (gtt + 1), :], XO[:, :],
                 reads=[XO_res], writes=[h.xb_res], tag="xbstore")


def build_B():
    import contextlib
    nc = bass.Bass("TRN2", target_bir_lowering=False)
    h = H()
    h.xb_in = nc.dram_tensor("xb_in", [TOK, D], F32, kind="ExternalInput")
    h.gate_w = nc.dram_tensor("gate_w", [D, NE], F32, kind="ExternalInput")
    h.WGU_pre = nc.dram_tensor("WGU_pre", [NE, NI, 128, 2048], F32, kind="ExternalInput")
    h.Wd = nc.dram_tensor("Wd", [NE, 2816, 1024], F32, kind="ExternalInput")
    h.SGU_pre = nc.dram_tensor("SGU_pre", [NIS, 128, 2048], F32, kind="ExternalInput")
    h.Sd = nc.dram_tensor("Sd", [1408, 1024], F32, kind="ExternalInput")
    h.xb = nc.dram_tensor("xb", [TOK, D], F32, kind="ExternalOutput")
    h.xb_res = Res("xb")
    h.xb_in_res = Res("xb_in")

    chans = {"p0": "pool", "p1": "pool", "p2": "pool", "p3": "pool", "px": "pool",
             "sk": "sp", "so": "sp"}
    em = Em(nc, chans)
    with contextlib.ExitStack() as outer:
        declare_consts(nc, outer, h)
        load_consts(em, h)
        emit_blockB(em, h, nc, outer)
        em.barrier()
        with nc.Block() as block:
            em.finalize_and_emit(block, None)
    return nc


def prep_B_inputs(inputs, core, xa_full):
    b = core // 4
    q0 = 2048 * (core % 4)
    cc = host_consts()
    Wg = np.asarray(inputs["L0_ffn_Wg"])
    Wu = np.asarray(inputs["L0_ffn_Wu"])
    Sg = np.asarray(inputs["L0_ffn_Sg"])
    Su = np.asarray(inputs["L0_ffn_Su"])
    pre = lambda W: np.ascontiguousarray(
        W.reshape(NE, 8, 128, NI, 128).transpose(0, 3, 2, 1, 4).reshape(NE, NI, 128, 1024))
    pres = lambda W: np.ascontiguousarray(
        W.reshape(8, 128, NIS, 128).transpose(2, 1, 0, 3).reshape(NIS, 128, 1024))
    m = {
        "xb_in": np.ascontiguousarray(xa_full[b, q0:q0 + 2048, :]),
        "gate_w": np.asarray(inputs["L0_ffn_gate_w"]),
        "WGU_pre": np.ascontiguousarray(np.concatenate([pre(Wg), pre(Wu)], axis=3)),
        "Wd": np.asarray(inputs["L0_ffn_Wd"]),
        "SGU_pre": np.ascontiguousarray(np.concatenate([pres(Sg), pres(Su)], axis=2)),
        "Sd": np.asarray(inputs["L0_ffn_Sd"]),
    }
    for k, v in cc.items():
        m["c_" + k] = v
    return m



NH = 16
HD = 64
NPRE = 48
C_SLAB = 512
NIC = 22
GAMMAS = [1.0 - 2.0 ** (-5.0 - hh) for hh in range(NH)]


def emit_ln_small(em, h, src_ap, src_res, n, outs, scratch, tag="gn"):
    sq, sq_res = scratch["sq"]
    cols, cols_res = scratch["cols"]
    ssum = cols[:, 0:1]; ssq = cols[:, 1:2]; mean = cols[:, 2:3]
    varr = cols[:, 3:4]; rstd = cols[:, 4:5]; negmr = cols[:, 5:6]; tmp = cols[:, 6:7]
    em.c("dve", lambda v: v.reduce_sum(ssum, src_ap, axis=AX.X),
         reads=[src_res], writes=[cols_res], tag=tag)
    em.c("act", lambda s: s.activation(sq[:, 0:n], src_ap, AF.Square, accum_out=ssq),
         reads=[src_res], writes=[sq_res, cols_res], tag=tag)
    em.c("dve", lambda v: v.tensor_scalar(mean, ssum, 1.0 / n, None, ALU.mult),
         reads=[cols_res], writes=[cols_res], tag=tag)
    em.c("dve", lambda v: v.tensor_scalar(varr, ssq, 1.0 / n, EPS, ALU.mult, ALU.add),
         reads=[cols_res], writes=[cols_res], tag=tag)
    em.c("dve", lambda v: v.tensor_tensor(tmp, mean, mean, ALU.mult),
         reads=[cols_res], writes=[cols_res], tag=tag)
    em.c("dve", lambda v: v.tensor_tensor(varr, varr, tmp, ALU.subtract),
         reads=[cols_res], writes=[cols_res], tag=tag)
    em.c("act", lambda s: s.activation(rstd, varr, AF.Ln),
         reads=[cols_res], writes=[cols_res], tag=tag)
    em.c("act", lambda s: s.activation(rstd, rstd, AF.Exp, scale=-0.5),
         reads=[cols_res], writes=[cols_res], tag=tag)
    em.c("dve", lambda v: v.scalar_tensor_tensor(negmr, mean, -1.0, rstd, ALU.mult, ALU.mult),
         reads=[cols_res], writes=[cols_res], tag=tag)
    for dst_ap, dst_res in outs:
        em.c("dve", lambda v, o=dst_ap: v.tensor_scalar(o, src_ap, rstd, negmr,
                                                        ALU.mult, ALU.add),
             reads=[src_res, cols_res], writes=[dst_res], tag=tag + "_app")


def emit_blockC(em, h, nc, outer_ctx):
    import contextlib
    XTOK, XTOK_res = alloc_sb(outer_ctx, nc, "XTOKC", [128, NT * 1024], BF16)
    XNA, XNA_res = alloc_sb(outer_ctx, nc, "XNA", [128, NT * 1024], BF16)
    SQ, SQ_res = alloc_sb(outer_ctx, nc, "SQln", [128, 1024], F32)
    COLS, COLS_res = alloc_sb(outer_ctx, nc, "COLS", [128, 16], F32)
    ln_scratch = {"sq": (SQ, SQ_res), "cols": (COLS, COLS_res)}
    T1, T1_res = alloc_sb(outer_ctx, nc, "T1", [128, 1024], F32)
    XO0, XO0_res = alloc_sb(outer_ctx, nc, "XO0", [128, 1024], F32)
    XOs = [(XO0, XO0_res), (XO0, XO0_res)]
    BCT, BCT_res = alloc_sb(outer_ctx, nc, "BCT", [128, 4 * 1024], BF16)
    GQ, GQ_res = alloc_sb(outer_ctx, nc, "GQ", [128, NH], F32)
    GKN, GKN_res = alloc_sb(outer_ctx, nc, "GKN", [128, NH], F32)
    GKS, GKS_res = alloc_sb(outer_ctx, nc, "GKS", [128, NH], F32)
    GKSP, GKSP_res = alloc_sb(outer_ctx, nc, "GKSP", [128, NPRE * NH], F32)
    GDEC, GDEC_res = alloc_sb(outer_ctx, nc, "GDEC", [64, NPRE * NH], F32)
    MASKT, MASKT_res = alloc_sb(outer_ctx, nc, "MASKT", [128, 128], BF16)
    SST, SST_res = alloc_sb(outer_ctx, nc, "SST", [64, NH * HD], F32)
    SSTB, SSTB_res = alloc_sb(outer_ctx, nc, "SSTB", [64, NH * HD], BF16)
    BR, BR_res = alloc_sb(outer_ctx, nc, "BR", [1, 5 * 1024], BF16)

    PS0, PS0_res = alloc_ps(outer_ctx, nc, "PS0", [128, 1024], F32)
    PS1, PS1_res = alloc_ps(outer_ctx, nc, "PS1", [128, 1024], F32)
    PSC, PSC_res = alloc_ps(outer_ctx, nc, "PSC", [128, 512], F32)
    PSO, PSO_res = alloc_ps(outer_ctx, nc, "PSO", [128, 512], F32)
    PSS, PSS_res = PSC, PSC_res
    PSt, _pstres = alloc_ps(outer_ctx, nc, "PSt", [128, 2048], BF16)
    PSt_res = [_pstres, Res("PSt1")]

    for t, d, r in [(GQ, h.GQ_d, GQ_res), (GKN, h.GKN_d, GKN_res), (GKS, h.GKS_d, GKS_res),
                    (GKSP, h.GKSP_d, GKSP_res), (GDEC, h.GDEC_d, GDEC_res),
                    (MASKT, h.MASKT_d, MASKT_res)]:
        em.d("sk", t[:, :], d[:, :], writes=[r], tag="gld")
    off = 0
    bias_off = {}
    for nm, dd, ln in [("bq", h.bq, 1024), ("bk", h.bk, 1024), ("bv", h.bv, 1024),
                       ("bg", h.bg, 1024), ("bo", h.bo, 1024)]:
        em.d("p1", BR[0:1, off:off + ln], dd[:, :], writes=[BR_res], tag="bld")
        bias_off[nm] = off
        off += ln
    NRMR, NRMR_res = alloc_sb(outer_ctx, nc, "NRMR", [1, 1024], F32)
    for j, dd in enumerate([h.norm1_w, h.norm1_b, h.gn_w, h.gn_b]):
        em.d("sk", NRMR[0:1, :], dd[:, :], reads=[NRMR_res], writes=[NRMR_res])
        for jj in range(2):
            pb = PS0[:, 0:512]
            em.c("pe", lambda t, r=NRMR[0:1, 512 * jj:512 * (jj + 1)]:
                 t.matmul(pb, h.ones1x128f[0:1, :], r, start=True, stop=True),
                 reads=[NRMR_res, h.ones1x128f_res], writes=[PS0_res], tag="bcmm")
            dst = BCT[:, 1024 * j + 512 * jj:1024 * j + 512 * (jj + 1)]
            em.c("dve", lambda v, o=dst: v.tensor_copy(o, pb),
                 reads=[PS0_res], writes=[BCT_res], tag="bcev")
    W1B = BCT[:, 0:1024]; B1B = BCT[:, 1024:2048]
    GNW = BCT[:, 2048:3072]; GNB = BCT[:, 3072:4096]

    em.c("dve", lambda v: v.memset(SST[:, :], 0.0), writes=[SST_res])
    em.c("dve", lambda v: v.memset(SSTB[:, :], 0.0), writes=[SSTB_res])

    for t in range(NT):
        em.d(["px", "p0"], XTOK[:, 1024 * t:1024 * (t + 1)],
             h.xc_in[128 * t:128 * (t + 1), :], reads=[h.xc_in_res],
             writes=[XTOK_res], tag="xload")

    for tt in range(NT):
        src = XTOK[:, 1024 * tt:1024 * (tt + 1)]
        emit_ln(em, h, src, XTOK_res, [(T1[:, :], T1_res)], ln_scratch, tag="ln1")
        em.c("dve", lambda v: v.tensor_tensor(T1[:, :], T1[:, :], W1B, ALU.mult),
             reads=[T1_res, BCT_res], writes=[T1_res], tag="xnw")
        dst = XNA[:, 1024 * tt:1024 * (tt + 1)]
        em.c("dve", lambda v, o=dst: v.tensor_tensor(o, T1[:, :], B1B, ALU.add),
             reads=[T1_res, BCT_res], writes=[XNA_res], tag="xnb")

    with contextlib.ExitStack() as ph2:
        WQB, WQB_res = alloc_sb(ph2, nc, "WQB", [128, 8 * 1024], BF16)
        WKB, WKB_res = alloc_sb(ph2, nc, "WKB", [128, 8 * 1024], BF16)
        WVB, WVB_res = alloc_sb(ph2, nc, "WVB", [128, 8 * 1024], BF16)
        WGB_, WGB__res = alloc_sb(ph2, nc, "WGB_", [128, 8 * 1024], BF16)
        WOB, WOB_res = alloc_sb(ph2, nc, "WOB", [128, 8 * 1024], BF16)
        XNC, XNC_res = alloc_sb(ph2, nc, "XNC", [128, 8 * 128], BF16)
        QTC, QTC_res = alloc_sb(ph2, nc, "QTC", [128, 1024], BF16)
        KTC, KTC_res = alloc_sb(ph2, nc, "KTC", [128, 1024], BF16)
        VTC, VTC_res = alloc_sb(ph2, nc, "VTC", [128, 1024], BF16)
        GTC, GTC_res = alloc_sb(ph2, nc, "GTC", [128, 1024], BF16)
        QS, QS_res = alloc_sb(ph2, nc, "QS", [128, 64], BF16)
        KS1, KS1_res = alloc_sb(ph2, nc, "KS1", [128, 64], BF16)
        KS2, KS2_res = alloc_sb(ph2, nc, "KS2", [128, 64], BF16)
        QSF, QSF_res = alloc_sb(ph2, nc, "QSF", [64, 128], BF16)
        KSF, KSF_res = alloc_sb(ph2, nc, "KSF", [64, 128], BF16)
        AIN, AIN_res = alloc_sb(ph2, nc, "AIN", [128, 128], BF16)
        TGN, TGN_res = alloc_sb(ph2, nc, "TGN", [128, 64], F32)
        QSb, QSb_res = alloc_sb(ph2, nc, "QSb", [128, 64], BF16)
        KS1b, KS1b_res = alloc_sb(ph2, nc, "KS1b", [128, 64], BF16)
        QSFb, QSFb_res = alloc_sb(ph2, nc, "QSFb", [64, 128], BF16)
        KSFb, KSFb_res = alloc_sb(ph2, nc, "KSFb", [64, 128], BF16)
        AINb, AINb_res = alloc_sb(ph2, nc, "AINb", [128, 128], BF16)
        TGNb, TGNb_res = alloc_sb(ph2, nc, "TGNb", [128, 64], F32)
        SQg, SQg_res = alloc_sb(ph2, nc, "SQg", [128, 64], F32)
        COLSg, COLSg_res = alloc_sb(ph2, nc, "COLSg", [128, 16], F32)
        gn_scr0 = {"sq": (SQg, SQg_res), "cols": (COLSg, COLSg_res)}
        COLSg2, COLSg2_res = alloc_sb(ph2, nc, "COLSg2", [128, 16], F32)
        gn_scr1 = {"sq": (SQg, SQg_res), "cols": (COLSg2, COLSg2_res)}
        RGC, RGC_res = alloc_sb(ph2, nc, "RGC", [128, 1024], BF16)
        RGT, RGT_res = XNC, XNC_res
        XPH, XPH_res = QTC, QTC_res

        for kk in range(NF):
            em.d("p1", WQB[:, 1024 * kk:1024 * (kk + 1)], h.wq[128 * kk:128 * (kk + 1), :],
                 writes=[WQB_res])
            em.d("p2", WKB[:, 1024 * kk:1024 * (kk + 1)], h.wk[128 * kk:128 * (kk + 1), :],
                 writes=[WKB_res])
            em.d("p3", WVB[:, 1024 * kk:1024 * (kk + 1)], h.wv[128 * kk:128 * (kk + 1), :],
                 writes=[WVB_res])
            em.d("p1", WGB_[:, 1024 * kk:1024 * (kk + 1)], h.wg[128 * kk:128 * (kk + 1), :],
                 writes=[WGB__res])
            em.d("p2", WOB[:, 1024 * kk:1024 * (kk + 1)], h.wo[128 * kk:128 * (kk + 1), :],
                 writes=[WOB_res])

        def proj(dst, dst_res, wsb, wsb_res, bias_name, ps, ps_res, act=None):
            for j in range(2):
                for kk in range(NF):
                    lhsT = XNC[:, 128 * kk:128 * (kk + 1)]
                    rhs = wsb[:, 1024 * kk + 512 * j:1024 * kk + 512 * (j + 1)]
                    em.c("pe", lambda t, o=ps[:, 512 * j:512 * (j + 1)], l=lhsT, r=rhs,
                         st=(kk == 0): t.matmul(o, l, r, start=st, stop=False),
                         reads=[XNC_res, wsb_res], writes=[ps_res], tag="cproj")
                boff = bias_off[bias_name]
                em.c("pe", lambda t, o=ps[:, 512 * j:512 * (j + 1)],
                     r=BR[0:1, boff + 512 * j:boff + 512 * (j + 1)]:
                     t.matmul(o, h.ones_row[0:1, 0:128], r, start=False, stop=True),
                     reads=[BR_res, h.ones_row_res], writes=[ps_res], tag="cbias")
            if act is None:
                em.c("act", lambda s, o=dst, i=ps[:, :]: s.copy(o, i),
                     reads=[ps_res], writes=[dst_res], tag="cevac")
            else:
                em.c("act", lambda s, o=dst, i=ps[:, :], a=act: s.activation(o, i, a),
                     reads=[ps_res], writes=[dst_res], tag="cevac")

        def state_update(hd_, scale_col, scale_res, dec, dec_res):
            em.c("act", lambda s, sc=scale_col: s.activation(
                KS2[:, :], KTC[:, 64 * hd_:64 * (hd_ + 1)], AF.Copy, scale=sc),
                reads=[KTC_res, scale_res], writes=[KS2_res], tag="ks2")
            pss = PSS[0:64, 128:192]
            em.c("pe", lambda t: t.matmul(pss, KS2[:, :], VTC[:, 64 * hd_:64 * (hd_ + 1)],
                                          start=True, stop=True),
                 reads=[KS2_res, VTC_res], writes=[PSS_res], tag="smm")
            scol = SST[:, 64 * hd_:64 * (hd_ + 1)]
            if dec_res is None:
                em.c("dve", lambda v, o=scol, d=dec: v.scalar_tensor_tensor(
                    o, o, d, pss, ALU.mult, ALU.add),
                    reads=[SST_res, PSS_res], writes=[SST_res], tag="sdec")
            else:
                em.c("dve", lambda v, o=scol, d=dec: v.scalar_tensor_tensor(
                    o, o, d, pss, ALU.mult, ALU.add),
                    reads=[SST_res, PSS_res, dec_res], writes=[SST_res], tag="sdec")
            sbcol = SSTB[:, 64 * hd_:64 * (hd_ + 1)]
            em.c("dve", lambda v, o=sbcol, i=scol: v.tensor_copy(o, i),
                 reads=[SST_res], writes=[SSTB_res], tag="sbf")

        for c in range(NPRE):
            em.d(["px", "p0"], XPH[:, :], h.xprefix[128 * c:128 * (c + 1), :],
                 reads=[h.xprefix_res], writes=[XPH_res], tag="xpload")
            emit_ln(em, h, XPH[:, :], XPH_res, [(T1[:, :], T1_res)], ln_scratch, tag="lnp")
            em.c("dve", lambda v: v.tensor_tensor(T1[:, :], T1[:, :], W1B, ALU.mult),
                 reads=[T1_res, BCT_res], writes=[T1_res], tag="xnpw")
            em.c("dve", lambda v: v.tensor_tensor(T1[:, :], T1[:, :], B1B, ALU.add),
                 reads=[T1_res, BCT_res], writes=[T1_res], tag="xnpb")
            em.c("dve", lambda v: v.tensor_copy(GTC[:, :], T1[:, :]),
                 reads=[T1_res], writes=[GTC_res], tag="xnpcast")
            for f in range(NF):
                src = GTC[:, 128 * f:128 * (f + 1)]
                dst = XNC[:, 128 * f:128 * (f + 1)]
                emit_transpose(em, h, src, dst, PSt, PSt_res, GTC_res, XNC_res)
            proj(KTC[:, :], KTC_res, WKB, WKB_res, "bk", PS0, PS0_res)
            proj(VTC[:, :], VTC_res, WVB, WVB_res, "bv", PS1, PS1_res)
            for hd_ in range(NH):
                state_update(hd_, GKSP[:, NH * c + hd_:NH * c + hd_ + 1], GKSP_res,
                             GDEC[:, NH * c + hd_:NH * c + hd_ + 1], GDEC_res)

        for c in range(NT):
            for f in range(NF):
                src = XNA[:, 1024 * c + 128 * f:1024 * c + 128 * (f + 1)]
                dst = XNC[:, 128 * f:128 * (f + 1)]
                emit_transpose(em, h, src, dst, PSt, PSt_res, XNA_res, XNC_res)
            proj(QTC[:, :], QTC_res, WQB, WQB_res, "bq", PS0, PS0_res)
            proj(KTC[:, :], KTC_res, WKB, WKB_res, "bk", PS1, PS1_res)
            proj(VTC[:, :], VTC_res, WVB, WVB_res, "bv", PS0, PS0_res)
            proj(GTC[:, :], GTC_res, WGB_, WGB__res, "bg", PS1, PS1_res, act=AF.Silu)
            if c == 0 and getattr(h, "dbg", None) is not None:
                for nm, t_, r_ in [("q", QTC, QTC_res), ("k", KTC, KTC_res),
                                   ("v", VTC, VTC_res), ("g", GTC, GTC_res)]:
                    em.d("so", h.dbg[nm][:, :], t_[:, :], reads=[r_],
                         writes=[h.dbg_res[nm]], tag="dbg")
            for hd_ in range(NH):
                qslice = QTC[:, 64 * hd_:64 * (hd_ + 1)]
                kslice = KTC[:, 64 * hd_:64 * (hd_ + 1)]
                vslice = VTC[:, 64 * hd_:64 * (hd_ + 1)]
                sbslice = SSTB[:, 64 * hd_:64 * (hd_ + 1)]
                em.c("act", lambda s, sc=GQ[:, hd_:hd_ + 1], i_=qslice: s.activation(
                    QS[:, :], i_, AF.Copy, scale=sc),
                    reads=[QTC_res, GQ_res], writes=[QS_res], tag="qsc")
                em.c("act", lambda s, sc=GKN[:, hd_:hd_ + 1], i_=kslice: s.activation(
                    KS1[:, :], i_, AF.Copy, scale=sc),
                    reads=[KTC_res, GKN_res], writes=[KS1_res], tag="ksc")
                emit_transpose(em, h, QS[:, :], QSF[:, :], PSt, PSt_res, QS_res, QSF_res)
                emit_transpose(em, h, KS1[:, :], KSF[:, :], PSt, PSt_res, KS1_res, KSF_res)
                psc = PSC[0:128, 0:128]
                em.c("pe", lambda t: t.matmul(psc, KSF[:, :], QSF[:, :], start=True,
                                              stop=True),
                     reads=[KSF_res, QSF_res], writes=[PSC_res], tag="sc_mm")
                em.c("dve", lambda v: v.tensor_tensor(AIN[:, :], psc, MASKT[:, :], ALU.mult),
                     reads=[PSC_res, MASKT_res], writes=[AIN_res], tag="mask")
                pso = PSO[0:128, 0:64]
                em.c("pe", lambda t, r=vslice: t.matmul(pso, AIN[:, :], r,
                                                        start=True, stop=False),
                     reads=[AIN_res, VTC_res], writes=[PSO_res], tag="o_intra")
                em.c("pe", lambda t, r=sbslice: t.matmul(pso, QSF[:, :], r,
                                                         start=False, stop=True),
                     reads=[QSF_res, SSTB_res], writes=[PSO_res], tag="o_inter")
                if c == 0 and hd_ == 0 and getattr(h, "dbg", None) is not None:
                    em.d("px", h.dbg["qs"][:, 0:64], QS[:, :], reads=[QS_res],
                         writes=[h.dbg_res["qs"]], tag="dbg")
                    em.d("px", h.dbg["ks"][:, 0:64], KS1[:, :], reads=[KS1_res],
                         writes=[h.dbg_res["ks"]], tag="dbg")
                    em.d("px", h.dbg["ain"][:, 0:128], AIN[:, :], reads=[AIN_res],
                         writes=[h.dbg_res["ain"]], tag="dbg")
                    em.c("dve", lambda v: v.tensor_copy(TGN[:, :], pso),
                         reads=[PSO_res], writes=[TGN_res], tag="dbgret")
                    em.d("so", h.dbg["ret"][:, 0:64], TGN[:, :], reads=[TGN_res],
                         writes=[h.dbg_res["ret"]], tag="dbg")
                state_update(hd_, GKS[:, hd_:hd_ + 1], GKS_res, GAMMAS[hd_] ** 128, None)
                emit_ln_small(em, h, pso, PSO_res, 64, [(TGN[:, :], TGN_res)], ln_scratch)
                em.c("dve", lambda v, w=GNW[:, 64 * hd_:64 * (hd_ + 1)]: v.tensor_tensor(
                    TGN[:, :], TGN[:, :], w, ALU.mult),
                    reads=[TGN_res, BCT_res], writes=[TGN_res], tag="gnw")
                em.c("dve", lambda v, b=GNB[:, 64 * hd_:64 * (hd_ + 1)]: v.tensor_tensor(
                    TGN[:, :], TGN[:, :], b, ALU.add),
                    reads=[TGN_res, BCT_res], writes=[TGN_res], tag="gnb")
                dst = RGC[:, 64 * hd_:64 * (hd_ + 1)]
                em.c("dve", lambda v, o=dst, g=GTC[:, 64 * hd_:64 * (hd_ + 1)]:
                     v.tensor_tensor(o, TGN[:, :], g, ALU.mult),
                     reads=[TGN_res, GTC_res], writes=[RGC_res], tag="gmul")
            if c == 0 and getattr(h, "dbg", None) is not None:
                em.d("so", h.dbg["rg"][:, :], RGC[:, :], reads=[RGC_res],
                     writes=[h.dbg_res["rg"]], tag="dbg")
            for f in range(NF):
                src = RGC[:, 128 * f:128 * (f + 1)]
                dst = RGT[:, 128 * f:128 * (f + 1)]
                emit_transpose(em, h, src, dst, PSt, PSt_res, RGC_res, RGT_res)
            for j in range(2):
                for kk in range(NF):
                    lhsT = RGT[:, 128 * kk:128 * (kk + 1)]
                    rhs = WOB[:, 1024 * kk + 512 * j:1024 * kk + 512 * (j + 1)]
                    em.c("pe", lambda t, o=PS0[:, 512 * j:512 * (j + 1)], l=lhsT, r=rhs,
                         st=(kk == 0): t.matmul(o, l, r, start=st, stop=False),
                         reads=[RGT_res, WOB_res], writes=[PS0_res], tag="oproj")
                boff = bias_off["bo"]
                em.c("pe", lambda t, o=PS0[:, 512 * j:512 * (j + 1)],
                     r=BR[0:1, boff + 512 * j:boff + 512 * (j + 1)]:
                     t.matmul(o, h.ones_row[0:1, 0:128], r, start=False, stop=True),
                     reads=[BR_res, h.ones_row_res], writes=[PS0_res], tag="obias")
            xt = XTOK[:, 1024 * c:1024 * (c + 1)]
            em.c("dve", lambda v, x=xt: v.tensor_tensor(T1[:, :], PS0[:, :], x, ALU.add),
                 reads=[PS0_res, XTOK_res], writes=[T1_res], tag="midres")
            XO, XO_res = XOs[c % 2]
            em.c("dve", lambda v, o=XO[:, :]: v.tensor_copy(o, T1[:, :]),
                 reads=[T1_res], writes=[XO_res], tag="midcp")
            em.d("so", h.xmid_dram[128 * c:128 * (c + 1), :], XO[:, :],
                 reads=[XO_res], writes=[h.xmid_res], tag="midstore")

    em.barrier()

    with contextlib.ExitStack() as ph3:
        XMH, XMH_res = alloc_sb(ph3, nc, "XMH", [128, 1024], F32)
        T2, T2_res = alloc_sb(ph3, nc, "T2", [128, 1024], F32)
        BR3, BR3_res = alloc_sb(ph3, nc, "BR3", [1, 2 * 2816 + 1024], BF16)
        AFS, AFS_res = alloc_sb(ph3, nc, "AFS", [128, NIC * C_SLAB], BF16)
        WIN0, WIN0_res = alloc_sb(ph3, nc, "WIN0", [128, 2048], BF16)
        WIN1, WIN1_res = alloc_sb(ph3, nc, "WIN1", [128, 2048], BF16)
        WOU0, WOU0_res = alloc_sb(ph3, nc, "WOU0", [128, 1024], BF16)
        WOU1, WOU1_res = alloc_sb(ph3, nc, "WOU1", [128, 1024], BF16)
        TMP, TMP_res = alloc_sb(ph3, nc, "TMPC", [128, 512], BF16)
        TMPc2, TMPc2_res = alloc_sb(ph3, nc, "TMPC2", [128, 512], BF16)
        TMPsC = [(TMP, TMP_res), (TMPc2, TMPc2_res)]
        X2C, X2C_res = alloc_sb(ph3, nc, "X2C", [128, 1024], BF16)
        WINs = [(WIN0, WIN0_res), (WIN1, WIN1_res)]
        WOUs = [(WOU0, WOU0_res), (WOU1, WOU1_res)]
        for nm, dd, ln in [("biu", h.b_in_u, 2816), ("big", h.b_in_g, 2816),
                           ("bout", h.b_out, 1024)]:
            em.d("p1", BR3[0:1, off - 5120:off - 5120 + ln], dd[:, :], writes=[BR3_res])
            bias_off[nm] = off
            off += ln
        BCT3, BCT3_res = alloc_sb(ph3, nc, "BCT3", [128, 2048], BF16)
        for j, dd in enumerate([h.norm2_w, h.norm2_b]):
            em.d("sk", NRMR[0:1, :], dd[:, :], reads=[NRMR_res], writes=[NRMR_res])
            for jj in range(2):
                pb = PS0[:, 0:512]
                em.c("pe", lambda t, r=NRMR[0:1, 512 * jj:512 * (jj + 1)]:
                     t.matmul(pb, h.ones1x128f[0:1, :], r, start=True, stop=True),
                     reads=[NRMR_res, h.ones1x128f_res], writes=[PS0_res], tag="bcmm3")
                dst = BCT3[:, 1024 * j + 512 * jj:1024 * j + 512 * (jj + 1)]
                em.c("dve", lambda v, o=dst: v.tensor_copy(o, pb),
                     reads=[PS0_res], writes=[BCT3_res], tag="bcev3")
        W2B = BCT3[:, 0:1024]
        B2B = BCT3[:, 1024:2048]

        for tt in range(NT):
            em.d(["px", "p0"], XMH[:, :], h.xmid_dram[128 * tt:128 * (tt + 1), :],
                 reads=[h.xmid_res], writes=[XMH_res], tag="xmload")
            emit_ln(em, h, XMH[:, :], XMH_res, [(T1[:, :], T1_res)], ln_scratch, tag="ln2")
            em.c("dve", lambda v: v.tensor_tensor(T1[:, :], T1[:, :], W2B, ALU.mult),
                 reads=[T1_res, BCT3_res], writes=[T1_res], tag="x2w")
            em.c("dve", lambda v: v.tensor_tensor(T2[:, :], T1[:, :], B2B, ALU.add),
                 reads=[T1_res, BCT3_res], writes=[T2_res], tag="x2b")
            em.c("dve", lambda v: v.tensor_copy(X2C[:, :], T2[:, :]),
                 reads=[T2_res], writes=[X2C_res], tag="x2cast")
            for f in range(NF):
                src = X2C[:, 128 * f:128 * (f + 1)]
                dst = XNA[:, 2048 * f + 128 * tt:2048 * f + 128 * (tt + 1)]
                emit_transpose(em, h, src, dst, PSt, PSt_res, X2C_res, XNA_res)
        X2T, X2T_res = XNA, XNA_res

        for s in range(4):
            for i in range(NIC):
                WIN, WIN_res = WINs[i % 2]
                em.d(WCH2[(2 * i) % 4], WIN[:, 0:1024], h.win_u_pre[i],
                     writes=[WIN_res], tag="wiuld")
                em.d(WCH2[(2 * i + 1) % 4], WIN[:, 1024:2048], h.win_g_pre[i],
                     writes=[WIN_res], tag="wigld")
                PSp, PSp_res = (PS0, PS0_res) if i % 2 == 0 else (PS1, PS1_res)
                for which, bias_name in ((0, "biu"), (1, "big")):
                    ps = PSp[:, 512 * which:512 * (which + 1)]
                    for kk in range(NF):
                        lhsT = WIN[:, 1024 * which + 128 * kk:1024 * which + 128 * (kk + 1)]
                        rhs = X2T[:, 2048 * kk + C_SLAB * s:2048 * kk + C_SLAB * (s + 1)]
                        em.c("pe", lambda t, o=ps, l=lhsT, r=rhs, st=(kk == 0):
                             t.matmul(o, l, r, start=st, stop=False),
                             reads=[WIN_res, X2T_res], writes=[PSp_res], tag="ug")
                    boff = bias_off[bias_name] - 5120 + 128 * i
                    em.c("pe", lambda t, o=ps, l=BR3[0:1, boff:boff + 128]:
                         t.matmul(o, l, h.ones_row[0:1, :], start=False, stop=True),
                         reads=[BR3_res, h.ones_row_res], writes=[PSp_res], tag="ugbias")
                em.c("act", lambda s_, i_=PSp[:, 0:512], tp=TMPsC[i % 2]: s_.activation(
                    tp[0][:, :], i_, AF.Silu), reads=[PSp_res], writes=[TMPsC[i % 2][1]],
                    tag="usilu")
                dst = AFS[:, C_SLAB * i:C_SLAB * (i + 1)]
                em.c("dve", lambda v, o=dst, tp=TMPsC[i % 2], i_=PSp[:, 512:1024]:
                     v.tensor_tensor(o, tp[0][:, :], i_, ALU.mult),
                     reads=[TMPsC[i % 2][1], PSp_res], writes=[AFS_res], tag="umul")
            for half in range(2):
                pss = [(PS0, PS0_res), (PS1, PS1_res)]
                for i in range(NIC):
                    WOU, WOU_res = WOUs[i % 2]
                    em.d(WCH2[i % 4], WOU[:, :], h.w_out_pad[128 * i:128 * (i + 1), :],
                         writes=[WOU_res], tag="would")
                    for tt2 in range(2):
                        tt = 2 * half + tt2
                        ps, ps_res = pss[tt2]
                        lhsT = AFS[:, C_SLAB * i + 128 * tt:C_SLAB * i + 128 * (tt + 1)]
                        for j in range(2):
                            rhs = WOU[:, 512 * j:512 * (j + 1)]
                            em.c("pe", lambda t, o=ps[:, 512 * j:512 * (j + 1)], l=lhsT,
                                 r=rhs, st=(i == 0): t.matmul(o, l, r, start=st, stop=False),
                                 reads=[AFS_res, WOU_res], writes=[ps_res], tag="wout")
                for tt2 in range(2):
                    tt = 2 * half + tt2
                    ps, ps_res = pss[tt2]
                    boff = bias_off["bout"] - 5120
                    for j in range(2):
                        em.c("pe", lambda t, o=ps[:, 512 * j:512 * (j + 1)],
                             r=BR3[0:1, boff + 512 * j:boff + 512 * (j + 1)]:
                             t.matmul(o, h.ones_row[0:1, 0:128], r, start=False, stop=True),
                             reads=[BR3_res, h.ones_row_res], writes=[ps_res], tag="woutb")
                    gtt = 4 * s + tt
                    em.d(["px", "p0"], XMH[:, :], h.xmid_dram[128 * gtt:128 * (gtt + 1), :],
                         reads=[h.xmid_res], writes=[XMH_res], tag="xm2")
                    em.c("dve", lambda v, i_=ps[:, :]: v.tensor_tensor(T1[:, :], i_,
                                                                       XMH[:, :], ALU.add),
                         reads=[ps_res, XMH_res], writes=[T1_res], tag="finres1")
                    xt = XTOK[:, 1024 * gtt:1024 * (gtt + 1)]
                    em.c("dve", lambda v, x=xt: v.tensor_tensor(T1[:, :], T1[:, :], x,
                                                                ALU.add),
                         reads=[T1_res, XTOK_res], writes=[T1_res], tag="finres2")
                    XO, XO_res = XOs[tt2]
                    emit_ln(em, h, T1[:, :], T1_res, [(XO[:, :], XO_res)], ln_scratch,
                            tag="lnC")
                    em.d("so", h.xcout[128 * gtt:128 * (gtt + 1), :], XO[:, :],
                         reads=[XO_res], writes=[h.xcout_res], tag="xcstore")


WCH2 = ["p0", "p1", "p2", "p3"]


def build_C(core, debug=False):
    import contextlib
    nc = bass.Bass("TRN2", target_bir_lowering=False)
    h = H()
    if debug:
        h.dbg = {}
        h.dbg_res = {}
        for nm in ["q", "k", "v", "g", "rg"]:
            h.dbg[nm] = nc.dram_tensor("dbg_" + nm, [128, 1024], BF16, kind="ExternalOutput")
            h.dbg_res[nm] = Res("dbg_" + nm)
        for nm in ["qs", "ks", "ain", "ret"]:
            h.dbg[nm] = nc.dram_tensor("dbg_" + nm, [128, 128], F32, kind="ExternalOutput")
            h.dbg_res[nm] = Res("dbg_" + nm)
    else:
        h.dbg = None
    h.xc_in = nc.dram_tensor("xc_in", [TOK, D], F32, kind="ExternalInput")
    h.xprefix = nc.dram_tensor("xprefix", [NPRE * 128, D], F32, kind="ExternalInput")
    for nm in ["wq", "wk", "wv", "wg", "wo"]:
        setattr(h, nm, nc.dram_tensor(nm, [D, D], F32, kind="ExternalInput"))
    for nm in ["bq", "bk", "bv", "bg", "bo", "b_out"]:
        setattr(h, nm, nc.dram_tensor(nm, [1, D], F32, kind="ExternalInput"))
    for nm in ["norm1_w", "norm1_b", "gn_w", "gn_b", "norm2_w", "norm2_b"]:
        setattr(h, nm, nc.dram_tensor(nm, [1, D], F32, kind="ExternalInput"))
    h.b_in_u = nc.dram_tensor("b_in_u", [1, 2816], F32, kind="ExternalInput")
    h.b_in_g = nc.dram_tensor("b_in_g", [1, 2816], F32, kind="ExternalInput")
    h.win_u_pre = nc.dram_tensor("win_u_pre", [NIC, 128, 1024], F32, kind="ExternalInput")
    h.win_g_pre = nc.dram_tensor("win_g_pre", [NIC, 128, 1024], F32, kind="ExternalInput")
    h.w_out_pad = nc.dram_tensor("w_out_pad", [NIC * 128, 1024], F32, kind="ExternalInput")
    h.GQ_d = nc.dram_tensor("GQ_d", [128, NH], F32, kind="ExternalInput")
    h.GKN_d = nc.dram_tensor("GKN_d", [128, NH], F32, kind="ExternalInput")
    h.GKS_d = nc.dram_tensor("GKS_d", [128, NH], F32, kind="ExternalInput")
    h.GKSP_d = nc.dram_tensor("GKSP_d", [128, NPRE * NH], F32, kind="ExternalInput")
    h.GDEC_d = nc.dram_tensor("GDEC_d", [64, NPRE * NH], F32, kind="ExternalInput")
    h.MASKT_d = nc.dram_tensor("MASKT_d", [128, 128], BF16, kind="ExternalInput")
    h.xcout = nc.dram_tensor("xcout", [TOK, D], F32, kind="ExternalOutput")
    h.xcout_res = Res("xcout")
    h.xc_in_res = Res("xc_in")
    h.xprefix_res = Res("xprefix")
    h.xmid_dram = nc.dram_tensor("xmid_dram", [TOK, D], F32)
    h.xmid_res = Res("xmid")

    chans = {"p0": "pool", "p1": "pool", "p2": "pool", "p3": "pool", "px": "pool",
             "sk": "sp", "so": "sp"}
    em = Em(nc, chans)
    with contextlib.ExitStack() as outer:
        declare_consts(nc, outer, h)
        load_consts(em, h)
        emit_blockC(em, h, nc, outer)
        em.barrier()
        with nc.Block() as block:
            em.finalize_and_emit(block, None)
    return nc


def prep_C_inputs(inputs, core, xb_full):
    b = core // 4
    q0 = 2048 * (core % 4)
    cc = host_consts()
    xpre = np.zeros((NPRE * 128, D), np.float32)
    xpre[:q0] = xb_full[b, :q0]
    nvalid = q0 // 128
    ii = np.arange(128, dtype=np.float64)
    GQ = np.stack([(g ** ii) / 8.0 for g in GAMMAS], 1).astype(np.float32)
    GKN = np.stack([g ** (-ii) for g in GAMMAS], 1).astype(np.float32)
    GKS = np.stack([g ** (128 - ii) for g in GAMMAS], 1).astype(np.float32)
    GKSP = np.zeros((128, NPRE * NH), np.float32)
    GDEC = np.ones((64, NPRE * NH), np.float32)
    for c in range(NPRE):
        for hh in range(NH):
            if c < nvalid:
                GKSP[:, NH * c + hh] = GKS[:, hh]
                GDEC[:, NH * c + hh] = GAMMAS[hh] ** 128
    maskT = np.zeros((128, 128), BFDT)
    for m_ in range(128):
        maskT[m_, m_:] = 1
    w_in = np.asarray(inputs["L1_attn_w_in"])
    b_in = np.asarray(inputs["L1_attn_b_in"])
    w_out = np.asarray(inputs["L1_attn_w_out"])
    wu = np.zeros((D, NIC * 128), np.float32); wu[:, :2730] = w_in[:, :2730]
    wgg = np.zeros((D, NIC * 128), np.float32); wgg[:, :2730] = w_in[:, 2730:]
    biu = np.zeros((1, NIC * 128), np.float32); biu[0, :2730] = b_in[:2730]
    big = np.zeros((1, NIC * 128), np.float32); big[0, :2730] = b_in[2730:]
    wop = np.zeros((NIC * 128, D), np.float32); wop[:2730] = w_out
    pre = lambda W: np.ascontiguousarray(
        W.reshape(8, 128, NIC, 128).transpose(2, 1, 0, 3).reshape(NIC, 128, 1024))
    m = {
        "xc_in": np.ascontiguousarray(xb_full[b, q0:q0 + 2048, :]),
        "xprefix": xpre,
        "wq": np.asarray(inputs["L1_attn_wq"]), "wk": np.asarray(inputs["L1_attn_wk"]),
        "wv": np.asarray(inputs["L1_attn_wv"]), "wg": np.asarray(inputs["L1_attn_wg"]),
        "wo": np.asarray(inputs["L1_attn_wo"]),
        "bq": np.asarray(inputs["L1_attn_bq"]).reshape(1, -1),
        "bk": np.asarray(inputs["L1_attn_bk"]).reshape(1, -1),
        "bv": np.asarray(inputs["L1_attn_bv"]).reshape(1, -1),
        "bg": np.asarray(inputs["L1_attn_bg"]).reshape(1, -1),
        "bo": np.asarray(inputs["L1_attn_bo"]).reshape(1, -1),
        "b_out": np.asarray(inputs["L1_attn_b_out"]).reshape(1, -1),
        "norm1_w": np.asarray(inputs["L1_attn_norm1_w"]).reshape(1, -1),
        "norm1_b": np.asarray(inputs["L1_attn_norm1_b"]).reshape(1, -1),
        "gn_w": np.asarray(inputs["L1_attn_gn_w"]).reshape(1, -1),
        "gn_b": np.asarray(inputs["L1_attn_gn_b"]).reshape(1, -1),
        "norm2_w": np.asarray(inputs["L1_attn_norm2_w"]).reshape(1, -1),
        "norm2_b": np.asarray(inputs["L1_attn_norm2_b"]).reshape(1, -1),
        "b_in_u": biu, "b_in_g": big,
        "win_u_pre": pre(wu), "win_g_pre": pre(wgg),
        "w_out_pad": wop,
        "GQ_d": GQ, "GKN_d": GKN, "GKS_d": GKS, "GKSP_d": GKSP, "GDEC_d": GDEC,
        "MASKT_d": maskT,
    }
    for k, v in cc.items():
        m["c_" + k] = v
    return m



NEXP = 64
PSLOT = 32
NLOC = 8


def emit_blockD(em, h, nc, outer_ctx):
    import contextlib
    XTOK, XTOK_res = alloc_sb(outer_ctx, nc, "XTOKD", [128, NT * 1024], BF16)
    SQ, SQ_res = alloc_sb(outer_ctx, nc, "SQln", [128, 1024], F32)
    COLS, COLS_res = alloc_sb(outer_ctx, nc, "COLS", [128, 16], F32)
    ln_scratch = {"sq": (SQ, SQ_res), "cols": (COLS, COLS_res)}
    T1, T1_res = alloc_sb(outer_ctx, nc, "T1", [128, 1024], F32)
    XO0, XO0_res = alloc_sb(outer_ctx, nc, "XO0", [128, 1024], F32)
    XOs = [(XO0, XO0_res), (XO0, XO0_res)]
    XPE, XPE_res = alloc_sb(outer_ctx, nc, "XPE", [128, 1024], F32)
    XS0, XS0_res = XPE, XPE_res
    XTN, XTN_res = alloc_sb(outer_ctx, nc, "XTN", [128, NF * 256], BF16)
    X2C, X2C_res = alloc_sb(outer_ctx, nc, "X2CD", [128, 1024], BF16)
    ph12 = contextlib.ExitStack()
    XT, XT_res = alloc_sb(ph12, nc, "XT_D", [128, NF * 2048], BF16)
    PHB, PHB_res = alloc_sb(ph12, nc, "PHB", [128, NF * 2048], BF16)
    DHB, DHB_res = alloc_sb(ph12, nc, "DHB", [128, NT * 2048], BF16)
    ESB, ESB_res = alloc_sb(ph12, nc, "ESB", [128, 2048], F32)
    CHT, CHT_res = alloc_sb(ph12, nc, "CHT", [128, 2048], BF16)
    RSP, RSP_res = alloc_sb(ph12, nc, "RSP", [128, 64], F32)
    RSN, RSN_res = alloc_sb(ph12, nc, "RSN", [128, 32], F32)
    PBR, PBR_res = alloc_sb(ph12, nc, "PBR", [1, 2048], BF16)

    PSL, PSL_res = alloc_ps(outer_ctx, nc, "PSL", [128, 2048], F32)
    PS0, PS0_res = alloc_ps(outer_ctx, nc, "PS0", [128, 1024], F32)
    PSt, _pstres = alloc_ps(outer_ctx, nc, "PSt", [128, 2048], BF16)
    PSt_res = [_pstres, Res("PSt1")]

    for t in range(NT):
        em.d(["px", "p0"], XTOK[:, 1024 * t:1024 * (t + 1)],
             h.xd_in[128 * t:128 * (t + 1), :], reads=[h.xd_in_res],
             writes=[XTOK_res], tag="xload")
    for t in range(NT):
        for f in range(NF):
            src = XTOK[:, 1024 * t + 128 * f:1024 * t + 128 * (f + 1)]
            dst = XT[:, 2048 * f + 128 * t:2048 * f + 128 * (t + 1)]
            emit_transpose(em, h, src, dst, PSt, PSt_res, XTOK_res, XT_res)
    for kk in range(NF):
        em.d("p1", PHB[:, 2048 * kk:2048 * (kk + 1)],
             h.phi_wT[128 * kk:128 * (kk + 1), :], writes=[PHB_res], tag="phld")
    em.d("p2", PBR[:, :], h.phi_b[:, :], writes=[PBR_res])

    for tt in range(NT):
        for j in range(4):
            pl = PSL[:, 512 * j:512 * (j + 1)]
            for kk in range(NF):
                lhsT = XT[:, 2048 * kk + 128 * tt:2048 * kk + 128 * (tt + 1)]
                rhs = PHB[:, 2048 * kk + 512 * j:2048 * kk + 512 * (j + 1)]
                em.c("pe", lambda t, o=pl, l=lhsT, r=rhs, st=(kk == 0):
                     t.matmul(o, l, r, start=st, stop=False),
                     reads=[XT_res, PHB_res], writes=[PSL_res], tag="phi")
            em.c("pe", lambda t, o=pl, r=PBR[0:1, 512 * j:512 * (j + 1)]:
                 t.matmul(o, h.ones_row[0:1, 0:128], r, start=False, stop=True),
                 reads=[PBR_res, h.ones_row_res], writes=[PSL_res], tag="phib")
        nmx = COLS[:, 8:9]
        em.c("dve", lambda v: v.reduce_max(nmx, PSL[:, :], axis=AX.X, negate=True),
             reads=[PSL_res], writes=[COLS_res], tag="dmax")
        em.c("act", lambda s: s.activation(ESB[:, :], PSL[:, :], AF.Exp, bias=nmx),
             reads=[PSL_res, COLS_res], writes=[ESB_res], tag="dexp")
        ev_np = ESB[:, :].rearrange("p (n q) -> p n q", q=PSLOT)
        em.c("dve", lambda v: v.reduce_sum(RSP[:, :], ev_np, axis=AX.X),
             reads=[ESB_res], writes=[RSP_res], tag="dsp")
        ev_pn = ESB[:, :].rearrange("p (n q) -> p q n", q=PSLOT)
        em.c("dve", lambda v: v.reduce_sum(RSN[:, :], ev_pn, axis=AX.X),
             reads=[ESB_res], writes=[RSN_res], tag="dsn")
        em.c("dve", lambda v: v.reciprocal(RSP[:, :], RSP[:, :]),
             reads=[RSP_res], writes=[RSP_res], tag="drsp")
        em.c("dve", lambda v: v.reciprocal(RSN[:, :], RSN[:, :]),
             reads=[RSN_res], writes=[RSN_res], tag="drsn")
        for n_ in range(NEXP):
            dst = DHB[:, 2048 * tt + PSLOT * n_:2048 * tt + PSLOT * (n_ + 1)]
            esl = ESB[:, PSLOT * n_:PSLOT * (n_ + 1)]
            em.c("dve", lambda v, o=dst, i_=esl, sc=RSP[:, n_:n_ + 1]:
                 v.tensor_scalar(o, i_, sc, None, ALU.mult),
                 reads=[ESB_res, RSP_res], writes=[DHB_res], tag="dhat")
        for p_ in range(PSLOT):
            dst = CHT[:, :].rearrange("p (n q) -> p q n", q=PSLOT)[:, p_, :]
            esl = ESB[:, :].rearrange("p (n q) -> p q n", q=PSLOT)[:, p_, :]
            em.c("dve", lambda v, o=dst, i_=esl, sc=RSN[:, p_:p_ + 1]:
                 v.tensor_scalar(o, i_, sc, None, ALU.mult),
                 reads=[ESB_res, RSN_res], writes=[CHT_res], tag="chat")
        for j in range(16):
            src = CHT[:, 128 * j:128 * (j + 1)]
            dst = X2C[:, 0:128]
            emit_transpose(em, h, src, dst, PSt, PSt_res, CHT_res, X2C_res)
            em.d("sq", h.cht_dram[128 * j:128 * (j + 1), 128 * tt:128 * (tt + 1)],
                 X2C[:, 0:128], reads=[X2C_res], writes=[h.cht_res], tag="chtst")

    for j in range(16):
        for jj in range(2):
            ps = PS0[:, 512 * jj:512 * (jj + 1)]
            for tt in range(NT):
                lhsT = DHB[:, 2048 * tt + 128 * j:2048 * tt + 128 * (j + 1)]
                rhs = XTOK[:, 1024 * tt + 512 * jj:1024 * tt + 512 * (jj + 1)]
                em.c("pe", lambda t, o=ps, l=lhsT, r=rhs, st=(tt == 0), sp=(tt == NT - 1):
                     t.matmul(o, l, r, start=st, stop=sp),
                     reads=[DHB_res, XTOK_res], writes=[PS0_res], tag="xtp")
        em.c("act", lambda s: s.copy(XPE[:, :], PS0[:, :]),
             reads=[PS0_res], writes=[XPE_res], tag="xtpev")
        em.d("sq", h.rs_in[128 * j:128 * (j + 1), :], XPE[:, :],
             reads=[XPE_res], writes=[h.rs_in_res], tag="xtpst")
    cc1 = em.c("pool", lambda g: g.collective_compute(
        "ReduceScatter", ALU.add, replica_groups=[list(range(8))],
        ins=[h.rs_in[:, :]], outs=[h.rs_out[:, :]]),
        reads=[h.rs_in_res], writes=[h.rs_out_res], tag="rs", async_=True)
    em.barrier()
    ph12.close()
    WEB0, WEB0_res = alloc_sb(outer_ctx, nc, "WEB0", [128, 8 * 1024], BF16)
    WEB1, WEB1_res = alloc_sb(outer_ctx, nc, "WEB1", [128, 8 * 1024], BF16)
    EBR, EBR_res = alloc_sb(outer_ctx, nc, "EBR", [1, 1024], BF16)
    Y32, Y32_res = alloc_sb(outer_ctx, nc, "Y32", [32, 1024], F32)
    YTB, YTB_res = alloc_sb(outer_ctx, nc, "YTB", [128, 16 * 1024], BF16)
    CTB0, CTB0_res = alloc_sb(outer_ctx, nc, "CTB0", [128, 128], BF16)
    CTB1, CTB1_res = alloc_sb(outer_ctx, nc, "CTB1", [128, 128], BF16)
    WEBs = [(WEB0, WEB0_res), (WEB1, WEB1_res)]
    CTBs = [(CTB0, CTB0_res), (CTB1, CTB1_res)]

    for half in range(2):
        em.d(["px", "p0"], XS0[:, :], h.rs_out[128 * half:128 * (half + 1), :],
             reads=[h.rs_out_res], writes=[XS0_res], tag="rsld")
        em.c("dve", lambda v: v.tensor_scalar_max(XS0[:, :], XS0[:, :], -33000.0),
             reads=[XS0_res], writes=[XS0_res], tag="clip1")
        em.c("dve", lambda v: v.tensor_scalar_min(XS0[:, :], XS0[:, :], 65000.0),
             reads=[XS0_res], writes=[XS0_res], tag="clip2")
        emit_ln(em, h, XS0[:, :], XS0_res, [(T1[:, :], T1_res)], ln_scratch, tag="lnD")
        em.c("act", lambda s: s.activation(X2C[:, :], T1[:, :], AF.Tanh),
             reads=[T1_res], writes=[X2C_res], tag="dtanh")
        for f in range(NF):
            src = X2C[:, 128 * f:128 * (f + 1)]
            dst = XTN[:, 256 * f + 128 * half:256 * f + 128 * (half + 1)]
            emit_transpose(em, h, src, dst, PSt, PSt_res, X2C_res, XTN_res)
    for e in range(NLOC):
        WEB, WEB_res = WEBs[e % 2]
        if e < 2:
            for kk in range(NF):
                em.d(["p1", "p2", "p3"][kk % 3], WEBs[e][0][:, 1024 * kk:1024 * (kk + 1)],
                     h.expert_w[e, 128 * kk:128 * (kk + 1), :],
                     writes=[WEBs[e][1]], tag="ewld")
        em.d("p3", EBR[:, :], h.expert_b[e:e + 1, :], reads=[EBR_res], writes=[EBR_res],
             tag="ebld")
        pp = PS0[0:32, :]
        for jj in range(2):
            for kk in range(NF):
                lhsT = XTN[:, 256 * kk + 32 * e:256 * kk + 32 * (e + 1)]
                rhs = WEB[:, 1024 * kk + 512 * jj:1024 * kk + 512 * (jj + 1)]
                em.c("pe", lambda t, o=pp[:, 512 * jj:512 * (jj + 1)], l=lhsT, r=rhs,
                     st=(kk == 0): t.matmul(o, l, r, start=st, stop=False),
                     reads=[XTN_res, WEB_res], writes=[PS0_res], tag="yt")
            em.c("pe", lambda t, o=pp[:, 512 * jj:512 * (jj + 1)],
                 r=EBR[0:1, 512 * jj:512 * (jj + 1)]: t.matmul(
                 o, h.ones_row[0:1, 0:32], r, start=False, stop=True),
                 reads=[EBR_res, h.ones_row_res], writes=[PS0_res], tag="ytb")
        em.c("dve", lambda v: v.tensor_copy(Y32[:, :], pp),
             reads=[PS0_res], writes=[Y32_res], tag="ytev")
        em.d("sq", h.ag_in[32 * e:32 * (e + 1), :], Y32[:, :],
             reads=[Y32_res], writes=[h.ag_in_res], tag="ytst")
        if e + 2 < NLOC:
            WEBn, WEBn_res = WEBs[e % 2]
            for kk in range(NF):
                em.d(["p1", "p2", "p3"][kk % 3], WEBn[:, 1024 * kk:1024 * (kk + 1)],
                     h.expert_w[e + 2, 128 * kk:128 * (kk + 1), :],
                     writes=[WEBn_res], tag="ewld2")
    cc2 = em.c("pool", lambda g: g.collective_compute(
        "AllGather", ALU.bypass, replica_groups=[list(range(8))],
        ins=[h.ag_in[:, :]], outs=[h.ag_out[:, :]]),
        reads=[h.ag_in_res], writes=[h.ag_out_res], tag="ag", async_=True)

    for j in range(16):
        em.d(["px", "p0"], YTB[:, 1024 * j:1024 * (j + 1)],
             h.ag_out[128 * j:128 * (j + 1), :], reads=[h.ag_out_res],
             writes=[YTB_res], tag="ytld")
    for tt in range(NT):
        if tt == 0:
            for j in range(2):
                CTB, CTB_res = CTBs[j % 2]
                em.d("sk", CTB[:, :], h.cht_dram[128 * j:128 * (j + 1), 0:128],
                     reads=[h.cht_res], writes=[CTB_res], tag="ctld")
        ps = PS0
        for j in range(16):
            CTB, CTB_res = CTBs[j % 2]
            for jj in range(2):
                rhs = YTB[:, 1024 * j + 512 * jj:1024 * j + 512 * (jj + 1)]
                em.c("pe", lambda t, o=ps[:, 512 * jj:512 * (jj + 1)], l=CTB[:, :], r=rhs,
                     st=(j == 0), sp=(j == 15), cr=CTB_res: t.matmul(o, l, r, start=st,
                                                                    stop=sp),
                     reads=[CTB_res, YTB_res], writes=[PS0_res], tag="ymm")
            nj, ntt = (j + 2, tt) if j + 2 < 16 else ((j + 2) % 16, tt + 1)
            if ntt < NT:
                CTBn, CTBn_res = CTBs[j % 2]
                em.d("sk", CTBn[:, :],
                     h.cht_dram[128 * nj:128 * (nj + 1), 128 * ntt:128 * (ntt + 1)],
                     reads=[h.cht_res], writes=[CTBn_res], tag="ctld2")
        xt = XTOK[:, 1024 * tt:1024 * (tt + 1)]
        em.c("dve", lambda v, x=xt: v.tensor_tensor(T1[:, :], PS0[:, :], x, ALU.add),
             reads=[PS0_res, XTOK_res], writes=[T1_res], tag="dres")
        XO, XO_res = XOs[tt % 2]
        emit_ln(em, h, T1[:, :], T1_res, [(XO[:, :], XO_res)], ln_scratch, tag="lnF")
        em.d("so", h.xout[128 * tt:128 * (tt + 1), :], XO[:, :],
             reads=[XO_res], writes=[h.xout_res], tag="xostore")


def build_D():
    import contextlib
    nc = bass.Bass("TRN2", target_bir_lowering=False)
    h = H()
    h.xd_in = nc.dram_tensor("xd_in", [TOK, D], F32, kind="ExternalInput")
    h.phi_wT = nc.dram_tensor("phi_wT", [D, 2048], F32, kind="ExternalInput")
    h.phi_b = nc.dram_tensor("phi_b", [1, 2048], F32, kind="ExternalInput")
    h.expert_w = nc.dram_tensor("expert_w", [NLOC, D, D], F32, kind="ExternalInput")
    h.expert_b = nc.dram_tensor("expert_b", [NLOC, D], F32, kind="ExternalInput")
    h.xout = nc.dram_tensor("xout", [TOK, D], F32, kind="ExternalOutput")
    h.xout_res = Res("xout")
    h.xd_in_res = Res("xd_in")
    h.cht_dram = nc.dram_tensor("cht_dram", [2048, 2048], BF16)
    h.cht_res = Res("cht")
    h.rs_in = nc.dram_tensor("rs_in", [2048, 1024], F32)
    h.rs_in_res = Res("rs_in")
    h.rs_out = nc.dram_tensor("rs_out", [256, 1024], F32)
    h.rs_out_res = Res("rs_out")
    h.ag_in = nc.dram_tensor("ag_in", [256, 1024], F32)
    h.ag_in_res = Res("ag_in")
    h.ag_out = nc.dram_tensor("ag_out", [2048, 1024], F32, addr_space="Shared")
    h.ag_out_res = Res("ag_out")

    chans = {"p0": "pool", "p1": "pool", "p2": "pool", "p3": "pool", "px": "pool",
             "sk": "sp", "sq": "sp", "so": "sp"}
    em = Em(nc, chans)
    with contextlib.ExitStack() as outer:
        declare_consts(nc, outer, h)
        load_consts(em, h)
        emit_blockD(em, h, nc, outer)
        em.barrier()
        with nc.Block() as block:
            em.finalize_and_emit(block, None)
    return nc


def prep_D_inputs(inputs, core, xc_full):
    b = core // 4
    q0 = 2048 * (core % 4)
    cc = host_consts()
    m = {
        "xd_in": np.ascontiguousarray(xc_full[b, q0:q0 + 2048, :]),
        "phi_wT": np.ascontiguousarray(np.asarray(inputs["L1_ffn_phi_w"]).T),
        "phi_b": np.asarray(inputs["L1_ffn_phi_b"]).reshape(1, -1),
        "expert_w": np.ascontiguousarray(np.asarray(inputs["L1_ffn_expert_w"])[8 * core:8 * core + 8]),
        "expert_b": np.ascontiguousarray(np.asarray(inputs["L1_ffn_expert_b"])[8 * core:8 * core + 8]),
    }
    for k, v in cc.items():
        m["c_" + k] = v
    return m



def _gather(res, key):
    out = np.zeros((2, 8192, 1024), np.float32)
    for c in range(NCORE):
        b, q0 = c // 4, 2048 * (c % 4)
        out[b, q0:q0 + 2048] = res.results[c][key]
    return out


def kernel(**inputs):
    nc = build_fused()
    res = run_bass_kernel_spmd(nc, [prep_fused_inputs(inputs, c) for c in range(NCORE)],
                               core_ids=list(range(NCORE)))
    return _gather(res, "xout")



def build_fused():
    import contextlib
    nc = bass.Bass("TRN2", target_bir_lowering=False)
    h = H()
    h.x_own = nc.dram_tensor("x_own", [TOK, D], F32, kind="ExternalInput")
    h.x_batch = nc.dram_tensor("x_batch", [SEQ, D], F32, kind="ExternalInput")
    h.qkv_wT = nc.dram_tensor("qkv_wT", [D, 3 * D], F32, kind="ExternalInput")
    h.qkv_b = nc.dram_tensor("qkv_b", [1, 3 * D], F32, kind="ExternalInput")
    h.out_wT = nc.dram_tensor("out_wT", [D, D], F32, kind="ExternalInput")
    h.out_b = nc.dram_tensor("out_b", [1, D], F32, kind="ExternalInput")
    h.gate_w = nc.dram_tensor("gate_w", [D, NE], F32, kind="ExternalInput")
    h.WGU_pre = nc.dram_tensor("WGU_pre", [NE, NI, 128, 2048], F32, kind="ExternalInput")
    h.Wd = nc.dram_tensor("Wd", [NE, 2816, 1024], F32, kind="ExternalInput")
    h.SGU_pre = nc.dram_tensor("SGU_pre", [NIS, 128, 2048], F32, kind="ExternalInput")
    h.Sd = nc.dram_tensor("Sd", [1408, 1024], F32, kind="ExternalInput")
    for nm in ["wq", "wk", "wv", "wg", "wo"]:
        setattr(h, nm, nc.dram_tensor(nm, [D, D], F32, kind="ExternalInput"))
    for nm in ["bq", "bk", "bv", "bg", "bo", "b_out"]:
        setattr(h, nm, nc.dram_tensor(nm, [1, D], F32, kind="ExternalInput"))
    for nm in ["norm1_w", "norm1_b", "gn_w", "gn_b", "norm2_w", "norm2_b"]:
        setattr(h, nm, nc.dram_tensor(nm, [1, D], F32, kind="ExternalInput"))
    h.b_in_u = nc.dram_tensor("b_in_u", [1, 2816], F32, kind="ExternalInput")
    h.b_in_g = nc.dram_tensor("b_in_g", [1, 2816], F32, kind="ExternalInput")
    h.win_u_pre = nc.dram_tensor("win_u_pre", [NIC, 128, 1024], F32, kind="ExternalInput")
    h.win_g_pre = nc.dram_tensor("win_g_pre", [NIC, 128, 1024], F32, kind="ExternalInput")
    h.w_out_pad = nc.dram_tensor("w_out_pad", [NIC * 128, 1024], F32, kind="ExternalInput")
    h.GQ_d = nc.dram_tensor("GQ_d", [128, NH], F32, kind="ExternalInput")
    h.GKN_d = nc.dram_tensor("GKN_d", [128, NH], F32, kind="ExternalInput")
    h.GKS_d = nc.dram_tensor("GKS_d", [128, NH], F32, kind="ExternalInput")
    h.GKSP_d = nc.dram_tensor("GKSP_d", [128, NPRE * NH], F32, kind="ExternalInput")
    h.GDEC_d = nc.dram_tensor("GDEC_d", [64, NPRE * NH], F32, kind="ExternalInput")
    h.MASKT_d = nc.dram_tensor("MASKT_d", [128, 128], BF16, kind="ExternalInput")
    h.phi_wT = nc.dram_tensor("phi_wT", [D, 2048], F32, kind="ExternalInput")
    h.phi_b = nc.dram_tensor("phi_b", [1, 2048], F32, kind="ExternalInput")
    h.expert_w = nc.dram_tensor("expert_w", [NLOC, D, D], F32, kind="ExternalInput")
    h.expert_b = nc.dram_tensor("expert_b", [NLOC, D], F32, kind="ExternalInput")
    h.xout = nc.dram_tensor("xout", [TOK, D], F32, kind="ExternalOutput")
    h.xout_res = Res("xout")
    h.K_dram = nc.dram_tensor("K_dram", [D, SEQ], BF16); h.K_dram_res = Res("K")
    h.V_dram = nc.dram_tensor("V_dram", [D, SEQ], BF16); h.V_dram_res = Res("V")
    h.Q_dram = nc.dram_tensor("Q_dram", [D, TOK], BF16); h.Q_dram_res = Res("Q")
    h.xa = nc.dram_tensor("xa_i", [TOK, D], F32); h.xa_res = Res("xa")
    h.xb = nc.dram_tensor("xb_i", [TOK, D], F32); h.xb_res = Res("xb")
    h.xcout = nc.dram_tensor("xc_i", [TOK, D], F32); h.xcout_res = Res("xc")
    h.xb_all = nc.dram_tensor("xb_all", [SEQ, D], F32); h.xb_all_res = Res("xb_all")
    h.xmid_dram = nc.dram_tensor("xmid_dram", [TOK, D], F32); h.xmid_res = Res("xmid")
    h.cht_dram = nc.dram_tensor("cht_dram", [2048, 2048], BF16); h.cht_res = Res("cht")
    h.rs_in = nc.dram_tensor("rs_in", [2048, 1024], F32); h.rs_in_res = Res("rs_in")
    h.rs_out = nc.dram_tensor("rs_out", [256, 1024], F32); h.rs_out_res = Res("rs_out")
    h.ag_in = nc.dram_tensor("ag_in", [256, 1024], F32); h.ag_in_res = Res("ag_in")
    h.ag_out = nc.dram_tensor("ag_out", [2048, 1024], F32, addr_space="Shared")
    h.ag_out_res = Res("ag_out")
    h.xb_in = h.xa
    h.xb_in_res = h.xa_res
    h.xc_in = h.xb
    h.xc_in_res = h.xb_res
    h.xprefix = h.xb_all
    h.xprefix_res = h.xb_all_res
    h.xd_in = h.xcout
    h.xd_in_res = h.xcout_res
    h.dbg = None

    chans = {"p0": "pool", "p1": "pool", "p2": "pool", "p3": "pool", "px": "pool",
             "pw": "pool", "sk": "sp", "sq": "sp", "so": "sp"}
    em = Em(nc, chans)
    with contextlib.ExitStack() as outer:
        declare_consts(nc, outer, h)
        load_consts(em, h)
        with contextlib.ExitStack() as sA:
            emit_blockA(em, h, None, nc, sA)
        em.barrier()
        with contextlib.ExitStack() as sB:
            emit_blockB_fused(em, h, nc, sB)
        em.barrier()
        em.c("pool", lambda g: g.collective_compute(
            "AllGather", ALU.bypass, replica_groups=[[0, 1, 2, 3], [4, 5, 6, 7]],
            ins=[h.xb[:, :]], outs=[h.xb_all[:, :]]),
            reads=[h.xb_res], writes=[h.xb_all_res], tag="agxb", async_=True)
        with contextlib.ExitStack() as sC:
            emit_blockC_fused(em, h, nc, sC)
        em.barrier()
        with contextlib.ExitStack() as sD:
            emit_blockD(em, h, nc, sD)
        em.barrier()
        with nc.Block() as block:
            em.finalize_and_emit(block, None)
    return nc


def emit_blockB_fused(em, h, nc, ctx):
    emit_blockB(em, h, nc, ctx)


def emit_blockC_fused(em, h, nc, ctx):
    emit_blockC(em, h, nc, ctx)


def prep_fused_inputs(inputs, core):
    m = {}
    m.update(prep_A_inputs(inputs, core))
    dummy = np.zeros((2, 8192, 1024), np.float32)
    mb = prep_B_inputs(inputs, core, dummy); mb.pop("xb_in")
    m.update(mb)
    mc = prep_C_inputs(inputs, core, dummy); mc.pop("xc_in"); mc.pop("xprefix")
    m.update(mc)
    md = prep_D_inputs(inputs, core, dummy); md.pop("xd_in")
    m.update(md)
    return m


# version 05
# speedup vs baseline: 1.3116x; 1.0309x over previous
import math
import numpy as np
import ml_dtypes
import concourse.bass as bass
import concourse.mybir as mybir
from concourse.bass_utils import run_bass_kernel_spmd

F32 = mybir.dt.float32
BF16 = mybir.dt.bfloat16
AF = mybir.ActivationFunctionType
ALU = mybir.AluOpType
AX = mybir.AxisListType
BFDT = ml_dtypes.bfloat16

NCORE = 8
TOK = 2048
NT = TOK // 128
D = 1024
NF = D // 128
SEQ = 8192
EPS = 1e-5


class Res:
    __slots__ = ("name", "last_w", "readers")
    def __init__(self, name=""):
        self.name = name
        self.last_w = None
        self.readers = []

class Op:
    __slots__ = ("idx", "stream", "timeline", "kind", "fn", "deps", "inc", "val", "tag",
                 "async_")
    def __init__(self, idx, stream, timeline, kind, fn, tag=""):
        self.async_ = False
        self.idx = idx
        self.stream = stream
        self.timeline = timeline
        self.kind = kind
        self.fn = fn
        self.deps = []
        self.inc = False
        self.val = None
        self.tag = tag

COMPUTE_STREAMS = ("pe", "act", "dve", "pool", "sp")

class Em:
    def __init__(self, nc, chans):
        self.nc = nc
        self.ops = []
        self.streams = {s: [] for s in COMPUTE_STREAMS}
        self.chan_stream = dict(chans)
        self.chan_count = {c: 0 for c in chans}
        self.chan_last = {c: None for c in chans}
        self.grp_rr = {}

    def _mkop(self, stream, timeline, kind, fn, reads, writes, deps, tag):
        op = Op(len(self.ops), stream, timeline, kind, fn, tag)
        dd = []
        for r in reads:
            if r.last_w is not None:
                dd.append(r.last_w)
        for w in writes:
            if w.last_w is not None:
                dd.append(w.last_w)
            dd.extend(w.readers)
        dd.extend(deps)
        keep = []
        seen = set()
        for d in dd:
            if d is op:
                continue
            if (d.timeline == op.timeline and d.kind == "c" and op.kind == "c"
                    and d.timeline == "pe"):
                continue
            if (d.kind == "c" and op.kind == "d" and d.stream == op.stream
                    and not d.async_):
                continue
            if id(d) not in seen:
                seen.add(id(d))
                keep.append(d)
        op.deps = keep
        for r in reads:
            r.readers.append(op)
        for w in writes:
            w.last_w = op
            w.readers = []
        self.ops.append(op)
        self.streams[stream].append(op)
        return op

    def c(self, stream, fn, reads=(), writes=(), deps=(), tag="", async_=False):
        op = self._mkop(stream, stream, "c", fn, reads, writes, deps, tag)
        op.async_ = async_
        return op

    def d(self, chan, out_ap, in_ap, reads=(), writes=(), deps=(), tag="", **kw):
        if isinstance(chan, (list, tuple)):
            key = id(chan)
            i = self.grp_rr.get(key, 0)
            self.grp_rr[key] = i + 1
            chan = chan[i % len(chan)]
        stream = self.chan_stream[chan]
        fn = lambda eng, o=out_ap, i=in_ap, kw=kw: eng.dma_start(out=o, in_=i, **kw)
        deps = list(deps)
        if self.chan_last[chan] is not None:
            deps.append(self.chan_last[chan])
        op = self._mkop(stream, chan, "d", fn, reads, writes, deps, tag)
        self.chan_last[chan] = op
        return op

    def barrier(self):
        lasts = {}
        for op in self.ops:
            if op.kind != "w":
                lasts[op.timeline] = op
        ops = list(lasts.values())
        for s in COMPUTE_STREAMS:
            def mk(eng):
                return None
            o = Op(len(self.ops), s, s, "w", mk, "barrier")
            o.deps = [d for d in ops if not (d.timeline == s and d.kind == "c")]
            self.ops.append(o)
            self.streams[s].append(o)

    def finalize_and_emit(self, block, engines):
        for op in self.ops:
            for dpp in op.deps:
                dpp.inc = True
        cnt = {}
        for op in self.ops:
            if op.kind == "d":
                c = cnt.get(op.timeline, 0) + 16
                cnt[op.timeline] = c
                op.val = c
                op.inc = True
            elif op.inc and op.kind != "w":
                c = cnt.get(op.timeline, 0) + 1
                cnt[op.timeline] = c
                op.val = c
        used = sorted(set(op.timeline for op in self.ops if op.inc))
        sems = {t: self.nc.alloc_semaphore(name=f"sem_{t}") for t in used}
        self.sems = sems
        self.final_counts = dict(cnt)

        def run_stream(eng, stream_name):
            waited = {}
            for op in self.streams[stream_name]:
                for dpp in op.deps:
                    sem = sems[dpp.timeline]
                    if waited.get(dpp.timeline, 0) < dpp.val:
                        eng.wait_ge(sem, dpp.val)
                        waited[dpp.timeline] = dpp.val
                if op.kind == "w":
                    continue
                instr = op.fn(eng)
                if op.inc:
                    instr.then_inc(sems[op.timeline], 16 if op.kind == "d" else 1)

        @block.sync
        def _(sync):
            run_stream(sync, "sp")

        @block.tensor
        def _(t):
            run_stream(t, "pe")

        @block.scalar
        def _(s):
            run_stream(s, "act")

        @block.vector
        def _(v):
            run_stream(v, "dve")

        @block.gpsimd
        def _(g):
            run_stream(g, "pool")



class H:
    pass


_ALLOC_N = [0]

def alloc_sb(ctx, nc, name, shape, dtype):
    _ALLOC_N[0] += 1
    t = ctx.enter_context(nc.sbuf_tensor(f"{name}_{_ALLOC_N[0]}", shape, dtype))
    return t, Res(name)


def alloc_ps(ctx, nc, name, shape, dtype):
    _ALLOC_N[0] += 1
    t = ctx.enter_context(nc.psum_tensor(f"{name}_{_ALLOC_N[0]}", shape, dtype))
    return t, Res(name)


def emit_transpose(em, h, src_ap, dst_ap, psum, psum_res, src_res, dst_res, evac="dve"):
    n = src_ap.shape[1]
    p = src_ap.shape[0]
    par = h.tr_par = (getattr(h, "tr_par", 0) + 1) % 2
    pt = psum[0:n, 1024 * par:1024 * par + p]
    idn = h.ident[0:p, 0:p]
    pres = psum_res[par]
    em.c("pe", lambda t, o=pt, i=src_ap, e=idn: t.transpose(o, i, e),
         reads=[src_res, h.ident_res], writes=[pres], tag="tr")
    em.c(evac, lambda v, o=dst_ap, i=pt: v.tensor_copy(o, i),
         reads=[pres], writes=[dst_res], tag="tr_ev")


def emit_ln(em, h, src_ap, src_res, outs, scratch, tag="ln"):
    sq, sq_res = scratch["sq"]
    cols, cols_res = scratch["cols"]
    ssum = cols[:, 0:1]; ssq = cols[:, 1:2]; mean = cols[:, 2:3]
    varr = cols[:, 3:4]; rstd = cols[:, 4:5]; negmr = cols[:, 5:6]
    em.c("dve", lambda v: v.reduce_sum(ssum, src_ap, axis=AX.X),
         reads=[src_res], writes=[cols_res], tag=tag + "_sum")
    em.c("act", lambda s: s.activation(sq[:, :], src_ap, AF.Square, accum_out=ssq),
         reads=[src_res], writes=[sq_res, cols_res], tag=tag + "_sq")
    em.c("dve", lambda v: v.tensor_scalar(mean, ssum, 1.0 / 1024.0, None, ALU.mult),
         reads=[cols_res], writes=[cols_res], tag=tag)
    em.c("dve", lambda v: v.tensor_scalar(varr, ssq, 1.0 / 1024.0, EPS, ALU.mult, ALU.add),
         reads=[cols_res], writes=[cols_res], tag=tag)
    tmp = cols[:, 6:7]
    em.c("dve", lambda v: v.tensor_tensor(tmp, mean, mean, ALU.mult),
         reads=[cols_res], writes=[cols_res], tag=tag)
    em.c("dve", lambda v: v.tensor_tensor(varr, varr, tmp, ALU.subtract),
         reads=[cols_res], writes=[cols_res], tag=tag)
    em.c("act", lambda s: s.activation(rstd, varr, AF.Ln),
         reads=[cols_res], writes=[cols_res], tag=tag + "_ln")
    em.c("act", lambda s: s.activation(rstd, rstd, AF.Exp, scale=-0.5),
         reads=[cols_res], writes=[cols_res], tag=tag + "_exp")
    em.c("dve", lambda v: v.scalar_tensor_tensor(negmr, mean, -1.0, rstd, ALU.mult, ALU.mult),
         reads=[cols_res], writes=[cols_res], tag=tag)
    for dst_ap, dst_res in outs:
        em.c("dve", lambda v, o=dst_ap: v.tensor_scalar(o, src_ap, rstd, negmr, ALU.mult, ALU.add),
             reads=[src_res, cols_res], writes=[dst_res], tag=tag + "_app")



def host_consts():
    c = {}
    c["ident"] = np.eye(128, dtype=BFDT)
    c["ones_row"] = np.ones((1, 512), dtype=BFDT)
    c["ones64c"] = np.ones((64, 1), dtype=BFDT)
    c["ones1x64f"] = np.ones((1, 64), dtype=np.float32)
    c["ones1x128f"] = np.ones((1, 128), dtype=np.float32)
    halfA = np.zeros((128, 1), dtype=BFDT); halfA[0:64] = 1
    halfB = np.zeros((128, 1), dtype=BFDT); halfB[64:128] = 1
    c["halfA"] = halfA
    c["halfB"] = halfB
    return c


def declare_consts(nc, ctx, h):
    specs = [
        ("ident", [128, 128], BF16),
        ("ones_row", [1, 512], BF16),
        ("ones64c", [64, 1], BF16),
        ("ones1x64f", [1, 64], F32),
        ("ones1x128f", [1, 128], F32),
        ("halfA", [128, 1], BF16),
        ("halfB", [128, 1], BF16),
    ]
    h.const_dram = {}
    for name, shape, dt in specs:
        dram = nc.dram_tensor("c_" + name, shape, dt, kind="ExternalInput")
        sb, res = alloc_sb(ctx, nc, "s_" + name, shape, dt)
        h.const_dram[name] = dram
        setattr(h, name, sb)
        setattr(h, name + "_res", res)

def load_consts(em, h):
    for name in h.const_dram:
        dram = h.const_dram[name]
        sb = getattr(h, name)
        res = getattr(h, name + "_res")
        em.d("sk", sb[:, :], dram[:, :], writes=[res], tag="const")



A_PHASES = {0, 1, 2, 3, 4}

def emit_blockA(em, h, ctx, nc, outer_ctx):
    import contextlib

    XHAT, XHAT_res = alloc_sb(outer_ctx, nc, "XHAT", [128, NT * 1024], BF16)
    WOUT, WOUT_res = alloc_sb(outer_ctx, nc, "WOUT", [128, 8 * 1024], BF16)
    BOUT, BOUT_res = alloc_sb(outer_ctx, nc, "BOUT", [1, 1024], BF16)
    SQ, SQ_res = alloc_sb(outer_ctx, nc, "SQln", [128, 1024], F32)
    COLS, COLS_res = alloc_sb(outer_ctx, nc, "COLS", [128, 16], F32)
    ln_scratch = {"sq": (SQ, SQ_res), "cols": (COLS, COLS_res)}
    QMX, QMX_res = alloc_sb(outer_ctx, nc, "QMX", [1, 16], F32)
    KMX, KMX_res = alloc_sb(outer_ctx, nc, "KMX", [1, 16], F32)
    NTMP, NTMP_res = alloc_sb(outer_ctx, nc, "NTMP", [1, 4], F32)
    em.c("dve", lambda v: v.memset(QMX[:, :], 0.0), writes=[QMX_res])
    em.c("dve", lambda v: v.memset(KMX[:, :], 0.0), writes=[KMX_res])

    PS0, PS0_res = alloc_ps(outer_ctx, nc, "PS0", [128, 1024], F32)
    PS1, PS1_res = alloc_ps(outer_ctx, nc, "PS1", [128, 1024], F32)
    PS2, PS2_res = alloc_ps(outer_ctx, nc, "PS2", [128, 1024], F32)
    PSt, _pstres = alloc_ps(outer_ctx, nc, "PSt", [128, 2048], BF16)
    PSt_res = [_pstres, Res("PSt1")]

    for f in range(NF):
        em.d("pw", WOUT[:, 1024 * f:1024 * (f + 1)], h.out_wT[128 * f:128 * (f + 1), :],
             writes=[WOUT_res], tag="woutld")
    em.d("pw", BOUT[:, :], h.out_b[:, :], writes=[BOUT_res])

    with contextlib.ExitStack() as ph12:
        XT, XT_res = alloc_sb(ph12, nc, "XT", [128, NF * 2048], BF16)
        WQKV, WQKV_res = alloc_sb(ph12, nc, "WQKV", [128, 8 * 3072], BF16)
        BQKV, BQKV_res = alloc_sb(ph12, nc, "BQKV", [1, 3072], BF16)
        XBH0, XBH0_res = alloc_sb(ph12, nc, "XBH0", [128, 4 * 1024], BF16)
        XBH1, XBH1_res = alloc_sb(ph12, nc, "XBH1", [128, 4 * 1024], BF16)
        XBT0, XBT0_res = alloc_sb(ph12, nc, "XBT0", [128, 8 * 512], BF16)
        XBT1, XBT1_res = alloc_sb(ph12, nc, "XBT1", [128, 8 * 512], BF16)
        KVE, KVE_res = alloc_sb(ph12, nc, "KVE", [128, 4 * 512], BF16)
        SQ2, SQ2_res = alloc_sb(ph12, nc, "SQ2", [128, 512], BF16)
        XBHs = [(XBH0, XBH0_res), (XBH1, XBH1_res)]
        XBTs = [(XBT0, XBT0_res), (XBT1, XBT1_res)]

        for kk in range(NF):
            em.d("pw", WQKV[:, 3072 * kk:3072 * (kk + 1)],
                 h.qkv_wT[128 * kk:128 * (kk + 1), :], writes=[WQKV_res], tag="wqkvld")
        em.d("pw", BQKV[:, :], h.qkv_b[:, :], writes=[BQKV_res])

        for t in range(NT):
            em.d("px", XHAT[:, 1024 * t:1024 * (t + 1)],
                 h.x_own[128 * t:128 * (t + 1), :], writes=[XHAT_res], tag="xload")
        for t in range(NT):
            for f in range(NF):
                src = XHAT[:, 1024 * t + 128 * f:1024 * t + 128 * (f + 1)]
                dst = XT[:, 2048 * f + 128 * t:2048 * f + 128 * (t + 1)]
                emit_transpose(em, h, src, dst, PSt, PSt_res, XHAT_res, XT_res)

        pspool = [(PS0, PS0_res), (PS1, PS1_res), (PS2, PS2_res)]
        pi = 0
        for f in range(NF if 1 in A_PHASES else 0):
            for s in range(4):
                ps, ps_res = pspool[pi % 3]; pi += 1
                pp = ps[:, 0:512]
                for kk in range(NF):
                    lhsT = WQKV[:, 3072 * kk + 128 * f:3072 * kk + 128 * (f + 1)]
                    rhs = XT[:, 2048 * kk + 512 * s:2048 * kk + 512 * (s + 1)]
                    em.c("pe", lambda t, o=pp, l=lhsT, r=rhs, st=(kk == 0): t.matmul(
                        o, l, r, start=st, stop=False),
                        reads=[WQKV_res, XT_res], writes=[ps_res], tag="qproj")
                brow = BQKV[0:1, 128 * f:128 * (f + 1)]
                em.c("pe", lambda t, o=pp, l=brow: t.matmul(
                    o, l, h.ones_row[0:1, :], start=False, stop=True),
                    reads=[BQKV_res, h.ones_row_res], writes=[ps_res], tag="qbias")
                qe = KVE[:, 0:512]
                em.c("act", lambda s_, o=qe, i=pp: s_.copy(o, i),
                     reads=[ps_res], writes=[KVE_res], tag="qevac")
                em.d("sq", h.Q_dram[128 * f:128 * (f + 1), 512 * s:512 * (s + 1)],
                     qe, reads=[KVE_res], writes=[h.Q_dram_res], tag="qstore")
                em.c("act", lambda s_, i=qe: s_.activation(SQ2[:, :], i, AF.Square),
                     reads=[KVE_res], writes=[SQ2_res], tag="qsq")
                for half, hh in ((h.halfA[:, :], 0), (h.halfB[:, :], 1)):
                    head = 2 * f + hh
                    pn = ps[0:1, 0:512]
                    em.c("pe", lambda t, l=half, o=pn: t.matmul(o, l, SQ2[:, :], start=True, stop=True),
                         reads=[SQ2_res, h.halfA_res, h.halfB_res], writes=[ps_res], tag="qn")
                    em.c("dve", lambda v, i=pn: v.reduce_max(NTMP[0:1, 0:1], i, axis=AX.X),
                         reads=[ps_res], writes=[NTMP_res], tag="qnev")
                    qmc = QMX[0:1, head:head + 1]
                    em.c("dve", lambda v, o=qmc: v.tensor_tensor(o, o, NTMP[0:1, 0:1], ALU.max),
                         reads=[NTMP_res, QMX_res], writes=[QMX_res], tag="qnmax")

        for s16 in range(16 if 2 in A_PHASES else 0):
            par = s16 % 2
            XBH, XBH_res = XBHs[par]
            XBT, XBT_res = XBTs[par]
            for j in range(4):
                tok0 = 512 * s16 + 128 * j
                em.d("px", XBH[:, 1024 * j:1024 * (j + 1)],
                     h.x_batch[tok0:tok0 + 128, :], writes=[XBH_res], tag="xbload")
            for j in range(4):
                for f in range(NF):
                    src = XBH[:, 1024 * j + 128 * f:1024 * j + 128 * (f + 1)]
                    dst = XBT[:, 512 * f + 128 * j:512 * f + 128 * (j + 1)]
                    emit_transpose(em, h, src, dst, PSt, PSt_res, XBH_res, XBT_res)
            for kvi, (dram, dram_res) in ((1, (h.K_dram, h.K_dram_res)),
                                          (2, (h.V_dram, h.V_dram_res))):
                for f in range(NF):
                    ps, ps_res = pspool[pi % 3]; pi += 1
                    pp = ps[:, 0:512]
                    outc = 1024 * kvi + 128 * f
                    for kk in range(NF):
                        lhsT = WQKV[:, 3072 * kk + outc:3072 * kk + outc + 128]
                        rhs = XBT[:, 512 * kk:512 * (kk + 1)]
                        em.c("pe", lambda t, o=pp, l=lhsT, r=rhs, st=(kk == 0): t.matmul(
                            o, l, r, start=st, stop=False),
                            reads=[WQKV_res, XBT_res], writes=[ps_res], tag="kvproj")
                    brow = BQKV[0:1, outc:outc + 128]
                    em.c("pe", lambda t, o=pp, l=brow: t.matmul(
                        o, l, h.ones_row[0:1, :], start=False, stop=True),
                        reads=[BQKV_res, h.ones_row_res], writes=[ps_res], tag="kvbias")
                    kve = KVE[:, 512 * (kvi - 1 + 2 * (f % 2)):512 * (kvi + 2 * (f % 2))]
                    em.c("act", lambda s_, o=kve, i=pp: s_.copy(o, i),
                         reads=[ps_res], writes=[KVE_res], tag="kvevac")
                    em.d("sk", dram[128 * f:128 * (f + 1), 512 * s16:512 * (s16 + 1)],
                         kve, reads=[KVE_res], writes=[dram_res], tag="kvstore")
                    if kvi == 1:
                        em.c("act", lambda s_, i=kve: s_.activation(SQ2[:, :], i, AF.Square),
                             reads=[KVE_res], writes=[SQ2_res], tag="ksq")
                        for half, hh in ((h.halfA[:, :], 0), (h.halfB[:, :], 1)):
                            head = 2 * f + hh
                            pn = ps[0:1, 0:512]
                            em.c("pe", lambda t, l=half, o=pn: t.matmul(o, l, SQ2[:, :],
                                                                        start=True, stop=True),
                                 reads=[SQ2_res, h.halfA_res, h.halfB_res],
                                 writes=[ps_res], tag="kn")
                            em.c("dve", lambda v, i=pn: v.reduce_max(NTMP[0:1, 1:2], i, axis=AX.X),
                                 reads=[ps_res], writes=[NTMP_res], tag="knev")
                            kmc = KMX[0:1, head:head + 1]
                            em.c("dve", lambda v, o=kmc: v.tensor_tensor(o, o, NTMP[0:1, 1:2], ALU.max),
                                 reads=[NTMP_res, KMX_res], writes=[KMX_res], tag="knmax")

    em.barrier()

    with contextlib.ExitStack() as ph34:
        ATT, ATT_res = alloc_sb(ph34, nc, "ATT", [128, 8 * 2048], BF16)
        KH0, KH0_res = alloc_sb(ph34, nc, "KH0", [64, 8192], BF16)
        KH1, KH1_res = alloc_sb(ph34, nc, "KH1", [64, 8192], BF16)
        VH0, VH0_res = alloc_sb(ph34, nc, "VH0", [64, 8192], BF16)
        VH1, VH1_res = alloc_sb(ph34, nc, "VH1", [64, 8192], BF16)
        QH0, QH0_res = alloc_sb(ph34, nc, "QH0", [64, 2048], BF16)
        QH1, QH1_res = alloc_sb(ph34, nc, "QH1", [64, 2048], BF16)
        VA, VA_res = alloc_sb(ph34, nc, "VA", [128, 64 * 65], BF16)
        PT0, PT0_res = alloc_sb(ph34, nc, "PTb0", [128, 1024], BF16)
        PT1, PT1_res = alloc_sb(ph34, nc, "PTb1", [128, 1024], BF16)
        RB, RB_res = alloc_sb(ph34, nc, "RB", [64, 1024], F32)
        RCP, RCP_res = alloc_sb(ph34, nc, "RCP", [1, 1024], F32)
        BCOL, BCOL_res = alloc_sb(ph34, nc, "BCOL", [128, 16], F32)
        SCL, SCL_res = alloc_sb(ph34, nc, "SCL", [1, 16], F32)
        KHs = [(KH0, KH0_res), (KH1, KH1_res)]
        VHs = [(VH0, VH0_res), (VH1, VH1_res)]
        QHs = [(QH0, QH0_res), (QH1, QH1_res)]
        PTs = [(PT0, PT0_res), (PT1, PT1_res)]

        em.c("pool", lambda g: g.memset(VA[:, :], 1.0), writes=[VA_res], tag="vaones")

        def issue_head_loads(hd):
            KH, KH_res = KHs[hd % 2]
            VH, VH_res = VHs[hd % 2]
            QH, QH_res = QHs[hd % 2]
            em.d("sk", KH[:, :], h.K_dram[64 * hd:64 * (hd + 1), :],
                 reads=[h.K_dram_res], writes=[KH_res], tag="khld")
            em.d("sk", VH[:, :], h.V_dram[64 * hd:64 * (hd + 1), :],
                 reads=[h.V_dram_res], writes=[VH_res], tag="vhld")
            em.d("sq", QH[:, :], h.Q_dram[64 * hd:64 * (hd + 1), :],
                 reads=[h.Q_dram_res], writes=[QH_res], tag="qhld")

        nheads = 16 if 3 in A_PHASES else 0
        if nheads:
            issue_head_loads(0)
            issue_head_loads(1)
        for hd in range(nheads):
            KH, KH_res = KHs[hd % 2]
            VH, VH_res = VHs[hd % 2]
            QH, QH_res = QHs[hd % 2]
            pr = SCL[0:1, 2:3]; mm_ = SCL[0:1, 3:4]
            em.c("dve", lambda v, a=QMX[0:1, hd:hd + 1], b=KMX[0:1, hd:hd + 1]:
                 v.tensor_tensor(pr, a, b, ALU.mult),
                 reads=[QMX_res, KMX_res], writes=[SCL_res])
            em.c("act", lambda s: s.activation(mm_, pr, AF.Ln),
                 reads=[SCL_res], writes=[SCL_res])
            em.c("act", lambda s: s.activation(mm_, mm_, AF.Exp, scale=0.5),
                 reads=[SCL_res], writes=[SCL_res])
            pb = PS1[0:128, 0:1]
            em.c("pe", lambda t: t.matmul(pb, h.ones1x128f[0:1, :], mm_, start=True, stop=True),
                 reads=[SCL_res, h.ones1x128f_res], writes=[PS1_res], tag="bcolmm")
            bcol = BCOL[:, hd:hd + 1]
            em.c("act", lambda s, o=bcol: s.activation(o, pb, AF.Copy, scale=-0.125),
                 reads=[PS1_res], writes=[BCOL_res], tag="bcol")

            for kt in range(64):
                src = VH[:, 128 * kt:128 * (kt + 1)]
                dst = VA[:, 65 * kt:65 * kt + 64]
                emit_transpose(em, h, src, dst, PSt, PSt_res, VH_res, VA_res)

            for qh in range(2):
                PA, PA_res = PS0, PS0_res

                def emit_pv(kt):
                    PT, PT_res = PTs[kt % 2]
                    va = VA[:, 65 * kt:65 * (kt + 1)]
                    for j in range(2):
                        em.c("pe", lambda t, o=PA[0:65, 512 * j:512 * (j + 1)], l=va,
                             r=PT[:, 512 * j:512 * (j + 1)], st=(kt == 0), sp=(kt == 63):
                             t.matmul(o, l, r, start=st, stop=sp),
                             reads=[VA_res, PT_res], writes=[PA_res], tag="pv")

                for kt in range(64):
                    PB, PB_res = (PS1, PS1_res) if kt % 2 == 0 else (PS2, PS2_res)
                    PT, PT_res = PTs[kt % 2]
                    lhsT = KH[:, 128 * kt:128 * (kt + 1)]
                    for j in range(2):
                        rhs = QH[:, 1024 * qh + 512 * j:1024 * qh + 512 * (j + 1)]
                        em.c("pe", lambda t, o=PB[:, 512 * j:512 * (j + 1)], l=lhsT, r=rhs:
                             t.matmul(o, l, r, start=True, stop=True),
                             reads=[KH_res, QH_res], writes=[PB_res], tag="scores")
                    em.c("act", lambda s, o=PT[:, :], i=PB[:, :], b=BCOL[:, hd:hd + 1]:
                         s.activation(o, i, AF.Exp, bias=b, scale=0.125),
                         reads=[PB_res, BCOL_res], writes=[PT_res], tag="exp")
                    if kt > 0:
                        emit_pv(kt - 1)
                emit_pv(63)
                em.c("dve", lambda v: v.reciprocal(RCP[0:1, :], PA[64:65, :]),
                     reads=[PA_res], writes=[RCP_res], tag="recip")
                for j in range(2):
                    PBX, PBX_res = (PS1, PS1_res) if j == 0 else (PS2, PS2_res)
                    pbc = PBX[0:64, 0:512]
                    em.c("pe", lambda t, r=RCP[0:1, 512 * j:512 * (j + 1)], o=pbc:
                         t.matmul(o, h.ones1x64f[0:1, :], r, start=True, stop=True),
                         reads=[RCP_res, h.ones1x64f_res], writes=[PBX_res], tag="rbmm")
                    em.c("dve", lambda v, o=RB[:, 512 * j:512 * (j + 1)], i=pbc: v.tensor_copy(o, i),
                         reads=[PBX_res], writes=[RB_res], tag="rbev")
                attd = ATT[64 * (hd % 2):64 * (hd % 2) + 64,
                           2048 * (hd // 2) + 1024 * qh:2048 * (hd // 2) + 1024 * (qh + 1)]
                em.c("dve", lambda v, o=attd: v.tensor_tensor(o, PA[0:64, :], RB[:, :], ALU.mult),
                     reads=[PA_res, RB_res], writes=[ATT_res], tag="attnorm")
            if hd + 2 < nheads:
                issue_head_loads(hd + 2)

        T1, T1_res = alloc_sb(ph34, nc, "T1", [128, 1024], F32)
        XO0, XO0_res = alloc_sb(ph34, nc, "XO0", [128, 1024], F32)
        XO1, XO1_res = alloc_sb(ph34, nc, "XO1", [128, 1024], F32)
        XOs = [(XO0, XO0_res), (XO1, XO1_res)]
        for tt in range(NT if 4 in A_PHASES else 0):
            ps, ps_res = (PS1, PS1_res) if tt % 2 == 0 else (PS2, PS2_res)
            for f in range(NF):
                lhsT = ATT[:, 2048 * f + 128 * tt:2048 * f + 128 * (tt + 1)]
                for j in range(2):
                    rhs = WOUT[:, 1024 * f + 512 * j:1024 * f + 512 * (j + 1)]
                    em.c("pe", lambda t, o=ps[:, 512 * j:512 * (j + 1)], l=lhsT, r=rhs,
                         st=(f == 0): t.matmul(o, l, r, start=st, stop=False),
                         reads=[ATT_res, WOUT_res], writes=[ps_res], tag="oproj")
            for j in range(2):
                em.c("pe", lambda t, o=ps[:, 512 * j:512 * (j + 1)],
                     r=BOUT[0:1, 512 * j:512 * (j + 1)]: t.matmul(
                     o, h.ones_row[0:1, 0:128], r, start=False, stop=True),
                     reads=[BOUT_res, h.ones_row_res], writes=[ps_res], tag="obias")
            xh = XHAT[:, 1024 * tt:1024 * (tt + 1)]
            em.c("dve", lambda v, i=ps[:, :], x=xh: v.tensor_tensor(T1[:, :], i, x, ALU.add),
                 reads=[ps_res, XHAT_res], writes=[T1_res], tag="resid")
            XO, XO_res = XOs[tt % 2]
            emit_ln(em, h, T1[:, :], T1_res, [(XO[:, :], XO_res)], ln_scratch, tag="lnA")
            em.d("so", h.xa[128 * tt:128 * (tt + 1), :], XO[:, :],
                 reads=[XO_res], writes=[h.xa_res], tag="xastore")
        if 4 not in A_PHASES:
            for tt in range(NT):
                em.d("px", h.xa[128 * tt:128 * (tt + 1), :],
                     XHAT[:, 1024 * tt:1024 * (tt + 1)],
                     reads=[XHAT_res], writes=[h.xa_res], tag="xadump")


def build_A(core):
    import contextlib
    nc = bass.Bass("TRN2", target_bir_lowering=False)
    h = H()
    h.x_own = nc.dram_tensor("x_own", [TOK, D], F32, kind="ExternalInput")
    h.x_batch = nc.dram_tensor("x_batch", [SEQ, D], F32, kind="ExternalInput")
    h.qkv_wT = nc.dram_tensor("qkv_wT", [D, 3 * D], F32, kind="ExternalInput")
    h.qkv_b = nc.dram_tensor("qkv_b", [1, 3 * D], F32, kind="ExternalInput")
    h.out_wT = nc.dram_tensor("out_wT", [D, D], F32, kind="ExternalInput")
    h.out_b = nc.dram_tensor("out_b", [1, D], F32, kind="ExternalInput")
    h.xa = nc.dram_tensor("xa", [TOK, D], F32, kind="ExternalOutput")
    h.xa_res = Res("xa")
    h.K_dram = nc.dram_tensor("K_dram", [D, SEQ], BF16)
    h.V_dram = nc.dram_tensor("V_dram", [D, SEQ], BF16)
    h.Q_dram = nc.dram_tensor("Q_dram", [D, TOK], BF16)
    h.K_dram_res = Res("K_dram")
    h.V_dram_res = Res("V_dram")
    h.Q_dram_res = Res("Q_dram")

    chans = {"pw": "pool", "px": "pool", "sk": "sp", "sq": "sp", "so": "sp"}
    em = Em(nc, chans)
    with contextlib.ExitStack() as outer:
        declare_consts(nc, outer, h)
        load_consts(em, h)
        emit_blockA(em, h, None, nc, outer)
        em.barrier()
        with nc.Block() as block:
            em.finalize_and_emit(block, None)
    return nc


def prep_A_inputs(inputs, core):
    b = core // 4
    q0 = 2048 * (core % 4)
    x = np.asarray(inputs["L0_attn_x"])
    cc = host_consts()
    m = {
        "x_own": np.ascontiguousarray(x[b, q0:q0 + 2048, :]),
        "x_batch": np.ascontiguousarray(x[b]),
        "qkv_wT": np.ascontiguousarray(np.asarray(inputs["L0_attn_qkv_w"]).T),
        "qkv_b": np.asarray(inputs["L0_attn_qkv_b"]).reshape(1, -1),
        "out_wT": np.ascontiguousarray(np.asarray(inputs["L0_attn_out_w"]).T),
        "out_b": np.asarray(inputs["L0_attn_out_b"]).reshape(1, -1),
    }
    for k, v in cc.items():
        m["c_" + k] = v
    return m



NE = 8
NI = 22
NIS = 11
B_SLAB = 512
NSL = TOK // B_SLAB
NTT = B_SLAB // 128


def emit_blockB(em, h, nc, outer_ctx):
    import contextlib
    XTOK, XTOK_res = alloc_sb(outer_ctx, nc, "XTOK", [128, NT * 1024], BF16)
    XT, XT_res = alloc_sb(outer_ctx, nc, "XT_B", [128, NF * 2048], BF16)
    SQ, SQ_res = alloc_sb(outer_ctx, nc, "SQln", [128, 1024], F32)
    COLS, COLS_res = alloc_sb(outer_ctx, nc, "COLS", [128, 16], F32)
    ln_scratch = {"sq": (SQ, SQ_res), "cols": (COLS, COLS_res)}

    PS0, PS0_res = alloc_ps(outer_ctx, nc, "PS0", [128, 1024], F32)
    PS1, PS1_res = alloc_ps(outer_ctx, nc, "PS1", [128, 1024], F32)
    PS2, PS2_res = alloc_ps(outer_ctx, nc, "PS2", [128, 1024], F32)
    PSt, _pstres = alloc_ps(outer_ctx, nc, "PSt", [128, 2048], BF16)
    PSt_res = [_pstres, Res("PSt1")]

    ACC, ACC_res = alloc_sb(outer_ctx, nc, "ACC", [128, NTT * 1024], F32)
    AES, AES_res = alloc_sb(outer_ctx, nc, "AES", [128, NI * B_SLAB], BF16)
    WGB0, WGB0_res = alloc_sb(outer_ctx, nc, "WGB0", [128, 2048], BF16)
    WGB1, WGB1_res = alloc_sb(outer_ctx, nc, "WGB1", [128, 2048], BF16)
    WGB2, WGB2_res = alloc_sb(outer_ctx, nc, "WGB2", [128, 2048], BF16)
    WGB3, WGB3_res = alloc_sb(outer_ctx, nc, "WGB3", [128, 2048], BF16)
    WDE, WDE_res = alloc_sb(outer_ctx, nc, "WDE", [128, NI * 1024], BF16)
    SDB, SDB_res = alloc_sb(outer_ctx, nc, "SDB", [128, NIS * 1024], BF16)
    GWB, GWB_res = alloc_sb(outer_ctx, nc, "GWB", [128, 64], BF16)
    LG, LG_res = alloc_sb(outer_ctx, nc, "LG", [128, 8], F32)
    GCOL, GCOL_res = alloc_sb(outer_ctx, nc, "GCOL", [128, 8], F32)
    CMB, CMB_res = alloc_sb(outer_ctx, nc, "CMB", [128, NT * 8], F32)
    TMP, TMP_res = alloc_sb(outer_ctx, nc, "TMPB", [128, 512], BF16)
    TMPb, TMPb_res = alloc_sb(outer_ctx, nc, "TMPBb", [128, 512], BF16)
    TMPs = [(TMP, TMP_res), (TMPb, TMPb_res)]
    T1, T1_res = alloc_sb(outer_ctx, nc, "T1", [128, 1024], F32)
    SAS, SAS_res = AES, AES_res
    XO0, XO0_res = alloc_sb(outer_ctx, nc, "XO0", [128, 1024], F32)
    XO1, XO1_res = alloc_sb(outer_ctx, nc, "XO1", [128, 1024], F32)
    XOs = [(XO0, XO0_res), (XO1, XO1_res)]
    WGBs = [(WGB0, WGB0_res), (WGB1, WGB1_res), (WGB2, WGB2_res), (WGB3, WGB3_res)]
    WCH = ["p0", "p1", "p2", "p3"]

    for t in range(NT):
        em.d(["px", "p0"], XTOK[:, 1024 * t:1024 * (t + 1)],
             h.xb_in[128 * t:128 * (t + 1), :], reads=[h.xb_in_res],
             writes=[XTOK_res], tag="xload")
    for t in range(NT):
        for f in range(NF):
            src = XTOK[:, 1024 * t + 128 * f:1024 * t + 128 * (f + 1)]
            dst = XT[:, 2048 * f + 128 * t:2048 * f + 128 * (t + 1)]
            emit_transpose(em, h, src, dst, PSt, PSt_res, XTOK_res, XT_res)
    for kk in range(NF):
        em.d("p1", GWB[:, 8 * kk:8 * (kk + 1)], h.gate_w[128 * kk:128 * (kk + 1), :],
             writes=[GWB_res], tag="gwld")
    for i in range(NIS):
        em.d("p2", SDB[:, 1024 * i:1024 * (i + 1)], h.Sd[128 * i:128 * (i + 1), :],
             writes=[SDB_res])

    for tt in range(NT):
        pg = PS0[:, 0:8]
        for kk in range(NF):
            lhsT = XT[:, 2048 * kk + 128 * tt:2048 * kk + 128 * (tt + 1)]
            gr = GWB[:, 8 * kk:8 * (kk + 1)]
            em.c("pe", lambda t_, l=lhsT, r=gr, st=(kk == 0), sp=(kk == NF - 1):
                 t_.matmul(pg, l, r, start=st, stop=sp),
                 reads=[XT_res, GWB_res], writes=[PS0_res], tag="gate")
        em.c("dve", lambda v: v.tensor_copy(LG[:, :], pg),
             reads=[PS0_res], writes=[LG_res], tag="gateev")
        c0 = COLS[:, 8:9]; c1 = COLS[:, 9:10]; c2 = COLS[:, 10:11]
        c3 = COLS[:, 11:12]; c4 = COLS[:, 12:13]
        em.c("dve", lambda v: v.reduce_max(c0, LG[:, :], axis=AX.X),
             reads=[LG_res], writes=[COLS_res], tag="g_v1")
        em.c("dve", lambda v: v.tensor_scalar(GCOL[:, :], LG[:, :], c0, None, ALU.is_equal),
             reads=[LG_res, COLS_res], writes=[GCOL_res], tag="g_m1")
        em.c("dve", lambda v: v.scalar_tensor_tensor(LG[:, :], GCOL[:, :], -1e30, LG[:, :],
                                                     ALU.mult, ALU.add),
             reads=[GCOL_res, LG_res], writes=[LG_res], tag="g_l2")
        em.c("dve", lambda v: v.reduce_max(c1, LG[:, :], axis=AX.X),
             reads=[LG_res], writes=[COLS_res], tag="g_v2")
        em.c("dve", lambda v: v.tensor_tensor(c2, c1, c0, ALU.subtract),
             reads=[COLS_res], writes=[COLS_res], tag="g_d21")
        em.c("act", lambda s: s.activation(c2, c2, AF.Exp),
             reads=[COLS_res], writes=[COLS_res], tag="g_exp")
        em.c("dve", lambda v: v.tensor_scalar(c2, c2, 1.0, None, ALU.add),
             reads=[COLS_res], writes=[COLS_res], tag="g_1p")
        em.c("dve", lambda v: v.reciprocal(c3, c2),
             reads=[COLS_res], writes=[COLS_res], tag="g_p1")
        em.c("dve", lambda v: v.tensor_scalar(c4, c3, -1.0, 1.0, ALU.mult, ALU.add),
             reads=[COLS_res], writes=[COLS_res], tag="g_p2")
        em.c("dve", lambda v: v.tensor_scalar(LG[:, :], LG[:, :], c1, None, ALU.is_equal),
             reads=[LG_res, COLS_res], writes=[LG_res], tag="g_m2")
        cmb = CMB[:, 8 * tt:8 * (tt + 1)]
        em.c("dve", lambda v, o=cmb: v.tensor_scalar(o, GCOL[:, :], c3, None, ALU.mult),
             reads=[GCOL_res, COLS_res], writes=[CMB_res], tag="g_c1")
        em.c("dve", lambda v, o=cmb: v.scalar_tensor_tensor(o, LG[:, :], c4, o,
                                                            ALU.mult, ALU.add),
             reads=[LG_res, COLS_res, CMB_res], writes=[CMB_res], tag="g_c2")

    first_wde_load = [True]

    def load_wde(e):
        for i in range(NI):
            em.d("px", WDE[:, 1024 * i:1024 * (i + 1)],
                 h.Wd[e, 128 * i:128 * (i + 1), :], writes=[WDE_res], tag="wdld")

    for s in range(NSL):
        for i in range(NIS):
            WGB, WGB_res_ = WGBs[i % 4]
            em.d(WCH[i % 4], WGB[:, :], h.SGU_pre[i],
                 writes=[WGB_res_], tag="sgld")
            PSp, PSp_res = (PS0, PS0_res) if i % 2 == 0 else (PS1, PS1_res)
            for which in (0, 1):
                ps = PSp[:, 512 * which:512 * (which + 1)]
                for kk in range(NF):
                    lhsT = WGB[:, 1024 * which + 128 * kk:1024 * which + 128 * (kk + 1)]
                    rhs = XT[:, 2048 * kk + B_SLAB * s:2048 * kk + B_SLAB * (s + 1)]
                    em.c("pe", lambda t_, o=ps, l=lhsT, r=rhs, st=(kk == 0),
                         sp=(kk == NF - 1): t_.matmul(o, l, r, start=st, stop=sp),
                         reads=[WGB_res_, XT_res], writes=[PSp_res], tag="sgu")
            em.c("act", lambda s_, i_=PSp[:, 0:512], tp=TMPs[i % 2]: s_.activation(
                tp[0][:, :], i_, AF.Silu), reads=[PSp_res], writes=[TMPs[i % 2][1]],
                tag="ssilu")
            dst = SAS[:, B_SLAB * i:B_SLAB * (i + 1)]
            em.c("dve", lambda v, o=dst, tp=TMPs[i % 2], i_=PSp[:, 512:1024]:
                 v.tensor_tensor(o, tp[0][:, :], i_, ALU.mult),
                 reads=[TMPs[i % 2][1], PSp_res], writes=[SAS_res], tag="smul")
        for tt in range(NTT):
            ps, ps_res = (PS2, PS2_res) if tt % 2 == 0 else (PS0, PS0_res)
            for i in range(NIS):
                lhsT = SAS[:, B_SLAB * i + 128 * tt:B_SLAB * i + 128 * (tt + 1)]
                for j in range(2):
                    rhs = SDB[:, 1024 * i + 512 * j:1024 * i + 512 * (j + 1)]
                    em.c("pe", lambda t_, o=ps[:, 512 * j:512 * (j + 1)], l=lhsT, r=rhs,
                         st=(i == 0), sp=(i == NIS - 1): t_.matmul(o, l, r, start=st, stop=sp),
                         reads=[SAS_res, SDB_res], writes=[ps_res], tag="sd")
            acc = ACC[:, 1024 * tt:1024 * (tt + 1)]
            em.c("act", lambda s_, o=acc, i_=ps[:, :]: s_.copy(o, i_),
                 reads=[ps_res], writes=[ACC_res], tag="accinit")

        for e in range(NE):
            if first_wde_load[0]:
                load_wde(e)
                first_wde_load[0] = False
            for i in range(NI):
                WGB, WGB_res_ = WGBs[i % 4]
                em.d(WCH[i % 4], WGB[:, :], h.WGU_pre[e, i],
                     writes=[WGB_res_], tag="wgld")
                PSp, PSp_res = (PS0, PS0_res) if i % 2 == 0 else (PS1, PS1_res)
                for which in (0, 1):
                    ps = PSp[:, 512 * which:512 * (which + 1)]
                    for kk in range(NF):
                        lhsT = WGB[:, 1024 * which + 128 * kk:1024 * which + 128 * (kk + 1)]
                        rhs = XT[:, 2048 * kk + B_SLAB * s:2048 * kk + B_SLAB * (s + 1)]
                        em.c("pe", lambda t_, o=ps, l=lhsT, r=rhs, st=(kk == 0),
                             sp=(kk == NF - 1): t_.matmul(o, l, r, start=st, stop=sp),
                             reads=[WGB_res_, XT_res], writes=[PSp_res], tag="gu")
                em.c("act", lambda s_, i_=PSp[:, 0:512], tp=TMPs[i % 2]: s_.activation(
                    tp[0][:, :], i_, AF.Silu), reads=[PSp_res], writes=[TMPs[i % 2][1]],
                    tag="esilu")
                dst = AES[:, B_SLAB * i:B_SLAB * (i + 1)]
                em.c("dve", lambda v, o=dst, tp=TMPs[i % 2], i_=PSp[:, 512:1024]:
                     v.tensor_tensor(o, tp[0][:, :], i_, ALU.mult),
                     reads=[TMPs[i % 2][1], PSp_res], writes=[AES_res], tag="emul")
            for tt in range(NTT):
                ps, ps_res = (PS2, PS2_res) if tt % 2 == 0 else (PS1, PS1_res)
                for i in range(NI):
                    lhsT = AES[:, B_SLAB * i + 128 * tt:B_SLAB * i + 128 * (tt + 1)]
                    for j in range(2):
                        rhs = WDE[:, 1024 * i + 512 * j:1024 * i + 512 * (j + 1)]
                        em.c("pe", lambda t_, o=ps[:, 512 * j:512 * (j + 1)], l=lhsT, r=rhs,
                             st=(i == 0), sp=(i == NI - 1): t_.matmul(o, l, r, start=st,
                                                                      stop=sp),
                             reads=[AES_res, WDE_res], writes=[ps_res], tag="d")
                gtt = NTT * s + tt
                ccol = CMB[:, 8 * gtt + e:8 * gtt + e + 1]
                acc = ACC[:, 1024 * tt:1024 * (tt + 1)]
                em.c("dve", lambda v, o=acc, i_=ps[:, :], sc=ccol: v.scalar_tensor_tensor(
                    o, i_, sc, o, ALU.mult, ALU.add),
                    reads=[ps_res, CMB_res, ACC_res], writes=[ACC_res], tag="dmerge")
            if e + 1 < NE:
                load_wde(e + 1)
        first_wde_load[0] = True

        for tt in range(NTT):
            gtt = NTT * s + tt
            xtk = XTOK[:, 1024 * gtt:1024 * (gtt + 1)]
            acc = ACC[:, 1024 * tt:1024 * (tt + 1)]
            em.c("dve", lambda v, x=xtk, a=acc: v.tensor_tensor(T1[:, :], a, x, ALU.add),
                 reads=[ACC_res, XTOK_res], writes=[T1_res], tag="bresid")
            XO, XO_res = XOs[tt % 2]
            emit_ln(em, h, T1[:, :], T1_res, [(XO[:, :], XO_res)], ln_scratch, tag="lnB")
            em.d("so", h.xb[128 * gtt:128 * (gtt + 1), :], XO[:, :],
                 reads=[XO_res], writes=[h.xb_res], tag="xbstore")
            if getattr(h, "xb_bf", None) is not None:
                em.d("p2", h.xb_bf[128 * gtt:128 * (gtt + 1), :], XO[:, :],
                     reads=[XO_res], writes=[h.xb_bf_res], tag="xbbfstore")


def build_B():
    import contextlib
    nc = bass.Bass("TRN2", target_bir_lowering=False)
    h = H()
    h.xb_in = nc.dram_tensor("xb_in", [TOK, D], F32, kind="ExternalInput")
    h.gate_w = nc.dram_tensor("gate_w", [D, NE], F32, kind="ExternalInput")
    h.WGU_pre = nc.dram_tensor("WGU_pre", [NE, NI, 128, 2048], F32, kind="ExternalInput")
    h.Wd = nc.dram_tensor("Wd", [NE, 2816, 1024], F32, kind="ExternalInput")
    h.SGU_pre = nc.dram_tensor("SGU_pre", [NIS, 128, 2048], F32, kind="ExternalInput")
    h.Sd = nc.dram_tensor("Sd", [1408, 1024], F32, kind="ExternalInput")
    h.xb = nc.dram_tensor("xb", [TOK, D], F32, kind="ExternalOutput")
    h.xb_res = Res("xb")
    h.xb_in_res = Res("xb_in")

    chans = {"p0": "pool", "p1": "pool", "p2": "pool", "p3": "pool", "px": "pool",
             "sk": "sp", "so": "sp"}
    em = Em(nc, chans)
    with contextlib.ExitStack() as outer:
        declare_consts(nc, outer, h)
        load_consts(em, h)
        emit_blockB(em, h, nc, outer)
        em.barrier()
        with nc.Block() as block:
            em.finalize_and_emit(block, None)
    return nc


def prep_B_inputs(inputs, core, xa_full):
    b = core // 4
    q0 = 2048 * (core % 4)
    cc = host_consts()
    Wg = np.asarray(inputs["L0_ffn_Wg"])
    Wu = np.asarray(inputs["L0_ffn_Wu"])
    Sg = np.asarray(inputs["L0_ffn_Sg"])
    Su = np.asarray(inputs["L0_ffn_Su"])
    pre = lambda W: np.ascontiguousarray(
        W.reshape(NE, 8, 128, NI, 128).transpose(0, 3, 2, 1, 4).reshape(NE, NI, 128, 1024))
    pres = lambda W: np.ascontiguousarray(
        W.reshape(8, 128, NIS, 128).transpose(2, 1, 0, 3).reshape(NIS, 128, 1024))
    m = {
        "xb_in": np.ascontiguousarray(xa_full[b, q0:q0 + 2048, :]),
        "gate_w": np.asarray(inputs["L0_ffn_gate_w"]),
        "WGU_pre": np.ascontiguousarray(np.concatenate([pre(Wg), pre(Wu)], axis=3)),
        "Wd": np.asarray(inputs["L0_ffn_Wd"]),
        "SGU_pre": np.ascontiguousarray(np.concatenate([pres(Sg), pres(Su)], axis=2)),
        "Sd": np.asarray(inputs["L0_ffn_Sd"]),
    }
    for k, v in cc.items():
        m["c_" + k] = v
    return m



NH = 16
HD = 64
NPRE = 48
C_SLAB = 512
NIC = 22
GAMMAS = [1.0 - 2.0 ** (-5.0 - hh) for hh in range(NH)]


def emit_ln_small(em, h, src_ap, src_res, n, outs, scratch, tag="gn"):
    sq, sq_res = scratch["sq"]
    cols, cols_res = scratch["cols"]
    ssum = cols[:, 0:1]; ssq = cols[:, 1:2]; mean = cols[:, 2:3]
    varr = cols[:, 3:4]; rstd = cols[:, 4:5]; negmr = cols[:, 5:6]; tmp = cols[:, 6:7]
    em.c("dve", lambda v: v.reduce_sum(ssum, src_ap, axis=AX.X),
         reads=[src_res], writes=[cols_res], tag=tag)
    em.c("act", lambda s: s.activation(sq[:, 0:n], src_ap, AF.Square, accum_out=ssq),
         reads=[src_res], writes=[sq_res, cols_res], tag=tag)
    em.c("dve", lambda v: v.tensor_scalar(mean, ssum, 1.0 / n, None, ALU.mult),
         reads=[cols_res], writes=[cols_res], tag=tag)
    em.c("dve", lambda v: v.tensor_scalar(varr, ssq, 1.0 / n, EPS, ALU.mult, ALU.add),
         reads=[cols_res], writes=[cols_res], tag=tag)
    em.c("dve", lambda v: v.tensor_tensor(tmp, mean, mean, ALU.mult),
         reads=[cols_res], writes=[cols_res], tag=tag)
    em.c("dve", lambda v: v.tensor_tensor(varr, varr, tmp, ALU.subtract),
         reads=[cols_res], writes=[cols_res], tag=tag)
    em.c("act", lambda s: s.activation(rstd, varr, AF.Ln),
         reads=[cols_res], writes=[cols_res], tag=tag)
    em.c("act", lambda s: s.activation(rstd, rstd, AF.Exp, scale=-0.5),
         reads=[cols_res], writes=[cols_res], tag=tag)
    em.c("dve", lambda v: v.scalar_tensor_tensor(negmr, mean, -1.0, rstd, ALU.mult, ALU.mult),
         reads=[cols_res], writes=[cols_res], tag=tag)
    for dst_ap, dst_res in outs:
        em.c("dve", lambda v, o=dst_ap: v.tensor_scalar(o, src_ap, rstd, negmr,
                                                        ALU.mult, ALU.add),
             reads=[src_res, cols_res], writes=[dst_res], tag=tag + "_app")


def emit_blockC(em, h, nc, outer_ctx):
    import contextlib
    XTOK, XTOK_res = alloc_sb(outer_ctx, nc, "XTOKC", [128, NT * 1024], BF16)
    XNA, XNA_res = alloc_sb(outer_ctx, nc, "XNA", [128, NT * 1024], BF16)
    SQ, SQ_res = alloc_sb(outer_ctx, nc, "SQln", [128, 1024], F32)
    COLS, COLS_res = alloc_sb(outer_ctx, nc, "COLS", [128, 16], F32)
    ln_scratch = {"sq": (SQ, SQ_res), "cols": (COLS, COLS_res)}
    T1, T1_res = alloc_sb(outer_ctx, nc, "T1", [128, 1024], F32)
    XO0, XO0_res = alloc_sb(outer_ctx, nc, "XO0", [128, 1024], F32)
    XOs = [(XO0, XO0_res), (XO0, XO0_res)]
    BCT, BCT_res = alloc_sb(outer_ctx, nc, "BCT", [128, 4 * 1024], BF16)
    GQ, GQ_res = alloc_sb(outer_ctx, nc, "GQ", [128, NH], F32)
    GKN, GKN_res = alloc_sb(outer_ctx, nc, "GKN", [128, NH], F32)
    GKS, GKS_res = alloc_sb(outer_ctx, nc, "GKS", [128, NH], F32)
    GKSP, GKSP_res = alloc_sb(outer_ctx, nc, "GKSP", [128, NPRE * NH], F32)
    GDEC, GDEC_res = alloc_sb(outer_ctx, nc, "GDEC", [64, NPRE * NH], F32)
    MASKT, MASKT_res = alloc_sb(outer_ctx, nc, "MASKT", [128, 128], BF16)
    SST, SST_res = alloc_sb(outer_ctx, nc, "SST", [64, NH * HD], F32)
    SSTB, SSTB_res = alloc_sb(outer_ctx, nc, "SSTB", [64, NH * HD], BF16)
    BR, BR_res = alloc_sb(outer_ctx, nc, "BR", [1, 5 * 1024], BF16)

    PS0, PS0_res = alloc_ps(outer_ctx, nc, "PS0", [128, 1024], F32)
    PS1, PS1_res = alloc_ps(outer_ctx, nc, "PS1", [128, 1024], F32)
    PSC, PSC_res = alloc_ps(outer_ctx, nc, "PSC", [128, 512], F32)
    PSO, PSO_res = alloc_ps(outer_ctx, nc, "PSO", [128, 512], F32)
    PSS, PSS_res = PSC, PSC_res
    PSt, _pstres = alloc_ps(outer_ctx, nc, "PSt", [128, 2048], BF16)
    PSt_res = [_pstres, Res("PSt1")]

    for t, d, r in [(GQ, h.GQ_d, GQ_res), (GKN, h.GKN_d, GKN_res), (GKS, h.GKS_d, GKS_res),
                    (GKSP, h.GKSP_d, GKSP_res), (GDEC, h.GDEC_d, GDEC_res),
                    (MASKT, h.MASKT_d, MASKT_res)]:
        em.d("sk", t[:, :], d[:, :], writes=[r], tag="gld")
    off = 0
    bias_off = {}
    for nm, dd, ln in [("bq", h.bq, 1024), ("bk", h.bk, 1024), ("bv", h.bv, 1024),
                       ("bg", h.bg, 1024), ("bo", h.bo, 1024)]:
        em.d("p1", BR[0:1, off:off + ln], dd[:, :], writes=[BR_res], tag="bld")
        bias_off[nm] = off
        off += ln
    NRMR, NRMR_res = alloc_sb(outer_ctx, nc, "NRMR", [1, 1024], F32)
    for j, dd in enumerate([h.norm1_w, h.norm1_b, h.gn_w, h.gn_b]):
        em.d("sk", NRMR[0:1, :], dd[:, :], reads=[NRMR_res], writes=[NRMR_res])
        for jj in range(2):
            pb = PS0[:, 0:512]
            em.c("pe", lambda t, r=NRMR[0:1, 512 * jj:512 * (jj + 1)]:
                 t.matmul(pb, h.ones1x128f[0:1, :], r, start=True, stop=True),
                 reads=[NRMR_res, h.ones1x128f_res], writes=[PS0_res], tag="bcmm")
            dst = BCT[:, 1024 * j + 512 * jj:1024 * j + 512 * (jj + 1)]
            em.c("dve", lambda v, o=dst: v.tensor_copy(o, pb),
                 reads=[PS0_res], writes=[BCT_res], tag="bcev")
    W1B = BCT[:, 0:1024]; B1B = BCT[:, 1024:2048]
    GNW = BCT[:, 2048:3072]; GNB = BCT[:, 3072:4096]

    em.c("dve", lambda v: v.memset(SST[:, :], 0.0), writes=[SST_res])
    em.c("dve", lambda v: v.memset(SSTB[:, :], 0.0), writes=[SSTB_res])

    for t in range(NT):
        em.d(["px", "p0"], XTOK[:, 1024 * t:1024 * (t + 1)],
             h.xc_in[128 * t:128 * (t + 1), :], reads=[h.xc_in_res],
             writes=[XTOK_res], tag="xload")

    for tt in range(NT):
        src = XTOK[:, 1024 * tt:1024 * (tt + 1)]
        emit_ln(em, h, src, XTOK_res, [(T1[:, :], T1_res)], ln_scratch, tag="ln1")
        em.c("dve", lambda v: v.tensor_tensor(T1[:, :], T1[:, :], W1B, ALU.mult),
             reads=[T1_res, BCT_res], writes=[T1_res], tag="xnw")
        dst = XNA[:, 1024 * tt:1024 * (tt + 1)]
        em.c("dve", lambda v, o=dst: v.tensor_tensor(o, T1[:, :], B1B, ALU.add),
             reads=[T1_res, BCT_res], writes=[XNA_res], tag="xnb")

    with contextlib.ExitStack() as ph2:
        WQB, WQB_res = alloc_sb(ph2, nc, "WQB", [128, 8 * 1024], BF16)
        WKB, WKB_res = alloc_sb(ph2, nc, "WKB", [128, 8 * 1024], BF16)
        WVB, WVB_res = alloc_sb(ph2, nc, "WVB", [128, 8 * 1024], BF16)
        WGB_, WGB__res = alloc_sb(ph2, nc, "WGB_", [128, 8 * 1024], BF16)
        WOB, WOB_res = alloc_sb(ph2, nc, "WOB", [128, 8 * 1024], BF16)
        XNC, XNC_res = alloc_sb(ph2, nc, "XNC", [128, 8 * 128], BF16)
        QTC, QTC_res = alloc_sb(ph2, nc, "QTC", [128, 1024], BF16)
        KTC, KTC_res = alloc_sb(ph2, nc, "KTC", [128, 1024], BF16)
        VTC, VTC_res = alloc_sb(ph2, nc, "VTC", [128, 1024], BF16)
        GTC, GTC_res = alloc_sb(ph2, nc, "GTC", [128, 1024], BF16)
        QS, QS_res = alloc_sb(ph2, nc, "QS", [128, 64], BF16)
        KS1, KS1_res = alloc_sb(ph2, nc, "KS1", [128, 64], BF16)
        KS2, KS2_res = alloc_sb(ph2, nc, "KS2", [128, 64], BF16)
        QSF, QSF_res = alloc_sb(ph2, nc, "QSF", [64, 128], BF16)
        KSF, KSF_res = alloc_sb(ph2, nc, "KSF", [64, 128], BF16)
        AIN, AIN_res = alloc_sb(ph2, nc, "AIN", [128, 128], BF16)
        TGN, TGN_res = alloc_sb(ph2, nc, "TGN", [128, 64], F32)
        QSb, QSb_res = alloc_sb(ph2, nc, "QSb", [128, 64], BF16)
        KS1b, KS1b_res = alloc_sb(ph2, nc, "KS1b", [128, 64], BF16)
        QSFb, QSFb_res = alloc_sb(ph2, nc, "QSFb", [64, 128], BF16)
        KSFb, KSFb_res = alloc_sb(ph2, nc, "KSFb", [64, 128], BF16)
        AINb, AINb_res = alloc_sb(ph2, nc, "AINb", [128, 128], BF16)
        TGNb, TGNb_res = alloc_sb(ph2, nc, "TGNb", [128, 64], F32)
        SQg, SQg_res = alloc_sb(ph2, nc, "SQg", [128, 64], F32)
        COLSg, COLSg_res = alloc_sb(ph2, nc, "COLSg", [128, 16], F32)
        gn_scr0 = {"sq": (SQg, SQg_res), "cols": (COLSg, COLSg_res)}
        COLSg2, COLSg2_res = alloc_sb(ph2, nc, "COLSg2", [128, 16], F32)
        gn_scr1 = {"sq": (SQg, SQg_res), "cols": (COLSg2, COLSg2_res)}
        RGC, RGC_res = alloc_sb(ph2, nc, "RGC", [128, 1024], BF16)
        RGT, RGT_res = XNC, XNC_res
        XPH, XPH_res = QTC, QTC_res

        for kk in range(NF):
            em.d("p1", WQB[:, 1024 * kk:1024 * (kk + 1)], h.wq[128 * kk:128 * (kk + 1), :],
                 writes=[WQB_res])
            em.d("p2", WKB[:, 1024 * kk:1024 * (kk + 1)], h.wk[128 * kk:128 * (kk + 1), :],
                 writes=[WKB_res])
            em.d("p3", WVB[:, 1024 * kk:1024 * (kk + 1)], h.wv[128 * kk:128 * (kk + 1), :],
                 writes=[WVB_res])
            em.d("p1", WGB_[:, 1024 * kk:1024 * (kk + 1)], h.wg[128 * kk:128 * (kk + 1), :],
                 writes=[WGB__res])
            em.d("p2", WOB[:, 1024 * kk:1024 * (kk + 1)], h.wo[128 * kk:128 * (kk + 1), :],
                 writes=[WOB_res])

        def proj(dst, dst_res, wsb, wsb_res, bias_name, ps, ps_res, act=None):
            for j in range(2):
                for kk in range(NF):
                    lhsT = XNC[:, 128 * kk:128 * (kk + 1)]
                    rhs = wsb[:, 1024 * kk + 512 * j:1024 * kk + 512 * (j + 1)]
                    em.c("pe", lambda t, o=ps[:, 512 * j:512 * (j + 1)], l=lhsT, r=rhs,
                         st=(kk == 0): t.matmul(o, l, r, start=st, stop=False),
                         reads=[XNC_res, wsb_res], writes=[ps_res], tag="cproj")
                boff = bias_off[bias_name]
                em.c("pe", lambda t, o=ps[:, 512 * j:512 * (j + 1)],
                     r=BR[0:1, boff + 512 * j:boff + 512 * (j + 1)]:
                     t.matmul(o, h.ones_row[0:1, 0:128], r, start=False, stop=True),
                     reads=[BR_res, h.ones_row_res], writes=[ps_res], tag="cbias")
            if act is None:
                em.c("act", lambda s, o=dst, i=ps[:, :]: s.copy(o, i),
                     reads=[ps_res], writes=[dst_res], tag="cevac")
            else:
                em.c("act", lambda s, o=dst, i=ps[:, :], a=act: s.activation(o, i, a),
                     reads=[ps_res], writes=[dst_res], tag="cevac")

        def state_update(hd_, scale_col, scale_res, dec, dec_res):
            em.c("act", lambda s, sc=scale_col: s.activation(
                KS2[:, :], KTC[:, 64 * hd_:64 * (hd_ + 1)], AF.Copy, scale=sc),
                reads=[KTC_res, scale_res], writes=[KS2_res], tag="ks2")
            pss = PSS[0:64, 128:192]
            em.c("pe", lambda t: t.matmul(pss, KS2[:, :], VTC[:, 64 * hd_:64 * (hd_ + 1)],
                                          start=True, stop=True),
                 reads=[KS2_res, VTC_res], writes=[PSS_res], tag="smm")
            scol = SST[:, 64 * hd_:64 * (hd_ + 1)]
            if dec_res is None:
                em.c("dve", lambda v, o=scol, d=dec: v.scalar_tensor_tensor(
                    o, o, d, pss, ALU.mult, ALU.add),
                    reads=[SST_res, PSS_res], writes=[SST_res], tag="sdec")
            else:
                em.c("dve", lambda v, o=scol, d=dec: v.scalar_tensor_tensor(
                    o, o, d, pss, ALU.mult, ALU.add),
                    reads=[SST_res, PSS_res, dec_res], writes=[SST_res], tag="sdec")
            sbcol = SSTB[:, 64 * hd_:64 * (hd_ + 1)]
            em.c("dve", lambda v, o=sbcol, i=scol: v.tensor_copy(o, i),
                 reads=[SST_res], writes=[SSTB_res], tag="sbf")

        for c in range(NPRE):
            em.d(["px", "p0"], XPH[:, :], h.xprefix[128 * c:128 * (c + 1), :],
                 reads=[h.xprefix_res], writes=[XPH_res], tag="xpload")
            emit_ln(em, h, XPH[:, :], XPH_res, [(T1[:, :], T1_res)], ln_scratch, tag="lnp")
            em.c("dve", lambda v: v.tensor_tensor(T1[:, :], T1[:, :], W1B, ALU.mult),
                 reads=[T1_res, BCT_res], writes=[T1_res], tag="xnpw")
            em.c("dve", lambda v: v.tensor_tensor(T1[:, :], T1[:, :], B1B, ALU.add),
                 reads=[T1_res, BCT_res], writes=[T1_res], tag="xnpb")
            em.c("dve", lambda v: v.tensor_copy(GTC[:, :], T1[:, :]),
                 reads=[T1_res], writes=[GTC_res], tag="xnpcast")
            for f in range(NF):
                src = GTC[:, 128 * f:128 * (f + 1)]
                dst = XNC[:, 128 * f:128 * (f + 1)]
                emit_transpose(em, h, src, dst, PSt, PSt_res, GTC_res, XNC_res)
            proj(KTC[:, :], KTC_res, WKB, WKB_res, "bk", PS0, PS0_res)
            proj(VTC[:, :], VTC_res, WVB, WVB_res, "bv", PS1, PS1_res)
            for hd_ in range(NH):
                state_update(hd_, GKSP[:, NH * c + hd_:NH * c + hd_ + 1], GKSP_res,
                             GDEC[:, NH * c + hd_:NH * c + hd_ + 1], GDEC_res)

        for c in range(NT):
            for f in range(NF):
                src = XNA[:, 1024 * c + 128 * f:1024 * c + 128 * (f + 1)]
                dst = XNC[:, 128 * f:128 * (f + 1)]
                emit_transpose(em, h, src, dst, PSt, PSt_res, XNA_res, XNC_res)
            proj(QTC[:, :], QTC_res, WQB, WQB_res, "bq", PS0, PS0_res)
            proj(KTC[:, :], KTC_res, WKB, WKB_res, "bk", PS1, PS1_res)
            proj(VTC[:, :], VTC_res, WVB, WVB_res, "bv", PS0, PS0_res)
            proj(GTC[:, :], GTC_res, WGB_, WGB__res, "bg", PS1, PS1_res, act=AF.Silu)
            if c == 0 and getattr(h, "dbg", None) is not None:
                for nm, t_, r_ in [("q", QTC, QTC_res), ("k", KTC, KTC_res),
                                   ("v", VTC, VTC_res), ("g", GTC, GTC_res)]:
                    em.d("so", h.dbg[nm][:, :], t_[:, :], reads=[r_],
                         writes=[h.dbg_res[nm]], tag="dbg")
            for hd_ in range(NH):
                qslice = QTC[:, 64 * hd_:64 * (hd_ + 1)]
                kslice = KTC[:, 64 * hd_:64 * (hd_ + 1)]
                vslice = VTC[:, 64 * hd_:64 * (hd_ + 1)]
                sbslice = SSTB[:, 64 * hd_:64 * (hd_ + 1)]
                em.c("act", lambda s, sc=GQ[:, hd_:hd_ + 1], i_=qslice: s.activation(
                    QS[:, :], i_, AF.Copy, scale=sc),
                    reads=[QTC_res, GQ_res], writes=[QS_res], tag="qsc")
                em.c("act", lambda s, sc=GKN[:, hd_:hd_ + 1], i_=kslice: s.activation(
                    KS1[:, :], i_, AF.Copy, scale=sc),
                    reads=[KTC_res, GKN_res], writes=[KS1_res], tag="ksc")
                emit_transpose(em, h, QS[:, :], QSF[:, :], PSt, PSt_res, QS_res, QSF_res)
                emit_transpose(em, h, KS1[:, :], KSF[:, :], PSt, PSt_res, KS1_res, KSF_res)
                psc = PSC[0:128, 0:128]
                em.c("pe", lambda t: t.matmul(psc, KSF[:, :], QSF[:, :], start=True,
                                              stop=True),
                     reads=[KSF_res, QSF_res], writes=[PSC_res], tag="sc_mm")
                em.c("dve", lambda v: v.tensor_tensor(AIN[:, :], psc, MASKT[:, :], ALU.mult),
                     reads=[PSC_res, MASKT_res], writes=[AIN_res], tag="mask")
                pso = PSO[0:128, 0:64]
                em.c("pe", lambda t, r=vslice: t.matmul(pso, AIN[:, :], r,
                                                        start=True, stop=False),
                     reads=[AIN_res, VTC_res], writes=[PSO_res], tag="o_intra")
                em.c("pe", lambda t, r=sbslice: t.matmul(pso, QSF[:, :], r,
                                                         start=False, stop=True),
                     reads=[QSF_res, SSTB_res], writes=[PSO_res], tag="o_inter")
                if c == 0 and hd_ == 0 and getattr(h, "dbg", None) is not None:
                    em.d("px", h.dbg["qs"][:, 0:64], QS[:, :], reads=[QS_res],
                         writes=[h.dbg_res["qs"]], tag="dbg")
                    em.d("px", h.dbg["ks"][:, 0:64], KS1[:, :], reads=[KS1_res],
                         writes=[h.dbg_res["ks"]], tag="dbg")
                    em.d("px", h.dbg["ain"][:, 0:128], AIN[:, :], reads=[AIN_res],
                         writes=[h.dbg_res["ain"]], tag="dbg")
                    em.c("dve", lambda v: v.tensor_copy(TGN[:, :], pso),
                         reads=[PSO_res], writes=[TGN_res], tag="dbgret")
                    em.d("so", h.dbg["ret"][:, 0:64], TGN[:, :], reads=[TGN_res],
                         writes=[h.dbg_res["ret"]], tag="dbg")
                state_update(hd_, GKS[:, hd_:hd_ + 1], GKS_res, GAMMAS[hd_] ** 128, None)
                emit_ln_small(em, h, pso, PSO_res, 64, [(TGN[:, :], TGN_res)], ln_scratch)
                em.c("dve", lambda v, w=GNW[:, 64 * hd_:64 * (hd_ + 1)]: v.tensor_tensor(
                    TGN[:, :], TGN[:, :], w, ALU.mult),
                    reads=[TGN_res, BCT_res], writes=[TGN_res], tag="gnw")
                em.c("dve", lambda v, b=GNB[:, 64 * hd_:64 * (hd_ + 1)]: v.tensor_tensor(
                    TGN[:, :], TGN[:, :], b, ALU.add),
                    reads=[TGN_res, BCT_res], writes=[TGN_res], tag="gnb")
                dst = RGC[:, 64 * hd_:64 * (hd_ + 1)]
                em.c("dve", lambda v, o=dst, g=GTC[:, 64 * hd_:64 * (hd_ + 1)]:
                     v.tensor_tensor(o, TGN[:, :], g, ALU.mult),
                     reads=[TGN_res, GTC_res], writes=[RGC_res], tag="gmul")
            if c == 0 and getattr(h, "dbg", None) is not None:
                em.d("so", h.dbg["rg"][:, :], RGC[:, :], reads=[RGC_res],
                     writes=[h.dbg_res["rg"]], tag="dbg")
            for f in range(NF):
                src = RGC[:, 128 * f:128 * (f + 1)]
                dst = RGT[:, 128 * f:128 * (f + 1)]
                emit_transpose(em, h, src, dst, PSt, PSt_res, RGC_res, RGT_res)
            for j in range(2):
                for kk in range(NF):
                    lhsT = RGT[:, 128 * kk:128 * (kk + 1)]
                    rhs = WOB[:, 1024 * kk + 512 * j:1024 * kk + 512 * (j + 1)]
                    em.c("pe", lambda t, o=PS0[:, 512 * j:512 * (j + 1)], l=lhsT, r=rhs,
                         st=(kk == 0): t.matmul(o, l, r, start=st, stop=False),
                         reads=[RGT_res, WOB_res], writes=[PS0_res], tag="oproj")
                boff = bias_off["bo"]
                em.c("pe", lambda t, o=PS0[:, 512 * j:512 * (j + 1)],
                     r=BR[0:1, boff + 512 * j:boff + 512 * (j + 1)]:
                     t.matmul(o, h.ones_row[0:1, 0:128], r, start=False, stop=True),
                     reads=[BR_res, h.ones_row_res], writes=[PS0_res], tag="obias")
            xt = XTOK[:, 1024 * c:1024 * (c + 1)]
            em.c("dve", lambda v, x=xt: v.tensor_tensor(T1[:, :], PS0[:, :], x, ALU.add),
                 reads=[PS0_res, XTOK_res], writes=[T1_res], tag="midres")
            XO, XO_res = XOs[c % 2]
            em.c("dve", lambda v, o=XO[:, :]: v.tensor_copy(o, T1[:, :]),
                 reads=[T1_res], writes=[XO_res], tag="midcp")
            em.d("so", h.xmid_dram[128 * c:128 * (c + 1), :], XO[:, :],
                 reads=[XO_res], writes=[h.xmid_res], tag="midstore")

    em.barrier()

    with contextlib.ExitStack() as ph3:
        XMH, XMH_res = alloc_sb(ph3, nc, "XMH", [128, 1024], F32)
        T2, T2_res = alloc_sb(ph3, nc, "T2", [128, 1024], F32)
        BR3, BR3_res = alloc_sb(ph3, nc, "BR3", [1, 2 * 2816 + 1024], BF16)
        AFS, AFS_res = alloc_sb(ph3, nc, "AFS", [128, NIC * C_SLAB], BF16)
        WIN0, WIN0_res = alloc_sb(ph3, nc, "WIN0", [128, 2048], BF16)
        WIN1, WIN1_res = alloc_sb(ph3, nc, "WIN1", [128, 2048], BF16)
        WOU0, WOU0_res = alloc_sb(ph3, nc, "WOU0", [128, 1024], BF16)
        WOU1, WOU1_res = alloc_sb(ph3, nc, "WOU1", [128, 1024], BF16)
        TMP, TMP_res = alloc_sb(ph3, nc, "TMPC", [128, 512], BF16)
        TMPc2, TMPc2_res = alloc_sb(ph3, nc, "TMPC2", [128, 512], BF16)
        TMPsC = [(TMP, TMP_res), (TMPc2, TMPc2_res)]
        X2C, X2C_res = alloc_sb(ph3, nc, "X2C", [128, 1024], BF16)
        WINs = [(WIN0, WIN0_res), (WIN1, WIN1_res)]
        WOUs = [(WOU0, WOU0_res), (WOU1, WOU1_res)]
        for nm, dd, ln in [("biu", h.b_in_u, 2816), ("big", h.b_in_g, 2816),
                           ("bout", h.b_out, 1024)]:
            em.d("p1", BR3[0:1, off - 5120:off - 5120 + ln], dd[:, :], writes=[BR3_res])
            bias_off[nm] = off
            off += ln
        BCT3, BCT3_res = alloc_sb(ph3, nc, "BCT3", [128, 2048], BF16)
        for j, dd in enumerate([h.norm2_w, h.norm2_b]):
            em.d("sk", NRMR[0:1, :], dd[:, :], reads=[NRMR_res], writes=[NRMR_res])
            for jj in range(2):
                pb = PS0[:, 0:512]
                em.c("pe", lambda t, r=NRMR[0:1, 512 * jj:512 * (jj + 1)]:
                     t.matmul(pb, h.ones1x128f[0:1, :], r, start=True, stop=True),
                     reads=[NRMR_res, h.ones1x128f_res], writes=[PS0_res], tag="bcmm3")
                dst = BCT3[:, 1024 * j + 512 * jj:1024 * j + 512 * (jj + 1)]
                em.c("dve", lambda v, o=dst: v.tensor_copy(o, pb),
                     reads=[PS0_res], writes=[BCT3_res], tag="bcev3")
        W2B = BCT3[:, 0:1024]
        B2B = BCT3[:, 1024:2048]

        for tt in range(NT):
            em.d(["px", "p0"], XMH[:, :], h.xmid_dram[128 * tt:128 * (tt + 1), :],
                 reads=[h.xmid_res], writes=[XMH_res], tag="xmload")
            emit_ln(em, h, XMH[:, :], XMH_res, [(T1[:, :], T1_res)], ln_scratch, tag="ln2")
            em.c("dve", lambda v: v.tensor_tensor(T1[:, :], T1[:, :], W2B, ALU.mult),
                 reads=[T1_res, BCT3_res], writes=[T1_res], tag="x2w")
            em.c("dve", lambda v: v.tensor_tensor(T2[:, :], T1[:, :], B2B, ALU.add),
                 reads=[T1_res, BCT3_res], writes=[T2_res], tag="x2b")
            em.c("dve", lambda v: v.tensor_copy(X2C[:, :], T2[:, :]),
                 reads=[T2_res], writes=[X2C_res], tag="x2cast")
            for f in range(NF):
                src = X2C[:, 128 * f:128 * (f + 1)]
                dst = XNA[:, 2048 * f + 128 * tt:2048 * f + 128 * (tt + 1)]
                emit_transpose(em, h, src, dst, PSt, PSt_res, X2C_res, XNA_res)
        X2T, X2T_res = XNA, XNA_res

        for s in range(4):
            for i in range(NIC):
                WIN, WIN_res = WINs[i % 2]
                em.d(WCH2[(2 * i) % 4], WIN[:, 0:1024], h.win_u_pre[i],
                     writes=[WIN_res], tag="wiuld")
                em.d(WCH2[(2 * i + 1) % 4], WIN[:, 1024:2048], h.win_g_pre[i],
                     writes=[WIN_res], tag="wigld")
                PSp, PSp_res = (PS0, PS0_res) if i % 2 == 0 else (PS1, PS1_res)
                for which, bias_name in ((0, "biu"), (1, "big")):
                    ps = PSp[:, 512 * which:512 * (which + 1)]
                    for kk in range(NF):
                        lhsT = WIN[:, 1024 * which + 128 * kk:1024 * which + 128 * (kk + 1)]
                        rhs = X2T[:, 2048 * kk + C_SLAB * s:2048 * kk + C_SLAB * (s + 1)]
                        em.c("pe", lambda t, o=ps, l=lhsT, r=rhs, st=(kk == 0):
                             t.matmul(o, l, r, start=st, stop=False),
                             reads=[WIN_res, X2T_res], writes=[PSp_res], tag="ug")
                    boff = bias_off[bias_name] - 5120 + 128 * i
                    em.c("pe", lambda t, o=ps, l=BR3[0:1, boff:boff + 128]:
                         t.matmul(o, l, h.ones_row[0:1, :], start=False, stop=True),
                         reads=[BR3_res, h.ones_row_res], writes=[PSp_res], tag="ugbias")
                em.c("act", lambda s_, i_=PSp[:, 0:512], tp=TMPsC[i % 2]: s_.activation(
                    tp[0][:, :], i_, AF.Silu), reads=[PSp_res], writes=[TMPsC[i % 2][1]],
                    tag="usilu")
                dst = AFS[:, C_SLAB * i:C_SLAB * (i + 1)]
                em.c("dve", lambda v, o=dst, tp=TMPsC[i % 2], i_=PSp[:, 512:1024]:
                     v.tensor_tensor(o, tp[0][:, :], i_, ALU.mult),
                     reads=[TMPsC[i % 2][1], PSp_res], writes=[AFS_res], tag="umul")
            for half in range(2):
                pss = [(PS0, PS0_res), (PS1, PS1_res)]
                for i in range(NIC):
                    WOU, WOU_res = WOUs[i % 2]
                    em.d(WCH2[i % 4], WOU[:, :], h.w_out_pad[128 * i:128 * (i + 1), :],
                         writes=[WOU_res], tag="would")
                    for tt2 in range(2):
                        tt = 2 * half + tt2
                        ps, ps_res = pss[tt2]
                        lhsT = AFS[:, C_SLAB * i + 128 * tt:C_SLAB * i + 128 * (tt + 1)]
                        for j in range(2):
                            rhs = WOU[:, 512 * j:512 * (j + 1)]
                            em.c("pe", lambda t, o=ps[:, 512 * j:512 * (j + 1)], l=lhsT,
                                 r=rhs, st=(i == 0): t.matmul(o, l, r, start=st, stop=False),
                                 reads=[AFS_res, WOU_res], writes=[ps_res], tag="wout")
                for tt2 in range(2):
                    tt = 2 * half + tt2
                    ps, ps_res = pss[tt2]
                    boff = bias_off["bout"] - 5120
                    for j in range(2):
                        em.c("pe", lambda t, o=ps[:, 512 * j:512 * (j + 1)],
                             r=BR3[0:1, boff + 512 * j:boff + 512 * (j + 1)]:
                             t.matmul(o, h.ones_row[0:1, 0:128], r, start=False, stop=True),
                             reads=[BR3_res, h.ones_row_res], writes=[ps_res], tag="woutb")
                    gtt = 4 * s + tt
                    em.d(["px", "p0"], XMH[:, :], h.xmid_dram[128 * gtt:128 * (gtt + 1), :],
                         reads=[h.xmid_res], writes=[XMH_res], tag="xm2")
                    em.c("dve", lambda v, i_=ps[:, :]: v.tensor_tensor(T1[:, :], i_,
                                                                       XMH[:, :], ALU.add),
                         reads=[ps_res, XMH_res], writes=[T1_res], tag="finres1")
                    xt = XTOK[:, 1024 * gtt:1024 * (gtt + 1)]
                    em.c("dve", lambda v, x=xt: v.tensor_tensor(T1[:, :], T1[:, :], x,
                                                                ALU.add),
                         reads=[T1_res, XTOK_res], writes=[T1_res], tag="finres2")
                    XO, XO_res = XOs[tt2]
                    emit_ln(em, h, T1[:, :], T1_res, [(XO[:, :], XO_res)], ln_scratch,
                            tag="lnC")
                    em.d("so", h.xcout[128 * gtt:128 * (gtt + 1), :], XO[:, :],
                         reads=[XO_res], writes=[h.xcout_res], tag="xcstore")


WCH2 = ["p0", "p1", "p2", "p3"]


def build_C(core, debug=False):
    import contextlib
    nc = bass.Bass("TRN2", target_bir_lowering=False)
    h = H()
    if debug:
        h.dbg = {}
        h.dbg_res = {}
        for nm in ["q", "k", "v", "g", "rg"]:
            h.dbg[nm] = nc.dram_tensor("dbg_" + nm, [128, 1024], BF16, kind="ExternalOutput")
            h.dbg_res[nm] = Res("dbg_" + nm)
        for nm in ["qs", "ks", "ain", "ret"]:
            h.dbg[nm] = nc.dram_tensor("dbg_" + nm, [128, 128], F32, kind="ExternalOutput")
            h.dbg_res[nm] = Res("dbg_" + nm)
    else:
        h.dbg = None
    h.xc_in = nc.dram_tensor("xc_in", [TOK, D], F32, kind="ExternalInput")
    h.xprefix = nc.dram_tensor("xprefix", [NPRE * 128, D], F32, kind="ExternalInput")
    for nm in ["wq", "wk", "wv", "wg", "wo"]:
        setattr(h, nm, nc.dram_tensor(nm, [D, D], F32, kind="ExternalInput"))
    for nm in ["bq", "bk", "bv", "bg", "bo", "b_out"]:
        setattr(h, nm, nc.dram_tensor(nm, [1, D], F32, kind="ExternalInput"))
    for nm in ["norm1_w", "norm1_b", "gn_w", "gn_b", "norm2_w", "norm2_b"]:
        setattr(h, nm, nc.dram_tensor(nm, [1, D], F32, kind="ExternalInput"))
    h.b_in_u = nc.dram_tensor("b_in_u", [1, 2816], F32, kind="ExternalInput")
    h.b_in_g = nc.dram_tensor("b_in_g", [1, 2816], F32, kind="ExternalInput")
    h.win_u_pre = nc.dram_tensor("win_u_pre", [NIC, 128, 1024], F32, kind="ExternalInput")
    h.win_g_pre = nc.dram_tensor("win_g_pre", [NIC, 128, 1024], F32, kind="ExternalInput")
    h.w_out_pad = nc.dram_tensor("w_out_pad", [NIC * 128, 1024], F32, kind="ExternalInput")
    h.GQ_d = nc.dram_tensor("GQ_d", [128, NH], F32, kind="ExternalInput")
    h.GKN_d = nc.dram_tensor("GKN_d", [128, NH], F32, kind="ExternalInput")
    h.GKS_d = nc.dram_tensor("GKS_d", [128, NH], F32, kind="ExternalInput")
    h.GKSP_d = nc.dram_tensor("GKSP_d", [128, NPRE * NH], F32, kind="ExternalInput")
    h.GDEC_d = nc.dram_tensor("GDEC_d", [64, NPRE * NH], F32, kind="ExternalInput")
    h.MASKT_d = nc.dram_tensor("MASKT_d", [128, 128], BF16, kind="ExternalInput")
    h.xcout = nc.dram_tensor("xcout", [TOK, D], F32, kind="ExternalOutput")
    h.xcout_res = Res("xcout")
    h.xc_in_res = Res("xc_in")
    h.xprefix_res = Res("xprefix")
    h.xmid_dram = nc.dram_tensor("xmid_dram", [TOK, D], F32)
    h.xmid_res = Res("xmid")

    chans = {"p0": "pool", "p1": "pool", "p2": "pool", "p3": "pool", "px": "pool",
             "sk": "sp", "so": "sp"}
    em = Em(nc, chans)
    with contextlib.ExitStack() as outer:
        declare_consts(nc, outer, h)
        load_consts(em, h)
        emit_blockC(em, h, nc, outer)
        em.barrier()
        with nc.Block() as block:
            em.finalize_and_emit(block, None)
    return nc


def prep_C_inputs(inputs, core, xb_full):
    b = core // 4
    q0 = 2048 * (core % 4)
    cc = host_consts()
    xpre = np.zeros((NPRE * 128, D), np.float32)
    xpre[:q0] = xb_full[b, :q0]
    nvalid = q0 // 128
    ii = np.arange(128, dtype=np.float64)
    GQ = np.stack([(g ** ii) / 8.0 for g in GAMMAS], 1).astype(np.float32)
    GKN = np.stack([g ** (-ii) for g in GAMMAS], 1).astype(np.float32)
    GKS = np.stack([g ** (128 - ii) for g in GAMMAS], 1).astype(np.float32)
    GKSP = np.zeros((128, NPRE * NH), np.float32)
    GDEC = np.ones((64, NPRE * NH), np.float32)
    for c in range(NPRE):
        for hh in range(NH):
            if c < nvalid:
                GKSP[:, NH * c + hh] = GKS[:, hh]
                GDEC[:, NH * c + hh] = GAMMAS[hh] ** 128
    maskT = np.zeros((128, 128), BFDT)
    for m_ in range(128):
        maskT[m_, m_:] = 1
    w_in = np.asarray(inputs["L1_attn_w_in"])
    b_in = np.asarray(inputs["L1_attn_b_in"])
    w_out = np.asarray(inputs["L1_attn_w_out"])
    wu = np.zeros((D, NIC * 128), np.float32); wu[:, :2730] = w_in[:, :2730]
    wgg = np.zeros((D, NIC * 128), np.float32); wgg[:, :2730] = w_in[:, 2730:]
    biu = np.zeros((1, NIC * 128), np.float32); biu[0, :2730] = b_in[:2730]
    big = np.zeros((1, NIC * 128), np.float32); big[0, :2730] = b_in[2730:]
    wop = np.zeros((NIC * 128, D), np.float32); wop[:2730] = w_out
    pre = lambda W: np.ascontiguousarray(
        W.reshape(8, 128, NIC, 128).transpose(2, 1, 0, 3).reshape(NIC, 128, 1024))
    m = {
        "xc_in": np.ascontiguousarray(xb_full[b, q0:q0 + 2048, :]),
        "xprefix": xpre,
        "wq": np.asarray(inputs["L1_attn_wq"]), "wk": np.asarray(inputs["L1_attn_wk"]),
        "wv": np.asarray(inputs["L1_attn_wv"]), "wg": np.asarray(inputs["L1_attn_wg"]),
        "wo": np.asarray(inputs["L1_attn_wo"]),
        "bq": np.asarray(inputs["L1_attn_bq"]).reshape(1, -1),
        "bk": np.asarray(inputs["L1_attn_bk"]).reshape(1, -1),
        "bv": np.asarray(inputs["L1_attn_bv"]).reshape(1, -1),
        "bg": np.asarray(inputs["L1_attn_bg"]).reshape(1, -1),
        "bo": np.asarray(inputs["L1_attn_bo"]).reshape(1, -1),
        "b_out": np.asarray(inputs["L1_attn_b_out"]).reshape(1, -1),
        "norm1_w": np.asarray(inputs["L1_attn_norm1_w"]).reshape(1, -1),
        "norm1_b": np.asarray(inputs["L1_attn_norm1_b"]).reshape(1, -1),
        "gn_w": np.asarray(inputs["L1_attn_gn_w"]).reshape(1, -1),
        "gn_b": np.asarray(inputs["L1_attn_gn_b"]).reshape(1, -1),
        "norm2_w": np.asarray(inputs["L1_attn_norm2_w"]).reshape(1, -1),
        "norm2_b": np.asarray(inputs["L1_attn_norm2_b"]).reshape(1, -1),
        "b_in_u": biu, "b_in_g": big,
        "win_u_pre": pre(wu), "win_g_pre": pre(wgg),
        "w_out_pad": wop,
        "GQ_d": GQ, "GKN_d": GKN, "GKS_d": GKS, "GKSP_d": GKSP, "GDEC_d": GDEC,
        "MASKT_d": maskT,
    }
    for k, v in cc.items():
        m["c_" + k] = v
    return m



NEXP = 64
PSLOT = 32
NLOC = 8


def emit_blockD(em, h, nc, outer_ctx):
    import contextlib
    XTOK, XTOK_res = alloc_sb(outer_ctx, nc, "XTOKD", [128, NT * 1024], BF16)
    SQ, SQ_res = alloc_sb(outer_ctx, nc, "SQln", [128, 1024], F32)
    COLS, COLS_res = alloc_sb(outer_ctx, nc, "COLS", [128, 16], F32)
    ln_scratch = {"sq": (SQ, SQ_res), "cols": (COLS, COLS_res)}
    T1, T1_res = alloc_sb(outer_ctx, nc, "T1", [128, 1024], F32)
    XO0, XO0_res = alloc_sb(outer_ctx, nc, "XO0", [128, 1024], F32)
    XOs = [(XO0, XO0_res), (XO0, XO0_res)]
    XPE, XPE_res = alloc_sb(outer_ctx, nc, "XPE", [128, 1024], F32)
    XS0, XS0_res = XPE, XPE_res
    XTN, XTN_res = alloc_sb(outer_ctx, nc, "XTN", [128, NF * 256], BF16)
    X2C, X2C_res = alloc_sb(outer_ctx, nc, "X2CD", [128, 1024], BF16)
    ph12 = contextlib.ExitStack()
    XT, XT_res = alloc_sb(ph12, nc, "XT_D", [128, NF * 2048], BF16)
    PHB, PHB_res = alloc_sb(ph12, nc, "PHB", [128, NF * 2048], BF16)
    DHB, DHB_res = alloc_sb(ph12, nc, "DHB", [128, NT * 2048], BF16)
    ESB, ESB_res = alloc_sb(ph12, nc, "ESB", [128, 2048], F32)
    CHT, CHT_res = alloc_sb(ph12, nc, "CHT", [128, 2048], BF16)
    RSP, RSP_res = alloc_sb(ph12, nc, "RSP", [128, 64], F32)
    RSN, RSN_res = alloc_sb(ph12, nc, "RSN", [128, 32], F32)
    PBR, PBR_res = alloc_sb(ph12, nc, "PBR", [1, 2048], BF16)

    PSL, PSL_res = alloc_ps(outer_ctx, nc, "PSL", [128, 2048], F32)
    PS0, PS0_res = alloc_ps(outer_ctx, nc, "PS0", [128, 1024], F32)
    PSt, _pstres = alloc_ps(outer_ctx, nc, "PSt", [128, 2048], BF16)
    PSt_res = [_pstres, Res("PSt1")]

    for t in range(NT):
        em.d(["px", "p0"], XTOK[:, 1024 * t:1024 * (t + 1)],
             h.xd_in[128 * t:128 * (t + 1), :], reads=[h.xd_in_res],
             writes=[XTOK_res], tag="xload")
    for t in range(NT):
        for f in range(NF):
            src = XTOK[:, 1024 * t + 128 * f:1024 * t + 128 * (f + 1)]
            dst = XT[:, 2048 * f + 128 * t:2048 * f + 128 * (t + 1)]
            emit_transpose(em, h, src, dst, PSt, PSt_res, XTOK_res, XT_res)
    for kk in range(NF):
        em.d("p1", PHB[:, 2048 * kk:2048 * (kk + 1)],
             h.phi_wT[128 * kk:128 * (kk + 1), :], writes=[PHB_res], tag="phld")
    em.d("p2", PBR[:, :], h.phi_b[:, :], writes=[PBR_res])

    for tt in range(NT):
        for j in range(4):
            pl = PSL[:, 512 * j:512 * (j + 1)]
            for kk in range(NF):
                lhsT = XT[:, 2048 * kk + 128 * tt:2048 * kk + 128 * (tt + 1)]
                rhs = PHB[:, 2048 * kk + 512 * j:2048 * kk + 512 * (j + 1)]
                em.c("pe", lambda t, o=pl, l=lhsT, r=rhs, st=(kk == 0):
                     t.matmul(o, l, r, start=st, stop=False),
                     reads=[XT_res, PHB_res], writes=[PSL_res], tag="phi")
            em.c("pe", lambda t, o=pl, r=PBR[0:1, 512 * j:512 * (j + 1)]:
                 t.matmul(o, h.ones_row[0:1, 0:128], r, start=False, stop=True),
                 reads=[PBR_res, h.ones_row_res], writes=[PSL_res], tag="phib")
        nmx = COLS[:, 8:9]
        em.c("dve", lambda v: v.reduce_max(nmx, PSL[:, :], axis=AX.X, negate=True),
             reads=[PSL_res], writes=[COLS_res], tag="dmax")
        em.c("act", lambda s: s.activation(ESB[:, :], PSL[:, :], AF.Exp, bias=nmx),
             reads=[PSL_res, COLS_res], writes=[ESB_res], tag="dexp")
        ev_np = ESB[:, :].rearrange("p (n q) -> p n q", q=PSLOT)
        em.c("dve", lambda v: v.reduce_sum(RSP[:, :], ev_np, axis=AX.X),
             reads=[ESB_res], writes=[RSP_res], tag="dsp")
        ev_pn = ESB[:, :].rearrange("p (n q) -> p q n", q=PSLOT)
        em.c("dve", lambda v: v.reduce_sum(RSN[:, :], ev_pn, axis=AX.X),
             reads=[ESB_res], writes=[RSN_res], tag="dsn")
        em.c("dve", lambda v: v.reciprocal(RSP[:, :], RSP[:, :]),
             reads=[RSP_res], writes=[RSP_res], tag="drsp")
        em.c("dve", lambda v: v.reciprocal(RSN[:, :], RSN[:, :]),
             reads=[RSN_res], writes=[RSN_res], tag="drsn")
        for n_ in range(NEXP):
            dst = DHB[:, 2048 * tt + PSLOT * n_:2048 * tt + PSLOT * (n_ + 1)]
            esl = ESB[:, PSLOT * n_:PSLOT * (n_ + 1)]
            em.c("dve", lambda v, o=dst, i_=esl, sc=RSP[:, n_:n_ + 1]:
                 v.tensor_scalar(o, i_, sc, None, ALU.mult),
                 reads=[ESB_res, RSP_res], writes=[DHB_res], tag="dhat")
        for p_ in range(PSLOT):
            dst = CHT[:, :].rearrange("p (n q) -> p q n", q=PSLOT)[:, p_, :]
            esl = ESB[:, :].rearrange("p (n q) -> p q n", q=PSLOT)[:, p_, :]
            em.c("dve", lambda v, o=dst, i_=esl, sc=RSN[:, p_:p_ + 1]:
                 v.tensor_scalar(o, i_, sc, None, ALU.mult),
                 reads=[ESB_res, RSN_res], writes=[CHT_res], tag="chat")
        for j in range(16):
            src = CHT[:, 128 * j:128 * (j + 1)]
            dst = X2C[:, 0:128]
            emit_transpose(em, h, src, dst, PSt, PSt_res, CHT_res, X2C_res)
            em.d("sq", h.cht_dram[tt, j], X2C[:, 0:128],
                 reads=[X2C_res], writes=[h.cht_res], tag="chtst")

    for j in range(16):
        for jj in range(2):
            ps = PS0[:, 512 * jj:512 * (jj + 1)]
            for tt in range(NT):
                lhsT = DHB[:, 2048 * tt + 128 * j:2048 * tt + 128 * (j + 1)]
                rhs = XTOK[:, 1024 * tt + 512 * jj:1024 * tt + 512 * (jj + 1)]
                em.c("pe", lambda t, o=ps, l=lhsT, r=rhs, st=(tt == 0), sp=(tt == NT - 1):
                     t.matmul(o, l, r, start=st, stop=sp),
                     reads=[DHB_res, XTOK_res], writes=[PS0_res], tag="xtp")
        em.c("act", lambda s: s.copy(XPE[:, :], PS0[:, :]),
             reads=[PS0_res], writes=[XPE_res], tag="xtpev")
        em.d("sq", h.rs_in[128 * j:128 * (j + 1), :], XPE[:, :],
             reads=[XPE_res], writes=[h.rs_in_res], tag="xtpst")
    cc1 = em.c("pool", lambda g: g.collective_compute(
        "ReduceScatter", ALU.add, replica_groups=[list(range(8))],
        ins=[h.rs_in[:, :]], outs=[h.rs_out[:, :]]),
        reads=[h.rs_in_res], writes=[h.rs_out_res], tag="rs", async_=True)
    em.barrier()
    ph12.close()
    WEB0, WEB0_res = alloc_sb(outer_ctx, nc, "WEB0", [128, 8 * 1024], BF16)
    WEB1, WEB1_res = alloc_sb(outer_ctx, nc, "WEB1", [128, 8 * 1024], BF16)
    EBR, EBR_res = alloc_sb(outer_ctx, nc, "EBR", [1, 1024], BF16)
    Y32, Y32_res = alloc_sb(outer_ctx, nc, "Y32", [32, 1024], F32)
    YTB, YTB_res = alloc_sb(outer_ctx, nc, "YTB", [128, 16 * 1024], BF16)
    CTB0, CTB0_res = alloc_sb(outer_ctx, nc, "CTB0", [128, 2048], BF16)
    CTB1, CTB1_res = alloc_sb(outer_ctx, nc, "CTB1", [128, 2048], BF16)
    WEBs = [(WEB0, WEB0_res), (WEB1, WEB1_res)]
    CTBs = [(CTB0, CTB0_res), (CTB1, CTB1_res)]

    for half in range(2):
        em.d(["px", "p0"], XS0[:, :], h.rs_out[128 * half:128 * (half + 1), :],
             reads=[h.rs_out_res], writes=[XS0_res], tag="rsld")
        em.c("dve", lambda v: v.tensor_scalar_max(XS0[:, :], XS0[:, :], -33000.0),
             reads=[XS0_res], writes=[XS0_res], tag="clip1")
        em.c("dve", lambda v: v.tensor_scalar_min(XS0[:, :], XS0[:, :], 65000.0),
             reads=[XS0_res], writes=[XS0_res], tag="clip2")
        emit_ln(em, h, XS0[:, :], XS0_res, [(T1[:, :], T1_res)], ln_scratch, tag="lnD")
        em.c("act", lambda s: s.activation(X2C[:, :], T1[:, :], AF.Tanh),
             reads=[T1_res], writes=[X2C_res], tag="dtanh")
        for f in range(NF):
            src = X2C[:, 128 * f:128 * (f + 1)]
            dst = XTN[:, 256 * f + 128 * half:256 * f + 128 * (half + 1)]
            emit_transpose(em, h, src, dst, PSt, PSt_res, X2C_res, XTN_res)
    for e in range(NLOC):
        WEB, WEB_res = WEBs[e % 2]
        if e < 2:
            for kk in range(NF):
                em.d(["p1", "p2", "p3"][kk % 3], WEBs[e][0][:, 1024 * kk:1024 * (kk + 1)],
                     h.expert_w[e, 128 * kk:128 * (kk + 1), :],
                     writes=[WEBs[e][1]], tag="ewld")
        em.d("p3", EBR[:, :], h.expert_b[e:e + 1, :], reads=[EBR_res], writes=[EBR_res],
             tag="ebld")
        pp = PS0[0:32, :]
        for jj in range(2):
            for kk in range(NF):
                lhsT = XTN[:, 256 * kk + 32 * e:256 * kk + 32 * (e + 1)]
                rhs = WEB[:, 1024 * kk + 512 * jj:1024 * kk + 512 * (jj + 1)]
                em.c("pe", lambda t, o=pp[:, 512 * jj:512 * (jj + 1)], l=lhsT, r=rhs,
                     st=(kk == 0): t.matmul(o, l, r, start=st, stop=False),
                     reads=[XTN_res, WEB_res], writes=[PS0_res], tag="yt")
            em.c("pe", lambda t, o=pp[:, 512 * jj:512 * (jj + 1)],
                 r=EBR[0:1, 512 * jj:512 * (jj + 1)]: t.matmul(
                 o, h.ones_row[0:1, 0:32], r, start=False, stop=True),
                 reads=[EBR_res, h.ones_row_res], writes=[PS0_res], tag="ytb")
        em.c("dve", lambda v: v.tensor_copy(Y32[:, :], pp),
             reads=[PS0_res], writes=[Y32_res], tag="ytev")
        em.d("sq", h.ag_in[32 * e:32 * (e + 1), :], Y32[:, :],
             reads=[Y32_res], writes=[h.ag_in_res], tag="ytst")
        if e + 2 < NLOC:
            WEBn, WEBn_res = WEBs[e % 2]
            for kk in range(NF):
                em.d(["p1", "p2", "p3"][kk % 3], WEBn[:, 1024 * kk:1024 * (kk + 1)],
                     h.expert_w[e + 2, 128 * kk:128 * (kk + 1), :],
                     writes=[WEBn_res], tag="ewld2")
    cc2 = em.c("pool", lambda g: g.collective_compute(
        "AllGather", ALU.bypass, replica_groups=[list(range(8))],
        ins=[h.ag_in[:, :]], outs=[h.ag_out[:, :]]),
        reads=[h.ag_in_res], writes=[h.ag_out_res], tag="ag", async_=True)

    for j in range(16):
        em.d(["px", "p0"], YTB[:, 1024 * j:1024 * (j + 1)],
             h.ag_out[128 * j:128 * (j + 1), :], reads=[h.ag_out_res],
             writes=[YTB_res], tag="ytld")
    def load_ct(tt):
        CTB, CTB_res = CTBs[tt % 2]
        for g in range(4):
            srcap = h.cht_dram[tt, 4 * g:4 * (g + 1)].rearrange("j p c -> p j c")
            dstap = CTB[:, 512 * g:512 * (g + 1)].rearrange("p (j c) -> p j c", c=128)
            em.d("sk", dstap, srcap, reads=[h.cht_res], writes=[CTB_res], tag="ctld")

    load_ct(0)
    for tt in range(NT):
        if tt + 1 < NT:
            load_ct(tt + 1)
        CTB, CTB_res = CTBs[tt % 2]
        ps = PS0
        for j in range(16):
            for jj in range(2):
                rhs = YTB[:, 1024 * j + 512 * jj:1024 * j + 512 * (jj + 1)]
                em.c("pe", lambda t, o=ps[:, 512 * jj:512 * (jj + 1)],
                     l=CTB[:, 128 * j:128 * (j + 1)], r=rhs,
                     st=(j == 0), sp=(j == 15): t.matmul(o, l, r, start=st, stop=sp),
                     reads=[CTB_res, YTB_res], writes=[PS0_res], tag="ymm")
        xt = XTOK[:, 1024 * tt:1024 * (tt + 1)]
        em.c("dve", lambda v, x=xt: v.tensor_tensor(T1[:, :], PS0[:, :], x, ALU.add),
             reads=[PS0_res, XTOK_res], writes=[T1_res], tag="dres")
        XO, XO_res = XOs[tt % 2]
        emit_ln(em, h, T1[:, :], T1_res, [(XO[:, :], XO_res)], ln_scratch, tag="lnF")
        em.d("so", h.xout[128 * tt:128 * (tt + 1), :], XO[:, :],
             reads=[XO_res], writes=[h.xout_res], tag="xostore")


def build_D():
    import contextlib
    nc = bass.Bass("TRN2", target_bir_lowering=False)
    h = H()
    h.xd_in = nc.dram_tensor("xd_in", [TOK, D], F32, kind="ExternalInput")
    h.phi_wT = nc.dram_tensor("phi_wT", [D, 2048], F32, kind="ExternalInput")
    h.phi_b = nc.dram_tensor("phi_b", [1, 2048], F32, kind="ExternalInput")
    h.expert_w = nc.dram_tensor("expert_w", [NLOC, D, D], F32, kind="ExternalInput")
    h.expert_b = nc.dram_tensor("expert_b", [NLOC, D], F32, kind="ExternalInput")
    h.xout = nc.dram_tensor("xout", [TOK, D], F32, kind="ExternalOutput")
    h.xout_res = Res("xout")
    h.xd_in_res = Res("xd_in")
    h.cht_dram = nc.dram_tensor("cht_dram", [NT, 16, 128, 128], BF16)
    h.cht_res = Res("cht")
    h.rs_in = nc.dram_tensor("rs_in", [2048, 1024], F32)
    h.rs_in_res = Res("rs_in")
    h.rs_out = nc.dram_tensor("rs_out", [256, 1024], F32)
    h.rs_out_res = Res("rs_out")
    h.ag_in = nc.dram_tensor("ag_in", [256, 1024], F32)
    h.ag_in_res = Res("ag_in")
    h.ag_out = nc.dram_tensor("ag_out", [2048, 1024], F32, addr_space="Shared")
    h.ag_out_res = Res("ag_out")

    chans = {"p0": "pool", "p1": "pool", "p2": "pool", "p3": "pool", "px": "pool",
             "sk": "sp", "sq": "sp", "so": "sp"}
    em = Em(nc, chans)
    with contextlib.ExitStack() as outer:
        declare_consts(nc, outer, h)
        load_consts(em, h)
        emit_blockD(em, h, nc, outer)
        em.barrier()
        with nc.Block() as block:
            em.finalize_and_emit(block, None)
    return nc


def prep_D_inputs(inputs, core, xc_full):
    b = core // 4
    q0 = 2048 * (core % 4)
    cc = host_consts()
    m = {
        "xd_in": np.ascontiguousarray(xc_full[b, q0:q0 + 2048, :]),
        "phi_wT": np.ascontiguousarray(np.asarray(inputs["L1_ffn_phi_w"]).T),
        "phi_b": np.asarray(inputs["L1_ffn_phi_b"]).reshape(1, -1),
        "expert_w": np.ascontiguousarray(np.asarray(inputs["L1_ffn_expert_w"])[8 * core:8 * core + 8]),
        "expert_b": np.ascontiguousarray(np.asarray(inputs["L1_ffn_expert_b"])[8 * core:8 * core + 8]),
    }
    for k, v in cc.items():
        m["c_" + k] = v
    return m



def _gather(res, key):
    out = np.zeros((2, 8192, 1024), np.float32)
    for c in range(NCORE):
        b, q0 = c // 4, 2048 * (c % 4)
        out[b, q0:q0 + 2048] = res.results[c][key]
    return out


def kernel(**inputs):
    nc = build_fused()
    res = run_bass_kernel_spmd(nc, [prep_fused_inputs(inputs, c) for c in range(NCORE)],
                               core_ids=list(range(NCORE)))
    return _gather(res, "xout")



def build_fused():
    import contextlib
    nc = bass.Bass("TRN2", target_bir_lowering=False)
    h = H()
    h.x_own = nc.dram_tensor("x_own", [TOK, D], F32, kind="ExternalInput")
    h.x_batch = nc.dram_tensor("x_batch", [SEQ, D], F32, kind="ExternalInput")
    h.qkv_wT = nc.dram_tensor("qkv_wT", [D, 3 * D], F32, kind="ExternalInput")
    h.qkv_b = nc.dram_tensor("qkv_b", [1, 3 * D], F32, kind="ExternalInput")
    h.out_wT = nc.dram_tensor("out_wT", [D, D], F32, kind="ExternalInput")
    h.out_b = nc.dram_tensor("out_b", [1, D], F32, kind="ExternalInput")
    h.gate_w = nc.dram_tensor("gate_w", [D, NE], F32, kind="ExternalInput")
    h.WGU_pre = nc.dram_tensor("WGU_pre", [NE, NI, 128, 2048], F32, kind="ExternalInput")
    h.Wd = nc.dram_tensor("Wd", [NE, 2816, 1024], F32, kind="ExternalInput")
    h.SGU_pre = nc.dram_tensor("SGU_pre", [NIS, 128, 2048], F32, kind="ExternalInput")
    h.Sd = nc.dram_tensor("Sd", [1408, 1024], F32, kind="ExternalInput")
    for nm in ["wq", "wk", "wv", "wg", "wo"]:
        setattr(h, nm, nc.dram_tensor(nm, [D, D], F32, kind="ExternalInput"))
    for nm in ["bq", "bk", "bv", "bg", "bo", "b_out"]:
        setattr(h, nm, nc.dram_tensor(nm, [1, D], F32, kind="ExternalInput"))
    for nm in ["norm1_w", "norm1_b", "gn_w", "gn_b", "norm2_w", "norm2_b"]:
        setattr(h, nm, nc.dram_tensor(nm, [1, D], F32, kind="ExternalInput"))
    h.b_in_u = nc.dram_tensor("b_in_u", [1, 2816], F32, kind="ExternalInput")
    h.b_in_g = nc.dram_tensor("b_in_g", [1, 2816], F32, kind="ExternalInput")
    h.win_u_pre = nc.dram_tensor("win_u_pre", [NIC, 128, 1024], F32, kind="ExternalInput")
    h.win_g_pre = nc.dram_tensor("win_g_pre", [NIC, 128, 1024], F32, kind="ExternalInput")
    h.w_out_pad = nc.dram_tensor("w_out_pad", [NIC * 128, 1024], F32, kind="ExternalInput")
    h.GQ_d = nc.dram_tensor("GQ_d", [128, NH], F32, kind="ExternalInput")
    h.GKN_d = nc.dram_tensor("GKN_d", [128, NH], F32, kind="ExternalInput")
    h.GKS_d = nc.dram_tensor("GKS_d", [128, NH], F32, kind="ExternalInput")
    h.GKSP_d = nc.dram_tensor("GKSP_d", [128, NPRE * NH], F32, kind="ExternalInput")
    h.GDEC_d = nc.dram_tensor("GDEC_d", [64, NPRE * NH], F32, kind="ExternalInput")
    h.MASKT_d = nc.dram_tensor("MASKT_d", [128, 128], BF16, kind="ExternalInput")
    h.phi_wT = nc.dram_tensor("phi_wT", [D, 2048], F32, kind="ExternalInput")
    h.phi_b = nc.dram_tensor("phi_b", [1, 2048], F32, kind="ExternalInput")
    h.expert_w = nc.dram_tensor("expert_w", [NLOC, D, D], F32, kind="ExternalInput")
    h.expert_b = nc.dram_tensor("expert_b", [NLOC, D], F32, kind="ExternalInput")
    h.xout = nc.dram_tensor("xout", [TOK, D], F32, kind="ExternalOutput")
    h.xout_res = Res("xout")
    h.K_dram = nc.dram_tensor("K_dram", [D, SEQ], BF16); h.K_dram_res = Res("K")
    h.V_dram = nc.dram_tensor("V_dram", [D, SEQ], BF16); h.V_dram_res = Res("V")
    h.Q_dram = nc.dram_tensor("Q_dram", [D, TOK], BF16); h.Q_dram_res = Res("Q")
    h.xa = nc.dram_tensor("xa_i", [TOK, D], F32); h.xa_res = Res("xa")
    h.xb = nc.dram_tensor("xb_i", [TOK, D], F32); h.xb_res = Res("xb")
    h.xcout = nc.dram_tensor("xc_i", [TOK, D], F32); h.xcout_res = Res("xc")
    h.xb_all = nc.dram_tensor("xb_all", [SEQ, D], BF16); h.xb_all_res = Res("xb_all")
    h.xb_bf = nc.dram_tensor("xb_bf", [TOK, D], BF16); h.xb_bf_res = Res("xb_bf")
    h.xmid_dram = nc.dram_tensor("xmid_dram", [TOK, D], F32); h.xmid_res = Res("xmid")
    h.cht_dram = nc.dram_tensor("cht_dram", [NT, 16, 128, 128], BF16); h.cht_res = Res("cht")
    h.rs_in = nc.dram_tensor("rs_in", [2048, 1024], F32); h.rs_in_res = Res("rs_in")
    h.rs_out = nc.dram_tensor("rs_out", [256, 1024], F32); h.rs_out_res = Res("rs_out")
    h.ag_in = nc.dram_tensor("ag_in", [256, 1024], F32); h.ag_in_res = Res("ag_in")
    h.ag_out = nc.dram_tensor("ag_out", [2048, 1024], F32, addr_space="Shared")
    h.ag_out_res = Res("ag_out")
    h.xb_in = h.xa
    h.xb_in_res = h.xa_res
    h.xc_in = h.xb
    h.xc_in_res = h.xb_res
    h.xprefix = h.xb_all
    h.xprefix_res = h.xb_all_res
    h.xd_in = h.xcout
    h.xd_in_res = h.xcout_res
    h.dbg = None

    chans = {"p0": "pool", "p1": "pool", "p2": "pool", "p3": "pool", "px": "pool",
             "pw": "pool", "sk": "sp", "sq": "sp", "so": "sp"}
    em = Em(nc, chans)
    with contextlib.ExitStack() as outer:
        declare_consts(nc, outer, h)
        load_consts(em, h)
        with contextlib.ExitStack() as sA:
            emit_blockA(em, h, None, nc, sA)
        em.barrier()
        with contextlib.ExitStack() as sB:
            emit_blockB_fused(em, h, nc, sB)
        em.barrier()
        em.c("pool", lambda g: g.collective_compute(
            "AllGather", ALU.bypass, replica_groups=[[0, 1, 2, 3], [4, 5, 6, 7]],
            ins=[h.xb_bf[:, :]], outs=[h.xb_all[:, :]]),
            reads=[h.xb_bf_res], writes=[h.xb_all_res], tag="agxb", async_=True)
        with contextlib.ExitStack() as sC:
            emit_blockC_fused(em, h, nc, sC)
        em.barrier()
        with contextlib.ExitStack() as sD:
            emit_blockD(em, h, nc, sD)
        em.barrier()
        with nc.Block() as block:
            em.finalize_and_emit(block, None)
    return nc


def emit_blockB_fused(em, h, nc, ctx):
    emit_blockB(em, h, nc, ctx)


def emit_blockC_fused(em, h, nc, ctx):
    emit_blockC(em, h, nc, ctx)


def prep_fused_inputs(inputs, core):
    m = {}
    m.update(prep_A_inputs(inputs, core))
    dummy = np.zeros((2, 8192, 1024), np.float32)
    mb = prep_B_inputs(inputs, core, dummy); mb.pop("xb_in")
    m.update(mb)
    mc = prep_C_inputs(inputs, core, dummy); mc.pop("xc_in"); mc.pop("xprefix")
    m.update(mc)
    md = prep_D_inputs(inputs, core, dummy); md.pop("xd_in")
    m.update(md)
    return m
